# version 05
# baseline (speedup 1.0000x reference)
import math
from contextlib import ExitStack

import numpy as np
import ml_dtypes

import concourse.bass as bass
import concourse.mybir as mybir
from concourse.bass_core_rust import IndirectOffsetOnAxis
from concourse.bass_utils import run_bass_kernel_spmd

F32 = mybir.dt.float32
BF16 = mybir.dt.bfloat16
I32 = mybir.dt.int32
AF = mybir.ActivationFunctionType
ALU = mybir.AluOpType
AX = mybir.AxisListType

NC = 8
D = 2048
TT = 16384
TC = 2048
NTC = 16
B = 2
SEQ = 8192
KSEL = 1024
SSEL = 2048
HEADS = 16
KVH = 8
HD = 128
IM = 4096
IMLP = 8192
EPS_LN = 1e-5
EPS_RMS = 1e-6
OOB = 1.0e9
BISECT_ITERS = 44
BIG_NEG = -1.0e30
ISQD = 1.0 / math.sqrt(HD)

bf16 = ml_dtypes.bfloat16

_PROG = {}


def _np_f32(a):
    return np.ascontiguousarray(a, dtype=np.float32)



WCOLS = 819200 // NC


def build_p1():
    nc = bass.Bass("TRN2", target_bir_lowering=False)
    x = nc.dram_tensor("x", [TC, D], F32, kind="ExternalInput").ap()
    wrt = nc.dram_tensor("wrt", [128, D], F32, kind="ExternalInput").ap()
    wcat = nc.dram_tensor("wcat", [128, WCOLS], F32, kind="ExternalInput").ap()
    lnx = nc.dram_tensor("lnx", [TC, D], BF16, kind="ExternalOutput").ap()
    sc = nc.dram_tensor("sc", [128, NTC], F32, kind="ExternalOutput").ap()
    wcb = nc.dram_tensor("wcb", [128, WCOLS], BF16, kind="ExternalOutput").ap()

    es = ExitStack()
    with es:
        xt = [es.enter_context(nc.sbuf_tensor(f"xt{i}", [128, D], F32)) for i in range(2)]
        lt = [es.enter_context(nc.sbuf_tensor(f"lt{i}", [128, D], BF16)) for i in range(2)]
        scr = es.enter_context(nc.sbuf_tensor("scr", [128, D], F32))
        wrtb = es.enter_context(nc.sbuf_tensor("wrtb", [128, D], F32))
        w1 = es.enter_context(nc.sbuf_tensor("w1", [128, 1], F32))
        scb = es.enter_context(nc.sbuf_tensor("scb", [128, NTC], F32))
        sx = es.enter_context(nc.sbuf_tensor("sx", [128, 1], F32))
        ssq = es.enter_context(nc.sbuf_tensor("ssq", [128, 1], F32))
        xw = es.enter_context(nc.sbuf_tensor("xw", [128, 1], F32))
        m = es.enter_context(nc.sbuf_tensor("m", [128, 1], F32))
        var = es.enter_context(nc.sbuf_tensor("var", [128, 1], F32))
        sd = es.enter_context(nc.sbuf_tensor("sd", [128, 1], F32))
        rstd = es.enter_context(nc.sbuf_tensor("rstd", [128, 1], F32))
        nb = es.enter_context(nc.sbuf_tensor("nb", [128, 1], F32))
        t0 = es.enter_context(nc.sbuf_tensor("t0", [128, 1], F32))

        ld = es.enter_context(nc.semaphore("ld"))
        cm = es.enter_context(nc.semaphore("cm"))
        sq = es.enter_context(nc.semaphore("sq"))
        st = es.enter_context(nc.semaphore("st"))
        vself = es.enter_context(nc.semaphore("vself"))
        cw = es.enter_context(nc.semaphore("cw"))
        blk = es.enter_context(nc.Block())

        @blk.gpsimd
        def _(g):
            qc = WCOLS // 4
            for i in range(4):
                g.dma_start(out=wcb[:, i * qc:(i + 1) * qc],
                            in_=wcat[:, i * qc:(i + 1) * qc]).then_inc(cw, 16)

        @blk.sync
        def _(sy):
            sy.dma_start(out=wrtb[:], in_=wrt).then_inc(ld, 16)
            for j in range(NTC):
                if j >= 2:
                    sy.wait_ge(cm, 2 * j - 2)
                sy.dma_start(out=xt[j % 2][:], in_=x[j * 128:(j + 1) * 128, :]
                             ).then_inc(ld, 16)

        @blk.vector
        def _(v):
            vs = [0]
            def e(inst):
                vs[0] += 1
                inst.then_inc(vself, 1)
                v.wait_ge(vself, vs[0])
            v.wait_ge(ld, 16)
            v.reduce_sum(w1[:], wrtb[:], axis=AX.X)
            for j in range(NTC):
                xb = xt[j % 2]
                v.wait_ge(ld, 32 + j * 16)
                v.scalar_tensor_tensor(scr[:], xb[:], 1.0, xb[:], ALU.mult,
                                       ALU.mult, accum_out=ssq[:])
                v.reduce_sum(sx[:], xb[:], axis=AX.X)
                v.scalar_tensor_tensor(scr[:], xb[:], 1.0, wrtb[:], ALU.mult,
                                       ALU.mult, accum_out=xw[:])
                e(v.tensor_scalar_mul(m[:], sx[:], 1.0 / D))
                v.tensor_scalar_mul(var[:], ssq[:], 1.0 / D)
                e(v.tensor_mul(t0[:], m[:], m[:]))
                e(v.tensor_sub(var[:], var[:], t0[:]))
                v.tensor_scalar_add(var[:], var[:], EPS_LN).then_inc(cm, 1)
                v.wait_ge(sq, j + 1)
                e(v.reciprocal(rstd[:], sd[:]))
                e(v.tensor_mul(nb[:], m[:], rstd[:]))
                e(v.tensor_scalar_mul(nb[:], nb[:], -1.0))
                if j >= 2:
                    v.wait_ge(st, 16 * (j - 1))
                v.tensor_scalar(lt[j % 2][:], xb[:], rstd[:], nb[:],
                                ALU.mult, ALU.add)
                e(v.tensor_mul(t0[:], m[:], w1[:]))
                e(v.tensor_sub(t0[:], xw[:], t0[:]))
                v.tensor_mul(scb[:, j:j + 1], t0[:], rstd[:]).then_inc(cm, 1)

        @blk.scalar
        def _(s):
            for j in range(NTC):
                s.wait_ge(cm, 2 * j + 1)
                s.sqrt(sd[:], var[:]).then_inc(sq, 1)

        @blk.gpsimd
        def _(g):
            for j in range(NTC):
                g.wait_ge(cm, 2 * j + 2)
                g.dma_start(out=lnx[j * 128:(j + 1) * 128, :], in_=lt[j % 2][:]
                            ).then_inc(st, 16)
            g.dma_start(out=sc, in_=scb[:]).then_inc(st, 16)
            g.wait_ge(st, 16 * NTC + 16)
            g.wait_ge(cw, 64)
    return nc



def build_p2():
    nc = bass.Bass("TRN2", target_bir_lowering=False)
    ssb_d = nc.dram_tensor("ssb", [128, 128], F32, kind="ExternalInput").ap()
    lnxf = nc.dram_tensor("lnxf", [TT, D], BF16, kind="ExternalInput").ap()
    jc_d = nc.dram_tensor("jc", [128, 256], F32, kind="ExternalInput").ap()
    inv_d = nc.dram_tensor("inv", [128, 2], I32, kind="ExternalOutput").ap()
    cidx_d = nc.dram_tensor("cidxo", [128, 128], F32, kind="ExternalOutput").ap()
    selr_d = nc.dram_tensor("selr", [2, 128, D], BF16, kind="ExternalOutput").ap()

    ones_row_f = nc.inline_tensor(np.ones((1, 128), np.float32), name="ones_row_f")
    ones_row_b = nc.inline_tensor(np.ones((1, 128), bf16), name="ones_row_b")
    ones_col_b = nc.inline_tensor(np.ones((128, 1), bf16), name="ones_col_b")
    tri = np.zeros((128, 128), np.float32)
    for f in range(128):
        lo_ = 0 if f < 64 else 64
        tri[lo_:f, f] = 1.0
    trib_c = nc.inline_tensor(tri.astype(bf16), name="trib")
    ptri = (np.tril(np.ones((128, 128), np.float32)).T).astype(bf16)
    ptri_c = nc.inline_tensor(ptri, name="ptri")
    boff = np.zeros((128, 128), np.float32)
    boff[:, :64] = -1.0
    boff[:, 64:] = 1023.0
    boff_c = nc.inline_tensor(boff, name="boff")
    rhs2 = np.zeros((128, 256), np.float32)
    rhs2[:, 0::2] = np.arange(128, dtype=np.float32)[:, None]
    rhs2[:, 1::2] = np.arange(128, dtype=np.float32)[None, :]
    rhs2_c = nc.inline_tensor(rhs2.astype(bf16), name="rhs2")
    bigc = nc.inline_tensor(np.full((128, 128), OOB, np.float32), name="bigc")

    es = ExitStack()
    with es:
        ssb = es.enter_context(nc.sbuf_tensor("ssb_s", [128, 128], F32))
        jc = es.enter_context(nc.sbuf_tensor("jc_s", [128, 256], F32))
        onesrf = es.enter_context(nc.sbuf_tensor("onesrf", [1, 128], F32))
        onesrb = es.enter_context(nc.sbuf_tensor("onesrb", [1, 128], BF16))
        onescb = es.enter_context(nc.sbuf_tensor("onescb", [128, 1], BF16))
        trib = es.enter_context(nc.sbuf_tensor("trib_s", [128, 128], BF16))
        ptrib = es.enter_context(nc.sbuf_tensor("ptri_s", [128, 128], BF16))
        boffs = es.enter_context(nc.sbuf_tensor("boff_s", [128, 128], F32))
        rhs2s = es.enter_context(nc.sbuf_tensor("rhs2_s", [128, 256], BF16))
        bigs = es.enter_context(nc.sbuf_tensor("big_s", [128, 128], F32))

        lo = es.enter_context(nc.sbuf_tensor("lo", [1, 2], F32))
        hi = es.enter_context(nc.sbuf_tensor("hi", [1, 2], F32))
        mid = es.enter_context(nc.sbuf_tensor("mid", [1, 2], F32))
        trow = es.enter_context(nc.sbuf_tensor("trow", [1, 128], F32))
        cnt = es.enter_context(nc.sbuf_tensor("cnt", [1, 2], F32))
        flag = es.enter_context(nc.sbuf_tensor("flag", [1, 2], F32))
        dlt = es.enter_context(nc.sbuf_tensor("dlt", [1, 2], F32))
        m1 = es.enter_context(nc.sbuf_tensor("m1", [1, 2], F32))
        rr = es.enter_context(nc.sbuf_tensor("rr", [1, 2], F32))
        maskb = es.enter_context(nc.sbuf_tensor("maskb", [128, 128], BF16))
        strongb = es.enter_context(nc.sbuf_tensor("strongb", [128, 128], BF16))
        borderb = es.enter_context(nc.sbuf_tensor("borderb", [128, 128], BF16))
        selb = es.enter_context(nc.sbuf_tensor("selb", [128, 128], BF16))
        csrow = es.enter_context(nc.sbuf_tensor("csrow", [1, 128], BF16))
        csb = es.enter_context(nc.sbuf_tensor("csb", [128, 128], BF16))
        cum = es.enter_context(nc.sbuf_tensor("cum", [128, 128], F32))
        tle = es.enter_context(nc.sbuf_tensor("tle", [128, 128], BF16))
        cidx = es.enter_context(nc.sbuf_tensor("cidx", [128, 128], F32))
        cidxm = es.enter_context(nc.sbuf_tensor("cidxm", [128, 128], F32))
        oh = [es.enter_context(nc.sbuf_tensor(f"oh{i}", [128, 128], BF16))
              for i in range(4)]
        invf = es.enter_context(nc.sbuf_tensor("invf", [128, 4], F32))
        invi = es.enter_context(nc.sbuf_tensor("invi", [128, 2], I32))
        selr = es.enter_context(nc.sbuf_tensor("selr_s", [128, 2 * D], BF16))

        ps_tau = es.enter_context(nc.psum_tensor("ps_tau", [128, 128], F32))
        ps_cnt = es.enter_context(nc.psum_tensor("ps_cnt", [1, 128], F32))
        ps_cum = es.enter_context(nc.psum_tensor("ps_cum", [128, 128], F32))
        ps_csb = es.enter_context(nc.psum_tensor("ps_csb", [128, 128], F32))
        ps_inv = [es.enter_context(nc.psum_tensor(f"ps_inv{i}", [128, 2], F32))
                  for i in range(2)]

        ld = es.enter_context(nc.semaphore("ld"))
        v2t = es.enter_context(nc.semaphore("v2t"))
        t2v = es.enter_context(nc.semaphore("t2v"))
        so = es.enter_context(nc.semaphore("so"))
        vself = es.enter_context(nc.semaphore("vself"))
        blk = es.enter_context(nc.Block())

        BV = 2 * BISECT_ITERS
        V_HIP = BV + 1
        V_STR = BV + 2
        V_LOP = BV + 3
        V_BOR = BV + 4
        V_CS1 = BV + 5
        V_CB1 = BV + 6
        V_RRP = BV + 7
        V_SEL = BV + 8
        V_CS2 = BV + 9
        V_CB2 = BV + 10
        V_CID = BV + 11
        V_OH0 = BV + 12
        V_INV = V_OH0 + 256
        T_TAUH = BV + 1
        T_CNTH = BV + 2
        T_TAUL = BV + 3
        T_BOR = BV + 4
        T_CB1 = BV + 5
        T_TRI1 = BV + 6
        T_TAUR = BV + 7
        T_SEL = BV + 9
        T_CB2 = BV + 10
        T_TRI2 = BV + 11
        T_OH0 = BV + 12

        NLD = 10 * 16

        def v_tau_prep(v, src, sem=None):
            v.tensor_copy(trow[:, 0:64], src[0:1, 0:1].to_broadcast([1, 64]))
            i2 = v.tensor_copy(trow[:, 64:128], src[0:1, 1:2].to_broadcast([1, 64]))
            if sem is not None:
                i2.then_inc(sem, 1)

        @blk.sync
        def _(sy):
            sy.dma_start(out=ssb[:], in_=ssb_d).then_inc(ld, 16)
            sy.dma_start(out=jc[:], in_=jc_d).then_inc(ld, 16)
            sy.dma_start(out=onesrf[:], in_=ones_row_f.ap()).then_inc(ld, 16)
            sy.dma_start(out=onesrb[:], in_=ones_row_b.ap()).then_inc(ld, 16)
            sy.dma_start(out=onescb[:], in_=ones_col_b.ap()).then_inc(ld, 16)
            sy.dma_start(out=trib[:], in_=trib_c.ap()).then_inc(ld, 16)
            sy.dma_start(out=ptrib[:], in_=ptri_c.ap()).then_inc(ld, 16)
            sy.dma_start(out=boffs[:], in_=boff_c.ap()).then_inc(ld, 16)
            sy.dma_start(out=rhs2s[:], in_=rhs2_c.ap()).then_inc(ld, 16)
            sy.dma_start(out=bigs[:], in_=bigc.ap()).then_inc(ld, 16)

        @blk.vector
        def _(v):
            vs = [0]
            def e(inst):
                vs[0] += 1
                inst.then_inc(vself, 1)
                v.wait_ge(vself, vs[0])
            v.wait_ge(ld, NLD)
            e(v.memset(lo[:], -512.0))
            e(v.memset(hi[:], 512.0))
            for i in range(BISECT_ITERS):
                e(v.tensor_add(mid[:], lo[:], hi[:]))
                e(v.tensor_scalar_mul(mid[:], mid[:], 0.5))
                v_tau_prep(v, mid, v2t)
                v.wait_ge(t2v, 2 * i + 1)
                v.tensor_tensor(maskb[:], ssb[:], ps_tau[:], ALU.is_ge
                                ).then_inc(v2t, 1)
                v.wait_ge(t2v, 2 * i + 2)
                e(v.reduce_sum(cnt[:, 0:1], ps_cnt[:, 0:64], axis=AX.X))
                e(v.reduce_sum(cnt[:, 1:2], ps_cnt[:, 64:128], axis=AX.X))
                e(v.tensor_scalar(flag[:], cnt[:], float(KSEL), None, ALU.is_ge))
                e(v.tensor_sub(dlt[:], mid[:], lo[:]))
                e(v.tensor_mul(dlt[:], dlt[:], flag[:]))
                e(v.tensor_add(lo[:], lo[:], dlt[:]))
                e(v.tensor_sub(dlt[:], hi[:], mid[:]))
                e(v.tensor_mul(dlt[:], dlt[:], flag[:]))
                e(v.tensor_add(hi[:], mid[:], dlt[:]))
            v_tau_prep(v, hi, v2t)
            v.wait_ge(t2v, T_TAUH)
            v.tensor_tensor(strongb[:], ssb[:], ps_tau[:], ALU.is_ge
                            ).then_inc(v2t, 1)
            v.wait_ge(t2v, T_CNTH)
            e(v.reduce_sum(m1[:, 0:1], ps_cnt[:, 0:64], axis=AX.X))
            e(v.reduce_sum(m1[:, 1:2], ps_cnt[:, 64:128], axis=AX.X))
            e(v.tensor_scalar(rr[:], m1[:], float(KSEL), None, ALU.subtract))
            e(v.tensor_scalar_mul(rr[:], rr[:], -1.0))
            v_tau_prep(v, lo, v2t)
            v.wait_ge(t2v, T_TAUL)
            e(v.tensor_tensor(borderb[:], ssb[:], ps_tau[:], ALU.is_ge))
            v.tensor_sub(borderb[:], borderb[:], strongb[:]).then_inc(v2t, 1)
            v.wait_ge(t2v, T_BOR)
            v.tensor_copy(csrow[:], ps_cnt[:]).then_inc(v2t, 1)
            v.wait_ge(t2v, T_CB1)
            v.tensor_copy(csb[:], ps_csb[:]).then_inc(v2t, 1)
            v_tau_prep(v, rr, v2t)
            v.wait_ge(t2v, T_TAUR)
            e(v.tensor_copy(cum[:], ps_cum[:]))
            e(v.tensor_tensor(tle[:], cum[:], ps_tau[:], ALU.is_le))
            e(v.tensor_mul(tle[:], tle[:], borderb[:]))
            v.tensor_add(selb[:], strongb[:], tle[:]).then_inc(v2t, 1)
            v.wait_ge(t2v, T_SEL)
            v.tensor_copy(csrow[:], ps_cnt[:]).then_inc(v2t, 1)
            v.wait_ge(t2v, T_CB2)
            v.tensor_copy(csb[:], ps_csb[:]).then_inc(v2t, 1)
            v.wait_ge(t2v, T_TRI2)
            e(v.tensor_tensor(cidx[:], ps_cum[:], boffs[:], ALU.add))
            e(v.tensor_tensor(cidx[:], cidx[:], selb[:], ALU.mult))
            e(v.tensor_scalar(tle[:], selb[:], 1.0, None, ALU.subtract))
            e(v.tensor_scalar(cidx2t := tle[:], tle[:], -OOB, None, ALU.mult))
            v.tensor_tensor(cidxm[:], cidx[:], tle[:], ALU.add).then_inc(v2t, 1)
            for k in range(256):
                f, u = k // 2, k % 2
                slot = k % 4
                if k >= 4:
                    v.wait_ge(t2v, T_OH0 + (k - 4))
                v.tensor_scalar(oh[slot][:], jc[:, u * 128:(u + 1) * 128],
                                cidxm[:, f:f + 1], None, ALU.is_equal
                                ).then_inc(v2t, 1)
            v.wait_ge(t2v, T_OH0 + 255)
            e(v.tensor_copy(invf[:, 0:2], ps_inv[0][:]))
            e(v.tensor_copy(invf[:, 2:4], ps_inv[1][:]))
            e(v.tensor_scalar_mul(invf[:, 1:2], invf[:, 1:2], 128.0))
            e(v.tensor_add(invf[:, 0:1], invf[:, 0:1], invf[:, 1:2]))
            e(v.tensor_scalar_mul(invf[:, 3:4], invf[:, 3:4], 128.0))
            e(v.tensor_add(invf[:, 2:3], invf[:, 2:3], invf[:, 3:4]))
            e(v.tensor_copy(invi[:, 0:1], invf[:, 0:1]))
            v.tensor_copy(invi[:, 1:2], invf[:, 2:3]).then_inc(v2t, 1)

        @blk.tensor
        def _(t):
            t.wait_ge(ld, NLD)
            for i in range(BISECT_ITERS):
                t.wait_ge(v2t, 2 * i + 1)
                t.matmul(ps_tau[:], onesrf[:], trow[:], start=True, stop=True
                         ).then_inc(t2v, 1)
                t.wait_ge(v2t, 2 * i + 2)
                t.matmul(ps_cnt[:], onescb[:], maskb[:], start=True, stop=True
                         ).then_inc(t2v, 1)
            t.wait_ge(v2t, V_HIP)
            t.matmul(ps_tau[:], onesrf[:], trow[:], start=True, stop=True
                     ).then_inc(t2v, 1)
            t.wait_ge(v2t, V_STR)
            t.matmul(ps_cnt[:], onescb[:], strongb[:], start=True, stop=True
                     ).then_inc(t2v, 1)
            t.wait_ge(v2t, V_LOP)
            t.matmul(ps_tau[:], onesrf[:], trow[:], start=True, stop=True
                     ).then_inc(t2v, 1)
            t.wait_ge(v2t, V_BOR)
            t.matmul(ps_cum[:], ptrib[:], borderb[:], start=True, stop=False)
            t.matmul(ps_cnt[:], onescb[:], borderb[:], start=True, stop=True
                     ).then_inc(t2v, 1)
            t.wait_ge(v2t, V_CS1)
            t.matmul(ps_csb[:], csrow[:], onesrb[:], start=True, stop=True
                     ).then_inc(t2v, 1)
            t.wait_ge(v2t, V_CB1)
            t.matmul(ps_cum[:], csb[:], trib[:], start=False, stop=True
                     ).then_inc(t2v, 1)
            t.wait_ge(v2t, V_RRP)
            t.matmul(ps_tau[:], onesrf[:], trow[:], start=True, stop=True
                     ).then_inc(t2v, 1)
            t.wait_ge(v2t, V_SEL)
            t.matmul(ps_cum[:], ptrib[:], selb[:], start=True, stop=False)
            t.matmul(ps_cnt[:], onescb[:], selb[:], start=True, stop=True
                     ).then_inc(t2v, 2)
            t.wait_ge(v2t, V_CS2)
            t.matmul(ps_csb[:], csrow[:], onesrb[:], start=True, stop=True
                     ).then_inc(t2v, 1)
            t.wait_ge(v2t, V_CB2)
            t.matmul(ps_cum[:], csb[:], trib[:], start=False, stop=True
                     ).then_inc(t2v, 1)
            for k in range(256):
                f, u = k // 2, k % 2
                slot = k % 4
                t.wait_ge(v2t, V_OH0 + k)
                t.matmul(ps_inv[u][:], oh[slot][:], rhs2s[:, 2 * f:2 * f + 2],
                         start=(f == 0), stop=(f == 127)).then_inc(t2v, 1)

        @blk.gpsimd
        def _(g):
            g.wait_ge(v2t, V_CID)
            g.dma_start(out=cidx_d, in_=cidxm[:]).then_inc(so, 16)
            g.wait_ge(v2t, V_INV)
            g.dma_start(out=inv_d, in_=invi[:]).then_inc(so, 16)
            for a in range(2):
                g.indirect_dma_start(
                    selr[:, a * D:(a + 1) * D], None,
                    lnxf, IndirectOffsetOnAxis(invi[:, a:a + 1], 0),
                    bounds_check=TT - 1, oob_is_err=False,
                ).then_inc(so, 16)
            g.wait_ge(so, 64)
            g.dma_start(out=selr_d[0], in_=selr[:, 0:D]).then_inc(so, 16)
            g.dma_start(out=selr_d[1], in_=selr[:, D:2 * D]).then_inc(so, 16)
            g.wait_ge(so, 96)
    return nc



BATCH_OF_PCOL = [0, 1] + [0] * 7 + [1] * 7


def build_p3():
    nc = bass.Bass("TRN2", target_bir_lowering=False)
    sel_d = nc.dram_tensor("sel", [SSEL, D], BF16, kind="ExternalInput").ap()
    inv_d = nc.dram_tensor("invp", [128, NTC], I32, kind="ExternalInput").ap()
    costab = nc.dram_tensor("costab", [SEQ, 256], F32, kind="ExternalInput").ap()
    mask_d = nc.dram_tensor("mask", [2, 128, SSEL], F32, kind="ExternalInput").ap()
    wq_d = nc.dram_tensor("wq_l", [HEADS, 128, 16 * 128], BF16, kind="ExternalInput").ap()
    wk_d = nc.dram_tensor("wk_l", [KVH, 128, 16 * 128], BF16, kind="ExternalInput").ap()
    wv_d = nc.dram_tensor("wv_l", [KVH, 128, 16 * 128], BF16, kind="ExternalInput").ap()
    wo_d = nc.dram_tensor("wo_l", [16, 128, 16 * 128], BF16, kind="ExternalInput").ap()
    wg_d = nc.dram_tensor("wg_l", [64, 128, 16 * 128], BF16, kind="ExternalInput").ap()
    wu_d = nc.dram_tensor("wu_l", [64, 128, 16 * 128], BF16, kind="ExternalInput").ap()
    wd_d = nc.dram_tensor("wd_l", [64, 128, 16 * 128], BF16, kind="ExternalInput").ap()
    bq_d = nc.dram_tensor("bq_fm", [128, HEADS], F32, kind="ExternalInput").ap()
    bk_d = nc.dram_tensor("bk_fm", [128, KVH], F32, kind="ExternalInput").ap()
    bv_d = nc.dram_tensor("bv_fm", [128, KVH], F32, kind="ExternalInput").ap()
    ln1_d = nc.dram_tensor("ln1b", [128, D], F32, kind="ExternalInput").ap()
    ln2_d = nc.dram_tensor("ln2_fm", [128, NTC], F32, kind="ExternalInput").ap()
    proc_d = nc.dram_tensor("proc", [2, 128, D], F32, kind="ExternalOutput").ap()

    idn_c = nc.inline_tensor(np.eye(128, dtype=bf16), name="idn")
    idnf_c = nc.inline_tensor(np.eye(128, dtype=np.float32), name="idnf")
    rmat = np.zeros((128, 128), np.float32)
    for dp in range(64):
        rmat[dp + 64, dp] = -1.0
    for dp in range(64, 128):
        rmat[dp - 64, dp] = 1.0
    rmat_c = nc.inline_tensor(rmat.astype(bf16), name="rmat")
    onescb_c = nc.inline_tensor(np.ones((128, 1), bf16), name="onescb3")
    onesrb_c = nc.inline_tensor(np.ones((1, 128), bf16), name="onesrb3")
    pofs = np.zeros((128, NTC), np.float32)
    for k in range(NTC):
        pofs[:, k] = 8192.0 * BATCH_OF_PCOL[k]
    pofs_c = nc.inline_tensor(pofs, name="pofs")

    es = ExitStack()
    with es:
        buf1 = es.enter_context(nc.sbuf_tensor("buf1", [128, 32768], BF16))
        buf2 = es.enter_context(nc.sbuf_tensor("buf2", [128, NTC * D], BF16))
        x1 = es.enter_context(nc.sbuf_tensor("x1", [128, 4096], F32))
        pstage = es.enter_context(nc.sbuf_tensor("pstage", [128, 1024], F32))
        sfm_scrb = es.enter_context(nc.sbuf_tensor("sfm_scrb", [128, 4096], BF16))
        qsb = es.enter_context(nc.sbuf_tensor("qsb", [128, 4096], BF16))
        attsb = es.enter_context(nc.sbuf_tensor("attsb", [128, SSEL], BF16))
        attT = es.enter_context(nc.sbuf_tensor("attT", [128, SSEL], BF16))
        cosb = es.enter_context(nc.sbuf_tensor("cosb", [128, SSEL], BF16))
        sinb = es.enter_context(nc.sbuf_tensor("sinb", [128, SSEL], BF16))
        kro = es.enter_context(nc.sbuf_tensor("kro", [128, 512], BF16))
        scrf = es.enter_context(nc.sbuf_tensor("scrf", [128, 512], F32))
        scrf2 = es.enter_context(nc.sbuf_tensor("scrf2", [128, 512], F32))
        wslab = [es.enter_context(nc.sbuf_tensor(f"wslab{i}", [128, 2048], BF16))
                 for i in range(4)]
        bqs = es.enter_context(nc.sbuf_tensor("bqs", [128, HEADS], F32))
        bks = es.enter_context(nc.sbuf_tensor("bks", [128, KVH], F32))
        bvs = es.enter_context(nc.sbuf_tensor("bvs", [128, KVH], F32))
        ln2s = es.enter_context(nc.sbuf_tensor("ln2s", [128, NTC], F32))
        idn = es.enter_context(nc.sbuf_tensor("idn_s", [128, 128], BF16))
        idnf = es.enter_context(nc.sbuf_tensor("idnf_s", [128, 128], F32))
        rmt = es.enter_context(nc.sbuf_tensor("rmt_s", [128, 128], BF16))
        invs = es.enter_context(nc.sbuf_tensor("invs", [128, NTC], I32))
        posf = es.enter_context(nc.sbuf_tensor("posf", [128, NTC], F32))
        posi = es.enter_context(nc.sbuf_tensor("posi", [128, NTC], I32))
        pofs_s = es.enter_context(nc.sbuf_tensor("pofs_s", [128, NTC], F32))
        ssq = es.enter_context(nc.sbuf_tensor("ssq3", [128, 1], F32))
        rstd = es.enter_context(nc.sbuf_tensor("rstd3", [128, 1], F32))
        mx = es.enter_context(nc.sbuf_tensor("mx3", [128, 1], F32))
        nb2 = es.enter_context(nc.sbuf_tensor("nb23", [128, 1], F32))
        asum = es.enter_context(nc.sbuf_tensor("asum3", [128, 1], F32))
        rec = es.enter_context(nc.sbuf_tensor("rec3", [128, 1], F32))
        var2 = es.enter_context(nc.sbuf_tensor("var2", [1, 256], F32))
        rstd2 = es.enter_context(nc.sbuf_tensor("rstd2", [1, 256], F32))
        rstd2b = es.enter_context(nc.sbuf_tensor("rstd2b", [1, 256], BF16))
        onescb = es.enter_context(nc.sbuf_tensor("onescb3s", [128, 1], BF16))
        onesrb = es.enter_context(nc.sbuf_tensor("onesrb3s", [1, 128], BF16))

        psBig = es.enter_context(nc.psum_tensor("psBig", [128, 2048], F32))
        psA = es.enter_context(nc.psum_tensor("psA", [128, 512], F32))
        psW = es.enter_context(nc.psum_tensor("psW", [128, 512], F32))
        psT = es.enter_context(nc.psum_tensor("psT", [128, 1024], F32))

        def psTs(i):
            return psT[:, (i % 2) * 512:(i % 2) * 512 + 128]

        def psAV(a):
            return psA[:, 0:128] if a == 0 else psW[:, 0:128]

        ld = es.enter_context(nc.semaphore("ld"))
        gw = es.enter_context(nc.semaphore("gw"))
        gsl = [es.enter_context(nc.semaphore(f"gsl{i}")) for i in range(4)]
        tv = es.enter_context(nc.semaphore("tv"))
        vt = es.enter_context(nc.semaphore("vt"))
        sv = es.enter_context(nc.semaphore("sv"))
        vself = es.enter_context(nc.semaphore("vself"))
        so = es.enter_context(nc.semaphore("so"))
        blk = es.enter_context(nc.Block())

        C = dict(ld=0, gw=0, tv=0, vt=0, sv=0, so=0)
        G = [0, 0, 0, 0]
        ve_n = [0]

        def VE(v, inst):
            ve_n[0] += 1
            inst.then_inc(vself, 1)
            v.wait_ge(vself, ve_n[0])

        def kfm(kvh):
            return buf1[:, kvh * 2048:(kvh + 1) * 2048]

        def vtm(k):
            return buf1[:, 16384 + k * 1024:16384 + (k + 1) * 1024]

        def hfm(pt):
            return buf2[:, pt * 2048:(pt + 1) * 2048]

        ln1b_sb = None

        @blk.sync
        def _(sy):
            for k in range(NTC):
                sy.dma_start(out=buf1[:, k * D:(k + 1) * D],
                             in_=sel_d[k * 128:(k + 1) * 128, :]).then_inc(ld, 16)
            sy.dma_start(out=bqs[:], in_=bq_d).then_inc(ld, 16)
            sy.dma_start(out=bks[:], in_=bk_d).then_inc(ld, 16)
            sy.dma_start(out=bvs[:], in_=bv_d).then_inc(ld, 16)
            sy.dma_start(out=ln2s[:], in_=ln2_d).then_inc(ld, 16)
            sy.dma_start(out=idn[:], in_=idn_c.ap()).then_inc(ld, 16)
            sy.dma_start(out=idnf[:], in_=idnf_c.ap()).then_inc(ld, 16)
            sy.dma_start(out=rmt[:], in_=rmat_c.ap()).then_inc(ld, 16)
            sy.dma_start(out=onescb[:], in_=onescb_c.ap()).then_inc(ld, 16)
            sy.dma_start(out=onesrb[:], in_=onesrb_c.ap()).then_inc(ld, 16)
            sy.dma_start(out=pofs_s[:], in_=pofs_c.ap()).then_inc(ld, 16)
            sy.dma_start(out=invs[:], in_=inv_d).then_inc(ld, 16)

        @blk.gpsimd
        def _(g):
            g.dma_start(out=cosb[:], in_=ln1_d).then_inc(gw, 16)

        C["ld"] += (NTC + 11) * 16
        C["gw"] += 16
        LD_ALL = C["ld"]
        GW_LN1 = C["gw"]

        T0B = C["tv"]
        S0B = C["sv"]

        @blk.tensor
        def _(t):
            t.wait_ge(ld, LD_ALL)
            for i in range(32):
                k, pt = i % 2, i // 2
                if i >= 2:
                    t.wait_ge(sv, S0B + i - 1)
                t.matmul(psTs(i),
                         buf1[:, k * D + pt * 128:k * D + pt * 128 + 128],
                         idn[:], start=True, stop=True).then_inc(tv, 1)

        @blk.scalar
        def _(s):
            for i in range(32):
                k, pt = i % 2, i // 2
                s.wait_ge(tv, T0B + i + 1)
                s.copy(sfm_scrb[:, pt * 256 + k * 128:pt * 256 + (k + 1) * 128],
                       psTs(i)).then_inc(sv, 1)
        C["tv"] += 32
        C["sv"] += 32
        SV_OWN = C["sv"]

        VT1 = C["vt"]
        SV1 = C["sv"]

        @blk.vector
        def _(v):
            v.wait_ge(ld, LD_ALL)
            v.wait_ge(gw, GW_LN1)
            v.wait_ge(sv, SV_OWN)
            for k in range(NTC):
                tile = buf1[:, k * D:(k + 1) * D]
                VE(v, v.scalar_tensor_tensor(scrf[:], tile[:, 0:512], 1.0,
                                             tile[:, 0:512], ALU.mult,
                                             ALU.mult, accum_out=ssq[:]))
                for q4 in range(1, 4):
                    VE(v, v.scalar_tensor_tensor(
                        scrf[:], tile[:, q4 * 512:(q4 + 1) * 512], 1.0,
                        tile[:, q4 * 512:(q4 + 1) * 512], ALU.mult, ALU.mult,
                        accum_out=rec[:]))
                    VE(v, v.tensor_add(ssq[:], ssq[:], rec[:]))
                VE(v, v.tensor_scalar(ssq[:], ssq[:], 1.0 / D, EPS_RMS,
                                      ALU.mult, ALU.add))
                v.sem_inc(vt, 1)
                v.wait_ge(sv, SV1 + k + 1)
                VE(v, v.reciprocal(rstd[:], rstd[:]))
                VE(v, v.scalar_tensor_tensor(tile, tile, rstd[:], cosb[:],
                                             ALU.mult, ALU.mult))
            v.sem_inc(vt, 1)

        @blk.scalar
        def _(s):
            for k in range(NTC):
                s.wait_ge(vt, VT1 + k + 1)
                s.sqrt(rstd[:], ssq[:]).then_inc(sv, 1)
        C["vt"] += NTC + 1
        C["sv"] += NTC
        VT_H = C["vt"]

        @blk.vector
        def _(v):
            VE(v, v.tensor_copy(posf[:], invs[:]))
            VE(v, v.tensor_sub(posf[:], posf[:], pofs_s[:]))
            v.tensor_copy(posi[:], posf[:]).then_inc(vt, 1)
        C["vt"] += 1
        VT_POS = C["vt"]

        GW2 = C["gw"]
        TV2 = C["tv"]
        VT2b = C["vt"]

        @blk.gpsimd
        def _(g):
            g.wait_ge(vt, VT_POS)
            g.wait_ge(vt, VT_H)
            for qtr in range(4):
                if qtr >= 1:
                    g.wait_ge(vt, VT2b + 8 * qtr)
                for kk in range(4):
                    k = qtr * 4 + kk
                    g.indirect_dma_start(
                        pstage[:, kk * 256:(kk + 1) * 256], None, costab,
                        IndirectOffsetOnAxis(posi[:, k:k + 1], 0),
                        bounds_check=SEQ - 1, oob_is_err=False,
                    ).then_inc(gw, 16)

        @blk.tensor
        def _(t):
            for qtr in range(4):
                for kk in range(4):
                    k = qtr * 4 + kk
                    t.wait_ge(gw, GW2 + (qtr + 1) * 64)
                    if k >= 1:
                        t.wait_ge(vt, VT2b + 2 * k)
                    t.transpose(psA[:, 0:128],
                                pstage[:, kk * 256:kk * 256 + 128], idnf[:]
                                ).then_inc(tv, 1)
                    t.transpose(psW[:, 0:128],
                                pstage[:, kk * 256 + 128:kk * 256 + 256],
                                idnf[:]).then_inc(tv, 1)

        @blk.vector
        def _(v):
            for k in range(NTC):
                v.wait_ge(tv, TV2 + 2 * k + 1)
                v.tensor_copy(cosb[:, k * 128:(k + 1) * 128], psA[:, 0:128]
                              ).then_inc(vt, 1)
                v.wait_ge(tv, TV2 + 2 * k + 2)
                v.tensor_copy(sinb[:, k * 128:(k + 1) * 128], psW[:, 0:128]
                              ).then_inc(vt, 1)
        C["gw"] += 256
        C["tv"] += 32
        C["vt"] += 32

        T2B = C["tv"]
        S2B = C["sv"]

        @blk.tensor
        def _(t):
            t.wait_ge(vt, VT_H)
            for i in range(256):
                k, pt = i % NTC, i // NTC
                if i >= 2:
                    t.wait_ge(sv, S2B + i - 1)
                t.matmul(psTs(i),
                         buf1[:, k * D + pt * 128:k * D + pt * 128 + 128],
                         idn[:], start=True, stop=True).then_inc(tv, 1)

        @blk.scalar
        def _(s):
            for i in range(256):
                k, pt = i % NTC, i // NTC
                s.wait_ge(tv, T2B + i + 1)
                s.copy(buf2[:, pt * D + k * 128:pt * D + (k + 1) * 128],
                       psTs(i)).then_inc(sv, 1)
        C["tv"] += 256
        C["sv"] += 256
        SV_HFM = C["sv"]

        GW4 = C["gw"]
        TV4 = C["tv"]
        VT4 = C["vt"]

        G4 = list(G)

        @blk.gpsimd
        def _(g):
            g.wait_ge(sv, SV_HFM)
            for kvh in range(KVH):
                if kvh >= 2:
                    g.wait_ge(vt, VT4 + 12 * (kvh - 1))
                g.dma_start(out=wslab[kvh % 2][:], in_=wk_d[kvh]
                            ).then_inc(gsl[kvh % 2], 16)

        @blk.tensor
        def _(t):
            t.wait_ge(vt, VT4)
            for kvh in range(KVH):
                t.wait_ge(gsl[kvh % 2], G4[kvh % 2] + (kvh // 2 + 1) * 16)
                for ch in range(4):
                    if kvh + ch > 0:
                        t.wait_ge(vt, VT4 + 12 * kvh + 3 * ch)
                    for pt in range(NTC):
                        mmi = t.matmul(psA[:],
                                 wslab[kvh % 2][:, pt * 128:(pt + 1) * 128],
                                 hfm(pt)[:, ch * 512:(ch + 1) * 512],
                                 start=(pt == 0), stop=(pt == NTC - 1))
                    mmi.then_inc(tv, 1)
                    t.wait_ge(vt, VT4 + 12 * kvh + 3 * ch + 2)
                    t.matmul(psW[:], rmt[:], kro[:], start=True, stop=True
                             ).then_inc(tv, 1)

        @blk.vector
        def _(v):
            for kvh in range(KVH):
                for ch in range(4):
                    cs = slice(ch * 512, (ch + 1) * 512)
                    v.wait_ge(tv, TV4 + kvh * 8 + ch * 2 + 1)
                    VE(v, v.tensor_scalar(scrf[:], psA[:], bks[:, kvh:kvh + 1],
                                          None, ALU.add))
                    VE(v, v.tensor_copy(kro[:], scrf[:]))
                    v.sem_inc(vt, 2)
                    v.wait_ge(tv, TV4 + kvh * 8 + ch * 2 + 2)
                    VE(v, v.tensor_tensor(scrf2[:], psW[:], sinb[:, cs],
                                          ALU.mult))
                    VE(v, v.tensor_tensor(scrf[:], scrf[:], cosb[:, cs],
                                          ALU.mult))
                    VE(v, v.tensor_tensor(kfm(kvh)[:, cs], scrf[:], scrf2[:],
                                          ALU.add))
                    v.sem_inc(vt, 1)
        G[0] += (KVH // 2) * 16
        G[1] += (KVH // 2) * 16
        C["tv"] += KVH * 8
        C["vt"] += KVH * 12

        GW4V = C["gw"]
        TV4V = C["tv"]
        VT4V = C["vt"]
        SV4V = C["sv"]

        G4V = list(G)

        @blk.gpsimd
        def _(g):
            g.wait_ge(vt, VT4V)
            for kvh in range(KVH):
                if kvh >= 2:
                    g.wait_ge(vt, VT4V + 8 * (kvh - 1))
                g.dma_start(out=wslab[kvh % 2][:], in_=wv_d[kvh]
                            ).then_inc(gsl[kvh % 2], 16)

        @blk.tensor
        def _(t):
            t.wait_ge(vt, VT4V)
            for kvh in range(KVH):
                t.wait_ge(gsl[kvh % 2], G4V[kvh % 2] + (kvh // 2 + 1) * 16)
                for ch in range(4):
                    if kvh + ch > 0:
                        t.wait_ge(vt, VT4V + 8 * kvh + 2 * ch)
                    for pt in range(NTC):
                        mmi = t.matmul(psA[:],
                                 wslab[kvh % 2][:, pt * 128:(pt + 1) * 128],
                                 hfm(pt)[:, ch * 512:(ch + 1) * 512],
                                 start=(pt == 0), stop=(pt == NTC - 1))
                    mmi.then_inc(tv, 1)
                    t.wait_ge(vt, VT4V + 8 * kvh + 2 * ch + 2)
                    for b4 in range(4):
                        gidx = kvh * 16 + ch * 4 + b4
                        if gidx >= 2:
                            t.wait_ge(sv, SV4V + gidx - 1)
                        t.matmul(psTs(b4),
                                 kro[:, b4 * 128:(b4 + 1) * 128],
                                 idn[:], start=True, stop=True
                                 ).then_inc(tv, 1)

        @blk.vector
        def _(v):
            for kvh in range(KVH):
                for ch in range(4):
                    v.wait_ge(tv, TV4V + kvh * 20 + ch * 5 + 1)
                    VE(v, v.tensor_scalar(scrf[:], psA[:], bvs[:, kvh:kvh + 1],
                                          None, ALU.add))
                    VE(v, v.tensor_copy(kro[:], scrf[:]))
                    v.sem_inc(vt, 2)

        @blk.scalar
        def _(s):
            for kvh in range(KVH):
                for ch in range(4):
                    for b4 in range(4):
                        s.wait_ge(tv, TV4V + kvh * 20 + ch * 5 + 2 + b4)
                        ktok = ch * 4 + b4
                        s.copy(vtm(ktok)[:, kvh * 128:(kvh + 1) * 128],
                               psTs(b4)
                               ).then_inc(sv, 1)
        G[0] += (KVH // 2) * 16
        G[1] += (KVH // 2) * 16
        C["tv"] += KVH * 20
        C["vt"] += KVH * 8
        C["sv"] += KVH * 16

        GW5 = C["gw"]
        TV5 = C["tv"]
        VT5 = C["vt"]

        G5 = list(G)

        @blk.gpsimd
        def _(g):
            g.wait_ge(vt, VT5)
            for h in range(HEADS):
                if h >= 2:
                    g.wait_ge(vt, VT5 + 5 * (h - 1))
                g.dma_start(out=wslab[h % 2][:], in_=wq_d[h]
                            ).then_inc(gsl[h % 2], 16)

        @blk.tensor
        def _(t):
            t.wait_ge(vt, VT5)
            for h in range(HEADS):
                t.wait_ge(gsl[h % 2], G5[h % 2] + (h // 2 + 1) * 16)
                if h > 0:
                    t.wait_ge(vt, VT5 + 5 * (h - 1) + 2)
                for pt in range(NTC):
                    mmi = t.matmul(psA[:, 0:256],
                             wslab[h % 2][:, pt * 128:(pt + 1) * 128],
                             hfm(pt)[:, 0:256],
                             start=(pt == 0), stop=(pt == NTC - 1))
                mmi.then_inc(tv, 1)
                t.wait_ge(vt, VT5 + 5 * h + 2)
                t.matmul(psW[:, 0:256], rmt[:], kro[:, 0:256], start=True,
                         stop=True).then_inc(tv, 1)

        @blk.vector
        def _(v):
            for h in range(HEADS):
                v.wait_ge(tv, TV5 + 2 * h + 1)
                VE(v, v.tensor_scalar(scrf[:, 0:256], psA[:, 0:256],
                                      bqs[:, h:h + 1], None, ALU.add))
                VE(v, v.tensor_copy(kro[:, 0:256], scrf[:, 0:256]))
                v.sem_inc(vt, 2)
                v.wait_ge(tv, TV5 + 2 * h + 2)
                VE(v, v.tensor_tensor(scrf2[:, 0:256], psW[:, 0:256],
                                      sinb[:, 0:256], ALU.mult))
                VE(v, v.tensor_tensor(scrf[:, 0:256], scrf[:, 0:256],
                                      cosb[:, 0:256], ALU.mult))
                VE(v, v.tensor_tensor(qsb[:, h * 256:(h + 1) * 256],
                                      scrf[:, 0:256], scrf2[:, 0:256],
                                      ALU.add))
                v.sem_inc(vt, 3)
        G[0] += (HEADS // 2) * 16
        G[1] += (HEADS // 2) * 16
        C["tv"] += HEADS * 2
        C["vt"] += HEADS * 5
        VT_Q = C["vt"]

        @blk.gpsimd
        def _(g):
            g.wait_ge(vt, VT_Q)
            g.dma_start(out=cosb[:], in_=mask_d[0]).then_inc(gw, 16)
            g.dma_start(out=sinb[:], in_=mask_d[1]).then_inc(gw, 16)
        C["gw"] += 32
        GW_MASK = C["gw"]

        TV6 = C["tv"]
        VT6 = C["vt"]
        SV6 = C["sv"]
        TPI, VPI, SPI = 6, 3, 5

        def sc_inc(it):
            return TV6 + (it + 1 if it < 2 else 6 * it - 4)

        def trav_base(k):
            return TV6 + min(k + 2, 32) + 5 * k

        @blk.tensor
        def _(t):
            t.wait_ge(gw, GW_MASK)
            t.wait_ge(sv, SV6)

            def scores(it):
                a, h = it % 2, it // 2
                kvh = h // 2
                if it >= 1:
                    t.wait_ge(sv, SV6 + (it - 1) * SPI + 1)
                for ch in range(4):
                    mmi = t.matmul(psBig[:, ch * 512:(ch + 1) * 512],
                             qsb[:, h * 256 + a * 128:h * 256 + (a + 1) * 128],
                             kfm(kvh)[:, ch * 512:(ch + 1) * 512],
                             start=True, stop=True)
                mmi.then_inc(tv, 1)

            def trav(it):
                a, h = it % 2, it // 2
                kvh = h // 2
                t.wait_ge(vt, VT6 + 3 * it + 2)
                if it >= 2:
                    t.wait_ge(vt, VT6 + 3 * (it - 2) + 3)
                for b4 in range(4):
                    gb = it * 4 + b4
                    if gb >= 2:
                        pit, pb = (gb - 2) // 4, (gb - 2) % 4
                        t.wait_ge(sv, SV6 + pit * SPI + 2 + pb)
                    for q4 in range(4):
                        kt = b4 * 4 + q4
                        mmi = t.matmul(
                            psT[:, (b4 % 2) * 512 + q4 * 128:
                                (b4 % 2) * 512 + (q4 + 1) * 128],
                            attsb[:, kt * 128:(kt + 1) * 128], idn[:],
                            start=True, stop=True)
                    mmi.then_inc(tv, 1)
                for b4 in range(4):
                    t.wait_ge(sv, SV6 + it * SPI + 2 + b4)
                    for q4 in range(4):
                        kt = b4 * 4 + q4
                        mmi = t.matmul(psAV(a),
                                 vtm(kt)[:, kvh * 128:(kvh + 1) * 128],
                                 attT[:, kt * 128:(kt + 1) * 128],
                                 start=(kt == 0), stop=(kt == NTC - 1))
                mmi.then_inc(tv, 1)

            scores(0)
            scores(1)
            for k in range(32):
                trav(k)
                if k + 2 < 32:
                    scores(k + 2)

        @blk.vector
        def _(v):
            for it in range(32):
                a, h = it % 2, it // 2
                vb = VT6 + it * VPI
                v.wait_ge(tv, sc_inc(it))
                mbuf = cosb if a == 0 else sinb
                VE(v, v.tensor_tensor(psBig[:], psBig[:], mbuf[:], ALU.add))
                VE(v, v.reduce_max(mx[:], psBig[:], axis=AX.X))
                v.tensor_scalar_mul(nb2[:], mx[:], -ISQD).then_inc(vt, 1)
                v.wait_ge(sv, SV6 + it * SPI + 1)
                VE(v, v.reciprocal(rec[:], asum[:]))
                v.tensor_scalar(attsb[:], attsb[:], rec[:], None, ALU.mult
                                ).then_inc(vt, 1)
                v.wait_ge(tv, trav_base(it) + 5)
                v.tensor_copy(qsb[:, h * 256 + a * 128:h * 256 + (a + 1) * 128],
                              psAV(a)).then_inc(vt, 1)

        @blk.scalar
        def _(s):
            for it in range(32):
                vb = VT6 + it * VPI
                if it >= 1:
                    s.wait_ge(tv, trav_base(it - 1) + 4)
                s.wait_ge(vt, vb + 1)
                s.activation(attsb[:], psBig[:], AF.Exp, bias=nb2[:],
                             scale=ISQD, accum_out=asum[:]).then_inc(sv, 1)
                for b4 in range(4):
                    s.wait_ge(tv, trav_base(it) + 1 + b4)
                    s.copy(attT[:, b4 * 512:(b4 + 1) * 512],
                           psT[:, (b4 % 2) * 512:(b4 % 2) * 512 + 512]
                           ).then_inc(sv, 1)
        C["tv"] += 32 * TPI
        C["vt"] += 32 * VPI
        C["sv"] += 32 * SPI

        GW7 = C["gw"]
        TV7 = C["tv"]
        VT7 = C["vt"]

        G7 = list(G)

        @blk.gpsimd
        def _(g):
            g.wait_ge(vt, VT7)
            for mm_ in range(16):
                if mm_ >= 2:
                    g.wait_ge(vt, VT7 + mm_ - 1)
                g.dma_start(out=wslab[mm_ % 2][:], in_=wo_d[mm_]
                            ).then_inc(gsl[mm_ % 2], 16)

        @blk.tensor
        def _(t):
            t.wait_ge(vt, VT7)
            for mm_ in range(16):
                t.wait_ge(gsl[mm_ % 2], G7[mm_ % 2] + (mm_ // 2 + 1) * 16)
                if mm_ > 0:
                    t.wait_ge(vt, VT7 + mm_)
                for kt in range(NTC):
                    mmi = t.matmul(psW[:, 0:256],
                             wslab[mm_ % 2][:, kt * 128:(kt + 1) * 128],
                             qsb[:, kt * 256:(kt + 1) * 256],
                             start=(kt == 0), stop=(kt == NTC - 1))
                mmi.then_inc(tv, 1)

        @blk.vector
        def _(v):
            for mm_ in range(16):
                v.wait_ge(tv, TV7 + mm_ + 1)
                v.scalar_tensor_tensor(
                    x1[:, mm_ * 256:(mm_ + 1) * 256], psW[:, 0:256], 1.0,
                    sfm_scrb[:, mm_ * 256:(mm_ + 1) * 256], ALU.mult, ALU.add
                ).then_inc(vt, 1)
        G[0] += 8 * 16
        G[1] += 8 * 16
        C["tv"] += 16
        C["vt"] += 16
        VT_X1 = C["vt"]

        TV8 = C["tv"]
        VT8 = C["vt"]
        SV8 = C["sv"]

        @blk.scalar
        def _(s):
            s.wait_ge(vt, VT_X1)
            s.square(sfm_scrb[:], x1[:]).then_inc(sv, 1)
            s.wait_ge(vt, VT8 + 1)
            s.sqrt(rstd2[:], var2[:]).then_inc(sv, 1)

        @blk.tensor
        def _(t):
            t.wait_ge(sv, SV8 + 1)
            for mm_ in range(16):
                mmi = t.matmul(psW[0:1, 0:256], onescb[:],
                         sfm_scrb[:, mm_ * 256:(mm_ + 1) * 256],
                         start=(mm_ == 0), stop=(mm_ == 15))
            mmi.then_inc(tv, 1)
            t.wait_ge(vt, VT8 + 2)
            t.matmul(psW[:, 0:256], onesrb[:], rstd2b[:], start=True, stop=True
                     ).then_inc(tv, 1)

        @blk.vector
        def _(v):
            v.wait_ge(tv, TV8 + 1)
            VE(v, v.tensor_scalar(var2[:], psW[0:1, 0:256], 1.0 / D, EPS_RMS,
                                  ALU.mult, ALU.add))
            v.sem_inc(vt, 1)
            v.wait_ge(sv, SV8 + 2)
            VE(v, v.reciprocal(rstd2[:], rstd2[:]))
            v.tensor_copy(rstd2b[:], rstd2[:]).then_inc(vt, 1)
            v.wait_ge(tv, TV8 + 2)
            for mm_ in range(16):
                VE(v, v.tensor_tensor(scrf[:, 0:256],
                                      x1[:, mm_ * 256:(mm_ + 1) * 256],
                                      psW[:, 0:256], ALU.mult))
                VE(v, v.tensor_scalar(sfm_scrb[:, mm_ * 256:(mm_ + 1) * 256],
                                      scrf[:, 0:256], ln2s[:, mm_:mm_ + 1],
                                      None, ALU.mult))
            v.sem_inc(vt, 1)
        C["sv"] += 2
        C["tv"] += 2
        C["vt"] += 3
        VT_H2 = C["vt"]

        GW9 = C["gw"]
        TV9 = C["tv"]
        VT9 = C["vt"]
        SV9 = C["sv"]

        G9 = list(G)

        @blk.gpsimd
        def _(g):
            g.wait_ge(vt, VT_H2)
            for mi in range(64):
                if mi >= 2:
                    g.wait_ge(vt, VT9 + mi - 1)
                g.dma_start(out=wslab[mi % 2][:], in_=wg_d[mi]
                            ).then_inc(gsl[mi % 2], 16)
                g.dma_start(out=wslab[2 + mi % 2][:], in_=wu_d[mi]
                            ).then_inc(gsl[2 + mi % 2], 16)

        @blk.tensor
        def _(t):
            t.wait_ge(vt, VT_H2)
            for mi in range(64):
                t.wait_ge(gsl[mi % 2], G9[mi % 2] + (mi // 2 + 1) * 16)
                t.wait_ge(gsl[2 + mi % 2], G9[2 + mi % 2] + (mi // 2 + 1) * 16)
                if mi > 0:
                    t.wait_ge(vt, VT9 + mi)
                for pt in range(NTC):
                    mmi = t.matmul(psA[:, 0:256],
                             wslab[mi % 2][:, pt * 128:(pt + 1) * 128],
                             sfm_scrb[:, pt * 256:(pt + 1) * 256],
                             start=(pt == 0), stop=(pt == NTC - 1))
                mmi.then_inc(tv, 1)
                for pt in range(NTC):
                    mmi = t.matmul(psW[:, 0:256],
                             wslab[2 + mi % 2][:, pt * 128:(pt + 1) * 128],
                             sfm_scrb[:, pt * 256:(pt + 1) * 256],
                             start=(pt == 0), stop=(pt == NTC - 1))
                mmi.then_inc(tv, 1)

        @blk.scalar
        def _(s):
            for mi in range(64):
                s.wait_ge(tv, TV9 + 2 * mi + 1)
                if mi >= 1:
                    s.wait_ge(vt, VT9 + mi)
                s.activation(scrf[:, 0:256], psA[:, 0:256], AF.Silu
                             ).then_inc(sv, 1)

        @blk.vector
        def _(v):
            for mi in range(64):
                v.wait_ge(tv, TV9 + 2 * mi + 2)
                v.wait_ge(sv, SV9 + mi + 1)
                v.tensor_tensor(buf2[:, mi * 256:(mi + 1) * 256],
                                scrf[:, 0:256], psW[:, 0:256], ALU.mult
                                ).then_inc(vt, 1)
        G[0] += 32 * 16
        G[1] += 32 * 16
        G[2] += 32 * 16
        G[3] += 32 * 16
        C["tv"] += 64 * 2
        C["vt"] += 64
        C["sv"] += 64

        TV10 = C["tv"]
        VT10 = C["vt"]
        SO10 = C["so"]
        G10 = list(G)
        NP10 = 2

        def t10_base(p):
            return TV10 + p * 72

        def v10_base(p):
            return VT10 + p * 25

        def bank10(mo):
            if mo < 4:
                return psBig[:, mo * 512:mo * 512 + 256]
            if mo == 4:
                return psA[:, 0:256]
            if mo == 5:
                return psW[:, 0:256]
            return psT[:, (mo - 6) * 512:(mo - 6) * 512 + 256]

        @blk.gpsimd
        def _(g):
            g.wait_ge(tv, TV10)
            for p in range(NP10):
                for kt in range(64):
                    gi = p * 64 + kt
                    if gi >= 2:
                        gp, gk = (gi - 2) // 64, (gi - 2) % 64
                        g.wait_ge(tv, TV10 + gp * 72 + gk + 1)
                    g.dma_start(out=wslab[gi % 2][:], in_=wd_d[kt]
                                ).then_inc(gsl[gi % 2], 16)

        @blk.tensor
        def _(t):
            t.wait_ge(vt, VT10)
            for p in range(NP10):
                if p >= 1:
                    t.wait_ge(vt, v10_base(p - 1) + 24)
                for kt in range(64):
                    gi = p * 64 + kt
                    t.wait_ge(gsl[gi % 2], G10[gi % 2] + (gi // 2 + 1) * 16)
                    for mo in range(8):
                        mm_ = p * 8 + mo
                        mmi = t.matmul(bank10(mo),
                                       wslab[gi % 2][:, mm_ * 128:(mm_ + 1) * 128],
                                       buf2[:, kt * 256:(kt + 1) * 256],
                                       start=(kt == 0), stop=(kt == 63))
                    mmi.then_inc(tv, 1)
                for mo in range(8):
                    t.wait_ge(vt, v10_base(p) + 3 * mo + 1)
                    bk = bank10(mo)
                    t.transpose(bk[:, 0:128], scrf2[:, 0:128], idnf[:])
                    t.transpose(bk[:, 128:256], scrf2[:, 128:256], idnf[:]
                                ).then_inc(tv, 1)

        @blk.vector
        def _(v):
            for p in range(NP10):
                v.wait_ge(tv, t10_base(p) + 64)
                if p >= 1:
                    v.wait_ge(so, SO10 + p * 64)
                for mo in range(8):
                    mm_ = p * 8 + mo
                    VE(v, v.scalar_tensor_tensor(
                        scrf2[:, 0:256],
                        bank10(mo), 1.0,
                        x1[:, mm_ * 256:(mm_ + 1) * 256], ALU.mult, ALU.add))
                    v.sem_inc(vt, 1)
                    if mo == 4:
                        v.wait_ge(so, SO10 + p * 64 + 32)
                    v.wait_ge(tv, t10_base(p) + 64 + mo + 1)
                    VE(v, v.tensor_copy(
                        pstage[:, (mo % 4) * 128:(mo % 4 + 1) * 128],
                        bank10(mo)[:, 0:128]))
                    VE(v, v.tensor_copy(
                        pstage[:, 512 + (mo % 4) * 128:512 + (mo % 4 + 1) * 128],
                        bank10(mo)[:, 128:256]))
                    v.sem_inc(vt, 2)
                v.sem_inc(vt, 1)

        @blk.sync
        def _(sy):
            for p in range(NP10):
                for hf in range(2):
                    sy.wait_ge(vt, v10_base(p) + 12 * (hf + 1))
                    off = p * 1024 + hf * 512
                    sy.dma_start(out=proc_d[0][:, off:off + 512],
                                 in_=pstage[:, 0:512]).then_inc(so, 16)
                    sy.dma_start(out=proc_d[1][:, off:off + 512],
                                 in_=pstage[:, 512:1024]).then_inc(so, 16)
            sy.wait_ge(so, SO10 + NP10 * 64)
        G[0] += 64 * 16
        G[1] += 64 * 16
        C["tv"] += NP10 * 72
        C["vt"] += NP10 * 25
        C["so"] += NP10 * 64
    return nc


CHT = 4
NCH = NTC // CHT


def build_p4():
    nc = bass.Bass("TRN2", target_bir_lowering=False)
    x_d = nc.dram_tensor("x", [TC, D], F32, kind="ExternalInput").ap()
    procf = nc.dram_tensor("procf", [SSEL, D], F32, kind="ExternalInput").ap()
    cidx_d = nc.dram_tensor("cidx_own", [128, NTC], F32, kind="ExternalInput").ap()
    msc_d = nc.dram_tensor("myscores", [128, NTC], F32, kind="ExternalInput").ap()
    wrt_d = nc.dram_tensor("wrt_fm", [128, NTC], F32, kind="ExternalInput").ap()
    wi_d = nc.dram_tensor("wi_l", [64, 128, 16 * 128], BF16, kind="ExternalInput").ap()
    cv_d = nc.dram_tensor("cv_l", [32, 128, 32 * 128], BF16, kind="ExternalInput").ap()
    wo2_d = nc.dram_tensor("wo2_l", [16, 128, 32 * 128], BF16, kind="ExternalInput").ap()
    dcol_d = nc.dram_tensor("dcol", [128, 32], F32, kind="ExternalInput").ap()
    xo_d = nc.dram_tensor("xo_fm", [NTC, 128, TC], F32, kind="ExternalOutput").ap()
    sc2_d = nc.dram_tensor("sc2", [1, TC], F32, kind="ExternalOutput").ap()
    lnx2_d = nc.dram_tensor("lnx2_fm", [NTC, 128, TC], BF16, kind="ExternalOutput").ap()
    xpre_d = nc.dram_tensor("xpre_fm", [128, NTC, TC], F32, kind="Internal").ap()

    idn_c = nc.inline_tensor(np.eye(128, dtype=bf16), name="idn4")
    idnf_c = nc.inline_tensor(np.eye(128, dtype=np.float32), name="idnf4")
    onescf_c = nc.inline_tensor(np.ones((128, 1), np.float32), name="onescf4")
    onesrf_c = nc.inline_tensor(np.ones((1, 128), np.float32), name="onesrf4")

    es = ExitStack()
    with es:
        lnfm = es.enter_context(nc.sbuf_tensor("lnfm", [128, 16 * 512], BF16))
        xcv = es.enter_context(nc.sbuf_tensor("xcv", [128, 32 * 512], BF16))
        slz = es.enter_context(nc.sbuf_tensor("slz", [128, 32 * 512], BF16))
        xt = [es.enter_context(nc.sbuf_tensor(f"xt4{i}", [128, D], F32))
              for i in range(2)]
        stage = es.enter_context(nc.sbuf_tensor("stage", [128, D], F32))
        lnf = es.enter_context(nc.sbuf_tensor("lnf", [128, D], F32))
        lnmb = es.enter_context(nc.sbuf_tensor("lnmb", [128, D], BF16))
        xps = es.enter_context(nc.sbuf_tensor("xps", [128, D], F32))
        wsl = [es.enter_context(nc.sbuf_tensor(f"wsl4{i}", [128, 4096], BF16))
               for i in range(2)]
        xrb = [es.enter_context(nc.sbuf_tensor(f"xrb{i}", [128, 512], F32))
               for i in range(2)]
        xog = es.enter_context(nc.sbuf_tensor("xog", [128, 16 * 512], F32))
        sqb = es.enter_context(nc.sbuf_tensor("sqb", [128, 1024], F32))
        scr5 = es.enter_context(nc.sbuf_tensor("scr5", [128, 1024], F32))
        scR = es.enter_context(nc.sbuf_tensor("scR", [1, TC], F32))
        cidxs = es.enter_context(nc.sbuf_tensor("cidxs", [128, NTC], F32))
        cidxi = es.enter_context(nc.sbuf_tensor("cidxi", [128, 1], I32))
        mscs = es.enter_context(nc.sbuf_tensor("mscs", [128, NTC], F32))
        wrts = es.enter_context(nc.sbuf_tensor("wrts", [128, NTC], F32))
        dcols = es.enter_context(nc.sbuf_tensor("dcols", [128, 32], F32))
        idn = es.enter_context(nc.sbuf_tensor("idn4s", [128, 128], BF16))
        idnf = es.enter_context(nc.sbuf_tensor("idnf4s", [128, 128], F32))
        onescf = es.enter_context(nc.sbuf_tensor("onescf4s", [128, 1], F32))
        onesrf = es.enter_context(nc.sbuf_tensor("onesrf4s", [1, 128], F32))
        ssq = es.enter_context(nc.sbuf_tensor("ssq4", [128, 1], F32))
        sx = es.enter_context(nc.sbuf_tensor("sx4", [128, 1], F32))
        m_ = es.enter_context(nc.sbuf_tensor("m4", [128, 1], F32))
        var = es.enter_context(nc.sbuf_tensor("var4", [128, 1], F32))
        sd = es.enter_context(nc.sbuf_tensor("sd4", [128, 1], F32))
        rstd = es.enter_context(nc.sbuf_tensor("rstd4", [128, 1], F32))
        nb = es.enter_context(nc.sbuf_tensor("nb4", [128, 1], F32))
        t0 = es.enter_context(nc.sbuf_tensor("t04", [128, 1], F32))
        g8 = es.enter_context(nc.sbuf_tensor("g84", [128, 1], F32))
        coef = es.enter_context(nc.sbuf_tensor("coef4", [128, 1], F32))
        w1 = es.enter_context(nc.sbuf_tensor("w14", [1, 1], F32))
        mrow = es.enter_context(nc.sbuf_tensor("mrow", [1, 512], F32))
        vrow = es.enter_context(nc.sbuf_tensor("vrow", [1, 512], F32))
        rrow = es.enter_context(nc.sbuf_tensor("rrow", [1, 512], F32))
        srow = es.enter_context(nc.sbuf_tensor("srow", [1, 512], F32))

        psA = es.enter_context(nc.psum_tensor("psA4", [128, 512], F32))
        psB = es.enter_context(nc.psum_tensor("psB4", [128, 512], F32))
        psT = es.enter_context(nc.psum_tensor("psT4", [128, 2048], BF16))
        psS = es.enter_context(nc.psum_tensor("psS4", [1, 512], F32))
        psQ = es.enter_context(nc.psum_tensor("psQ4", [1, 512], F32))
        psW2 = es.enter_context(nc.psum_tensor("psW24", [1, 512], F32))
        psM = psA
        psR = psB

        def psTs4(i):
            return psT[:, (i % 2) * 1024:(i % 2) * 1024 + 128]

        ld = es.enter_context(nc.semaphore("ld4"))
        gw = es.enter_context(nc.semaphore("gw4"))
        tv = es.enter_context(nc.semaphore("tv4"))
        vt = es.enter_context(nc.semaphore("vt4"))
        sv = es.enter_context(nc.semaphore("sv4"))
        vself = es.enter_context(nc.semaphore("vself4"))
        so = es.enter_context(nc.semaphore("so4"))
        lx = es.enter_context(nc.semaphore("lx4"))
        lr = es.enter_context(nc.semaphore("lr4"))
        sxo = es.enter_context(nc.semaphore("sxo4"))
        sst = es.enter_context(nc.semaphore("sst4"))
        gsl = [es.enter_context(nc.semaphore(f"gsl4{i}")) for i in range(2)]
        blk = es.enter_context(nc.Block())

        C = dict(ld=0, gw=0, tv=0, vt=0, sv=0, so=0, lx=0, lr=0, sxo=0, sst=0)
        G4 = [0, 0]
        ve_n = [0]

        def VE(v, inst):
            ve_n[0] += 1
            inst.then_inc(vself, 1)
            v.wait_ge(vself, ve_n[0])

        @blk.sync
        def _(sy):
            sy.dma_start(out=cidxs[:], in_=cidx_d).then_inc(ld, 16)
            sy.dma_start(out=mscs[:], in_=msc_d).then_inc(ld, 16)
            sy.dma_start(out=wrts[:], in_=wrt_d).then_inc(ld, 16)
            sy.dma_start(out=dcols[:], in_=dcol_d).then_inc(ld, 16)
            sy.dma_start(out=idn[:], in_=idn_c.ap()).then_inc(ld, 16)
            sy.dma_start(out=idnf[:], in_=idnf_c.ap()).then_inc(ld, 16)
            sy.dma_start(out=onescf[:], in_=onescf_c.ap()).then_inc(ld, 16)
            sy.dma_start(out=onesrf[:], in_=onesrf_c.ap()).then_inc(ld, 16)
        C["ld"] += 8 * 16
        LD0 = C["ld"]

        @blk.tensor
        def _(t):
            t.wait_ge(ld, LD0)
            for pt in range(NTC):
                mmi = t.matmul(psS[0:1, 0:1], wrts[:, pt:pt + 1], onescf[:],
                               start=(pt == 0), stop=(pt == NTC - 1))
            mmi.then_inc(tv, 1)
        C["tv"] += 1
        TV_W1 = C["tv"]

        @blk.vector
        def _(v):
            v.wait_ge(ld, LD0)
            v.memset(stage[:], 0.0).then_inc(vt, 1)
            v.wait_ge(tv, TV_W1)
            v.tensor_copy(w1[:], psS[0:1, 0:1]).then_inc(vt, 1)
        C["vt"] += 2

        tile_tv_base = {}

        def gincw(k):
            return 1 if k == 0 else (2 if k == 1 else 2 * k)

        for chx in range(NCH):
            SSTC = C["sst"]
            for jj in range(CHT):
                j = chx * CHT + jj
                LXJ = C["lx"]
                GWJ = C["gw"]
                VTJ = C["vt"]
                SVJ = C["sv"]
                TVJ = C["tv"]
                tile_tv_base[j] = TVJ

                @blk.sync
                def _(sy, j=j, SVJ=SVJ):
                    if j >= 2:
                        sy.wait_ge(tv, tile_tv_base[j - 2] + 32)
                    sy.dma_start(out=xt[j % 2][:],
                                 in_=x_d[j * 128:(j + 1) * 128, :]
                                 ).then_inc(lx, 16)
                    sy.wait_ge(sv, SVJ + 37)
                    sy.dma_start(
                        out=xpre_d[:, :, j * 128:(j + 1) * 128],
                        in_=xps[:].rearrange("p (f q) -> p f q", f=NTC),
                    ).then_inc(so, 16)
                C["lx"] += 16
                C["so"] += 16

                @blk.vector
                def _(v, j=j, LXJ=LXJ, GWJ=GWJ, VTJ=VTJ, SVJ=SVJ):
                    xb = xt[j % 2]
                    v.wait_ge(lx, LXJ + 16)
                    VE(v, v.scalar_tensor_tensor(lnf[:], xb[:], 1.0, xb[:],
                                                 ALU.mult, ALU.mult,
                                                 accum_out=ssq[:]))
                    VE(v, v.reduce_sum(sx[:], xb[:], axis=AX.X))
                    VE(v, v.tensor_scalar_mul(m_[:], sx[:], 1.0 / D))
                    VE(v, v.tensor_scalar_mul(var[:], ssq[:], 1.0 / D))
                    VE(v, v.tensor_mul(t0[:], m_[:], m_[:]))
                    VE(v, v.tensor_sub(var[:], var[:], t0[:]))
                    v.tensor_scalar_add(var[:], var[:], EPS_LN
                                        ).then_inc(vt, 1)
                    v.wait_ge(sv, SVJ + 1)
                    VE(v, v.reciprocal(rstd[:], sd[:]))
                    VE(v, v.tensor_mul(nb[:], m_[:], rstd[:]))
                    VE(v, v.tensor_scalar_mul(nb[:], nb[:], -1.0))
                    VE(v, v.tensor_scalar(t0[:], cidxs[:, j:j + 1], 4096.0,
                                          None, ALU.min))
                    v.tensor_copy(cidxi[:], t0[:]).then_inc(vt, 1)
                    VE(v, v.tensor_scalar(coef[:], cidxs[:, j:j + 1], 2048.0,
                                          None, ALU.is_lt))
                    v.wait_ge(sv, SVJ + 2)
                    VE(v, v.tensor_mul(coef[:], coef[:], g8[:]))
                    VE(v, v.tensor_scalar_mul(coef[:], coef[:], 0.5))
                    v.wait_ge(gw, GWJ + 16)
                    v.wait_ge(sv, SVJ + 3)
                    VE(v, v.tensor_sub(stage[:], stage[:], lnf[:]))
                    VE(v, v.tensor_scalar(stage[:], stage[:], coef[:], None,
                                          ALU.mult))
                    VE(v, v.scalar_tensor_tensor(stage[:], lnf[:], 0.5,
                                                 stage[:], ALU.mult, ALU.add))
                    VE(v, v.tensor_add(xb[:], xb[:], stage[:]))
                    v.sem_inc(vt, 1)
                    VE(v, v.scalar_tensor_tensor(stage[:], xb[:], 1.0, xb[:],
                                                 ALU.mult, ALU.mult,
                                                 accum_out=ssq[:]))
                    VE(v, v.reduce_sum(sx[:], xb[:], axis=AX.X))
                    VE(v, v.tensor_scalar_mul(m_[:], sx[:], 1.0 / D))
                    VE(v, v.tensor_scalar_mul(var[:], ssq[:], 1.0 / D))
                    VE(v, v.tensor_mul(t0[:], m_[:], m_[:]))
                    VE(v, v.tensor_sub(var[:], var[:], t0[:]))
                    v.tensor_scalar_add(var[:], var[:], EPS_LN
                                        ).then_inc(vt, 1)
                    v.wait_ge(sv, SVJ + 4)
                    VE(v, v.reciprocal(rstd[:], sd[:]))
                    VE(v, v.tensor_mul(nb[:], m_[:], rstd[:]))
                    VE(v, v.tensor_scalar_mul(nb[:], nb[:], -1.0))
                    v.sem_inc(vt, 1)
                    v.memset(stage[:], 0.0).then_inc(vt, 1)
                C["vt"] += 6

                @blk.scalar
                def _(s, j=j, jj=jj, VTJ=VTJ, SVJ=SVJ, TVJ=TVJ, SSTC=SSTC):
                    if jj == 0:
                        s.wait_ge(sst, SSTC)
                    s.wait_ge(vt, VTJ + 1)
                    s.sqrt(sd[:], var[:]).then_inc(sv, 1)
                    s.activation(g8[:], mscs[:, j:j + 1], AF.Sigmoid
                                 ).then_inc(sv, 1)
                    s.wait_ge(vt, VTJ + 2)
                    s.activation(lnf[:], xt[j % 2][:], AF.Identity,
                                 bias=nb[:], scale=rstd[:]).then_inc(sv, 1)
                    s.wait_ge(vt, VTJ + 4)
                    s.sqrt(sd[:], var[:]).then_inc(sv, 1)
                    s.wait_ge(vt, VTJ + 5)
                    s.activation(lnmb[:], xt[j % 2][:], AF.Identity,
                                 bias=nb[:], scale=rstd[:]).then_inc(sv, 1)
                    for pt in range(NTC):
                        s.wait_ge(tv, TVJ + pt + 1)
                        s.copy(lnfm[:, pt * 512 + jj * 128:
                                     pt * 512 + (jj + 1) * 128],
                               psTs4(pt)).then_inc(sv, 1)
                    if j >= 1:
                        s.wait_ge(so, 16 * j)
                    for pt in range(NTC):
                        s.wait_ge(tv, TVJ + 16 + pt + 1)
                        ps = psA if pt % 2 == 0 else psB
                        s.copy(xps[:, pt * 128:(pt + 1) * 128],
                               ps[:, 0:128]).then_inc(sv, 1)
                C["sv"] += 37

                @blk.gpsimd
                def _(g, VTJ=VTJ):
                    g.wait_ge(vt, VTJ + 2)
                    g.indirect_dma_start(
                        stage[:], None, procf,
                        IndirectOffsetOnAxis(cidxi[:], 0),
                        bounds_check=SSEL - 1, oob_is_err=False,
                    ).then_inc(gw, 16)
                C["gw"] += 16

                @blk.tensor
                def _(t, j=j, VTJ=VTJ, SVJ=SVJ):
                    t.wait_ge(sv, SVJ + 5)
                    for pt in range(NTC):
                        if pt >= 2:
                            t.wait_ge(sv, SVJ + pt + 4)
                        t.transpose(psTs4(pt),
                                    lnmb[:, pt * 128:(pt + 1) * 128], idn[:]
                                    ).then_inc(tv, 1)
                    t.wait_ge(vt, VTJ + 3)
                    for pt in range(NTC):
                        ps = psA if pt % 2 == 0 else psB
                        if pt >= 2:
                            t.wait_ge(sv, SVJ + pt + 20)
                        t.transpose(ps[:, 0:128],
                                    xt[j % 2][:, pt * 128:(pt + 1) * 128],
                                    idnf[:]).then_inc(tv, 1)
                C["tv"] += 2 * NTC

            def lnslice(kt):
                return lnfm[:, kt * 512:(kt + 1) * 512]

            TVX = C["tv"]
            SVXz = C["sv"]
            GX = list(G4)

            @blk.gpsimd
            def _(g, GX=GX, TVX=TVX):
                g.wait_ge(tv, TVX)
                for m in range(64):
                    if m >= 2:
                        g.wait_ge(tv, TVX + m - 1)
                    g.dma_start(out=wsl[m % 2][:, 0:2048], in_=wi_d[m]
                                ).then_inc(gsl[m % 2], 16)

            @blk.tensor
            def _(t, GX=GX, TVX=TVX, SVXz=SVXz):
                t.wait_ge(sv, SVXz)
                for m in range(64):
                    t.wait_ge(gsl[m % 2], GX[m % 2] + (m // 2 + 1) * 16)
                    if m >= 2:
                        t.wait_ge(sv, SVXz + m - 1)
                    ps = psA if m % 2 == 0 else psB
                    for kt in range(NTC):
                        mmi = t.matmul(ps[:],
                                       wsl[m % 2][:, kt * 128:(kt + 1) * 128],
                                       lnslice(kt), start=(kt == 0),
                                       stop=(kt == NTC - 1))
                    mmi.then_inc(tv, 1)

            @blk.scalar
            def _(s, TVX=TVX):
                for m in range(64):
                    s.wait_ge(tv, TVX + m + 1)
                    ps = psA if m % 2 == 0 else psB
                    if m < 32:
                        s.copy(xcv[:, m * 512:(m + 1) * 512], ps[:]
                               ).then_inc(sv, 1)
                    else:
                        s.activation(slz[:, (m - 32) * 512:(m - 31) * 512],
                                     ps[:], AF.Silu).then_inc(sv, 1)
            G4[0] += 32 * 16
            G4[1] += 32 * 16
            C["tv"] += 64
            C["sv"] += 64

            TVC = C["tv"]
            SVC = C["sv"]
            VTC = C["vt"]
            GC = list(G4)

            @blk.gpsimd
            def _(g, GC=GC, TVC=TVC):
                g.wait_ge(tv, TVC)
                for m in range(32):
                    if m >= 2:
                        g.wait_ge(tv, TVC + m - 1)
                    g.dma_start(out=wsl[m % 2][:], in_=cv_d[m]
                                ).then_inc(gsl[m % 2], 16)

            @blk.tensor
            def _(t, GC=GC, TVC=TVC, SVC=SVC):
                t.wait_ge(sv, SVC)
                for m in range(32):
                    t.wait_ge(gsl[m % 2], GC[m % 2] + (m // 2 + 1) * 16)
                    if m >= 2:
                        t.wait_ge(sv, SVC + m - 1)
                    ps = psA if m % 2 == 0 else psB
                    for kt in range(32):
                        mmi = t.matmul(ps[:],
                                       wsl[m % 2][:, kt * 128:(kt + 1) * 128],
                                       xcv[:, kt * 512:(kt + 1) * 512],
                                       start=(kt == 0), stop=(kt == 31))
                    mmi.then_inc(tv, 1)

            @blk.scalar
            def _(s, TVC=TVC, SVC=SVC, VTC=VTC):
                for m in range(32):
                    s.wait_ge(tv, TVC + m + 1)
                    if m >= 2:
                        s.wait_ge(vt, VTC + m - 1)
                    ps = psA if m % 2 == 0 else psB
                    s.mul(scr5[:, (m % 2) * 512:(m % 2) * 512 + 512], ps[:],
                          dcols[:, m:m + 1]).then_inc(sv, 1)

            @blk.vector
            def _(v, SVC=SVC):
                for m in range(32):
                    v.wait_ge(sv, SVC + m + 1)
                    VE(v, v.tensor_tensor(slz[:, m * 512:(m + 1) * 512],
                                          scr5[:, (m % 2) * 512:
                                               (m % 2) * 512 + 512],
                                          slz[:, m * 512:(m + 1) * 512],
                                          ALU.mult))
                    v.sem_inc(vt, 1)
            G4[0] += 16 * 16
            G4[1] += 16 * 16
            C["tv"] += 32
            C["sv"] += 32
            C["vt"] += 32

            TVW = C["tv"]
            VTW = C["vt"]
            SVW = C["sv"]
            LRW = C["lr"]
            SXOC = C["sxo"]
            GW2 = list(G4)

            @blk.gpsimd
            def _(g, GW2=GW2, TVW=TVW):
                g.wait_ge(tv, TVW)
                for m in range(16):
                    if m >= 2:
                        g.wait_ge(tv, TVW + gincw(m - 2))
                    g.dma_start(out=wsl[m % 2][:], in_=wo2_d[m]
                                ).then_inc(gsl[m % 2], 16)

            @blk.sync
            def _(sy, chx=chx, LRW=LRW, VTW=VTW):
                sy.wait_ge(so, (chx + 1) * CHT * 16)
                for m in range(16):
                    if m >= 2:
                        sy.wait_ge(vt, VTW + m - 1)
                    sy.dma_start(out=xrb[m % 2][:],
                                 in_=xpre_d[:, m, chx * 512:(chx + 1) * 512]
                                 ).then_inc(lr, 16)
                for m in range(16):
                    sy.wait_ge(vt, VTW + m + 1)
                    sy.dma_start(out=xo_d[m][:, chx * 512:(chx + 1) * 512],
                                 in_=xog[:, m * 512:(m + 1) * 512]
                                 ).then_inc(sxo, 16)
                for pt in range(NTC):
                    sy.wait_ge(vt, VTW + 18 + pt + 1)
                    sy.dma_start(out=lnx2_d[pt][:, chx * 512:(chx + 1) * 512],
                                 in_=lnfm[:, pt * 512:(pt + 1) * 512]
                                 ).then_inc(sst, 16)

            @blk.tensor
            def _(t, GW2=GW2, TVW=TVW, VTW=VTW, SVW=SVW):
                t.wait_ge(vt, VTW)

                def grp(m):
                    t.wait_ge(gsl[m % 2], GW2[m % 2] + (m // 2 + 1) * 16)
                    if m >= 2:
                        t.wait_ge(vt, VTW + m - 1)
                    ps = psA if m % 2 == 0 else psB
                    for kt in range(32):
                        mmi = t.matmul(ps[:],
                                       wsl[m % 2][:, kt * 128:(kt + 1) * 128],
                                       slz[:, kt * 512:(kt + 1) * 512],
                                       start=(kt == 0), stop=(kt == 31))
                    mmi.then_inc(tv, 1)

                def stats(k):
                    t.wait_ge(vt, VTW + k + 1)
                    t.wait_ge(sv, SVW + k + 1)
                    xsl = xog[:, k * 512:(k + 1) * 512]
                    t.matmul(psS[:], onescf[:], xsl, start=(k == 0),
                             stop=(k == 15))
                    t.matmul(psQ[:], onescf[:],
                             sqb[:, (k % 2) * 512:(k % 2) * 512 + 512],
                             start=(k == 0), stop=(k == 15))
                    t.matmul(psW2[:], wrts[:, k:k + 1], xsl,
                             start=(k == 0), stop=(k == 15)).then_inc(tv, 1)

                grp(0)
                grp(1)
                for m in range(2, 16):
                    stats(m - 2)
                    grp(m)
                stats(14)
                stats(15)
                t.wait_ge(vt, VTW + 18)
                t.matmul(psM[:], onesrf[:], mrow[:], start=True, stop=True)
                t.matmul(psR[:], onesrf[:], rrow[:], start=True, stop=True
                         ).then_inc(tv, 1)

            @blk.vector
            def _(v, chx=chx, TVW=TVW, VTW=VTW, SVW=SVW, LRW=LRW, SXOC=SXOC):
                for m in range(16):
                    v.wait_ge(tv, TVW + gincw(m))
                    v.wait_ge(lr, LRW + (m // 2 + 1) * 32)
                    if m == 0:
                        v.wait_ge(sxo, SXOC)
                    ps = psA if m % 2 == 0 else psB
                    v.scalar_tensor_tensor(xog[:, m * 512:(m + 1) * 512],
                                           ps[:], 1.0, xrb[m % 2][:],
                                           ALU.mult, ALU.add).then_inc(vt, 1)
                v.wait_ge(tv, TVW + 32)
                VE(v, v.tensor_scalar_mul(mrow[:], psS[0:1, :], 1.0 / D))
                VE(v, v.tensor_scalar_mul(vrow[:], psQ[0:1, :], 1.0 / D))
                VE(v, v.tensor_mul(srow[:], mrow[:], mrow[:]))
                VE(v, v.tensor_sub(vrow[:], vrow[:], srow[:]))
                v.tensor_scalar_add(vrow[:], vrow[:], EPS_LN
                                    ).then_inc(vt, 1)
                v.wait_ge(sv, SVW + 17)
                VE(v, v.reciprocal(rrow[:], rrow[:]))
                VE(v, v.tensor_scalar(srow[:], mrow[:], w1[0:1, 0:1], None,
                                      ALU.mult))
                VE(v, v.tensor_tensor(srow[:], psW2[0:1, :], srow[:],
                                      ALU.subtract))
                VE(v, v.tensor_mul(scR[:, chx * 512:(chx + 1) * 512],
                                   srow[:], rrow[:]))
                v.sem_inc(vt, 1)
                v.wait_ge(tv, TVW + 33)
                for pt in range(NTC):
                    VE(v, v.tensor_tensor(scr5[:, 0:512],
                                          xog[:, pt * 512:(pt + 1) * 512],
                                          psM[:], ALU.subtract))
                    VE(v, v.tensor_tensor(lnfm[:, pt * 512:(pt + 1) * 512],
                                          scr5[:, 0:512], psR[:], ALU.mult))
                    v.sem_inc(vt, 1)

            @blk.scalar
            def _(s, TVW=TVW, VTW=VTW):
                for m in range(16):
                    s.wait_ge(vt, VTW + m + 1)
                    if m >= 2:
                        s.wait_ge(tv, TVW + 2 * m - 1)
                    s.square(sqb[:, (m % 2) * 512:(m % 2) * 512 + 512],
                             xog[:, m * 512:(m + 1) * 512]).then_inc(sv, 1)
                s.wait_ge(vt, VTW + 17)
                s.sqrt(rrow[:], vrow[:]).then_inc(sv, 1)
            G4[0] += 8 * 16
            G4[1] += 8 * 16
            C["tv"] += 33
            C["vt"] += 16 + 2 + 16
            C["sv"] += 17
            C["lr"] += 16 * 16
            C["sxo"] += 16 * 16
            C["sst"] += 16 * 16

        VT_F = C["vt"]

        @blk.sync
        def _(sy):
            sy.wait_ge(vt, VT_F)
            sy.dma_start(out=sc2_d, in_=scR[:]).then_inc(sst, 16)
            sy.wait_ge(sst, C["sst"] + 16)
            sy.wait_ge(sxo, C["sxo"])
            sy.wait_ge(so, C["so"])
        C["sst"] += 16
    return nc



def build_p7():
    nc = bass.Bass("TRN2", target_bir_lowering=False)
    x_d = nc.dram_tensor("x", [TC, D], F32, kind="ExternalInput").ap()
    procf = nc.dram_tensor("procf", [SSEL, D], F32, kind="ExternalInput").ap()
    cidx_d = nc.dram_tensor("cidx_own", [128, NTC], F32, kind="ExternalInput").ap()
    msc_d = nc.dram_tensor("myscores", [128, NTC], F32, kind="ExternalInput").ap()
    out_d = nc.dram_tensor("out", [TC, D], F32, kind="ExternalOutput").ap()

    es = ExitStack()
    with es:
        xt = [es.enter_context(nc.sbuf_tensor(f"xt7{i}", [128, D], F32))
              for i in range(2)]
        ot = [es.enter_context(nc.sbuf_tensor(f"ot7{i}", [128, D], F32))
              for i in range(2)]
        stage = es.enter_context(nc.sbuf_tensor("stage7", [128, D], F32))
        lnf = es.enter_context(nc.sbuf_tensor("lnf7", [128, D], F32))
        cidxs = es.enter_context(nc.sbuf_tensor("cidxs7", [128, NTC], F32))
        cidxi = es.enter_context(nc.sbuf_tensor("cidxi7", [128, 1], I32))
        mscs = es.enter_context(nc.sbuf_tensor("mscs7", [128, NTC], F32))
        ssq = es.enter_context(nc.sbuf_tensor("ssq7", [128, 1], F32))
        sx = es.enter_context(nc.sbuf_tensor("sx7", [128, 1], F32))
        m_ = es.enter_context(nc.sbuf_tensor("m7", [128, 1], F32))
        var = es.enter_context(nc.sbuf_tensor("var7", [128, 1], F32))
        sd = es.enter_context(nc.sbuf_tensor("sd7", [128, 1], F32))
        rstd = es.enter_context(nc.sbuf_tensor("rstd7", [128, 1], F32))
        nb = es.enter_context(nc.sbuf_tensor("nb7", [128, 1], F32))
        t0 = es.enter_context(nc.sbuf_tensor("t07", [128, 1], F32))
        g8 = es.enter_context(nc.sbuf_tensor("g87", [128, 1], F32))
        coef = es.enter_context(nc.sbuf_tensor("coef7", [128, 1], F32))

        ld = es.enter_context(nc.semaphore("ld7"))
        lxs = [es.enter_context(nc.semaphore(f"lxs7{i}")) for i in range(2)]
        gw = es.enter_context(nc.semaphore("gw7"))
        vt = es.enter_context(nc.semaphore("vt7"))
        sv = es.enter_context(nc.semaphore("sv7"))
        vself = es.enter_context(nc.semaphore("vself7"))
        sol = [es.enter_context(nc.semaphore(f"sol7{i}")) for i in range(2)]
        blk = es.enter_context(nc.Block())

        ve_n = [0]

        def VE(v, inst):
            ve_n[0] += 1
            inst.then_inc(vself, 1)
            v.wait_ge(vself, ve_n[0])

        @blk.sync
        def _(sy):
            sy.dma_start(out=cidxs[:], in_=cidx_d).then_inc(ld, 16)
            sy.dma_start(out=mscs[:], in_=msc_d).then_inc(ld, 16)
            for j in range(NTC):
                if j >= 2:
                    sy.wait_ge(vt, 5 * (j - 1))
                sy.dma_start(out=xt[j % 2][:],
                             in_=x_d[j * 128:(j + 1) * 128, :]
                             ).then_inc(lxs[j % 2], 16)

        @blk.vector
        def _(v):
            v.wait_ge(ld, 32)
            VE(v, v.memset(stage[:], 0.0))
            for j in range(NTC):
                xb = xt[j % 2]
                v.wait_ge(lxs[j % 2], (j // 2 + 1) * 16)
                VE(v, v.scalar_tensor_tensor(lnf[:], xb[:], 1.0, xb[:],
                                             ALU.mult, ALU.mult,
                                             accum_out=ssq[:]))
                VE(v, v.reduce_sum(sx[:], xb[:], axis=AX.X))
                VE(v, v.tensor_scalar_mul(m_[:], sx[:], 1.0 / D))
                VE(v, v.tensor_scalar_mul(var[:], ssq[:], 1.0 / D))
                VE(v, v.tensor_mul(t0[:], m_[:], m_[:]))
                VE(v, v.tensor_sub(var[:], var[:], t0[:]))
                v.tensor_scalar_add(var[:], var[:], EPS_LN).then_inc(vt, 1)
                v.wait_ge(sv, 2 * j + 1)
                VE(v, v.reciprocal(rstd[:], sd[:]))
                VE(v, v.tensor_mul(nb[:], m_[:], rstd[:]))
                VE(v, v.tensor_scalar_mul(nb[:], nb[:], -1.0))
                VE(v, v.tensor_scalar(lnf[:], xb[:], rstd[:], nb[:],
                                      ALU.mult, ALU.add))
                VE(v, v.tensor_scalar(t0[:], cidxs[:, j:j + 1], 4096.0,
                                      None, ALU.min))
                v.tensor_copy(cidxi[:], t0[:]).then_inc(vt, 1)
                VE(v, v.tensor_scalar(coef[:], cidxs[:, j:j + 1], 2048.0,
                                      None, ALU.is_lt))
                v.wait_ge(sv, 2 * j + 2)
                VE(v, v.tensor_mul(coef[:], coef[:], g8[:]))
                VE(v, v.tensor_scalar_mul(coef[:], coef[:], 0.5))
                v.wait_ge(gw, (j + 1) * 16)
                VE(v, v.tensor_sub(stage[:], stage[:], lnf[:]))
                VE(v, v.tensor_scalar(stage[:], stage[:], coef[:], None,
                                      ALU.mult))
                VE(v, v.scalar_tensor_tensor(stage[:], lnf[:], 0.5, stage[:],
                                             ALU.mult, ALU.add))
                v.wait_ge(sol[j % 2], (j // 2) * 16)
                VE(v, v.tensor_tensor(ot[j % 2][:], xb[:], stage[:], ALU.add))
                v.sem_inc(vt, 1)
                v.memset(stage[:], 0.0).then_inc(vt, 2)

        @blk.scalar
        def _(s):
            for j in range(NTC):
                s.wait_ge(vt, 5 * j + 1)
                s.sqrt(sd[:], var[:]).then_inc(sv, 1)
                s.activation(g8[:], mscs[:, j:j + 1], AF.Sigmoid).then_inc(sv, 1)

        @blk.gpsimd
        def _(g):
            for j in range(NTC):
                g.wait_ge(vt, 5 * j + 2)
                g.indirect_dma_start(
                    stage[:], None, procf,
                    IndirectOffsetOnAxis(cidxi[:], 0),
                    bounds_check=SSEL - 1, oob_is_err=False,
                ).then_inc(gw, 16)
                g.wait_ge(vt, 5 * j + 3)
                g.dma_start(out=out_d[j * 128:(j + 1) * 128, :],
                            in_=ot[j % 2][:]).then_inc(sol[j % 2], 16)
            g.wait_ge(sol[0], 8 * 16)
            g.wait_ge(sol[1], 8 * 16)
    return nc


def _get_prog(name, builder):
    if name not in _PROG:
        _PROG[name] = builder()
    return _PROG[name]


def _launch(name, builder, in_maps, **kw):
    nc = _get_prog(name, builder)
    res = run_bass_kernel_spmd(nc, in_maps, core_ids=list(range(NC)), **kw)
    return res.results



_WSPEC = [("wq", 16, 2048), ("wk", 8, 2048), ("wv", 8, 2048),
          ("wo", 16, 2048), ("wg", 64, 2048), ("wu", 64, 2048),
          ("wd", 64, 2048), ("wi", 64, 2048), ("cv", 32, 4096),
          ("wo2", 16, 4096)]


def _wcat_build(tensors):
    return np.ascontiguousarray(np.concatenate(
        [t.transpose(1, 0, 2).reshape(128, -1) for t in tensors], axis=1))


def _wcat_split(wcb):
    out = {}
    off = 0
    for name, m, k in _WSPEC:
        w = wcb[:, off:off + m * k]
        off += m * k
        out[name] = np.ascontiguousarray(
            w.reshape(128, m, k).transpose(1, 0, 2))
    return out


def _slabs_out(w):
    K, M = w.shape
    return _np_f32(w.reshape(K // 128, 128, M // 128, 128
                             ).transpose(2, 1, 0, 3).reshape(M // 128, 128, K))


def _make_costab():
    inv_freq = 1.0 / (10000.0 ** (np.arange(0, HD, 2, dtype=np.float64) / HD))
    ang = np.arange(SEQ, dtype=np.float64)[:, None] * inv_freq[None, :]
    cos = np.cos(ang).astype(np.float32)
    sin = np.sin(ang).astype(np.float32)
    return _np_f32(np.concatenate(
        [cos, cos, sin, sin], axis=1))


def _perm_tiles(c):
    rest = [t for t in range(NTC) if t not in (c, c + 8)]
    return [c, c + 8] + rest


def _masks_for_core(c):
    pt = _perm_tiles(c)
    ktrue = np.concatenate([np.arange(t * 128, (t + 1) * 128) for t in pt])
    masks = np.zeros((2, 128, SSEL), np.float32)
    for a, tile in enumerate([c, c + 8]):
        qtrue = tile * 128 + np.arange(128)
        masks[a] = np.where(ktrue[None, :] <= qtrue[:, None], 0.0, BIG_NEG)
    return _np_f32(masks)


def _fm16(vec):
    return _np_f32(np.asarray(vec).reshape(NTC, 128).T)


def kernel(**inputs):
    xin = np.asarray(inputs["L0_attn_hidden_states"], dtype=np.float32)
    xf = np.ascontiguousarray(xin.reshape(TT, D))

    w_router = np.asarray(inputs["L0_ffn_w_router"], np.float32)
    wrt_tiled = _np_f32(np.tile(w_router[None, :], (128, 1)))
    wrt_fm = _fm16(w_router)

    wq_l = _slabs_out(np.asarray(inputs["L0_ffn_wq"], np.float32))
    wk_l = _slabs_out(np.asarray(inputs["L0_ffn_wk"], np.float32))
    wv_l = _slabs_out(np.asarray(inputs["L0_ffn_wv"], np.float32))
    wo_l = _slabs_out(np.asarray(inputs["L0_ffn_wo"], np.float32))
    wg_l = _slabs_out(np.asarray(inputs["L0_ffn_w_gate"], np.float32))
    wu_l = _slabs_out(np.asarray(inputs["L0_ffn_w_up"], np.float32))
    wd_l = _np_f32(np.asarray(inputs["L0_ffn_w_down"], np.float32
                              ).reshape(64, 128, D))
    bq_fm = _fm16(inputs["L0_ffn_bq"])
    bk_fm = _np_f32(np.asarray(inputs["L0_ffn_bk"], np.float32).reshape(8, 128).T)
    bv_fm = _np_f32(np.asarray(inputs["L0_ffn_bv"], np.float32).reshape(8, 128).T)
    ln1b = _np_f32(np.tile(np.asarray(inputs["L0_ffn_ln1_w"], np.float32
                                      )[None, :], (128, 1)))
    ln2_fm = _fm16(inputs["L0_ffn_ln2_w"])
    costab = _make_costab()

    wi_l = _slabs_out(np.asarray(inputs["L0_attn_W_in"], np.float32).T)
    cv_l = _slabs_out(np.asarray(inputs["L0_attn_conv_w"], np.float32).T)
    wo2_l = _slabs_out(np.asarray(inputs["L0_attn_W_out"], np.float32).T)
    dcol = _np_f32(np.asarray(inputs["L0_attn_D"], np.float32).reshape(32, 128).T)

    jc_cores = []
    for c in range(NC):
        jcv = np.zeros((128, 256), np.float32)
        jcv[:, 0:128] = np.arange(c * 128, (c + 1) * 128, dtype=np.float32)[None, :]
        jcv[:, 128:256] = np.arange((c + 8) * 128, (c + 9) * 128,
                                    dtype=np.float32)[None, :]
        jc_cores.append(_np_f32(jcv))

    def ffn_round(ssb, lnx_full, x_tm_cores, myscores_cores):
        r2 = _launch("p2", build_p2,
                     [{"ssb": ssb, "lnxf": lnx_full, "jc": jc_cores[c]}
                      for c in range(NC)])
        cidx = np.asarray(r2[0]["cidxo"], np.float32)
        inv_full = np.zeros(SSEL, np.int32)
        sel_full = np.zeros((SSEL, D), bf16)
        for c in range(NC):
            inv = np.asarray(r2[c]["inv"])
            inv_full[c * 128:(c + 1) * 128] = inv[:, 0]
            inv_full[(c + 8) * 128:(c + 9) * 128] = inv[:, 1]
            selr = np.asarray(r2[c]["selr"])
            sel_full[c * 128:(c + 1) * 128] = selr[0]
            sel_full[(c + 8) * 128:(c + 9) * 128] = selr[1]
        in3 = []
        for c in range(NC):
            pt = _perm_tiles(c)
            rows = np.concatenate([np.arange(t * 128, (t + 1) * 128)
                                   for t in pt])
            sel_p = np.ascontiguousarray(sel_full[rows])
            inv_p = np.ascontiguousarray(
                np.stack([inv_full[t * 128:(t + 1) * 128] for t in pt],
                         axis=1))
            in3.append({
                "sel": sel_p, "invp": inv_p, "costab": costab,
                "mask": _masks_for_core(c),
                "wq_l": wq_l, "wk_l": wk_l, "wv_l": wv_l, "wo_l": wo_l,
                "wg_l": wg_l, "wu_l": wu_l, "wd_l": wd_l,
                "bq_fm": bq_fm, "bk_fm": bk_fm, "bv_fm": bv_fm,
                "ln1b": ln1b, "ln2_fm": ln2_fm,
            })
        r3 = _launch("p3", build_p3, in3)
        procf = np.zeros((SSEL, D), np.float32)
        for c in range(NC):
            pr = np.asarray(r3[c]["proc"])
            procf[c * 128:(c + 1) * 128] = pr[0]
            procf[(c + 8) * 128:(c + 9) * 128] = pr[1]
        return procf, cidx

    wcat = _wcat_build([wq_l, wk_l, wv_l, wo_l, wg_l, wu_l, wd_l,
                        wi_l, cv_l, wo2_l])
    r1 = _launch("p1", build_p1,
                 [{"x": np.ascontiguousarray(xf[c * TC:(c + 1) * TC]),
                   "wrt": wrt_tiled,
                   "wcat": np.ascontiguousarray(
                       wcat[:, c * WCOLS:(c + 1) * WCOLS])}
                  for c in range(NC)])
    wb = _wcat_split(np.concatenate(
        [np.asarray(r1[c]["wcb"]) for c in range(NC)], axis=1))
    wq_l, wk_l, wv_l, wo_l = wb["wq"], wb["wk"], wb["wv"], wb["wo"]
    wg_l, wu_l, wd_l = wb["wg"], wb["wu"], wb["wd"]
    wi_l, cv_l, wo2_l = wb["wi"], wb["cv"], wb["wo2"]
    lnx_full = np.ascontiguousarray(
        np.concatenate([np.asarray(r1[c]["lnx"]) for c in range(NC)], axis=0))
    ssb = np.ascontiguousarray(
        np.concatenate([np.asarray(r1[c]["sc"]) for c in range(NC)], axis=1))
    msc1 = [np.ascontiguousarray(np.asarray(r1[c]["sc"])) for c in range(NC)]

    procf1, cidx1 = ffn_round(ssb, lnx_full, None, msc1)

    in4 = []
    for c in range(NC):
        in4.append({
            "x": np.ascontiguousarray(xf[c * TC:(c + 1) * TC]),
            "procf": procf1,
            "cidx_own": np.ascontiguousarray(cidx1[:, c * NTC:(c + 1) * NTC]),
            "myscores": msc1[c],
            "wrt_fm": wrt_fm,
            "wi_l": wi_l, "cv_l": cv_l, "wo2_l": wo2_l, "dcol": dcol,
        })
    r4 = _launch("p4", build_p4, in4)
    x2_cores = []
    lnx2_parts = []
    msc2 = []
    ssb2_cols = []
    for c in range(NC):
        xo = np.asarray(r4[c]["xo_fm"])
        x2_cores.append(np.ascontiguousarray(
            xo.transpose(2, 0, 1).reshape(TC, D)))
        l2fm = np.asarray(r4[c]["lnx2_fm"])
        lnx2_parts.append(np.ascontiguousarray(
            l2fm.transpose(2, 0, 1).reshape(TC, D)))
        sc2 = np.asarray(r4[c]["sc2"]).reshape(NTC, 128).T
        msc2.append(np.ascontiguousarray(sc2))
        ssb2_cols.append(sc2)
    lnx2_full = np.ascontiguousarray(np.concatenate(lnx2_parts, axis=0))
    ssb2 = np.ascontiguousarray(np.concatenate(ssb2_cols, axis=1))

    procf2, cidx2 = ffn_round(ssb2, lnx2_full, None, msc2)

    in7 = []
    for c in range(NC):
        in7.append({
            "x": x2_cores[c],
            "procf": procf2,
            "cidx_own": np.ascontiguousarray(cidx2[:, c * NTC:(c + 1) * NTC]),
            "myscores": msc2[c],
        })
    r7 = _launch("p7", build_p7, in7)
    out = np.concatenate([np.asarray(r7[c]["out"]) for c in range(NC)], axis=0)
    return np.ascontiguousarray(out.reshape(B, SEQ, D), dtype=np.float32)


# version 07
# speedup vs baseline: 1.1559x; 1.1559x over previous
import math
from contextlib import ExitStack

import numpy as np
import ml_dtypes

import concourse.bass as bass
import concourse.mybir as mybir
from concourse.bass_core_rust import IndirectOffsetOnAxis
from concourse.bass_utils import run_bass_kernel_spmd

F32 = mybir.dt.float32
BF16 = mybir.dt.bfloat16
I32 = mybir.dt.int32
AF = mybir.ActivationFunctionType
ALU = mybir.AluOpType
AX = mybir.AxisListType

NC = 8
D = 2048
TT = 16384
TC = 2048
NTC = 16
B = 2
SEQ = 8192
KSEL = 1024
SSEL = 2048
HEADS = 16
KVH = 8
HD = 128
IM = 4096
IMLP = 8192
EPS_LN = 1e-5
EPS_RMS = 1e-6
OOB = 1.0e9
BISECT_ITERS = 44
BIG_NEG = -1.0e30
ISQD = 1.0 / math.sqrt(HD)

bf16 = ml_dtypes.bfloat16

_PROG = {}


def _np_f32(a):
    return np.ascontiguousarray(a, dtype=np.float32)



WCOLS = 819200 // NC


def build_p1():
    nc = bass.Bass("TRN2", target_bir_lowering=False)
    x = nc.dram_tensor("x", [TC, D], F32, kind="ExternalInput").ap()
    wrt = nc.dram_tensor("wrt", [128, D], F32, kind="ExternalInput").ap()
    wcat = nc.dram_tensor("wcat", [128, WCOLS], F32, kind="ExternalInput").ap()
    lnx = nc.dram_tensor("lnx", [TC, D], BF16, kind="ExternalOutput").ap()
    sc = nc.dram_tensor("sc", [128, NTC], F32, kind="ExternalOutput").ap()
    wcb = nc.dram_tensor("wcb", [128, WCOLS], BF16, kind="ExternalOutput").ap()

    es = ExitStack()
    with es:
        xt = [es.enter_context(nc.sbuf_tensor(f"xt{i}", [128, D], F32)) for i in range(2)]
        lt = [es.enter_context(nc.sbuf_tensor(f"lt{i}", [128, D], BF16)) for i in range(2)]
        scr = es.enter_context(nc.sbuf_tensor("scr", [128, D], F32))
        wrtb = es.enter_context(nc.sbuf_tensor("wrtb", [128, D], F32))
        w1 = es.enter_context(nc.sbuf_tensor("w1", [128, 1], F32))
        scb = es.enter_context(nc.sbuf_tensor("scb", [128, NTC], F32))
        sx = es.enter_context(nc.sbuf_tensor("sx", [128, 1], F32))
        ssq = es.enter_context(nc.sbuf_tensor("ssq", [128, 1], F32))
        xw = es.enter_context(nc.sbuf_tensor("xw", [128, 1], F32))
        m = es.enter_context(nc.sbuf_tensor("m", [128, 1], F32))
        var = es.enter_context(nc.sbuf_tensor("var", [128, 1], F32))
        sd = es.enter_context(nc.sbuf_tensor("sd", [128, 1], F32))
        rstd = es.enter_context(nc.sbuf_tensor("rstd", [128, 1], F32))
        nb = es.enter_context(nc.sbuf_tensor("nb", [128, 1], F32))
        t0 = es.enter_context(nc.sbuf_tensor("t0", [128, 1], F32))

        ld = es.enter_context(nc.semaphore("ld"))
        lxs = [es.enter_context(nc.semaphore(f"lxs{i}")) for i in range(2)]
        cm = es.enter_context(nc.semaphore("cm"))
        sq = es.enter_context(nc.semaphore("sq"))
        st = es.enter_context(nc.semaphore("st"))
        vself = es.enter_context(nc.semaphore("vself"))
        cw = es.enter_context(nc.semaphore("cw"))
        blk = es.enter_context(nc.Block())

        @blk.gpsimd
        def _(g):
            qc = WCOLS // 4
            for i in range(4):
                g.dma_start(out=wcb[:, i * qc:(i + 1) * qc],
                            in_=wcat[:, i * qc:(i + 1) * qc]).then_inc(cw, 16)

        @blk.sync
        def _(sy):
            sy.dma_start(out=wrtb[:], in_=wrt).then_inc(ld, 16)
            for j in range(NTC):
                if j >= 2:
                    sy.wait_ge(cm, 2 * j - 2)
                sy.dma_start(out=xt[j % 2][:], in_=x[j * 128:(j + 1) * 128, :]
                             ).then_inc(lxs[j % 2], 16)

        @blk.vector
        def _(v):
            vs = [0]
            def e(inst):
                vs[0] += 1
                inst.then_inc(vself, 1)
                v.wait_ge(vself, vs[0])
            v.wait_ge(ld, 16)
            v.reduce_sum(w1[:], wrtb[:], axis=AX.X)
            for j in range(NTC):
                xb = xt[j % 2]
                v.wait_ge(lxs[j % 2], (j // 2 + 1) * 16)
                v.scalar_tensor_tensor(scr[:], xb[:], 1.0, xb[:], ALU.mult,
                                       ALU.mult, accum_out=ssq[:])
                v.reduce_sum(sx[:], xb[:], axis=AX.X)
                v.scalar_tensor_tensor(scr[:], xb[:], 1.0, wrtb[:], ALU.mult,
                                       ALU.mult, accum_out=xw[:])
                e(v.tensor_scalar_mul(m[:], sx[:], 1.0 / D))
                v.tensor_scalar_mul(var[:], ssq[:], 1.0 / D)
                e(v.tensor_mul(t0[:], m[:], m[:]))
                e(v.tensor_sub(var[:], var[:], t0[:]))
                v.tensor_scalar_add(var[:], var[:], EPS_LN).then_inc(cm, 1)
                v.wait_ge(sq, j + 1)
                e(v.reciprocal(rstd[:], sd[:]))
                e(v.tensor_mul(nb[:], m[:], rstd[:]))
                e(v.tensor_scalar_mul(nb[:], nb[:], -1.0))
                if j >= 2:
                    v.wait_ge(st, 16 * (j - 1))
                v.tensor_scalar(lt[j % 2][:], xb[:], rstd[:], nb[:],
                                ALU.mult, ALU.add)
                e(v.tensor_mul(t0[:], m[:], w1[:]))
                e(v.tensor_sub(t0[:], xw[:], t0[:]))
                v.tensor_mul(scb[:, j:j + 1], t0[:], rstd[:]).then_inc(cm, 1)

        @blk.scalar
        def _(s):
            for j in range(NTC):
                s.wait_ge(cm, 2 * j + 1)
                s.sqrt(sd[:], var[:]).then_inc(sq, 1)

        @blk.gpsimd
        def _(g):
            for j in range(NTC):
                g.wait_ge(cm, 2 * j + 2)
                g.dma_start(out=lnx[j * 128:(j + 1) * 128, :], in_=lt[j % 2][:]
                            ).then_inc(st, 16)
            g.dma_start(out=sc, in_=scb[:]).then_inc(st, 16)
            g.wait_ge(st, 16 * NTC + 16)
            g.wait_ge(cw, 64)
    return nc



def build_p2():
    nc = bass.Bass("TRN2", target_bir_lowering=False)
    ssb_d = nc.dram_tensor("ssb", [128, 128], F32, kind="ExternalInput").ap()
    lnxf = nc.dram_tensor("lnxf", [TT, D], BF16, kind="ExternalInput").ap()
    jc_d = nc.dram_tensor("jc", [128, 256], F32, kind="ExternalInput").ap()
    inv_d = nc.dram_tensor("inv", [128, 2], I32, kind="ExternalOutput").ap()
    cidx_d = nc.dram_tensor("cidxo", [128, 128], F32, kind="ExternalOutput").ap()
    selr_d = nc.dram_tensor("selr", [2, 128, D], BF16, kind="ExternalOutput").ap()

    ones_row_f = nc.inline_tensor(np.ones((1, 128), np.float32), name="ones_row_f")
    ones_row_b = nc.inline_tensor(np.ones((1, 128), bf16), name="ones_row_b")
    ones_col_b = nc.inline_tensor(np.ones((128, 1), bf16), name="ones_col_b")
    tri = np.zeros((128, 128), np.float32)
    for f in range(128):
        lo_ = 0 if f < 64 else 64
        tri[lo_:f, f] = 1.0
    trib_c = nc.inline_tensor(tri.astype(bf16), name="trib")
    ptri = (np.tril(np.ones((128, 128), np.float32)).T).astype(bf16)
    ptri_c = nc.inline_tensor(ptri, name="ptri")
    boff = np.zeros((128, 128), np.float32)
    boff[:, :64] = -1.0
    boff[:, 64:] = 1023.0
    boff_c = nc.inline_tensor(boff, name="boff")
    rhs2 = np.zeros((128, 256), np.float32)
    rhs2[:, 0::2] = np.arange(128, dtype=np.float32)[:, None]
    rhs2[:, 1::2] = np.arange(128, dtype=np.float32)[None, :]
    rhs2_c = nc.inline_tensor(rhs2.astype(bf16), name="rhs2")
    bigc = nc.inline_tensor(np.full((128, 128), OOB, np.float32), name="bigc")

    es = ExitStack()
    with es:
        ssb = es.enter_context(nc.sbuf_tensor("ssb_s", [128, 128], F32))
        jc = es.enter_context(nc.sbuf_tensor("jc_s", [128, 256], F32))
        onesrf = es.enter_context(nc.sbuf_tensor("onesrf", [1, 128], F32))
        onesrb = es.enter_context(nc.sbuf_tensor("onesrb", [1, 128], BF16))
        onescb = es.enter_context(nc.sbuf_tensor("onescb", [128, 1], BF16))
        trib = es.enter_context(nc.sbuf_tensor("trib_s", [128, 128], BF16))
        ptrib = es.enter_context(nc.sbuf_tensor("ptri_s", [128, 128], BF16))
        boffs = es.enter_context(nc.sbuf_tensor("boff_s", [128, 128], F32))
        rhs2s = es.enter_context(nc.sbuf_tensor("rhs2_s", [128, 256], BF16))
        bigs = es.enter_context(nc.sbuf_tensor("big_s", [128, 128], F32))

        lo = es.enter_context(nc.sbuf_tensor("lo", [1, 2], F32))
        hi = es.enter_context(nc.sbuf_tensor("hi", [1, 2], F32))
        mid = es.enter_context(nc.sbuf_tensor("mid", [1, 2], F32))
        trow = es.enter_context(nc.sbuf_tensor("trow", [1, 128], F32))
        cnt = es.enter_context(nc.sbuf_tensor("cnt", [1, 2], F32))
        flag = es.enter_context(nc.sbuf_tensor("flag", [1, 2], F32))
        dlt = es.enter_context(nc.sbuf_tensor("dlt", [1, 2], F32))
        m1 = es.enter_context(nc.sbuf_tensor("m1", [1, 2], F32))
        rr = es.enter_context(nc.sbuf_tensor("rr", [1, 2], F32))
        maskb = es.enter_context(nc.sbuf_tensor("maskb", [128, 128], BF16))
        strongb = es.enter_context(nc.sbuf_tensor("strongb", [128, 128], BF16))
        borderb = es.enter_context(nc.sbuf_tensor("borderb", [128, 128], BF16))
        selb = es.enter_context(nc.sbuf_tensor("selb", [128, 128], BF16))
        csrow = es.enter_context(nc.sbuf_tensor("csrow", [1, 128], BF16))
        csb = es.enter_context(nc.sbuf_tensor("csb", [128, 128], BF16))
        cum = es.enter_context(nc.sbuf_tensor("cum", [128, 128], F32))
        tle = es.enter_context(nc.sbuf_tensor("tle", [128, 128], BF16))
        cidx = es.enter_context(nc.sbuf_tensor("cidx", [128, 128], F32))
        cidxm = es.enter_context(nc.sbuf_tensor("cidxm", [128, 128], F32))
        oh = [es.enter_context(nc.sbuf_tensor(f"oh{i}", [128, 128], BF16))
              for i in range(4)]
        invf = es.enter_context(nc.sbuf_tensor("invf", [128, 4], F32))
        invi = es.enter_context(nc.sbuf_tensor("invi", [128, 2], I32))
        selr = es.enter_context(nc.sbuf_tensor("selr_s", [128, 2 * D], BF16))

        ps_tau = es.enter_context(nc.psum_tensor("ps_tau", [128, 128], F32))
        ps_cnt = es.enter_context(nc.psum_tensor("ps_cnt", [1, 128], F32))
        ps_cum = es.enter_context(nc.psum_tensor("ps_cum", [128, 128], F32))
        ps_csb = es.enter_context(nc.psum_tensor("ps_csb", [128, 128], F32))
        ps_inv = [es.enter_context(nc.psum_tensor(f"ps_inv{i}", [128, 2], F32))
                  for i in range(2)]

        ld = es.enter_context(nc.semaphore("ld"))
        v2t = es.enter_context(nc.semaphore("v2t"))
        t2v = es.enter_context(nc.semaphore("t2v"))
        so = es.enter_context(nc.semaphore("so"))
        vself = es.enter_context(nc.semaphore("vself"))
        blk = es.enter_context(nc.Block())

        BV = 2 * BISECT_ITERS
        V_HIP = BV + 1
        V_STR = BV + 2
        V_LOP = BV + 3
        V_BOR = BV + 4
        V_CS1 = BV + 5
        V_CB1 = BV + 6
        V_RRP = BV + 7
        V_SEL = BV + 8
        V_CS2 = BV + 9
        V_CB2 = BV + 10
        V_CID = BV + 11
        V_OH0 = BV + 12
        V_INV = V_OH0 + 256
        T_TAUH = BV + 1
        T_CNTH = BV + 2
        T_TAUL = BV + 3
        T_BOR = BV + 4
        T_CB1 = BV + 5
        T_TRI1 = BV + 6
        T_TAUR = BV + 7
        T_SEL = BV + 9
        T_CB2 = BV + 10
        T_TRI2 = BV + 11
        T_OH0 = BV + 12

        NLD = 10 * 16

        def v_tau_prep(v, src, sem=None):
            v.tensor_copy(trow[:, 0:64], src[0:1, 0:1].to_broadcast([1, 64]))
            i2 = v.tensor_copy(trow[:, 64:128], src[0:1, 1:2].to_broadcast([1, 64]))
            if sem is not None:
                i2.then_inc(sem, 1)

        @blk.sync
        def _(sy):
            sy.dma_start(out=ssb[:], in_=ssb_d).then_inc(ld, 16)
            sy.dma_start(out=jc[:], in_=jc_d).then_inc(ld, 16)
            sy.dma_start(out=onesrf[:], in_=ones_row_f.ap()).then_inc(ld, 16)
            sy.dma_start(out=onesrb[:], in_=ones_row_b.ap()).then_inc(ld, 16)
            sy.dma_start(out=onescb[:], in_=ones_col_b.ap()).then_inc(ld, 16)
            sy.dma_start(out=trib[:], in_=trib_c.ap()).then_inc(ld, 16)
            sy.dma_start(out=ptrib[:], in_=ptri_c.ap()).then_inc(ld, 16)
            sy.dma_start(out=boffs[:], in_=boff_c.ap()).then_inc(ld, 16)
            sy.dma_start(out=rhs2s[:], in_=rhs2_c.ap()).then_inc(ld, 16)
            sy.dma_start(out=bigs[:], in_=bigc.ap()).then_inc(ld, 16)

        @blk.vector
        def _(v):
            vs = [0]
            def e(inst):
                vs[0] += 1
                inst.then_inc(vself, 1)
                v.wait_ge(vself, vs[0])
            v.wait_ge(ld, NLD)
            e(v.memset(lo[:], -512.0))
            e(v.memset(hi[:], 512.0))
            for i in range(BISECT_ITERS):
                e(v.tensor_add(mid[:], lo[:], hi[:]))
                e(v.tensor_scalar_mul(mid[:], mid[:], 0.5))
                v_tau_prep(v, mid, v2t)
                v.wait_ge(t2v, 2 * i + 1)
                v.tensor_tensor(maskb[:], ssb[:], ps_tau[:], ALU.is_ge
                                ).then_inc(v2t, 1)
                v.wait_ge(t2v, 2 * i + 2)
                e(v.reduce_sum(cnt[:, 0:1], ps_cnt[:, 0:64], axis=AX.X))
                e(v.reduce_sum(cnt[:, 1:2], ps_cnt[:, 64:128], axis=AX.X))
                e(v.tensor_scalar(flag[:], cnt[:], float(KSEL), None, ALU.is_ge))
                e(v.tensor_sub(dlt[:], mid[:], lo[:]))
                e(v.tensor_mul(dlt[:], dlt[:], flag[:]))
                e(v.tensor_add(lo[:], lo[:], dlt[:]))
                e(v.tensor_sub(dlt[:], hi[:], mid[:]))
                e(v.tensor_mul(dlt[:], dlt[:], flag[:]))
                e(v.tensor_add(hi[:], mid[:], dlt[:]))
            v_tau_prep(v, hi, v2t)
            v.wait_ge(t2v, T_TAUH)
            v.tensor_tensor(strongb[:], ssb[:], ps_tau[:], ALU.is_ge
                            ).then_inc(v2t, 1)
            v.wait_ge(t2v, T_CNTH)
            e(v.reduce_sum(m1[:, 0:1], ps_cnt[:, 0:64], axis=AX.X))
            e(v.reduce_sum(m1[:, 1:2], ps_cnt[:, 64:128], axis=AX.X))
            e(v.tensor_scalar(rr[:], m1[:], float(KSEL), None, ALU.subtract))
            e(v.tensor_scalar_mul(rr[:], rr[:], -1.0))
            v_tau_prep(v, lo, v2t)
            v.wait_ge(t2v, T_TAUL)
            e(v.tensor_tensor(borderb[:], ssb[:], ps_tau[:], ALU.is_ge))
            v.tensor_sub(borderb[:], borderb[:], strongb[:]).then_inc(v2t, 1)
            v.wait_ge(t2v, T_BOR)
            v.tensor_copy(csrow[:], ps_cnt[:]).then_inc(v2t, 1)
            v.wait_ge(t2v, T_CB1)
            v.tensor_copy(csb[:], ps_csb[:]).then_inc(v2t, 1)
            v_tau_prep(v, rr, v2t)
            v.wait_ge(t2v, T_TAUR)
            e(v.tensor_copy(cum[:], ps_cum[:]))
            e(v.tensor_tensor(tle[:], cum[:], ps_tau[:], ALU.is_le))
            e(v.tensor_mul(tle[:], tle[:], borderb[:]))
            v.tensor_add(selb[:], strongb[:], tle[:]).then_inc(v2t, 1)
            v.wait_ge(t2v, T_SEL)
            v.tensor_copy(csrow[:], ps_cnt[:]).then_inc(v2t, 1)
            v.wait_ge(t2v, T_CB2)
            v.tensor_copy(csb[:], ps_csb[:]).then_inc(v2t, 1)
            v.wait_ge(t2v, T_TRI2)
            e(v.tensor_tensor(cidx[:], ps_cum[:], boffs[:], ALU.add))
            e(v.tensor_tensor(cidx[:], cidx[:], selb[:], ALU.mult))
            e(v.tensor_scalar(tle[:], selb[:], 1.0, None, ALU.subtract))
            e(v.tensor_scalar(cidx2t := tle[:], tle[:], -OOB, None, ALU.mult))
            v.tensor_tensor(cidxm[:], cidx[:], tle[:], ALU.add).then_inc(v2t, 1)
            for k in range(256):
                f, u = k // 2, k % 2
                slot = k % 4
                if k >= 4:
                    v.wait_ge(t2v, T_OH0 + (k - 4))
                v.tensor_scalar(oh[slot][:], jc[:, u * 128:(u + 1) * 128],
                                cidxm[:, f:f + 1], None, ALU.is_equal
                                ).then_inc(v2t, 1)
            v.wait_ge(t2v, T_OH0 + 255)
            e(v.tensor_copy(invf[:, 0:2], ps_inv[0][:]))
            e(v.tensor_copy(invf[:, 2:4], ps_inv[1][:]))
            e(v.tensor_scalar_mul(invf[:, 1:2], invf[:, 1:2], 128.0))
            e(v.tensor_add(invf[:, 0:1], invf[:, 0:1], invf[:, 1:2]))
            e(v.tensor_scalar_mul(invf[:, 3:4], invf[:, 3:4], 128.0))
            e(v.tensor_add(invf[:, 2:3], invf[:, 2:3], invf[:, 3:4]))
            e(v.tensor_copy(invi[:, 0:1], invf[:, 0:1]))
            v.tensor_copy(invi[:, 1:2], invf[:, 2:3]).then_inc(v2t, 1)

        @blk.tensor
        def _(t):
            t.wait_ge(ld, NLD)
            for i in range(BISECT_ITERS):
                t.wait_ge(v2t, 2 * i + 1)
                t.matmul(ps_tau[:], onesrf[:], trow[:], start=True, stop=True
                         ).then_inc(t2v, 1)
                t.wait_ge(v2t, 2 * i + 2)
                t.matmul(ps_cnt[:], onescb[:], maskb[:], start=True, stop=True
                         ).then_inc(t2v, 1)
            t.wait_ge(v2t, V_HIP)
            t.matmul(ps_tau[:], onesrf[:], trow[:], start=True, stop=True
                     ).then_inc(t2v, 1)
            t.wait_ge(v2t, V_STR)
            t.matmul(ps_cnt[:], onescb[:], strongb[:], start=True, stop=True
                     ).then_inc(t2v, 1)
            t.wait_ge(v2t, V_LOP)
            t.matmul(ps_tau[:], onesrf[:], trow[:], start=True, stop=True
                     ).then_inc(t2v, 1)
            t.wait_ge(v2t, V_BOR)
            t.matmul(ps_cum[:], ptrib[:], borderb[:], start=True, stop=False)
            t.matmul(ps_cnt[:], onescb[:], borderb[:], start=True, stop=True
                     ).then_inc(t2v, 1)
            t.wait_ge(v2t, V_CS1)
            t.matmul(ps_csb[:], csrow[:], onesrb[:], start=True, stop=True
                     ).then_inc(t2v, 1)
            t.wait_ge(v2t, V_CB1)
            t.matmul(ps_cum[:], csb[:], trib[:], start=False, stop=True
                     ).then_inc(t2v, 1)
            t.wait_ge(v2t, V_RRP)
            t.matmul(ps_tau[:], onesrf[:], trow[:], start=True, stop=True
                     ).then_inc(t2v, 1)
            t.wait_ge(v2t, V_SEL)
            t.matmul(ps_cum[:], ptrib[:], selb[:], start=True, stop=False)
            t.matmul(ps_cnt[:], onescb[:], selb[:], start=True, stop=True
                     ).then_inc(t2v, 2)
            t.wait_ge(v2t, V_CS2)
            t.matmul(ps_csb[:], csrow[:], onesrb[:], start=True, stop=True
                     ).then_inc(t2v, 1)
            t.wait_ge(v2t, V_CB2)
            t.matmul(ps_cum[:], csb[:], trib[:], start=False, stop=True
                     ).then_inc(t2v, 1)
            for k in range(256):
                f, u = k // 2, k % 2
                slot = k % 4
                t.wait_ge(v2t, V_OH0 + k)
                t.matmul(ps_inv[u][:], oh[slot][:], rhs2s[:, 2 * f:2 * f + 2],
                         start=(f == 0), stop=(f == 127)).then_inc(t2v, 1)

        @blk.gpsimd
        def _(g):
            g.wait_ge(v2t, V_CID)
            g.dma_start(out=cidx_d, in_=cidxm[:]).then_inc(so, 16)
            g.wait_ge(v2t, V_INV)
            g.dma_start(out=inv_d, in_=invi[:]).then_inc(so, 16)
            for a in range(2):
                g.indirect_dma_start(
                    selr[:, a * D:(a + 1) * D], None,
                    lnxf, IndirectOffsetOnAxis(invi[:, a:a + 1], 0),
                    bounds_check=TT - 1, oob_is_err=False,
                ).then_inc(so, 16)
            g.wait_ge(so, 64)
            g.dma_start(out=selr_d[0], in_=selr[:, 0:D]).then_inc(so, 16)
            g.dma_start(out=selr_d[1], in_=selr[:, D:2 * D]).then_inc(so, 16)
            g.wait_ge(so, 96)
    return nc



BATCH_OF_PCOL = [0, 1] + [0] * 7 + [1] * 7


def build_p3():
    nc = bass.Bass("TRN2", target_bir_lowering=False)
    sel_d = nc.dram_tensor("sel", [SSEL, D], BF16, kind="ExternalInput").ap()
    inv_d = nc.dram_tensor("invp", [128, NTC], I32, kind="ExternalInput").ap()
    costab = nc.dram_tensor("costab", [SEQ, 256], F32, kind="ExternalInput").ap()
    mask_d = nc.dram_tensor("mask", [2, 128, SSEL], F32, kind="ExternalInput").ap()
    wq_d = nc.dram_tensor("wq_l", [HEADS, 128, 16 * 128], BF16, kind="ExternalInput").ap()
    wk_d = nc.dram_tensor("wk_l", [KVH, 128, 16 * 128], BF16, kind="ExternalInput").ap()
    wv_d = nc.dram_tensor("wv_l", [KVH, 128, 16 * 128], BF16, kind="ExternalInput").ap()
    wo_d = nc.dram_tensor("wo_l", [16, 128, 16 * 128], BF16, kind="ExternalInput").ap()
    wg_d = nc.dram_tensor("wg_l", [64, 128, 16 * 128], BF16, kind="ExternalInput").ap()
    wu_d = nc.dram_tensor("wu_l", [64, 128, 16 * 128], BF16, kind="ExternalInput").ap()
    wd_d = nc.dram_tensor("wd_l", [64, 128, 16 * 128], BF16, kind="ExternalInput").ap()
    bq_d = nc.dram_tensor("bq_fm", [128, HEADS], F32, kind="ExternalInput").ap()
    bk_d = nc.dram_tensor("bk_fm", [128, KVH], F32, kind="ExternalInput").ap()
    bv_d = nc.dram_tensor("bv_fm", [128, KVH], F32, kind="ExternalInput").ap()
    ln1_d = nc.dram_tensor("ln1b", [128, D], F32, kind="ExternalInput").ap()
    ln2_d = nc.dram_tensor("ln2_fm", [128, NTC], F32, kind="ExternalInput").ap()
    proc_d = nc.dram_tensor("proc", [2, 128, D], F32, kind="ExternalOutput").ap()

    idn_c = nc.inline_tensor(np.eye(128, dtype=bf16), name="idn")
    idnf_c = nc.inline_tensor(np.eye(128, dtype=np.float32), name="idnf")
    rmat = np.zeros((128, 128), np.float32)
    for dp in range(64):
        rmat[dp + 64, dp] = -1.0
    for dp in range(64, 128):
        rmat[dp - 64, dp] = 1.0
    rmat_c = nc.inline_tensor(rmat.astype(bf16), name="rmat")
    onescb_c = nc.inline_tensor(np.ones((128, 1), bf16), name="onescb3")
    onesrb_c = nc.inline_tensor(np.ones((1, 128), bf16), name="onesrb3")
    pofs = np.zeros((128, NTC), np.float32)
    for k in range(NTC):
        pofs[:, k] = 8192.0 * BATCH_OF_PCOL[k]
    pofs_c = nc.inline_tensor(pofs, name="pofs")

    es = ExitStack()
    with es:
        buf1 = es.enter_context(nc.sbuf_tensor("buf1", [128, 32768], BF16))
        buf2 = es.enter_context(nc.sbuf_tensor("buf2", [128, NTC * D], BF16))
        x1 = es.enter_context(nc.sbuf_tensor("x1", [128, 4096], F32))
        pstage = es.enter_context(nc.sbuf_tensor("pstage", [128, 1024], F32))
        sfm_scrb = es.enter_context(nc.sbuf_tensor("sfm_scrb", [128, 4096], BF16))
        qsb = es.enter_context(nc.sbuf_tensor("qsb", [128, 4096], BF16))
        attsb = es.enter_context(nc.sbuf_tensor("attsb", [128, SSEL], BF16))
        attT = es.enter_context(nc.sbuf_tensor("attT", [128, SSEL], BF16))
        cosb = es.enter_context(nc.sbuf_tensor("cosb", [128, SSEL], BF16))
        sinb = es.enter_context(nc.sbuf_tensor("sinb", [128, SSEL], BF16))
        kro = es.enter_context(nc.sbuf_tensor("kro", [128, 512], BF16))
        scrf = es.enter_context(nc.sbuf_tensor("scrf", [128, 512], F32))
        scrf2 = es.enter_context(nc.sbuf_tensor("scrf2", [128, 512], F32))
        wslab = [es.enter_context(nc.sbuf_tensor(f"wslab{i}", [128, 2048], BF16))
                 for i in range(4)]
        bqs = es.enter_context(nc.sbuf_tensor("bqs", [128, HEADS], F32))
        bks = es.enter_context(nc.sbuf_tensor("bks", [128, KVH], F32))
        bvs = es.enter_context(nc.sbuf_tensor("bvs", [128, KVH], F32))
        ln2s = es.enter_context(nc.sbuf_tensor("ln2s", [128, NTC], F32))
        idn = es.enter_context(nc.sbuf_tensor("idn_s", [128, 128], BF16))
        idnf = es.enter_context(nc.sbuf_tensor("idnf_s", [128, 128], F32))
        rmt = es.enter_context(nc.sbuf_tensor("rmt_s", [128, 128], BF16))
        invs = es.enter_context(nc.sbuf_tensor("invs", [128, NTC], I32))
        posf = es.enter_context(nc.sbuf_tensor("posf", [128, NTC], F32))
        posi = es.enter_context(nc.sbuf_tensor("posi", [128, NTC], I32))
        pofs_s = es.enter_context(nc.sbuf_tensor("pofs_s", [128, NTC], F32))
        ssq = es.enter_context(nc.sbuf_tensor("ssq3", [128, 1], F32))
        rstd = es.enter_context(nc.sbuf_tensor("rstd3", [128, 1], F32))
        mx = es.enter_context(nc.sbuf_tensor("mx3", [128, 1], F32))
        nb2 = es.enter_context(nc.sbuf_tensor("nb23", [128, 1], F32))
        asum = es.enter_context(nc.sbuf_tensor("asum3", [128, 1], F32))
        rec = es.enter_context(nc.sbuf_tensor("rec3", [128, 1], F32))
        var2 = es.enter_context(nc.sbuf_tensor("var2", [1, 256], F32))
        rstd2 = es.enter_context(nc.sbuf_tensor("rstd2", [1, 256], F32))
        rstd2b = es.enter_context(nc.sbuf_tensor("rstd2b", [1, 256], BF16))
        onescb = es.enter_context(nc.sbuf_tensor("onescb3s", [128, 1], BF16))
        onesrb = es.enter_context(nc.sbuf_tensor("onesrb3s", [1, 128], BF16))

        psBig = es.enter_context(nc.psum_tensor("psBig", [128, 2048], F32))
        psA = es.enter_context(nc.psum_tensor("psA", [128, 512], F32))
        psW = es.enter_context(nc.psum_tensor("psW", [128, 512], F32))
        psT = es.enter_context(nc.psum_tensor("psT", [128, 1024], F32))

        def psTs(i):
            return psT[:, (i % 2) * 512:(i % 2) * 512 + 128]

        def psAV(a):
            return psA[:, 0:128] if a == 0 else psW[:, 0:128]

        ld = es.enter_context(nc.semaphore("ld"))
        gw = es.enter_context(nc.semaphore("gw"))
        gsl = [es.enter_context(nc.semaphore(f"gsl{i}")) for i in range(4)]
        tv = es.enter_context(nc.semaphore("tv"))
        vt = es.enter_context(nc.semaphore("vt"))
        sv = es.enter_context(nc.semaphore("sv"))
        vself = es.enter_context(nc.semaphore("vself"))
        so = es.enter_context(nc.semaphore("so"))
        blk = es.enter_context(nc.Block())

        C = dict(ld=0, gw=0, tv=0, vt=0, sv=0, so=0)
        G = [0, 0, 0, 0]
        ve_n = [0]

        def VE(v, inst):
            ve_n[0] += 1
            inst.then_inc(vself, 1)
            v.wait_ge(vself, ve_n[0])

        def kfm(kvh):
            return buf1[:, kvh * 2048:(kvh + 1) * 2048]

        def vtm(k):
            return buf1[:, 16384 + k * 1024:16384 + (k + 1) * 1024]

        def hfm(pt):
            return buf2[:, pt * 2048:(pt + 1) * 2048]

        ln1b_sb = None

        @blk.sync
        def _(sy):
            for k in range(NTC):
                sy.dma_start(out=buf1[:, k * D:(k + 1) * D],
                             in_=sel_d[k * 128:(k + 1) * 128, :]).then_inc(ld, 16)
            sy.dma_start(out=bqs[:], in_=bq_d).then_inc(ld, 16)
            sy.dma_start(out=bks[:], in_=bk_d).then_inc(ld, 16)
            sy.dma_start(out=bvs[:], in_=bv_d).then_inc(ld, 16)
            sy.dma_start(out=ln2s[:], in_=ln2_d).then_inc(ld, 16)
            sy.dma_start(out=idn[:], in_=idn_c.ap()).then_inc(ld, 16)
            sy.dma_start(out=idnf[:], in_=idnf_c.ap()).then_inc(ld, 16)
            sy.dma_start(out=rmt[:], in_=rmat_c.ap()).then_inc(ld, 16)
            sy.dma_start(out=onescb[:], in_=onescb_c.ap()).then_inc(ld, 16)
            sy.dma_start(out=onesrb[:], in_=onesrb_c.ap()).then_inc(ld, 16)
            sy.dma_start(out=pofs_s[:], in_=pofs_c.ap()).then_inc(ld, 16)
            sy.dma_start(out=invs[:], in_=inv_d).then_inc(ld, 16)

        @blk.gpsimd
        def _(g):
            g.dma_start(out=cosb[:], in_=ln1_d).then_inc(gw, 16)

        C["ld"] += (NTC + 11) * 16
        C["gw"] += 16
        LD_ALL = C["ld"]
        GW_LN1 = C["gw"]

        T0B = C["tv"]
        S0B = C["sv"]

        @blk.tensor
        def _(t):
            t.wait_ge(ld, LD_ALL)
            for i in range(32):
                k, pt = i % 2, i // 2
                if i >= 2:
                    t.wait_ge(sv, S0B + i - 1)
                t.matmul(psTs(i),
                         buf1[:, k * D + pt * 128:k * D + pt * 128 + 128],
                         idn[:], start=True, stop=True).then_inc(tv, 1)

        @blk.scalar
        def _(s):
            for i in range(32):
                k, pt = i % 2, i // 2
                s.wait_ge(tv, T0B + i + 1)
                s.copy(sfm_scrb[:, pt * 256 + k * 128:pt * 256 + (k + 1) * 128],
                       psTs(i)).then_inc(sv, 1)
        C["tv"] += 32
        C["sv"] += 32
        SV_OWN = C["sv"]

        VT1 = C["vt"]
        SV1 = C["sv"]

        @blk.vector
        def _(v):
            v.wait_ge(ld, LD_ALL)
            v.wait_ge(gw, GW_LN1)
            v.wait_ge(sv, SV_OWN)
            for k in range(NTC):
                tile = buf1[:, k * D:(k + 1) * D]
                VE(v, v.scalar_tensor_tensor(scrf[:], tile[:, 0:512], 1.0,
                                             tile[:, 0:512], ALU.mult,
                                             ALU.mult, accum_out=ssq[:]))
                for q4 in range(1, 4):
                    VE(v, v.scalar_tensor_tensor(
                        scrf[:], tile[:, q4 * 512:(q4 + 1) * 512], 1.0,
                        tile[:, q4 * 512:(q4 + 1) * 512], ALU.mult, ALU.mult,
                        accum_out=rec[:]))
                    VE(v, v.tensor_add(ssq[:], ssq[:], rec[:]))
                VE(v, v.tensor_scalar(ssq[:], ssq[:], 1.0 / D, EPS_RMS,
                                      ALU.mult, ALU.add))
                v.sem_inc(vt, 1)
                v.wait_ge(sv, SV1 + k + 1)
                VE(v, v.reciprocal(rstd[:], rstd[:]))
                VE(v, v.scalar_tensor_tensor(tile, tile, rstd[:], cosb[:],
                                             ALU.mult, ALU.mult))
                v.sem_inc(vt, 1)
            v.sem_inc(vt, 1)

        @blk.scalar
        def _(s):
            for k in range(NTC):
                s.wait_ge(vt, VT1 + 2 * k + 1)
                s.sqrt(rstd[:], ssq[:]).then_inc(sv, 1)
        C["vt"] += 2 * NTC + 1
        C["sv"] += NTC
        VT_H = C["vt"]

        @blk.vector
        def _(v):
            VE(v, v.tensor_copy(posf[:], invs[:]))
            VE(v, v.tensor_sub(posf[:], posf[:], pofs_s[:]))
            v.tensor_copy(posi[:], posf[:]).then_inc(vt, 1)
        C["vt"] += 1
        VT_POS = C["vt"]

        GW2 = C["gw"]
        TV2 = C["tv"]
        VT2b = C["vt"]

        @blk.gpsimd
        def _(g):
            g.wait_ge(vt, VT_POS)
            for qtr in range(4):
                if qtr >= 1:
                    g.wait_ge(vt, VT2b + 8 * qtr)
                for kk in range(4):
                    k = qtr * 4 + kk
                    g.indirect_dma_start(
                        pstage[:, kk * 256:(kk + 1) * 256], None, costab,
                        IndirectOffsetOnAxis(posi[:, k:k + 1], 0),
                        bounds_check=SEQ - 1, oob_is_err=False,
                    ).then_inc(gw, 16)

        @blk.tensor
        def _(t):
            for qtr in range(4):
                for kk in range(4):
                    k = qtr * 4 + kk
                    t.wait_ge(gw, GW2 + (qtr + 1) * 64)
                    if k >= 1:
                        t.wait_ge(vt, VT2b + 2 * k)
                    t.transpose(psA[:, 0:128],
                                pstage[:, kk * 256:kk * 256 + 128], idnf[:]
                                ).then_inc(tv, 1)
                    t.transpose(psW[:, 0:128],
                                pstage[:, kk * 256 + 128:kk * 256 + 256],
                                idnf[:]).then_inc(tv, 1)

        @blk.vector
        def _(v):
            for k in range(NTC):
                v.wait_ge(tv, TV2 + 2 * k + 1)
                v.tensor_copy(cosb[:, k * 128:(k + 1) * 128], psA[:, 0:128]
                              ).then_inc(vt, 1)
                v.wait_ge(tv, TV2 + 2 * k + 2)
                v.tensor_copy(sinb[:, k * 128:(k + 1) * 128], psW[:, 0:128]
                              ).then_inc(vt, 1)
        C["gw"] += 256
        C["tv"] += 32
        C["vt"] += 32

        T2B = C["tv"]
        S2B = C["sv"]

        @blk.tensor
        def _(t):
            for i in range(256):
                k, pt = i // NTC, i % NTC
                if pt == 0:
                    t.wait_ge(vt, VT1 + 2 * k + 2)
                if i >= 2:
                    t.wait_ge(sv, S2B + i - 1)
                t.matmul(psTs(i),
                         buf1[:, k * D + pt * 128:k * D + pt * 128 + 128],
                         idn[:], start=True, stop=True).then_inc(tv, 1)

        @blk.scalar
        def _(s):
            for i in range(256):
                k, pt = i // NTC, i % NTC
                s.wait_ge(tv, T2B + i + 1)
                s.copy(buf2[:, pt * D + k * 128:pt * D + (k + 1) * 128],
                       psTs(i)).then_inc(sv, 1)
        C["tv"] += 256
        C["sv"] += 256
        SV_HFM = C["sv"]

        GW4 = C["gw"]
        TV4 = C["tv"]
        VT4 = C["vt"]

        G4 = list(G)

        @blk.gpsimd
        def _(g):
            for kvh in range(KVH):
                if kvh >= 2:
                    g.wait_ge(vt, VT4 + 12 * (kvh - 1))
                g.dma_start(out=wslab[kvh % 2][:], in_=wk_d[kvh]
                            ).then_inc(gsl[kvh % 2], 16)

        @blk.tensor
        def _(t):
            t.wait_ge(vt, VT4)
            t.wait_ge(sv, SV_HFM)
            for kvh in range(KVH):
                t.wait_ge(gsl[kvh % 2], G4[kvh % 2] + (kvh // 2 + 1) * 16)
                for ch in range(4):
                    if kvh + ch > 0:
                        t.wait_ge(vt, VT4 + 12 * kvh + 3 * ch)
                    for pt in range(NTC):
                        mmi = t.matmul(psA[:],
                                 wslab[kvh % 2][:, pt * 128:(pt + 1) * 128],
                                 hfm(pt)[:, ch * 512:(ch + 1) * 512],
                                 start=(pt == 0), stop=(pt == NTC - 1))
                    mmi.then_inc(tv, 1)
                    t.wait_ge(vt, VT4 + 12 * kvh + 3 * ch + 2)
                    t.matmul(psW[:], rmt[:], kro[:], start=True, stop=True
                             ).then_inc(tv, 1)

        @blk.vector
        def _(v):
            for kvh in range(KVH):
                for ch in range(4):
                    cs = slice(ch * 512, (ch + 1) * 512)
                    v.wait_ge(tv, TV4 + kvh * 8 + ch * 2 + 1)
                    VE(v, v.tensor_scalar(scrf[:], psA[:], bks[:, kvh:kvh + 1],
                                          None, ALU.add))
                    VE(v, v.tensor_copy(kro[:], scrf[:]))
                    v.sem_inc(vt, 2)
                    v.wait_ge(tv, TV4 + kvh * 8 + ch * 2 + 2)
                    VE(v, v.tensor_tensor(scrf2[:], psW[:], sinb[:, cs],
                                          ALU.mult))
                    VE(v, v.tensor_tensor(scrf[:], scrf[:], cosb[:, cs],
                                          ALU.mult))
                    VE(v, v.tensor_tensor(kfm(kvh)[:, cs], scrf[:], scrf2[:],
                                          ALU.add))
                    v.sem_inc(vt, 1)
        G[0] += (KVH // 2) * 16
        G[1] += (KVH // 2) * 16
        C["tv"] += KVH * 8
        C["vt"] += KVH * 12

        GW4V = C["gw"]
        TV4V = C["tv"]
        VT4V = C["vt"]
        SV4V = C["sv"]

        G4V = list(G)

        @blk.gpsimd
        def _(g):
            g.wait_ge(vt, VT4V)
            for kvh in range(KVH):
                if kvh >= 2:
                    g.wait_ge(vt, VT4V + 8 * (kvh - 1))
                g.dma_start(out=wslab[kvh % 2][:], in_=wv_d[kvh]
                            ).then_inc(gsl[kvh % 2], 16)

        @blk.tensor
        def _(t):
            t.wait_ge(vt, VT4V)
            for kvh in range(KVH):
                t.wait_ge(gsl[kvh % 2], G4V[kvh % 2] + (kvh // 2 + 1) * 16)
                for ch in range(4):
                    if kvh + ch > 0:
                        t.wait_ge(vt, VT4V + 8 * kvh + 2 * ch)
                    for pt in range(NTC):
                        mmi = t.matmul(psA[:],
                                 wslab[kvh % 2][:, pt * 128:(pt + 1) * 128],
                                 hfm(pt)[:, ch * 512:(ch + 1) * 512],
                                 start=(pt == 0), stop=(pt == NTC - 1))
                    mmi.then_inc(tv, 1)
                    t.wait_ge(vt, VT4V + 8 * kvh + 2 * ch + 2)
                    for b4 in range(4):
                        gidx = kvh * 16 + ch * 4 + b4
                        if gidx >= 2:
                            t.wait_ge(sv, SV4V + gidx - 1)
                        t.matmul(psTs(b4),
                                 kro[:, b4 * 128:(b4 + 1) * 128],
                                 idn[:], start=True, stop=True
                                 ).then_inc(tv, 1)

        @blk.vector
        def _(v):
            for kvh in range(KVH):
                for ch in range(4):
                    v.wait_ge(tv, TV4V + kvh * 20 + ch * 5 + 1)
                    VE(v, v.tensor_scalar(scrf[:], psA[:], bvs[:, kvh:kvh + 1],
                                          None, ALU.add))
                    VE(v, v.tensor_copy(kro[:], scrf[:]))
                    v.sem_inc(vt, 2)

        @blk.scalar
        def _(s):
            for kvh in range(KVH):
                for ch in range(4):
                    for b4 in range(4):
                        s.wait_ge(tv, TV4V + kvh * 20 + ch * 5 + 2 + b4)
                        ktok = ch * 4 + b4
                        s.copy(vtm(ktok)[:, kvh * 128:(kvh + 1) * 128],
                               psTs(b4)
                               ).then_inc(sv, 1)
        G[0] += (KVH // 2) * 16
        G[1] += (KVH // 2) * 16
        C["tv"] += KVH * 20
        C["vt"] += KVH * 8
        C["sv"] += KVH * 16

        GW5 = C["gw"]
        TV5 = C["tv"]
        VT5 = C["vt"]

        G5 = list(G)

        @blk.gpsimd
        def _(g):
            g.wait_ge(vt, VT5)
            for h in range(HEADS):
                if h >= 2:
                    g.wait_ge(vt, VT5 + 5 * (h - 1))
                g.dma_start(out=wslab[h % 2][:], in_=wq_d[h]
                            ).then_inc(gsl[h % 2], 16)

        @blk.tensor
        def _(t):
            t.wait_ge(vt, VT5)
            for h in range(HEADS):
                t.wait_ge(gsl[h % 2], G5[h % 2] + (h // 2 + 1) * 16)
                if h > 0:
                    t.wait_ge(vt, VT5 + 5 * (h - 1) + 2)
                for pt in range(NTC):
                    mmi = t.matmul(psA[:, 0:256],
                             wslab[h % 2][:, pt * 128:(pt + 1) * 128],
                             hfm(pt)[:, 0:256],
                             start=(pt == 0), stop=(pt == NTC - 1))
                mmi.then_inc(tv, 1)
                t.wait_ge(vt, VT5 + 5 * h + 2)
                t.matmul(psW[:, 0:256], rmt[:], kro[:, 0:256], start=True,
                         stop=True).then_inc(tv, 1)

        @blk.vector
        def _(v):
            for h in range(HEADS):
                v.wait_ge(tv, TV5 + 2 * h + 1)
                VE(v, v.tensor_scalar(scrf[:, 0:256], psA[:, 0:256],
                                      bqs[:, h:h + 1], None, ALU.add))
                VE(v, v.tensor_copy(kro[:, 0:256], scrf[:, 0:256]))
                v.sem_inc(vt, 2)
                v.wait_ge(tv, TV5 + 2 * h + 2)
                VE(v, v.tensor_tensor(scrf2[:, 0:256], psW[:, 0:256],
                                      sinb[:, 0:256], ALU.mult))
                VE(v, v.tensor_tensor(scrf[:, 0:256], scrf[:, 0:256],
                                      cosb[:, 0:256], ALU.mult))
                VE(v, v.tensor_tensor(qsb[:, h * 256:(h + 1) * 256],
                                      scrf[:, 0:256], scrf2[:, 0:256],
                                      ALU.add))
                v.sem_inc(vt, 3)
        G[0] += (HEADS // 2) * 16
        G[1] += (HEADS // 2) * 16
        C["tv"] += HEADS * 2
        C["vt"] += HEADS * 5
        VT_Q = C["vt"]

        @blk.gpsimd
        def _(g):
            g.wait_ge(vt, VT_Q)
            g.dma_start(out=cosb[:], in_=mask_d[0]).then_inc(gw, 16)
            g.dma_start(out=sinb[:], in_=mask_d[1]).then_inc(gw, 16)
        C["gw"] += 32
        GW_MASK = C["gw"]

        TV6 = C["tv"]
        VT6 = C["vt"]
        SV6 = C["sv"]
        TPI, VPI, SPI = 6, 3, 5

        def sc_inc(it):
            return TV6 + (it + 1 if it < 2 else 6 * it - 4)

        def trav_base(k):
            return TV6 + min(k + 2, 32) + 5 * k

        @blk.tensor
        def _(t):
            t.wait_ge(gw, GW_MASK)
            t.wait_ge(sv, SV6)

            def scores(it):
                a, h = it % 2, it // 2
                kvh = h // 2
                if it >= 1:
                    t.wait_ge(sv, SV6 + (it - 1) * SPI + 1)
                for ch in range(4):
                    mmi = t.matmul(psBig[:, ch * 512:(ch + 1) * 512],
                             qsb[:, h * 256 + a * 128:h * 256 + (a + 1) * 128],
                             kfm(kvh)[:, ch * 512:(ch + 1) * 512],
                             start=True, stop=True)
                mmi.then_inc(tv, 1)

            def trav(it):
                a, h = it % 2, it // 2
                kvh = h // 2
                t.wait_ge(vt, VT6 + 3 * it + 2)
                if it >= 2:
                    t.wait_ge(vt, VT6 + 3 * (it - 2) + 3)
                for b4 in range(4):
                    gb = it * 4 + b4
                    if gb >= 2:
                        pit, pb = (gb - 2) // 4, (gb - 2) % 4
                        t.wait_ge(sv, SV6 + pit * SPI + 2 + pb)
                    for q4 in range(4):
                        kt = b4 * 4 + q4
                        mmi = t.matmul(
                            psT[:, (b4 % 2) * 512 + q4 * 128:
                                (b4 % 2) * 512 + (q4 + 1) * 128],
                            attsb[:, kt * 128:(kt + 1) * 128], idn[:],
                            start=True, stop=True)
                    mmi.then_inc(tv, 1)
                for b4 in range(4):
                    t.wait_ge(sv, SV6 + it * SPI + 2 + b4)
                    for q4 in range(4):
                        kt = b4 * 4 + q4
                        mmi = t.matmul(psAV(a),
                                 vtm(kt)[:, kvh * 128:(kvh + 1) * 128],
                                 attT[:, kt * 128:(kt + 1) * 128],
                                 start=(kt == 0), stop=(kt == NTC - 1))
                mmi.then_inc(tv, 1)

            scores(0)
            scores(1)
            for k in range(32):
                trav(k)
                if k + 2 < 32:
                    scores(k + 2)

        @blk.vector
        def _(v):
            for it in range(32):
                a, h = it % 2, it // 2
                vb = VT6 + it * VPI
                v.wait_ge(tv, sc_inc(it))
                mbuf = cosb if a == 0 else sinb
                VE(v, v.tensor_tensor(psBig[:], psBig[:], mbuf[:], ALU.add))
                VE(v, v.reduce_max(mx[:], psBig[:], axis=AX.X))
                v.tensor_scalar_mul(nb2[:], mx[:], -ISQD).then_inc(vt, 1)
                v.wait_ge(sv, SV6 + it * SPI + 1)
                VE(v, v.reciprocal(rec[:], asum[:]))
                v.tensor_scalar(attsb[:], attsb[:], rec[:], None, ALU.mult
                                ).then_inc(vt, 1)
                v.wait_ge(tv, trav_base(it) + 5)
                v.tensor_copy(qsb[:, h * 256 + a * 128:h * 256 + (a + 1) * 128],
                              psAV(a)).then_inc(vt, 1)

        @blk.scalar
        def _(s):
            for it in range(32):
                vb = VT6 + it * VPI
                if it >= 1:
                    s.wait_ge(tv, trav_base(it - 1) + 4)
                s.wait_ge(vt, vb + 1)
                s.activation(attsb[:], psBig[:], AF.Exp, bias=nb2[:],
                             scale=ISQD, accum_out=asum[:]).then_inc(sv, 1)
                for b4 in range(4):
                    s.wait_ge(tv, trav_base(it) + 1 + b4)
                    s.copy(attT[:, b4 * 512:(b4 + 1) * 512],
                           psT[:, (b4 % 2) * 512:(b4 % 2) * 512 + 512]
                           ).then_inc(sv, 1)
        C["tv"] += 32 * TPI
        C["vt"] += 32 * VPI
        C["sv"] += 32 * SPI

        GW7 = C["gw"]
        TV7 = C["tv"]
        VT7 = C["vt"]

        G7 = list(G)

        @blk.gpsimd
        def _(g):
            g.wait_ge(vt, VT7)
            for mm_ in range(16):
                if mm_ >= 2:
                    g.wait_ge(vt, VT7 + mm_ - 1)
                g.dma_start(out=wslab[mm_ % 2][:], in_=wo_d[mm_]
                            ).then_inc(gsl[mm_ % 2], 16)

        @blk.tensor
        def _(t):
            t.wait_ge(vt, VT7)
            for mm_ in range(16):
                t.wait_ge(gsl[mm_ % 2], G7[mm_ % 2] + (mm_ // 2 + 1) * 16)
                if mm_ > 0:
                    t.wait_ge(vt, VT7 + mm_)
                for kt in range(NTC):
                    mmi = t.matmul(psW[:, 0:256],
                             wslab[mm_ % 2][:, kt * 128:(kt + 1) * 128],
                             qsb[:, kt * 256:(kt + 1) * 256],
                             start=(kt == 0), stop=(kt == NTC - 1))
                mmi.then_inc(tv, 1)

        @blk.vector
        def _(v):
            for mm_ in range(16):
                v.wait_ge(tv, TV7 + mm_ + 1)
                v.scalar_tensor_tensor(
                    x1[:, mm_ * 256:(mm_ + 1) * 256], psW[:, 0:256], 1.0,
                    sfm_scrb[:, mm_ * 256:(mm_ + 1) * 256], ALU.mult, ALU.add
                ).then_inc(vt, 1)
        G[0] += 8 * 16
        G[1] += 8 * 16
        C["tv"] += 16
        C["vt"] += 16
        VT_X1 = C["vt"]

        TV8 = C["tv"]
        VT8 = C["vt"]
        SV8 = C["sv"]

        @blk.scalar
        def _(s):
            s.wait_ge(vt, VT_X1)
            s.square(sfm_scrb[:], x1[:]).then_inc(sv, 1)
            s.wait_ge(vt, VT8 + 1)
            s.sqrt(rstd2[:], var2[:]).then_inc(sv, 1)

        @blk.tensor
        def _(t):
            t.wait_ge(sv, SV8 + 1)
            for mm_ in range(16):
                mmi = t.matmul(psW[0:1, 0:256], onescb[:],
                         sfm_scrb[:, mm_ * 256:(mm_ + 1) * 256],
                         start=(mm_ == 0), stop=(mm_ == 15))
            mmi.then_inc(tv, 1)
            t.wait_ge(vt, VT8 + 2)
            t.matmul(psW[:, 0:256], onesrb[:], rstd2b[:], start=True, stop=True
                     ).then_inc(tv, 1)

        @blk.vector
        def _(v):
            v.wait_ge(tv, TV8 + 1)
            VE(v, v.tensor_scalar(var2[:], psW[0:1, 0:256], 1.0 / D, EPS_RMS,
                                  ALU.mult, ALU.add))
            v.sem_inc(vt, 1)
            v.wait_ge(sv, SV8 + 2)
            VE(v, v.reciprocal(rstd2[:], rstd2[:]))
            v.tensor_copy(rstd2b[:], rstd2[:]).then_inc(vt, 1)
            v.wait_ge(tv, TV8 + 2)
            for mm_ in range(16):
                VE(v, v.tensor_tensor(scrf[:, 0:256],
                                      x1[:, mm_ * 256:(mm_ + 1) * 256],
                                      psW[:, 0:256], ALU.mult))
                VE(v, v.tensor_scalar(sfm_scrb[:, mm_ * 256:(mm_ + 1) * 256],
                                      scrf[:, 0:256], ln2s[:, mm_:mm_ + 1],
                                      None, ALU.mult))
            v.sem_inc(vt, 1)
        C["sv"] += 2
        C["tv"] += 2
        C["vt"] += 3
        VT_H2 = C["vt"]

        GW9 = C["gw"]
        TV9 = C["tv"]
        VT9 = C["vt"]
        SV9 = C["sv"]

        G9 = list(G)

        @blk.gpsimd
        def _(g):
            g.wait_ge(vt, VT_H2)
            for mi in range(64):
                if mi >= 2:
                    g.wait_ge(vt, VT9 + mi - 1)
                g.dma_start(out=wslab[mi % 2][:], in_=wg_d[mi]
                            ).then_inc(gsl[mi % 2], 16)
                g.dma_start(out=wslab[2 + mi % 2][:], in_=wu_d[mi]
                            ).then_inc(gsl[2 + mi % 2], 16)

        @blk.tensor
        def _(t):
            t.wait_ge(vt, VT_H2)
            for mi in range(64):
                t.wait_ge(gsl[mi % 2], G9[mi % 2] + (mi // 2 + 1) * 16)
                t.wait_ge(gsl[2 + mi % 2], G9[2 + mi % 2] + (mi // 2 + 1) * 16)
                if mi > 0:
                    t.wait_ge(vt, VT9 + mi)
                for pt in range(NTC):
                    mmi = t.matmul(psA[:, 0:256],
                             wslab[mi % 2][:, pt * 128:(pt + 1) * 128],
                             sfm_scrb[:, pt * 256:(pt + 1) * 256],
                             start=(pt == 0), stop=(pt == NTC - 1))
                mmi.then_inc(tv, 1)
                for pt in range(NTC):
                    mmi = t.matmul(psW[:, 0:256],
                             wslab[2 + mi % 2][:, pt * 128:(pt + 1) * 128],
                             sfm_scrb[:, pt * 256:(pt + 1) * 256],
                             start=(pt == 0), stop=(pt == NTC - 1))
                mmi.then_inc(tv, 1)

        @blk.scalar
        def _(s):
            for mi in range(64):
                s.wait_ge(tv, TV9 + 2 * mi + 1)
                if mi >= 1:
                    s.wait_ge(vt, VT9 + mi)
                s.activation(scrf[:, 0:256], psA[:, 0:256], AF.Silu
                             ).then_inc(sv, 1)

        @blk.vector
        def _(v):
            for mi in range(64):
                v.wait_ge(tv, TV9 + 2 * mi + 2)
                v.wait_ge(sv, SV9 + mi + 1)
                v.tensor_tensor(buf2[:, mi * 256:(mi + 1) * 256],
                                scrf[:, 0:256], psW[:, 0:256], ALU.mult
                                ).then_inc(vt, 1)
        G[0] += 32 * 16
        G[1] += 32 * 16
        G[2] += 32 * 16
        G[3] += 32 * 16
        C["tv"] += 64 * 2
        C["vt"] += 64
        C["sv"] += 64

        TV10 = C["tv"]
        VT10 = C["vt"]
        SO10 = C["so"]
        G10 = list(G)
        NP10 = 2

        def t10_base(p):
            return TV10 + p * 72

        def v10_base(p):
            return VT10 + p * 25

        def bank10(mo):
            if mo < 4:
                return psBig[:, mo * 512:mo * 512 + 256]
            if mo == 4:
                return psA[:, 0:256]
            if mo == 5:
                return psW[:, 0:256]
            return psT[:, (mo - 6) * 512:(mo - 6) * 512 + 256]

        @blk.gpsimd
        def _(g):
            g.wait_ge(tv, TV10)
            for p in range(NP10):
                for kt in range(64):
                    gi = p * 64 + kt
                    if gi >= 2:
                        gp, gk = (gi - 2) // 64, (gi - 2) % 64
                        g.wait_ge(tv, TV10 + gp * 72 + gk + 1)
                    g.dma_start(out=wslab[gi % 2][:], in_=wd_d[kt]
                                ).then_inc(gsl[gi % 2], 16)

        @blk.tensor
        def _(t):
            t.wait_ge(vt, VT10)
            for p in range(NP10):
                if p >= 1:
                    t.wait_ge(vt, v10_base(p - 1) + 24)
                for kt in range(64):
                    gi = p * 64 + kt
                    t.wait_ge(gsl[gi % 2], G10[gi % 2] + (gi // 2 + 1) * 16)
                    for mo in range(8):
                        mm_ = p * 8 + mo
                        mmi = t.matmul(bank10(mo),
                                       wslab[gi % 2][:, mm_ * 128:(mm_ + 1) * 128],
                                       buf2[:, kt * 256:(kt + 1) * 256],
                                       start=(kt == 0), stop=(kt == 63))
                    mmi.then_inc(tv, 1)
                for mo in range(8):
                    t.wait_ge(vt, v10_base(p) + 3 * mo + 1)
                    bk = bank10(mo)
                    t.transpose(bk[:, 0:128], scrf2[:, 0:128], idnf[:])
                    t.transpose(bk[:, 128:256], scrf2[:, 128:256], idnf[:]
                                ).then_inc(tv, 1)

        @blk.vector
        def _(v):
            for p in range(NP10):
                v.wait_ge(tv, t10_base(p) + 64)
                if p >= 1:
                    v.wait_ge(so, SO10 + p * 64)
                for mo in range(8):
                    mm_ = p * 8 + mo
                    VE(v, v.scalar_tensor_tensor(
                        scrf2[:, 0:256],
                        bank10(mo), 1.0,
                        x1[:, mm_ * 256:(mm_ + 1) * 256], ALU.mult, ALU.add))
                    v.sem_inc(vt, 1)
                    if mo == 4:
                        v.wait_ge(so, SO10 + p * 64 + 32)
                    v.wait_ge(tv, t10_base(p) + 64 + mo + 1)
                    VE(v, v.tensor_copy(
                        pstage[:, (mo % 4) * 128:(mo % 4 + 1) * 128],
                        bank10(mo)[:, 0:128]))
                    VE(v, v.tensor_copy(
                        pstage[:, 512 + (mo % 4) * 128:512 + (mo % 4 + 1) * 128],
                        bank10(mo)[:, 128:256]))
                    v.sem_inc(vt, 2)
                v.sem_inc(vt, 1)

        @blk.sync
        def _(sy):
            for p in range(NP10):
                for hf in range(2):
                    sy.wait_ge(vt, v10_base(p) + 12 * (hf + 1))
                    off = p * 1024 + hf * 512
                    sy.dma_start(out=proc_d[0][:, off:off + 512],
                                 in_=pstage[:, 0:512]).then_inc(so, 16)
                    sy.dma_start(out=proc_d[1][:, off:off + 512],
                                 in_=pstage[:, 512:1024]).then_inc(so, 16)
            sy.wait_ge(so, SO10 + NP10 * 64)
        G[0] += 64 * 16
        G[1] += 64 * 16
        C["tv"] += NP10 * 72
        C["vt"] += NP10 * 25
        C["so"] += NP10 * 64
    return nc


CHT = 4
NCH = NTC // CHT


def build_p4():
    nc = bass.Bass("TRN2", target_bir_lowering=False)
    x_d = nc.dram_tensor("x", [TC, D], F32, kind="ExternalInput").ap()
    procf = nc.dram_tensor("procf", [SSEL, D], F32, kind="ExternalInput").ap()
    cidx_d = nc.dram_tensor("cidx_own", [128, NTC], F32, kind="ExternalInput").ap()
    msc_d = nc.dram_tensor("myscores", [128, NTC], F32, kind="ExternalInput").ap()
    wrt_d = nc.dram_tensor("wrt_fm", [128, NTC], F32, kind="ExternalInput").ap()
    wi_d = nc.dram_tensor("wi_l", [64, 128, 16 * 128], BF16, kind="ExternalInput").ap()
    cv_d = nc.dram_tensor("cv_l", [32, 128, 32 * 128], BF16, kind="ExternalInput").ap()
    wo2_d = nc.dram_tensor("wo2_l", [16, 128, 32 * 128], BF16, kind="ExternalInput").ap()
    dcol_d = nc.dram_tensor("dcol", [128, 32], F32, kind="ExternalInput").ap()
    xo_d = nc.dram_tensor("xo_fm", [NTC, 128, TC], F32, kind="ExternalOutput").ap()
    sc2_d = nc.dram_tensor("sc2", [1, TC], F32, kind="ExternalOutput").ap()
    lnx2_d = nc.dram_tensor("lnx2_fm", [NTC, 128, TC], BF16, kind="ExternalOutput").ap()
    xpre_d = nc.dram_tensor("xpre_fm", [128, NTC, TC], F32, kind="Internal").ap()

    idn_c = nc.inline_tensor(np.eye(128, dtype=bf16), name="idn4")
    idnf_c = nc.inline_tensor(np.eye(128, dtype=np.float32), name="idnf4")
    onescf_c = nc.inline_tensor(np.ones((128, 1), np.float32), name="onescf4")
    onesrf_c = nc.inline_tensor(np.ones((1, 128), np.float32), name="onesrf4")

    es = ExitStack()
    with es:
        lnfm = es.enter_context(nc.sbuf_tensor("lnfm", [128, 16 * 512], BF16))
        xcv = es.enter_context(nc.sbuf_tensor("xcv", [128, 32 * 512], BF16))
        slz = es.enter_context(nc.sbuf_tensor("slz", [128, 32 * 512], BF16))
        xt = [es.enter_context(nc.sbuf_tensor(f"xt4{i}", [128, D], F32))
              for i in range(2)]
        stage = es.enter_context(nc.sbuf_tensor("stage", [128, D], F32))
        lnf = es.enter_context(nc.sbuf_tensor("lnf", [128, D], F32))
        lnmb = es.enter_context(nc.sbuf_tensor("lnmb", [128, D], BF16))
        xps = es.enter_context(nc.sbuf_tensor("xps", [128, D], F32))
        wsl = [es.enter_context(nc.sbuf_tensor(f"wsl4{i}", [128, 4096], BF16))
               for i in range(2)]
        xrb = [es.enter_context(nc.sbuf_tensor(f"xrb{i}", [128, 512], F32))
               for i in range(2)]
        xog = es.enter_context(nc.sbuf_tensor("xog", [128, 16 * 512], F32))
        sqb = es.enter_context(nc.sbuf_tensor("sqb", [128, 1024], F32))
        scr5 = es.enter_context(nc.sbuf_tensor("scr5", [128, 1024], F32))
        scR = es.enter_context(nc.sbuf_tensor("scR", [1, TC], F32))
        cidxs = es.enter_context(nc.sbuf_tensor("cidxs", [128, NTC], F32))
        cidxi = es.enter_context(nc.sbuf_tensor("cidxi", [128, 1], I32))
        mscs = es.enter_context(nc.sbuf_tensor("mscs", [128, NTC], F32))
        wrts = es.enter_context(nc.sbuf_tensor("wrts", [128, NTC], F32))
        dcols = es.enter_context(nc.sbuf_tensor("dcols", [128, 32], F32))
        idn = es.enter_context(nc.sbuf_tensor("idn4s", [128, 128], BF16))
        idnf = es.enter_context(nc.sbuf_tensor("idnf4s", [128, 128], F32))
        onescf = es.enter_context(nc.sbuf_tensor("onescf4s", [128, 1], F32))
        onesrf = es.enter_context(nc.sbuf_tensor("onesrf4s", [1, 128], F32))
        ssq = es.enter_context(nc.sbuf_tensor("ssq4", [128, 1], F32))
        sx = es.enter_context(nc.sbuf_tensor("sx4", [128, 1], F32))
        m_ = es.enter_context(nc.sbuf_tensor("m4", [128, 1], F32))
        var = es.enter_context(nc.sbuf_tensor("var4", [128, 1], F32))
        sd = es.enter_context(nc.sbuf_tensor("sd4", [128, 1], F32))
        rstd = es.enter_context(nc.sbuf_tensor("rstd4", [128, 1], F32))
        nb = es.enter_context(nc.sbuf_tensor("nb4", [128, 1], F32))
        t0 = es.enter_context(nc.sbuf_tensor("t04", [128, 1], F32))
        g8 = es.enter_context(nc.sbuf_tensor("g84", [128, 1], F32))
        coef = es.enter_context(nc.sbuf_tensor("coef4", [128, 1], F32))
        w1 = es.enter_context(nc.sbuf_tensor("w14", [1, 1], F32))
        mrow = es.enter_context(nc.sbuf_tensor("mrow", [1, 512], F32))
        vrow = es.enter_context(nc.sbuf_tensor("vrow", [1, 512], F32))
        rrow = es.enter_context(nc.sbuf_tensor("rrow", [1, 512], F32))
        srow = es.enter_context(nc.sbuf_tensor("srow", [1, 512], F32))

        psA = es.enter_context(nc.psum_tensor("psA4", [128, 512], F32))
        psB = es.enter_context(nc.psum_tensor("psB4", [128, 512], F32))
        psT = es.enter_context(nc.psum_tensor("psT4", [128, 2048], BF16))
        psS = es.enter_context(nc.psum_tensor("psS4", [1, 512], F32))
        psQ = es.enter_context(nc.psum_tensor("psQ4", [1, 512], F32))
        psW2 = es.enter_context(nc.psum_tensor("psW24", [1, 512], F32))
        psM = psA
        psR = psB

        def psTs4(i):
            return psT[:, (i % 2) * 1024:(i % 2) * 1024 + 128]

        ld = es.enter_context(nc.semaphore("ld4"))
        gw = es.enter_context(nc.semaphore("gw4"))
        tv = es.enter_context(nc.semaphore("tv4"))
        vt = es.enter_context(nc.semaphore("vt4"))
        sv = es.enter_context(nc.semaphore("sv4"))
        vself = es.enter_context(nc.semaphore("vself4"))
        so = es.enter_context(nc.semaphore("so4"))
        lx = es.enter_context(nc.semaphore("lx4"))
        lr = es.enter_context(nc.semaphore("lr4"))
        sxo = es.enter_context(nc.semaphore("sxo4"))
        sst = es.enter_context(nc.semaphore("sst4"))
        gsl = [es.enter_context(nc.semaphore(f"gsl4{i}")) for i in range(2)]
        blk = es.enter_context(nc.Block())

        C = dict(ld=0, gw=0, tv=0, vt=0, sv=0, so=0, lx=0, lr=0, sxo=0, sst=0)
        G4 = [0, 0]
        ve_n = [0]

        def VE(v, inst):
            ve_n[0] += 1
            inst.then_inc(vself, 1)
            v.wait_ge(vself, ve_n[0])

        @blk.sync
        def _(sy):
            sy.dma_start(out=cidxs[:], in_=cidx_d).then_inc(ld, 16)
            sy.dma_start(out=mscs[:], in_=msc_d).then_inc(ld, 16)
            sy.dma_start(out=wrts[:], in_=wrt_d).then_inc(ld, 16)
            sy.dma_start(out=dcols[:], in_=dcol_d).then_inc(ld, 16)
            sy.dma_start(out=idn[:], in_=idn_c.ap()).then_inc(ld, 16)
            sy.dma_start(out=idnf[:], in_=idnf_c.ap()).then_inc(ld, 16)
            sy.dma_start(out=onescf[:], in_=onescf_c.ap()).then_inc(ld, 16)
            sy.dma_start(out=onesrf[:], in_=onesrf_c.ap()).then_inc(ld, 16)
        C["ld"] += 8 * 16
        LD0 = C["ld"]

        @blk.tensor
        def _(t):
            t.wait_ge(ld, LD0)
            for pt in range(NTC):
                mmi = t.matmul(psS[0:1, 0:1], wrts[:, pt:pt + 1], onescf[:],
                               start=(pt == 0), stop=(pt == NTC - 1))
            mmi.then_inc(tv, 1)
        C["tv"] += 1
        TV_W1 = C["tv"]

        @blk.vector
        def _(v):
            v.wait_ge(ld, LD0)
            v.memset(stage[:], 0.0).then_inc(vt, 1)
            v.wait_ge(tv, TV_W1)
            v.tensor_copy(w1[:], psS[0:1, 0:1]).then_inc(vt, 1)
        C["vt"] += 2

        tile_tv_base = {}

        def gincw(k):
            return 1 if k == 0 else (2 if k == 1 else 2 * k)

        for chx in range(NCH):
            SSTC = C["sst"]
            for jj in range(CHT):
                j = chx * CHT + jj
                LXJ = C["lx"]
                GWJ = C["gw"]
                VTJ = C["vt"]
                SVJ = C["sv"]
                TVJ = C["tv"]
                tile_tv_base[j] = TVJ

                @blk.sync
                def _(sy, j=j, SVJ=SVJ):
                    if j >= 2:
                        sy.wait_ge(tv, tile_tv_base[j - 2] + 32)
                    sy.dma_start(out=xt[j % 2][:],
                                 in_=x_d[j * 128:(j + 1) * 128, :]
                                 ).then_inc(lx, 16)
                    sy.wait_ge(sv, SVJ + 41)
                    sy.dma_start(
                        out=xpre_d[:, :, j * 128:(j + 1) * 128],
                        in_=xps[:].rearrange("p (f q) -> p f q", f=NTC),
                    ).then_inc(so, 16)
                C["lx"] += 16
                C["so"] += 16

                @blk.vector
                def _(v, j=j, LXJ=LXJ, GWJ=GWJ, VTJ=VTJ, SVJ=SVJ):
                    xb = xt[j % 2]
                    v.wait_ge(sv, SVJ + 2)
                    VE(v, v.tensor_scalar_mul(m_[:], sx[:], 1.0 / D))
                    VE(v, v.tensor_scalar_mul(var[:], ssq[:], 1.0 / D))
                    VE(v, v.tensor_mul(t0[:], m_[:], m_[:]))
                    VE(v, v.tensor_sub(var[:], var[:], t0[:]))
                    v.tensor_scalar_add(var[:], var[:], EPS_LN
                                        ).then_inc(vt, 1)
                    v.wait_ge(sv, SVJ + 3)
                    VE(v, v.reciprocal(rstd[:], sd[:]))
                    VE(v, v.tensor_mul(nb[:], m_[:], rstd[:]))
                    VE(v, v.tensor_scalar_mul(nb[:], nb[:], -1.0))
                    VE(v, v.tensor_scalar(t0[:], cidxs[:, j:j + 1], 4096.0,
                                          None, ALU.min))
                    v.tensor_copy(cidxi[:], t0[:]).then_inc(vt, 1)
                    VE(v, v.tensor_scalar(coef[:], cidxs[:, j:j + 1], 2048.0,
                                          None, ALU.is_lt))
                    v.wait_ge(sv, SVJ + 4)
                    VE(v, v.tensor_mul(coef[:], coef[:], g8[:]))
                    VE(v, v.tensor_scalar_mul(coef[:], coef[:], 0.5))
                    v.wait_ge(gw, GWJ + 16)
                    v.wait_ge(sv, SVJ + 5)
                    VE(v, v.tensor_sub(stage[:], stage[:], lnf[:]))
                    VE(v, v.tensor_scalar(stage[:], stage[:], coef[:], None,
                                          ALU.mult))
                    VE(v, v.scalar_tensor_tensor(stage[:], lnf[:], 0.5,
                                                 stage[:], ALU.mult, ALU.add))
                    VE(v, v.tensor_add(xb[:], xb[:], stage[:]))
                    v.sem_inc(vt, 1)
                    v.wait_ge(sv, SVJ + 7)
                    VE(v, v.tensor_scalar_mul(m_[:], sx[:], 1.0 / D))
                    VE(v, v.tensor_scalar_mul(var[:], ssq[:], 1.0 / D))
                    VE(v, v.tensor_mul(t0[:], m_[:], m_[:]))
                    VE(v, v.tensor_sub(var[:], var[:], t0[:]))
                    v.tensor_scalar_add(var[:], var[:], EPS_LN
                                        ).then_inc(vt, 1)
                    v.wait_ge(sv, SVJ + 8)
                    VE(v, v.reciprocal(rstd[:], sd[:]))
                    VE(v, v.tensor_mul(nb[:], m_[:], rstd[:]))
                    VE(v, v.tensor_scalar_mul(nb[:], nb[:], -1.0))
                    v.sem_inc(vt, 1)
                    v.sem_inc(vt, 1)
                C["vt"] += 6

                @blk.scalar
                def _(s, j=j, jj=jj, LXJ=LXJ, VTJ=VTJ, SVJ=SVJ, TVJ=TVJ,
                      SSTC=SSTC):
                    if jj == 0:
                        s.wait_ge(sst, SSTC)
                    xb = xt[j % 2]
                    s.wait_ge(lx, LXJ + 16)
                    s.activation(lnf[:], xb[:], AF.Square,
                                 accum_out=ssq[:]).then_inc(sv, 1)
                    s.wait_ge(sv, SVJ + 1)
                    s.activation(lnf[:], xb[:], AF.Identity,
                                 accum_out=sx[:]).then_inc(sv, 1)
                    s.wait_ge(vt, VTJ + 1)
                    s.sqrt(sd[:], var[:]).then_inc(sv, 1)
                    s.activation(g8[:], mscs[:, j:j + 1], AF.Sigmoid
                                 ).then_inc(sv, 1)
                    s.wait_ge(vt, VTJ + 2)
                    s.activation(lnf[:], xb[:], AF.Identity,
                                 bias=nb[:], scale=rstd[:]).then_inc(sv, 1)
                    s.wait_ge(vt, VTJ + 3)
                    s.activation(stage[:], xb[:], AF.Square,
                                 accum_out=ssq[:]).then_inc(sv, 1)
                    s.wait_ge(sv, SVJ + 6)
                    s.activation(stage[:], xb[:], AF.Identity,
                                 accum_out=sx[:]).then_inc(sv, 1)
                    s.wait_ge(vt, VTJ + 4)
                    s.sqrt(sd[:], var[:]).then_inc(sv, 1)
                    s.wait_ge(vt, VTJ + 5)
                    s.activation(lnmb[:], xb[:], AF.Identity,
                                 bias=nb[:], scale=rstd[:]).then_inc(sv, 1)
                    for pt in range(NTC):
                        s.wait_ge(tv, TVJ + pt + 1)
                        s.copy(lnfm[:, pt * 512 + jj * 128:
                                     pt * 512 + (jj + 1) * 128],
                               psTs4(pt)).then_inc(sv, 1)
                    if j >= 1:
                        s.wait_ge(so, 16 * j)
                    for pt in range(NTC):
                        s.wait_ge(tv, TVJ + 16 + pt + 1)
                        ps = psA if pt % 2 == 0 else psB
                        s.copy(xps[:, pt * 128:(pt + 1) * 128],
                               ps[:, 0:128]).then_inc(sv, 1)
                C["sv"] += 41

                @blk.gpsimd
                def _(g, VTJ=VTJ):
                    g.wait_ge(vt, VTJ + 2)
                    g.indirect_dma_start(
                        stage[:], None, procf,
                        IndirectOffsetOnAxis(cidxi[:], 0),
                        bounds_check=SSEL - 1, oob_is_err=False,
                    ).then_inc(gw, 16)
                C["gw"] += 16

                @blk.tensor
                def _(t, j=j, VTJ=VTJ, SVJ=SVJ):
                    t.wait_ge(sv, SVJ + 9)
                    for pt in range(NTC):
                        if pt >= 2:
                            t.wait_ge(sv, SVJ + pt + 8)
                        t.transpose(psTs4(pt),
                                    lnmb[:, pt * 128:(pt + 1) * 128], idn[:]
                                    ).then_inc(tv, 1)
                    t.wait_ge(vt, VTJ + 3)
                    for pt in range(NTC):
                        ps = psA if pt % 2 == 0 else psB
                        if pt >= 2:
                            t.wait_ge(sv, SVJ + pt + 24)
                        t.transpose(ps[:, 0:128],
                                    xt[j % 2][:, pt * 128:(pt + 1) * 128],
                                    idnf[:]).then_inc(tv, 1)
                C["tv"] += 2 * NTC

            def lnslice(kt):
                return lnfm[:, kt * 512:(kt + 1) * 512]

            TVX = C["tv"]
            SVXz = C["sv"]
            GX = list(G4)

            @blk.gpsimd
            def _(g, GX=GX, TVX=TVX):
                g.wait_ge(tv, TVX)
                for m in range(64):
                    if m >= 2:
                        g.wait_ge(tv, TVX + m - 1)
                    g.dma_start(out=wsl[m % 2][:, 0:2048], in_=wi_d[m]
                                ).then_inc(gsl[m % 2], 16)

            @blk.tensor
            def _(t, GX=GX, TVX=TVX, SVXz=SVXz):
                t.wait_ge(sv, SVXz)
                for m in range(64):
                    t.wait_ge(gsl[m % 2], GX[m % 2] + (m // 2 + 1) * 16)
                    if m >= 2:
                        t.wait_ge(sv, SVXz + m - 1)
                    ps = psA if m % 2 == 0 else psB
                    for kt in range(NTC):
                        mmi = t.matmul(ps[:],
                                       wsl[m % 2][:, kt * 128:(kt + 1) * 128],
                                       lnslice(kt), start=(kt == 0),
                                       stop=(kt == NTC - 1))
                    mmi.then_inc(tv, 1)

            @blk.scalar
            def _(s, TVX=TVX):
                for m in range(64):
                    s.wait_ge(tv, TVX + m + 1)
                    ps = psA if m % 2 == 0 else psB
                    if m < 32:
                        s.copy(xcv[:, m * 512:(m + 1) * 512], ps[:]
                               ).then_inc(sv, 1)
                    else:
                        s.activation(slz[:, (m - 32) * 512:(m - 31) * 512],
                                     ps[:], AF.Silu).then_inc(sv, 1)
            G4[0] += 32 * 16
            G4[1] += 32 * 16
            C["tv"] += 64
            C["sv"] += 64

            TVC = C["tv"]
            SVC = C["sv"]
            VTC = C["vt"]
            GC = list(G4)

            @blk.gpsimd
            def _(g, GC=GC, TVC=TVC):
                g.wait_ge(tv, TVC)
                for m in range(32):
                    if m >= 2:
                        g.wait_ge(tv, TVC + m - 1)
                    g.dma_start(out=wsl[m % 2][:], in_=cv_d[m]
                                ).then_inc(gsl[m % 2], 16)

            @blk.tensor
            def _(t, GC=GC, TVC=TVC, SVC=SVC):
                t.wait_ge(sv, SVC)
                for m in range(32):
                    t.wait_ge(gsl[m % 2], GC[m % 2] + (m // 2 + 1) * 16)
                    if m >= 2:
                        t.wait_ge(sv, SVC + m - 1)
                    ps = psA if m % 2 == 0 else psB
                    for kt in range(32):
                        mmi = t.matmul(ps[:],
                                       wsl[m % 2][:, kt * 128:(kt + 1) * 128],
                                       xcv[:, kt * 512:(kt + 1) * 512],
                                       start=(kt == 0), stop=(kt == 31))
                    mmi.then_inc(tv, 1)

            @blk.scalar
            def _(s, TVC=TVC, SVC=SVC, VTC=VTC):
                for m in range(32):
                    s.wait_ge(tv, TVC + m + 1)
                    if m >= 2:
                        s.wait_ge(vt, VTC + m - 1)
                    ps = psA if m % 2 == 0 else psB
                    s.mul(scr5[:, (m % 2) * 512:(m % 2) * 512 + 512], ps[:],
                          dcols[:, m:m + 1]).then_inc(sv, 1)

            @blk.vector
            def _(v, SVC=SVC):
                for m in range(32):
                    v.wait_ge(sv, SVC + m + 1)
                    VE(v, v.tensor_tensor(slz[:, m * 512:(m + 1) * 512],
                                          scr5[:, (m % 2) * 512:
                                               (m % 2) * 512 + 512],
                                          slz[:, m * 512:(m + 1) * 512],
                                          ALU.mult))
                    v.sem_inc(vt, 1)
            G4[0] += 16 * 16
            G4[1] += 16 * 16
            C["tv"] += 32
            C["sv"] += 32
            C["vt"] += 32

            TVW = C["tv"]
            VTW = C["vt"]
            SVW = C["sv"]
            LRW = C["lr"]
            SXOC = C["sxo"]
            GW2 = list(G4)

            @blk.gpsimd
            def _(g, GW2=GW2, TVW=TVW):
                g.wait_ge(tv, TVW)
                for m in range(16):
                    if m >= 2:
                        g.wait_ge(tv, TVW + gincw(m - 2))
                    g.dma_start(out=wsl[m % 2][:], in_=wo2_d[m]
                                ).then_inc(gsl[m % 2], 16)

            @blk.sync
            def _(sy, chx=chx, LRW=LRW, VTW=VTW):
                sy.wait_ge(so, (chx + 1) * CHT * 16)
                for m in range(16):
                    if m >= 2:
                        sy.wait_ge(vt, VTW + m - 1)
                    sy.dma_start(out=xrb[m % 2][:],
                                 in_=xpre_d[:, m, chx * 512:(chx + 1) * 512]
                                 ).then_inc(lr, 16)
                for m in range(16):
                    sy.wait_ge(vt, VTW + m + 1)
                    sy.dma_start(out=xo_d[m][:, chx * 512:(chx + 1) * 512],
                                 in_=xog[:, m * 512:(m + 1) * 512]
                                 ).then_inc(sxo, 16)
                for pt in range(NTC):
                    sy.wait_ge(vt, VTW + 18 + pt + 1)
                    sy.dma_start(out=lnx2_d[pt][:, chx * 512:(chx + 1) * 512],
                                 in_=lnfm[:, pt * 512:(pt + 1) * 512]
                                 ).then_inc(sst, 16)

            @blk.tensor
            def _(t, GW2=GW2, TVW=TVW, VTW=VTW, SVW=SVW):
                t.wait_ge(vt, VTW)

                def grp(m):
                    t.wait_ge(gsl[m % 2], GW2[m % 2] + (m // 2 + 1) * 16)
                    if m >= 2:
                        t.wait_ge(vt, VTW + m - 1)
                    ps = psA if m % 2 == 0 else psB
                    for kt in range(32):
                        mmi = t.matmul(ps[:],
                                       wsl[m % 2][:, kt * 128:(kt + 1) * 128],
                                       slz[:, kt * 512:(kt + 1) * 512],
                                       start=(kt == 0), stop=(kt == 31))
                    mmi.then_inc(tv, 1)

                def stats(k):
                    t.wait_ge(vt, VTW + k + 1)
                    t.wait_ge(sv, SVW + k + 1)
                    xsl = xog[:, k * 512:(k + 1) * 512]
                    t.matmul(psS[:], onescf[:], xsl, start=(k == 0),
                             stop=(k == 15))
                    t.matmul(psQ[:], onescf[:],
                             sqb[:, (k % 2) * 512:(k % 2) * 512 + 512],
                             start=(k == 0), stop=(k == 15))
                    t.matmul(psW2[:], wrts[:, k:k + 1], xsl,
                             start=(k == 0), stop=(k == 15)).then_inc(tv, 1)

                grp(0)
                grp(1)
                for m in range(2, 16):
                    stats(m - 2)
                    grp(m)
                stats(14)
                stats(15)
                t.wait_ge(vt, VTW + 18)
                t.matmul(psM[:], onesrf[:], mrow[:], start=True, stop=True)
                t.matmul(psR[:], onesrf[:], rrow[:], start=True, stop=True
                         ).then_inc(tv, 1)

            @blk.vector
            def _(v, chx=chx, TVW=TVW, VTW=VTW, SVW=SVW, LRW=LRW, SXOC=SXOC):
                for m in range(16):
                    v.wait_ge(tv, TVW + gincw(m))
                    v.wait_ge(lr, LRW + (m // 2 + 1) * 32)
                    if m == 0:
                        v.wait_ge(sxo, SXOC)
                    ps = psA if m % 2 == 0 else psB
                    v.scalar_tensor_tensor(xog[:, m * 512:(m + 1) * 512],
                                           ps[:], 1.0, xrb[m % 2][:],
                                           ALU.mult, ALU.add).then_inc(vt, 1)
                v.wait_ge(tv, TVW + 32)
                VE(v, v.tensor_scalar_mul(mrow[:], psS[0:1, :], 1.0 / D))
                VE(v, v.tensor_scalar_mul(vrow[:], psQ[0:1, :], 1.0 / D))
                VE(v, v.tensor_mul(srow[:], mrow[:], mrow[:]))
                VE(v, v.tensor_sub(vrow[:], vrow[:], srow[:]))
                v.tensor_scalar_add(vrow[:], vrow[:], EPS_LN
                                    ).then_inc(vt, 1)
                v.wait_ge(sv, SVW + 17)
                VE(v, v.reciprocal(rrow[:], rrow[:]))
                VE(v, v.tensor_scalar(srow[:], mrow[:], w1[0:1, 0:1], None,
                                      ALU.mult))
                VE(v, v.tensor_tensor(srow[:], psW2[0:1, :], srow[:],
                                      ALU.subtract))
                VE(v, v.tensor_mul(scR[:, chx * 512:(chx + 1) * 512],
                                   srow[:], rrow[:]))
                v.sem_inc(vt, 1)
                v.wait_ge(tv, TVW + 33)
                for pt in range(NTC):
                    VE(v, v.tensor_tensor(scr5[:, 0:512],
                                          xog[:, pt * 512:(pt + 1) * 512],
                                          psM[:], ALU.subtract))
                    VE(v, v.tensor_tensor(lnfm[:, pt * 512:(pt + 1) * 512],
                                          scr5[:, 0:512], psR[:], ALU.mult))
                    v.sem_inc(vt, 1)

            @blk.scalar
            def _(s, TVW=TVW, VTW=VTW):
                for m in range(16):
                    s.wait_ge(vt, VTW + m + 1)
                    if m >= 2:
                        s.wait_ge(tv, TVW + 2 * m - 1)
                    s.square(sqb[:, (m % 2) * 512:(m % 2) * 512 + 512],
                             xog[:, m * 512:(m + 1) * 512]).then_inc(sv, 1)
                s.wait_ge(vt, VTW + 17)
                s.sqrt(rrow[:], vrow[:]).then_inc(sv, 1)
            G4[0] += 8 * 16
            G4[1] += 8 * 16
            C["tv"] += 33
            C["vt"] += 16 + 2 + 16
            C["sv"] += 17
            C["lr"] += 16 * 16
            C["sxo"] += 16 * 16
            C["sst"] += 16 * 16

        VT_F = C["vt"]

        @blk.sync
        def _(sy):
            sy.wait_ge(vt, VT_F)
            sy.dma_start(out=sc2_d, in_=scR[:]).then_inc(sst, 16)
            sy.wait_ge(sst, C["sst"] + 16)
            sy.wait_ge(sxo, C["sxo"])
            sy.wait_ge(so, C["so"])
        C["sst"] += 16
    return nc



def build_p7():
    nc = bass.Bass("TRN2", target_bir_lowering=False)
    x_d = nc.dram_tensor("x", [TC, D], F32, kind="ExternalInput").ap()
    procf = nc.dram_tensor("procf", [SSEL, D], F32, kind="ExternalInput").ap()
    cidx_d = nc.dram_tensor("cidx_own", [128, NTC], F32, kind="ExternalInput").ap()
    msc_d = nc.dram_tensor("myscores", [128, NTC], F32, kind="ExternalInput").ap()
    out_d = nc.dram_tensor("out", [TC, D], F32, kind="ExternalOutput").ap()

    es = ExitStack()
    with es:
        xt = [es.enter_context(nc.sbuf_tensor(f"xt7{i}", [128, D], F32))
              for i in range(2)]
        ot = [es.enter_context(nc.sbuf_tensor(f"ot7{i}", [128, D], F32))
              for i in range(2)]
        stage = es.enter_context(nc.sbuf_tensor("stage7", [128, D], F32))
        lnf = es.enter_context(nc.sbuf_tensor("lnf7", [128, D], F32))
        cidxs = es.enter_context(nc.sbuf_tensor("cidxs7", [128, NTC], F32))
        cidxi = es.enter_context(nc.sbuf_tensor("cidxi7", [128, 1], I32))
        mscs = es.enter_context(nc.sbuf_tensor("mscs7", [128, NTC], F32))
        ssq = es.enter_context(nc.sbuf_tensor("ssq7", [128, 1], F32))
        sx = es.enter_context(nc.sbuf_tensor("sx7", [128, 1], F32))
        m_ = es.enter_context(nc.sbuf_tensor("m7", [128, 1], F32))
        var = es.enter_context(nc.sbuf_tensor("var7", [128, 1], F32))
        sd = es.enter_context(nc.sbuf_tensor("sd7", [128, 1], F32))
        rstd = es.enter_context(nc.sbuf_tensor("rstd7", [128, 1], F32))
        nb = es.enter_context(nc.sbuf_tensor("nb7", [128, 1], F32))
        t0 = es.enter_context(nc.sbuf_tensor("t07", [128, 1], F32))
        g8 = es.enter_context(nc.sbuf_tensor("g87", [128, 1], F32))
        coef = es.enter_context(nc.sbuf_tensor("coef7", [128, 1], F32))

        ld = es.enter_context(nc.semaphore("ld7"))
        lxs = [es.enter_context(nc.semaphore(f"lxs7{i}")) for i in range(2)]
        gw = es.enter_context(nc.semaphore("gw7"))
        vt = es.enter_context(nc.semaphore("vt7"))
        sv = es.enter_context(nc.semaphore("sv7"))
        vself = es.enter_context(nc.semaphore("vself7"))
        sol = [es.enter_context(nc.semaphore(f"sol7{i}")) for i in range(2)]
        blk = es.enter_context(nc.Block())

        ve_n = [0]

        def VE(v, inst):
            ve_n[0] += 1
            inst.then_inc(vself, 1)
            v.wait_ge(vself, ve_n[0])

        @blk.sync
        def _(sy):
            sy.dma_start(out=cidxs[:], in_=cidx_d).then_inc(ld, 16)
            sy.dma_start(out=mscs[:], in_=msc_d).then_inc(ld, 16)
            for j in range(NTC):
                if j >= 2:
                    sy.wait_ge(vt, 5 * (j - 1))
                sy.dma_start(out=xt[j % 2][:],
                             in_=x_d[j * 128:(j + 1) * 128, :]
                             ).then_inc(lxs[j % 2], 16)

        @blk.vector
        def _(v):
            v.wait_ge(ld, 32)
            VE(v, v.memset(stage[:], 0.0))
            for j in range(NTC):
                xb = xt[j % 2]
                v.wait_ge(lxs[j % 2], (j // 2 + 1) * 16)
                VE(v, v.scalar_tensor_tensor(lnf[:], xb[:], 1.0, xb[:],
                                             ALU.mult, ALU.mult,
                                             accum_out=ssq[:]))
                VE(v, v.reduce_sum(sx[:], xb[:], axis=AX.X))
                VE(v, v.tensor_scalar_mul(m_[:], sx[:], 1.0 / D))
                VE(v, v.tensor_scalar_mul(var[:], ssq[:], 1.0 / D))
                VE(v, v.tensor_mul(t0[:], m_[:], m_[:]))
                VE(v, v.tensor_sub(var[:], var[:], t0[:]))
                v.tensor_scalar_add(var[:], var[:], EPS_LN).then_inc(vt, 1)
                v.wait_ge(sv, 2 * j + 1)
                VE(v, v.reciprocal(rstd[:], sd[:]))
                VE(v, v.tensor_mul(nb[:], m_[:], rstd[:]))
                VE(v, v.tensor_scalar_mul(nb[:], nb[:], -1.0))
                VE(v, v.tensor_scalar(lnf[:], xb[:], rstd[:], nb[:],
                                      ALU.mult, ALU.add))
                VE(v, v.tensor_scalar(t0[:], cidxs[:, j:j + 1], 4096.0,
                                      None, ALU.min))
                v.tensor_copy(cidxi[:], t0[:]).then_inc(vt, 1)
                VE(v, v.tensor_scalar(coef[:], cidxs[:, j:j + 1], 2048.0,
                                      None, ALU.is_lt))
                v.wait_ge(sv, 2 * j + 2)
                VE(v, v.tensor_mul(coef[:], coef[:], g8[:]))
                VE(v, v.tensor_scalar_mul(coef[:], coef[:], 0.5))
                v.wait_ge(gw, (j + 1) * 16)
                VE(v, v.tensor_sub(stage[:], stage[:], lnf[:]))
                VE(v, v.tensor_scalar(stage[:], stage[:], coef[:], None,
                                      ALU.mult))
                VE(v, v.scalar_tensor_tensor(stage[:], lnf[:], 0.5, stage[:],
                                             ALU.mult, ALU.add))
                v.wait_ge(sol[j % 2], (j // 2) * 16)
                VE(v, v.tensor_tensor(ot[j % 2][:], xb[:], stage[:], ALU.add))
                v.sem_inc(vt, 1)
                v.memset(stage[:], 0.0).then_inc(vt, 2)

        @blk.scalar
        def _(s):
            for j in range(NTC):
                s.wait_ge(vt, 5 * j + 1)
                s.sqrt(sd[:], var[:]).then_inc(sv, 1)
                s.activation(g8[:], mscs[:, j:j + 1], AF.Sigmoid).then_inc(sv, 1)

        @blk.gpsimd
        def _(g):
            for j in range(NTC):
                g.wait_ge(vt, 5 * j + 2)
                g.indirect_dma_start(
                    stage[:], None, procf,
                    IndirectOffsetOnAxis(cidxi[:], 0),
                    bounds_check=SSEL - 1, oob_is_err=False,
                ).then_inc(gw, 16)
                g.wait_ge(vt, 5 * j + 3)
                g.dma_start(out=out_d[j * 128:(j + 1) * 128, :],
                            in_=ot[j % 2][:]).then_inc(sol[j % 2], 16)
            g.wait_ge(sol[0], 8 * 16)
            g.wait_ge(sol[1], 8 * 16)
    return nc


def _get_prog(name, builder):
    if name not in _PROG:
        _PROG[name] = builder()
    return _PROG[name]


def _launch(name, builder, in_maps, **kw):
    nc = _get_prog(name, builder)
    res = run_bass_kernel_spmd(nc, in_maps, core_ids=list(range(NC)), **kw)
    return res.results



_WSPEC = [("wq", 16, 2048), ("wk", 8, 2048), ("wv", 8, 2048),
          ("wo", 16, 2048), ("wg", 64, 2048), ("wu", 64, 2048),
          ("wd", 64, 2048), ("wi", 64, 2048), ("cv", 32, 4096),
          ("wo2", 16, 4096)]


def _wcat_build(tensors):
    return np.ascontiguousarray(np.concatenate(
        [t.transpose(1, 0, 2).reshape(128, -1) for t in tensors], axis=1))


def _wcat_split(wcb):
    out = {}
    off = 0
    for name, m, k in _WSPEC:
        w = wcb[:, off:off + m * k]
        off += m * k
        out[name] = np.ascontiguousarray(
            w.reshape(128, m, k).transpose(1, 0, 2))
    return out


def _slabs_out(w):
    K, M = w.shape
    return _np_f32(w.reshape(K // 128, 128, M // 128, 128
                             ).transpose(2, 1, 0, 3).reshape(M // 128, 128, K))


def _make_costab():
    inv_freq = 1.0 / (10000.0 ** (np.arange(0, HD, 2, dtype=np.float64) / HD))
    ang = np.arange(SEQ, dtype=np.float64)[:, None] * inv_freq[None, :]
    cos = np.cos(ang).astype(np.float32)
    sin = np.sin(ang).astype(np.float32)
    return _np_f32(np.concatenate(
        [cos, cos, sin, sin], axis=1))


def _perm_tiles(c):
    rest = [t for t in range(NTC) if t not in (c, c + 8)]
    return [c, c + 8] + rest


def _masks_for_core(c):
    pt = _perm_tiles(c)
    ktrue = np.concatenate([np.arange(t * 128, (t + 1) * 128) for t in pt])
    masks = np.zeros((2, 128, SSEL), np.float32)
    for a, tile in enumerate([c, c + 8]):
        qtrue = tile * 128 + np.arange(128)
        masks[a] = np.where(ktrue[None, :] <= qtrue[:, None], 0.0, BIG_NEG)
    return _np_f32(masks)


def _fm16(vec):
    return _np_f32(np.asarray(vec).reshape(NTC, 128).T)


def kernel(**inputs):
    xin = np.asarray(inputs["L0_attn_hidden_states"], dtype=np.float32)
    xf = np.ascontiguousarray(xin.reshape(TT, D))

    w_router = np.asarray(inputs["L0_ffn_w_router"], np.float32)
    wrt_tiled = _np_f32(np.tile(w_router[None, :], (128, 1)))
    wrt_fm = _fm16(w_router)

    wq_l = _slabs_out(np.asarray(inputs["L0_ffn_wq"], np.float32))
    wk_l = _slabs_out(np.asarray(inputs["L0_ffn_wk"], np.float32))
    wv_l = _slabs_out(np.asarray(inputs["L0_ffn_wv"], np.float32))
    wo_l = _slabs_out(np.asarray(inputs["L0_ffn_wo"], np.float32))
    wg_l = _slabs_out(np.asarray(inputs["L0_ffn_w_gate"], np.float32))
    wu_l = _slabs_out(np.asarray(inputs["L0_ffn_w_up"], np.float32))
    wd_l = _np_f32(np.asarray(inputs["L0_ffn_w_down"], np.float32
                              ).reshape(64, 128, D))
    bq_fm = _fm16(inputs["L0_ffn_bq"])
    bk_fm = _np_f32(np.asarray(inputs["L0_ffn_bk"], np.float32).reshape(8, 128).T)
    bv_fm = _np_f32(np.asarray(inputs["L0_ffn_bv"], np.float32).reshape(8, 128).T)
    ln1b = _np_f32(np.tile(np.asarray(inputs["L0_ffn_ln1_w"], np.float32
                                      )[None, :], (128, 1)))
    ln2_fm = _fm16(inputs["L0_ffn_ln2_w"])
    costab = _make_costab()

    wi_l = _slabs_out(np.asarray(inputs["L0_attn_W_in"], np.float32).T)
    cv_l = _slabs_out(np.asarray(inputs["L0_attn_conv_w"], np.float32).T)
    wo2_l = _slabs_out(np.asarray(inputs["L0_attn_W_out"], np.float32).T)
    dcol = _np_f32(np.asarray(inputs["L0_attn_D"], np.float32).reshape(32, 128).T)

    jc_cores = []
    for c in range(NC):
        jcv = np.zeros((128, 256), np.float32)
        jcv[:, 0:128] = np.arange(c * 128, (c + 1) * 128, dtype=np.float32)[None, :]
        jcv[:, 128:256] = np.arange((c + 8) * 128, (c + 9) * 128,
                                    dtype=np.float32)[None, :]
        jc_cores.append(_np_f32(jcv))

    def ffn_round(ssb, lnx_full, x_tm_cores, myscores_cores):
        r2 = _launch("p2", build_p2,
                     [{"ssb": ssb, "lnxf": lnx_full, "jc": jc_cores[c]}
                      for c in range(NC)])
        cidx = np.asarray(r2[0]["cidxo"], np.float32)
        inv_full = np.zeros(SSEL, np.int32)
        sel_full = np.zeros((SSEL, D), bf16)
        for c in range(NC):
            inv = np.asarray(r2[c]["inv"])
            inv_full[c * 128:(c + 1) * 128] = inv[:, 0]
            inv_full[(c + 8) * 128:(c + 9) * 128] = inv[:, 1]
            selr = np.asarray(r2[c]["selr"])
            sel_full[c * 128:(c + 1) * 128] = selr[0]
            sel_full[(c + 8) * 128:(c + 9) * 128] = selr[1]
        in3 = []
        for c in range(NC):
            pt = _perm_tiles(c)
            rows = np.concatenate([np.arange(t * 128, (t + 1) * 128)
                                   for t in pt])
            sel_p = np.ascontiguousarray(sel_full[rows])
            inv_p = np.ascontiguousarray(
                np.stack([inv_full[t * 128:(t + 1) * 128] for t in pt],
                         axis=1))
            in3.append({
                "sel": sel_p, "invp": inv_p, "costab": costab,
                "mask": _masks_for_core(c),
                "wq_l": wq_l, "wk_l": wk_l, "wv_l": wv_l, "wo_l": wo_l,
                "wg_l": wg_l, "wu_l": wu_l, "wd_l": wd_l,
                "bq_fm": bq_fm, "bk_fm": bk_fm, "bv_fm": bv_fm,
                "ln1b": ln1b, "ln2_fm": ln2_fm,
            })
        r3 = _launch("p3", build_p3, in3)
        procf = np.zeros((SSEL, D), np.float32)
        for c in range(NC):
            pr = np.asarray(r3[c]["proc"])
            procf[c * 128:(c + 1) * 128] = pr[0]
            procf[(c + 8) * 128:(c + 9) * 128] = pr[1]
        return procf, cidx

    wcat = _wcat_build([wq_l, wk_l, wv_l, wo_l, wg_l, wu_l, wd_l,
                        wi_l, cv_l, wo2_l])
    r1 = _launch("p1", build_p1,
                 [{"x": np.ascontiguousarray(xf[c * TC:(c + 1) * TC]),
                   "wrt": wrt_tiled,
                   "wcat": np.ascontiguousarray(
                       wcat[:, c * WCOLS:(c + 1) * WCOLS])}
                  for c in range(NC)])
    wb = _wcat_split(np.concatenate(
        [np.asarray(r1[c]["wcb"]) for c in range(NC)], axis=1))
    wq_l, wk_l, wv_l, wo_l = wb["wq"], wb["wk"], wb["wv"], wb["wo"]
    wg_l, wu_l, wd_l = wb["wg"], wb["wu"], wb["wd"]
    wi_l, cv_l, wo2_l = wb["wi"], wb["cv"], wb["wo2"]
    lnx_full = np.ascontiguousarray(
        np.concatenate([np.asarray(r1[c]["lnx"]) for c in range(NC)], axis=0))
    ssb = np.ascontiguousarray(
        np.concatenate([np.asarray(r1[c]["sc"]) for c in range(NC)], axis=1))
    msc1 = [np.ascontiguousarray(np.asarray(r1[c]["sc"])) for c in range(NC)]

    procf1, cidx1 = ffn_round(ssb, lnx_full, None, msc1)

    in4 = []
    for c in range(NC):
        in4.append({
            "x": np.ascontiguousarray(xf[c * TC:(c + 1) * TC]),
            "procf": procf1,
            "cidx_own": np.ascontiguousarray(cidx1[:, c * NTC:(c + 1) * NTC]),
            "myscores": msc1[c],
            "wrt_fm": wrt_fm,
            "wi_l": wi_l, "cv_l": cv_l, "wo2_l": wo2_l, "dcol": dcol,
        })
    r4 = _launch("p4", build_p4, in4)
    x2_cores = []
    lnx2_parts = []
    msc2 = []
    ssb2_cols = []
    for c in range(NC):
        xo = np.asarray(r4[c]["xo_fm"])
        x2_cores.append(np.ascontiguousarray(
            xo.transpose(2, 0, 1).reshape(TC, D)))
        l2fm = np.asarray(r4[c]["lnx2_fm"])
        lnx2_parts.append(np.ascontiguousarray(
            l2fm.transpose(2, 0, 1).reshape(TC, D)))
        sc2 = np.asarray(r4[c]["sc2"]).reshape(NTC, 128).T
        msc2.append(np.ascontiguousarray(sc2))
        ssb2_cols.append(sc2)
    lnx2_full = np.ascontiguousarray(np.concatenate(lnx2_parts, axis=0))
    ssb2 = np.ascontiguousarray(np.concatenate(ssb2_cols, axis=1))

    procf2, cidx2 = ffn_round(ssb2, lnx2_full, None, msc2)

    in7 = []
    for c in range(NC):
        in7.append({
            "x": x2_cores[c],
            "procf": procf2,
            "cidx_own": np.ascontiguousarray(cidx2[:, c * NTC:(c + 1) * NTC]),
            "myscores": msc2[c],
        })
    r7 = _launch("p7", build_p7, in7)
    out = np.concatenate([np.asarray(r7[c]["out"]) for c in range(NC)], axis=0)
    return np.ascontiguousarray(out.reshape(B, SEQ, D), dtype=np.float32)


# version 09
# speedup vs baseline: 1.1671x; 1.0097x over previous
import math
from contextlib import ExitStack

import numpy as np
import ml_dtypes

import concourse.bass as bass
import concourse.mybir as mybir
from concourse.bass_core_rust import IndirectOffsetOnAxis
from concourse.bass_utils import run_bass_kernel_spmd

F32 = mybir.dt.float32
BF16 = mybir.dt.bfloat16
I32 = mybir.dt.int32
AF = mybir.ActivationFunctionType
ALU = mybir.AluOpType
AX = mybir.AxisListType

NC = 8
D = 2048
TT = 16384
TC = 2048
NTC = 16
B = 2
SEQ = 8192
KSEL = 1024
SSEL = 2048
HEADS = 16
KVH = 8
HD = 128
IM = 4096
IMLP = 8192
EPS_LN = 1e-5
EPS_RMS = 1e-6
OOB = 1.0e9
BISECT_ITERS = 44
BIG_NEG = -1.0e30
ISQD = 1.0 / math.sqrt(HD)

bf16 = ml_dtypes.bfloat16

_PROG = {}


def _np_f32(a):
    return np.ascontiguousarray(a, dtype=np.float32)



WCOLS = 819200 // NC


def build_p1():
    nc = bass.Bass("TRN2", target_bir_lowering=False)
    x = nc.dram_tensor("x", [TC, D], F32, kind="ExternalInput").ap()
    wrt = nc.dram_tensor("wrt", [128, D], F32, kind="ExternalInput").ap()
    wcat = nc.dram_tensor("wcat", [128, WCOLS], F32, kind="ExternalInput").ap()
    lnx = nc.dram_tensor("lnx", [TC, D], BF16, kind="ExternalOutput").ap()
    sc = nc.dram_tensor("sc", [128, NTC], F32, kind="ExternalOutput").ap()
    wcb = nc.dram_tensor("wcb", [128, WCOLS], BF16, kind="ExternalOutput").ap()

    es = ExitStack()
    with es:
        xt = [es.enter_context(nc.sbuf_tensor(f"xt{i}", [128, D], F32)) for i in range(2)]
        lt = [es.enter_context(nc.sbuf_tensor(f"lt{i}", [128, D], BF16)) for i in range(2)]
        scr = es.enter_context(nc.sbuf_tensor("scr", [128, D], F32))
        wrtb = es.enter_context(nc.sbuf_tensor("wrtb", [128, D], F32))
        w1 = es.enter_context(nc.sbuf_tensor("w1", [128, 1], F32))
        scb = es.enter_context(nc.sbuf_tensor("scb", [128, NTC], F32))
        sx = es.enter_context(nc.sbuf_tensor("sx", [128, 1], F32))
        ssq = es.enter_context(nc.sbuf_tensor("ssq", [128, 1], F32))
        xw = es.enter_context(nc.sbuf_tensor("xw", [128, 1], F32))
        m = es.enter_context(nc.sbuf_tensor("m", [128, 1], F32))
        var = es.enter_context(nc.sbuf_tensor("var", [128, 1], F32))
        sd = es.enter_context(nc.sbuf_tensor("sd", [128, 1], F32))
        rstd = es.enter_context(nc.sbuf_tensor("rstd", [128, 1], F32))
        nb = es.enter_context(nc.sbuf_tensor("nb", [128, 1], F32))
        t0 = es.enter_context(nc.sbuf_tensor("t0", [128, 1], F32))

        ld = es.enter_context(nc.semaphore("ld"))
        lxs = [es.enter_context(nc.semaphore(f"lxs{i}")) for i in range(2)]
        cm = es.enter_context(nc.semaphore("cm"))
        sq = es.enter_context(nc.semaphore("sq"))
        st = es.enter_context(nc.semaphore("st"))
        vself = es.enter_context(nc.semaphore("vself"))
        cw = es.enter_context(nc.semaphore("cw"))
        blk = es.enter_context(nc.Block())

        @blk.gpsimd
        def _(g):
            qc = WCOLS // 4
            for i in range(4):
                g.dma_start(out=wcb[:, i * qc:(i + 1) * qc],
                            in_=wcat[:, i * qc:(i + 1) * qc]).then_inc(cw, 16)

        @blk.sync
        def _(sy):
            sy.dma_start(out=wrtb[:], in_=wrt).then_inc(ld, 16)
            for j in range(NTC):
                if j >= 2:
                    sy.wait_ge(cm, 2 * j - 2)
                sy.dma_start(out=xt[j % 2][:], in_=x[j * 128:(j + 1) * 128, :]
                             ).then_inc(lxs[j % 2], 16)

        @blk.vector
        def _(v):
            vs = [0]
            def e(inst):
                vs[0] += 1
                inst.then_inc(vself, 1)
                v.wait_ge(vself, vs[0])
            v.wait_ge(ld, 16)
            v.reduce_sum(w1[:], wrtb[:], axis=AX.X)
            for j in range(NTC):
                xb = xt[j % 2]
                v.wait_ge(lxs[j % 2], (j // 2 + 1) * 16)
                e(v.scalar_tensor_tensor(scr[:], xb[:], 1.0, xb[:], ALU.mult,
                                         ALU.mult, accum_out=ssq[:]))
                e(v.reduce_sum(sx[:], xb[:], axis=AX.X))
                e(v.scalar_tensor_tensor(scr[:], xb[:], 1.0, wrtb[:],
                                         ALU.mult, ALU.mult,
                                         accum_out=xw[:]))
                e(v.tensor_scalar_mul(m[:], sx[:], 1.0 / D))
                e(v.tensor_scalar_mul(var[:], ssq[:], 1.0 / D))
                e(v.tensor_mul(t0[:], m[:], m[:]))
                e(v.tensor_sub(var[:], var[:], t0[:]))
                v.tensor_scalar_add(var[:], var[:], EPS_LN).then_inc(cm, 1)
                v.wait_ge(sq, j + 1)
                e(v.reciprocal(rstd[:], sd[:]))
                e(v.tensor_mul(nb[:], m[:], rstd[:]))
                e(v.tensor_scalar_mul(nb[:], nb[:], -1.0))
                if j >= 2:
                    v.wait_ge(st, (j // 2) * 32)
                e(v.tensor_scalar(lt[j % 2][:], xb[:], rstd[:], nb[:],
                                  ALU.mult, ALU.add))
                e(v.tensor_mul(t0[:], m[:], w1[:]))
                e(v.tensor_sub(t0[:], xw[:], t0[:]))
                v.tensor_mul(scb[:, j:j + 1], t0[:], rstd[:]).then_inc(cm, 1)

        @blk.scalar
        def _(s):
            for j in range(NTC):
                s.wait_ge(cm, 2 * j + 1)
                s.sqrt(sd[:], var[:]).then_inc(sq, 1)

        @blk.gpsimd
        def _(g):
            for j in range(NTC):
                g.wait_ge(cm, 2 * j + 2)
                g.dma_start(out=lnx[j * 128:(j + 1) * 128, :], in_=lt[j % 2][:]
                            ).then_inc(st, 16)
            g.dma_start(out=sc, in_=scb[:]).then_inc(st, 16)
            g.wait_ge(st, 16 * NTC + 16)
            g.wait_ge(cw, 64)
    return nc



def build_p2():
    nc = bass.Bass("TRN2", target_bir_lowering=False)
    ssb_d = nc.dram_tensor("ssb", [128, 128], F32, kind="ExternalInput").ap()
    lnxf = nc.dram_tensor("lnxf", [TT, D], BF16, kind="ExternalInput").ap()
    jc_d = nc.dram_tensor("jc", [128, 256], F32, kind="ExternalInput").ap()
    inv_d = nc.dram_tensor("inv", [128, 2], I32, kind="ExternalOutput").ap()
    cidx_d = nc.dram_tensor("cidxo", [128, 128], F32, kind="ExternalOutput").ap()
    selr_d = nc.dram_tensor("selr", [2, 128, D], BF16, kind="ExternalOutput").ap()

    ones_row_f = nc.inline_tensor(np.ones((1, 128), np.float32), name="ones_row_f")
    ones_row_b = nc.inline_tensor(np.ones((1, 128), bf16), name="ones_row_b")
    ones_col_b = nc.inline_tensor(np.ones((128, 1), bf16), name="ones_col_b")
    tri = np.zeros((128, 128), np.float32)
    for f in range(128):
        lo_ = 0 if f < 64 else 64
        tri[lo_:f, f] = 1.0
    trib_c = nc.inline_tensor(tri.astype(bf16), name="trib")
    ptri = (np.tril(np.ones((128, 128), np.float32)).T).astype(bf16)
    ptri_c = nc.inline_tensor(ptri, name="ptri")
    boff = np.zeros((128, 128), np.float32)
    boff[:, :64] = -1.0
    boff[:, 64:] = 1023.0
    boff_c = nc.inline_tensor(boff, name="boff")
    rhs2 = np.zeros((128, 256), np.float32)
    rhs2[:, 0::2] = np.arange(128, dtype=np.float32)[:, None]
    rhs2[:, 1::2] = np.arange(128, dtype=np.float32)[None, :]
    rhs2_c = nc.inline_tensor(rhs2.astype(bf16), name="rhs2")
    bigc = nc.inline_tensor(np.full((128, 128), OOB, np.float32), name="bigc")

    es = ExitStack()
    with es:
        ssb = es.enter_context(nc.sbuf_tensor("ssb_s", [128, 128], F32))
        jc = es.enter_context(nc.sbuf_tensor("jc_s", [128, 256], F32))
        onesrf = es.enter_context(nc.sbuf_tensor("onesrf", [1, 128], F32))
        onesrb = es.enter_context(nc.sbuf_tensor("onesrb", [1, 128], BF16))
        onescb = es.enter_context(nc.sbuf_tensor("onescb", [128, 1], BF16))
        trib = es.enter_context(nc.sbuf_tensor("trib_s", [128, 128], BF16))
        ptrib = es.enter_context(nc.sbuf_tensor("ptri_s", [128, 128], BF16))
        boffs = es.enter_context(nc.sbuf_tensor("boff_s", [128, 128], F32))
        rhs2s = es.enter_context(nc.sbuf_tensor("rhs2_s", [128, 256], BF16))
        bigs = es.enter_context(nc.sbuf_tensor("big_s", [128, 128], F32))

        lo = es.enter_context(nc.sbuf_tensor("lo", [1, 2], F32))
        hi = es.enter_context(nc.sbuf_tensor("hi", [1, 2], F32))
        mid = es.enter_context(nc.sbuf_tensor("mid", [1, 2], F32))
        trow = es.enter_context(nc.sbuf_tensor("trow", [1, 128], F32))
        cnt = es.enter_context(nc.sbuf_tensor("cnt", [1, 2], F32))
        flag = es.enter_context(nc.sbuf_tensor("flag", [1, 2], F32))
        dlt = es.enter_context(nc.sbuf_tensor("dlt", [1, 2], F32))
        m1 = es.enter_context(nc.sbuf_tensor("m1", [1, 2], F32))
        rr = es.enter_context(nc.sbuf_tensor("rr", [1, 2], F32))
        maskb = es.enter_context(nc.sbuf_tensor("maskb", [128, 128], BF16))
        strongb = es.enter_context(nc.sbuf_tensor("strongb", [128, 128], BF16))
        borderb = es.enter_context(nc.sbuf_tensor("borderb", [128, 128], BF16))
        selb = es.enter_context(nc.sbuf_tensor("selb", [128, 128], BF16))
        csrow = es.enter_context(nc.sbuf_tensor("csrow", [1, 128], BF16))
        csb = es.enter_context(nc.sbuf_tensor("csb", [128, 128], BF16))
        cum = es.enter_context(nc.sbuf_tensor("cum", [128, 128], F32))
        tle = es.enter_context(nc.sbuf_tensor("tle", [128, 128], BF16))
        cidx = es.enter_context(nc.sbuf_tensor("cidx", [128, 128], F32))
        cidxm = es.enter_context(nc.sbuf_tensor("cidxm", [128, 128], F32))
        oh = [es.enter_context(nc.sbuf_tensor(f"oh{i}", [128, 128], BF16))
              for i in range(4)]
        invf = es.enter_context(nc.sbuf_tensor("invf", [128, 4], F32))
        invi = es.enter_context(nc.sbuf_tensor("invi", [128, 2], I32))
        selr = es.enter_context(nc.sbuf_tensor("selr_s", [128, 2 * D], BF16))

        ps_tau = es.enter_context(nc.psum_tensor("ps_tau", [128, 128], F32))
        ps_cnt = es.enter_context(nc.psum_tensor("ps_cnt", [1, 128], F32))
        ps_cum = es.enter_context(nc.psum_tensor("ps_cum", [128, 128], F32))
        ps_csb = es.enter_context(nc.psum_tensor("ps_csb", [128, 128], F32))
        ps_inv = [es.enter_context(nc.psum_tensor(f"ps_inv{i}", [128, 2], F32))
                  for i in range(2)]

        ld = es.enter_context(nc.semaphore("ld"))
        v2t = es.enter_context(nc.semaphore("v2t"))
        t2v = es.enter_context(nc.semaphore("t2v"))
        so = es.enter_context(nc.semaphore("so"))
        vself = es.enter_context(nc.semaphore("vself"))
        blk = es.enter_context(nc.Block())

        BV = 2 * BISECT_ITERS
        V_HIP = BV + 1
        V_STR = BV + 2
        V_LOP = BV + 3
        V_BOR = BV + 4
        V_CS1 = BV + 5
        V_CB1 = BV + 6
        V_RRP = BV + 7
        V_SEL = BV + 8
        V_CS2 = BV + 9
        V_CB2 = BV + 10
        V_CID = BV + 11
        V_OH0 = BV + 12
        V_INV = V_OH0 + 256
        T_TAUH = BV + 1
        T_CNTH = BV + 2
        T_TAUL = BV + 3
        T_BOR = BV + 4
        T_CB1 = BV + 5
        T_TRI1 = BV + 6
        T_TAUR = BV + 7
        T_SEL = BV + 9
        T_CB2 = BV + 10
        T_TRI2 = BV + 11
        T_OH0 = BV + 12

        NLD = 10 * 16

        def v_tau_prep(v, src, sem=None):
            v.tensor_copy(trow[:, 0:64], src[0:1, 0:1].to_broadcast([1, 64]))
            i2 = v.tensor_copy(trow[:, 64:128], src[0:1, 1:2].to_broadcast([1, 64]))
            if sem is not None:
                i2.then_inc(sem, 1)

        @blk.sync
        def _(sy):
            sy.dma_start(out=ssb[:], in_=ssb_d).then_inc(ld, 16)
            sy.dma_start(out=jc[:], in_=jc_d).then_inc(ld, 16)
            sy.dma_start(out=onesrf[:], in_=ones_row_f.ap()).then_inc(ld, 16)
            sy.dma_start(out=onesrb[:], in_=ones_row_b.ap()).then_inc(ld, 16)
            sy.dma_start(out=onescb[:], in_=ones_col_b.ap()).then_inc(ld, 16)
            sy.dma_start(out=trib[:], in_=trib_c.ap()).then_inc(ld, 16)
            sy.dma_start(out=ptrib[:], in_=ptri_c.ap()).then_inc(ld, 16)
            sy.dma_start(out=boffs[:], in_=boff_c.ap()).then_inc(ld, 16)
            sy.dma_start(out=rhs2s[:], in_=rhs2_c.ap()).then_inc(ld, 16)
            sy.dma_start(out=bigs[:], in_=bigc.ap()).then_inc(ld, 16)

        @blk.vector
        def _(v):
            vs = [0]
            def e(inst):
                vs[0] += 1
                inst.then_inc(vself, 1)
                v.wait_ge(vself, vs[0])
            v.wait_ge(ld, NLD)
            e(v.memset(lo[:], -512.0))
            e(v.memset(hi[:], 512.0))
            for i in range(BISECT_ITERS):
                e(v.tensor_add(mid[:], lo[:], hi[:]))
                e(v.tensor_scalar_mul(mid[:], mid[:], 0.5))
                v_tau_prep(v, mid, v2t)
                v.wait_ge(t2v, 2 * i + 1)
                v.tensor_tensor(maskb[:], ssb[:], ps_tau[:], ALU.is_ge
                                ).then_inc(v2t, 1)
                v.wait_ge(t2v, 2 * i + 2)
                e(v.reduce_sum(cnt[:, 0:1], ps_cnt[:, 0:64], axis=AX.X))
                e(v.reduce_sum(cnt[:, 1:2], ps_cnt[:, 64:128], axis=AX.X))
                e(v.tensor_scalar(flag[:], cnt[:], float(KSEL), None, ALU.is_ge))
                e(v.tensor_sub(dlt[:], mid[:], lo[:]))
                e(v.tensor_mul(dlt[:], dlt[:], flag[:]))
                e(v.tensor_add(lo[:], lo[:], dlt[:]))
                e(v.tensor_sub(dlt[:], hi[:], mid[:]))
                e(v.tensor_mul(dlt[:], dlt[:], flag[:]))
                e(v.tensor_add(hi[:], mid[:], dlt[:]))
            v_tau_prep(v, hi, v2t)
            v.wait_ge(t2v, T_TAUH)
            v.tensor_tensor(strongb[:], ssb[:], ps_tau[:], ALU.is_ge
                            ).then_inc(v2t, 1)
            v.wait_ge(t2v, T_CNTH)
            e(v.reduce_sum(m1[:, 0:1], ps_cnt[:, 0:64], axis=AX.X))
            e(v.reduce_sum(m1[:, 1:2], ps_cnt[:, 64:128], axis=AX.X))
            e(v.tensor_scalar(rr[:], m1[:], float(KSEL), None, ALU.subtract))
            e(v.tensor_scalar_mul(rr[:], rr[:], -1.0))
            v_tau_prep(v, lo, v2t)
            v.wait_ge(t2v, T_TAUL)
            e(v.tensor_tensor(borderb[:], ssb[:], ps_tau[:], ALU.is_ge))
            v.tensor_sub(borderb[:], borderb[:], strongb[:]).then_inc(v2t, 1)
            v.wait_ge(t2v, T_BOR)
            v.tensor_copy(csrow[:], ps_cnt[:]).then_inc(v2t, 1)
            v.wait_ge(t2v, T_CB1)
            v.tensor_copy(csb[:], ps_csb[:]).then_inc(v2t, 1)
            v_tau_prep(v, rr, v2t)
            v.wait_ge(t2v, T_TAUR)
            e(v.tensor_copy(cum[:], ps_cum[:]))
            e(v.tensor_tensor(tle[:], cum[:], ps_tau[:], ALU.is_le))
            e(v.tensor_mul(tle[:], tle[:], borderb[:]))
            v.tensor_add(selb[:], strongb[:], tle[:]).then_inc(v2t, 1)
            v.wait_ge(t2v, T_SEL)
            v.tensor_copy(csrow[:], ps_cnt[:]).then_inc(v2t, 1)
            v.wait_ge(t2v, T_CB2)
            v.tensor_copy(csb[:], ps_csb[:]).then_inc(v2t, 1)
            v.wait_ge(t2v, T_TRI2)
            e(v.tensor_tensor(cidx[:], ps_cum[:], boffs[:], ALU.add))
            e(v.tensor_tensor(cidx[:], cidx[:], selb[:], ALU.mult))
            e(v.tensor_scalar(tle[:], selb[:], 1.0, None, ALU.subtract))
            e(v.tensor_scalar(cidx2t := tle[:], tle[:], -OOB, None, ALU.mult))
            v.tensor_tensor(cidxm[:], cidx[:], tle[:], ALU.add).then_inc(v2t, 1)
            for k in range(256):
                f, u = k // 2, k % 2
                slot = k % 4
                if k >= 4:
                    v.wait_ge(t2v, T_OH0 + (k - 4))
                v.tensor_scalar(oh[slot][:], jc[:, u * 128:(u + 1) * 128],
                                cidxm[:, f:f + 1], None, ALU.is_equal
                                ).then_inc(v2t, 1)
            v.wait_ge(t2v, T_OH0 + 255)
            e(v.tensor_copy(invf[:, 0:2], ps_inv[0][:]))
            e(v.tensor_copy(invf[:, 2:4], ps_inv[1][:]))
            e(v.tensor_scalar_mul(invf[:, 1:2], invf[:, 1:2], 128.0))
            e(v.tensor_add(invf[:, 0:1], invf[:, 0:1], invf[:, 1:2]))
            e(v.tensor_scalar_mul(invf[:, 3:4], invf[:, 3:4], 128.0))
            e(v.tensor_add(invf[:, 2:3], invf[:, 2:3], invf[:, 3:4]))
            e(v.tensor_copy(invi[:, 0:1], invf[:, 0:1]))
            v.tensor_copy(invi[:, 1:2], invf[:, 2:3]).then_inc(v2t, 1)

        @blk.tensor
        def _(t):
            t.wait_ge(ld, NLD)
            for i in range(BISECT_ITERS):
                t.wait_ge(v2t, 2 * i + 1)
                t.matmul(ps_tau[:], onesrf[:], trow[:], start=True, stop=True
                         ).then_inc(t2v, 1)
                t.wait_ge(v2t, 2 * i + 2)
                t.matmul(ps_cnt[:], onescb[:], maskb[:], start=True, stop=True
                         ).then_inc(t2v, 1)
            t.wait_ge(v2t, V_HIP)
            t.matmul(ps_tau[:], onesrf[:], trow[:], start=True, stop=True
                     ).then_inc(t2v, 1)
            t.wait_ge(v2t, V_STR)
            t.matmul(ps_cnt[:], onescb[:], strongb[:], start=True, stop=True
                     ).then_inc(t2v, 1)
            t.wait_ge(v2t, V_LOP)
            t.matmul(ps_tau[:], onesrf[:], trow[:], start=True, stop=True
                     ).then_inc(t2v, 1)
            t.wait_ge(v2t, V_BOR)
            t.matmul(ps_cum[:], ptrib[:], borderb[:], start=True, stop=False)
            t.matmul(ps_cnt[:], onescb[:], borderb[:], start=True, stop=True
                     ).then_inc(t2v, 1)
            t.wait_ge(v2t, V_CS1)
            t.matmul(ps_csb[:], csrow[:], onesrb[:], start=True, stop=True
                     ).then_inc(t2v, 1)
            t.wait_ge(v2t, V_CB1)
            t.matmul(ps_cum[:], csb[:], trib[:], start=False, stop=True
                     ).then_inc(t2v, 1)
            t.wait_ge(v2t, V_RRP)
            t.matmul(ps_tau[:], onesrf[:], trow[:], start=True, stop=True
                     ).then_inc(t2v, 1)
            t.wait_ge(v2t, V_SEL)
            t.matmul(ps_cum[:], ptrib[:], selb[:], start=True, stop=False)
            t.matmul(ps_cnt[:], onescb[:], selb[:], start=True, stop=True
                     ).then_inc(t2v, 2)
            t.wait_ge(v2t, V_CS2)
            t.matmul(ps_csb[:], csrow[:], onesrb[:], start=True, stop=True
                     ).then_inc(t2v, 1)
            t.wait_ge(v2t, V_CB2)
            t.matmul(ps_cum[:], csb[:], trib[:], start=False, stop=True
                     ).then_inc(t2v, 1)
            for k in range(256):
                f, u = k // 2, k % 2
                slot = k % 4
                t.wait_ge(v2t, V_OH0 + k)
                t.matmul(ps_inv[u][:], oh[slot][:], rhs2s[:, 2 * f:2 * f + 2],
                         start=(f == 0), stop=(f == 127)).then_inc(t2v, 1)

        @blk.gpsimd
        def _(g):
            g.wait_ge(v2t, V_CID)
            g.dma_start(out=cidx_d, in_=cidxm[:]).then_inc(so, 16)
            g.wait_ge(v2t, V_INV)
            g.dma_start(out=inv_d, in_=invi[:]).then_inc(so, 16)
            for a in range(2):
                g.indirect_dma_start(
                    selr[:, a * D:(a + 1) * D], None,
                    lnxf, IndirectOffsetOnAxis(invi[:, a:a + 1], 0),
                    bounds_check=TT - 1, oob_is_err=False,
                ).then_inc(so, 16)
            g.wait_ge(so, 64)
            g.dma_start(out=selr_d[0], in_=selr[:, 0:D]).then_inc(so, 16)
            g.dma_start(out=selr_d[1], in_=selr[:, D:2 * D]).then_inc(so, 16)
            g.wait_ge(so, 96)
    return nc



BATCH_OF_PCOL = [0, 1] + [0] * 7 + [1] * 7


def build_p3():
    nc = bass.Bass("TRN2", target_bir_lowering=False)
    sel_d = nc.dram_tensor("sel", [SSEL, D], BF16, kind="ExternalInput").ap()
    inv_d = nc.dram_tensor("invp", [128, NTC], I32, kind="ExternalInput").ap()
    costab = nc.dram_tensor("costab", [SEQ, 256], F32, kind="ExternalInput").ap()
    mask_d = nc.dram_tensor("mask", [2, 128, SSEL], F32, kind="ExternalInput").ap()
    wq_d = nc.dram_tensor("wq_l", [HEADS, 128, 16 * 128], BF16, kind="ExternalInput").ap()
    wk_d = nc.dram_tensor("wk_l", [KVH, 128, 16 * 128], BF16, kind="ExternalInput").ap()
    wv_d = nc.dram_tensor("wv_l", [KVH, 128, 16 * 128], BF16, kind="ExternalInput").ap()
    wo_d = nc.dram_tensor("wo_l", [16, 128, 16 * 128], BF16, kind="ExternalInput").ap()
    wg_d = nc.dram_tensor("wg_l", [64, 128, 16 * 128], BF16, kind="ExternalInput").ap()
    wu_d = nc.dram_tensor("wu_l", [64, 128, 16 * 128], BF16, kind="ExternalInput").ap()
    wd_d = nc.dram_tensor("wd_l", [64, 128, 16 * 128], BF16, kind="ExternalInput").ap()
    bq_d = nc.dram_tensor("bq_fm", [128, HEADS], F32, kind="ExternalInput").ap()
    bk_d = nc.dram_tensor("bk_fm", [128, KVH], F32, kind="ExternalInput").ap()
    bv_d = nc.dram_tensor("bv_fm", [128, KVH], F32, kind="ExternalInput").ap()
    ln1_d = nc.dram_tensor("ln1b", [128, D], F32, kind="ExternalInput").ap()
    ln2_d = nc.dram_tensor("ln2_fm", [128, NTC], F32, kind="ExternalInput").ap()
    proc_d = nc.dram_tensor("proc", [2, 128, D], F32, kind="ExternalOutput").ap()

    idn_c = nc.inline_tensor(np.eye(128, dtype=bf16), name="idn")
    idnf_c = nc.inline_tensor(np.eye(128, dtype=np.float32), name="idnf")
    rmat = np.zeros((128, 128), np.float32)
    for dp in range(64):
        rmat[dp + 64, dp] = -1.0
    for dp in range(64, 128):
        rmat[dp - 64, dp] = 1.0
    rmat_c = nc.inline_tensor(rmat.astype(bf16), name="rmat")
    onescb_c = nc.inline_tensor(np.ones((128, 1), bf16), name="onescb3")
    onesrb_c = nc.inline_tensor(np.ones((1, 128), bf16), name="onesrb3")
    pofs = np.zeros((128, NTC), np.float32)
    for k in range(NTC):
        pofs[:, k] = 8192.0 * BATCH_OF_PCOL[k]
    pofs_c = nc.inline_tensor(pofs, name="pofs")

    es = ExitStack()
    with es:
        buf1 = es.enter_context(nc.sbuf_tensor("buf1", [128, 32768], BF16))
        buf2 = es.enter_context(nc.sbuf_tensor("buf2", [128, NTC * D], BF16))
        x1 = es.enter_context(nc.sbuf_tensor("x1", [128, 4096], F32))
        pstage = es.enter_context(nc.sbuf_tensor("pstage", [128, 1024], F32))
        sfm_scrb = es.enter_context(nc.sbuf_tensor("sfm_scrb", [128, 4096], BF16))
        qsb = es.enter_context(nc.sbuf_tensor("qsb", [128, 4096], BF16))
        attsb = es.enter_context(nc.sbuf_tensor("attsb", [128, SSEL], BF16))
        attT = es.enter_context(nc.sbuf_tensor("attT", [128, SSEL], BF16))
        cosb = es.enter_context(nc.sbuf_tensor("cosb", [128, SSEL], BF16))
        sinb = es.enter_context(nc.sbuf_tensor("sinb", [128, SSEL], BF16))
        kro = es.enter_context(nc.sbuf_tensor("kro", [128, 512], BF16))
        scrf = es.enter_context(nc.sbuf_tensor("scrf", [128, 512], F32))
        scrf2 = es.enter_context(nc.sbuf_tensor("scrf2", [128, 512], F32))
        wslab = [es.enter_context(nc.sbuf_tensor(f"wslab{i}", [128, 2048], BF16))
                 for i in range(4)]
        bqs = es.enter_context(nc.sbuf_tensor("bqs", [128, HEADS], F32))
        bks = es.enter_context(nc.sbuf_tensor("bks", [128, KVH], F32))
        bvs = es.enter_context(nc.sbuf_tensor("bvs", [128, KVH], F32))
        ln2s = es.enter_context(nc.sbuf_tensor("ln2s", [128, NTC], F32))
        idn = es.enter_context(nc.sbuf_tensor("idn_s", [128, 128], BF16))
        idnf = es.enter_context(nc.sbuf_tensor("idnf_s", [128, 128], F32))
        rmt = es.enter_context(nc.sbuf_tensor("rmt_s", [128, 128], BF16))
        invs = es.enter_context(nc.sbuf_tensor("invs", [128, NTC], I32))
        posf = es.enter_context(nc.sbuf_tensor("posf", [128, NTC], F32))
        posi = es.enter_context(nc.sbuf_tensor("posi", [128, NTC], I32))
        pofs_s = es.enter_context(nc.sbuf_tensor("pofs_s", [128, NTC], F32))
        ssq = es.enter_context(nc.sbuf_tensor("ssq3", [128, 1], F32))
        rstd = es.enter_context(nc.sbuf_tensor("rstd3", [128, 1], F32))
        mx = es.enter_context(nc.sbuf_tensor("mx3", [128, 1], F32))
        nb2 = es.enter_context(nc.sbuf_tensor("nb23", [128, 1], F32))
        asum = es.enter_context(nc.sbuf_tensor("asum3", [128, 1], F32))
        rec = es.enter_context(nc.sbuf_tensor("rec3", [128, 1], F32))
        var2 = es.enter_context(nc.sbuf_tensor("var2", [1, 256], F32))
        rstd2 = es.enter_context(nc.sbuf_tensor("rstd2", [1, 256], F32))
        rstd2b = es.enter_context(nc.sbuf_tensor("rstd2b", [1, 256], BF16))
        onescb = es.enter_context(nc.sbuf_tensor("onescb3s", [128, 1], BF16))
        onesrb = es.enter_context(nc.sbuf_tensor("onesrb3s", [1, 128], BF16))

        psBig = es.enter_context(nc.psum_tensor("psBig", [128, 2048], F32))
        psA = es.enter_context(nc.psum_tensor("psA", [128, 512], F32))
        psW = es.enter_context(nc.psum_tensor("psW", [128, 512], F32))
        psT = es.enter_context(nc.psum_tensor("psT", [128, 1024], F32))

        def psTs(i):
            return psT[:, (i % 2) * 512:(i % 2) * 512 + 128]

        def psAV(a):
            return psA[:, 0:128] if a == 0 else psW[:, 0:128]

        ld = es.enter_context(nc.semaphore("ld"))
        gw = es.enter_context(nc.semaphore("gw"))
        gsl = [es.enter_context(nc.semaphore(f"gsl{i}")) for i in range(4)]
        tv = es.enter_context(nc.semaphore("tv"))
        vt = es.enter_context(nc.semaphore("vt"))
        sv = es.enter_context(nc.semaphore("sv"))
        vself = es.enter_context(nc.semaphore("vself"))
        so = es.enter_context(nc.semaphore("so"))
        blk = es.enter_context(nc.Block())

        C = dict(ld=0, gw=0, tv=0, vt=0, sv=0, so=0)
        G = [0, 0, 0, 0]
        ve_n = [0]

        def VE(v, inst):
            ve_n[0] += 1
            inst.then_inc(vself, 1)
            v.wait_ge(vself, ve_n[0])

        def kfm(kvh):
            return buf1[:, kvh * 2048:(kvh + 1) * 2048]

        def vtm(k):
            return buf1[:, 16384 + k * 1024:16384 + (k + 1) * 1024]

        def hfm(pt):
            return buf2[:, pt * 2048:(pt + 1) * 2048]

        ln1b_sb = None

        @blk.sync
        def _(sy):
            for k in range(NTC):
                sy.dma_start(out=buf1[:, k * D:(k + 1) * D],
                             in_=sel_d[k * 128:(k + 1) * 128, :]).then_inc(ld, 16)
            sy.dma_start(out=bqs[:], in_=bq_d).then_inc(ld, 16)
            sy.dma_start(out=bks[:], in_=bk_d).then_inc(ld, 16)
            sy.dma_start(out=bvs[:], in_=bv_d).then_inc(ld, 16)
            sy.dma_start(out=ln2s[:], in_=ln2_d).then_inc(ld, 16)
            sy.dma_start(out=idn[:], in_=idn_c.ap()).then_inc(ld, 16)
            sy.dma_start(out=idnf[:], in_=idnf_c.ap()).then_inc(ld, 16)
            sy.dma_start(out=rmt[:], in_=rmat_c.ap()).then_inc(ld, 16)
            sy.dma_start(out=onescb[:], in_=onescb_c.ap()).then_inc(ld, 16)
            sy.dma_start(out=onesrb[:], in_=onesrb_c.ap()).then_inc(ld, 16)
            sy.dma_start(out=pofs_s[:], in_=pofs_c.ap()).then_inc(ld, 16)
            sy.dma_start(out=invs[:], in_=inv_d).then_inc(ld, 16)

        @blk.gpsimd
        def _(g):
            g.dma_start(out=cosb[:], in_=ln1_d).then_inc(gw, 16)

        C["ld"] += (NTC + 11) * 16
        C["gw"] += 16
        LD_ALL = C["ld"]
        GW_LN1 = C["gw"]

        T0B = C["tv"]
        S0B = C["sv"]

        @blk.tensor
        def _(t):
            t.wait_ge(ld, LD_ALL)
            for i in range(32):
                k, pt = i % 2, i // 2
                if i >= 2:
                    t.wait_ge(sv, S0B + i - 1)
                t.matmul(psTs(i),
                         buf1[:, k * D + pt * 128:k * D + pt * 128 + 128],
                         idn[:], start=True, stop=True).then_inc(tv, 1)

        @blk.scalar
        def _(s):
            for i in range(32):
                k, pt = i % 2, i // 2
                s.wait_ge(tv, T0B + i + 1)
                s.copy(sfm_scrb[:, pt * 256 + k * 128:pt * 256 + (k + 1) * 128],
                       psTs(i)).then_inc(sv, 1)
        C["tv"] += 32
        C["sv"] += 32
        SV_OWN = C["sv"]

        VT1 = C["vt"]
        SV1 = C["sv"]

        @blk.vector
        def _(v):
            v.wait_ge(ld, LD_ALL)
            v.wait_ge(gw, GW_LN1)
            v.wait_ge(sv, SV_OWN)
            for k in range(NTC):
                tile = buf1[:, k * D:(k + 1) * D]
                VE(v, v.scalar_tensor_tensor(scrf[:], tile[:, 0:512], 1.0,
                                             tile[:, 0:512], ALU.mult,
                                             ALU.mult, accum_out=ssq[:]))
                for q4 in range(1, 4):
                    VE(v, v.scalar_tensor_tensor(
                        scrf[:], tile[:, q4 * 512:(q4 + 1) * 512], 1.0,
                        tile[:, q4 * 512:(q4 + 1) * 512], ALU.mult, ALU.mult,
                        accum_out=rec[:]))
                    VE(v, v.tensor_add(ssq[:], ssq[:], rec[:]))
                VE(v, v.tensor_scalar(ssq[:], ssq[:], 1.0 / D, EPS_RMS,
                                      ALU.mult, ALU.add))
                v.sem_inc(vt, 1)
                v.wait_ge(sv, SV1 + k + 1)
                VE(v, v.reciprocal(rstd[:], rstd[:]))
                VE(v, v.scalar_tensor_tensor(tile, tile, rstd[:], cosb[:],
                                             ALU.mult, ALU.mult))
                v.sem_inc(vt, 1)
            v.sem_inc(vt, 1)

        @blk.scalar
        def _(s):
            for k in range(NTC):
                s.wait_ge(vt, VT1 + 2 * k + 1)
                s.sqrt(rstd[:], ssq[:]).then_inc(sv, 1)
        C["vt"] += 2 * NTC + 1
        C["sv"] += NTC
        VT_H = C["vt"]

        @blk.vector
        def _(v):
            VE(v, v.tensor_copy(posf[:], invs[:]))
            VE(v, v.tensor_sub(posf[:], posf[:], pofs_s[:]))
            v.tensor_copy(posi[:], posf[:]).then_inc(vt, 1)
        C["vt"] += 1
        VT_POS = C["vt"]

        GW2 = C["gw"]
        TV2 = C["tv"]
        VT2b = C["vt"]

        @blk.gpsimd
        def _(g):
            g.wait_ge(vt, VT_POS)
            for qtr in range(4):
                if qtr >= 1:
                    g.wait_ge(vt, VT2b + 8 * qtr)
                for kk in range(4):
                    k = qtr * 4 + kk
                    g.indirect_dma_start(
                        pstage[:, kk * 256:(kk + 1) * 256], None, costab,
                        IndirectOffsetOnAxis(posi[:, k:k + 1], 0),
                        bounds_check=SEQ - 1, oob_is_err=False,
                    ).then_inc(gw, 16)

        @blk.tensor
        def _(t):
            for qtr in range(4):
                for kk in range(4):
                    k = qtr * 4 + kk
                    t.wait_ge(gw, GW2 + (qtr + 1) * 64)
                    if k >= 1:
                        t.wait_ge(vt, VT2b + 2 * k)
                    t.transpose(psA[:, 0:128],
                                pstage[:, kk * 256:kk * 256 + 128], idnf[:]
                                ).then_inc(tv, 1)
                    t.transpose(psW[:, 0:128],
                                pstage[:, kk * 256 + 128:kk * 256 + 256],
                                idnf[:]).then_inc(tv, 1)

        @blk.vector
        def _(v):
            for k in range(NTC):
                v.wait_ge(tv, TV2 + 2 * k + 1)
                v.tensor_copy(cosb[:, k * 128:(k + 1) * 128], psA[:, 0:128]
                              ).then_inc(vt, 1)
                v.wait_ge(tv, TV2 + 2 * k + 2)
                v.tensor_copy(sinb[:, k * 128:(k + 1) * 128], psW[:, 0:128]
                              ).then_inc(vt, 1)
        C["gw"] += 256
        C["tv"] += 32
        C["vt"] += 32

        T2B = C["tv"]
        S2B = C["sv"]

        @blk.tensor
        def _(t):
            for i in range(256):
                k, pt = i // NTC, i % NTC
                if pt == 0:
                    t.wait_ge(vt, VT1 + 2 * k + 2)
                if i >= 2:
                    t.wait_ge(sv, S2B + i - 1)
                t.matmul(psTs(i),
                         buf1[:, k * D + pt * 128:k * D + pt * 128 + 128],
                         idn[:], start=True, stop=True).then_inc(tv, 1)

        @blk.scalar
        def _(s):
            for i in range(256):
                k, pt = i // NTC, i % NTC
                s.wait_ge(tv, T2B + i + 1)
                s.copy(buf2[:, pt * D + k * 128:pt * D + (k + 1) * 128],
                       psTs(i)).then_inc(sv, 1)
        C["tv"] += 256
        C["sv"] += 256
        SV_HFM = C["sv"]

        GW4 = C["gw"]
        TV4 = C["tv"]
        VT4 = C["vt"]

        G4 = list(G)

        @blk.gpsimd
        def _(g):
            for kvh in range(KVH):
                if kvh >= 2:
                    g.wait_ge(vt, VT4 + 12 * (kvh - 1))
                g.dma_start(out=wslab[kvh % 2][:], in_=wk_d[kvh]
                            ).then_inc(gsl[kvh % 2], 16)

        @blk.tensor
        def _(t):
            t.wait_ge(vt, VT4)
            t.wait_ge(sv, SV_HFM)
            for kvh in range(KVH):
                t.wait_ge(gsl[kvh % 2], G4[kvh % 2] + (kvh // 2 + 1) * 16)
                for ch in range(4):
                    if kvh + ch > 0:
                        t.wait_ge(vt, VT4 + 12 * kvh + 3 * ch)
                    for pt in range(NTC):
                        mmi = t.matmul(psA[:],
                                 wslab[kvh % 2][:, pt * 128:(pt + 1) * 128],
                                 hfm(pt)[:, ch * 512:(ch + 1) * 512],
                                 start=(pt == 0), stop=(pt == NTC - 1))
                    mmi.then_inc(tv, 1)
                    t.wait_ge(vt, VT4 + 12 * kvh + 3 * ch + 2)
                    t.matmul(psW[:], rmt[:], kro[:], start=True, stop=True
                             ).then_inc(tv, 1)

        @blk.vector
        def _(v):
            for kvh in range(KVH):
                for ch in range(4):
                    cs = slice(ch * 512, (ch + 1) * 512)
                    v.wait_ge(tv, TV4 + kvh * 8 + ch * 2 + 1)
                    VE(v, v.tensor_scalar(scrf[:], psA[:], bks[:, kvh:kvh + 1],
                                          None, ALU.add))
                    VE(v, v.tensor_copy(kro[:], scrf[:]))
                    v.sem_inc(vt, 2)
                    v.wait_ge(tv, TV4 + kvh * 8 + ch * 2 + 2)
                    VE(v, v.tensor_tensor(scrf2[:], psW[:], sinb[:, cs],
                                          ALU.mult))
                    VE(v, v.tensor_tensor(scrf[:], scrf[:], cosb[:, cs],
                                          ALU.mult))
                    VE(v, v.tensor_tensor(kfm(kvh)[:, cs], scrf[:], scrf2[:],
                                          ALU.add))
                    v.sem_inc(vt, 1)
        G[0] += (KVH // 2) * 16
        G[1] += (KVH // 2) * 16
        C["tv"] += KVH * 8
        C["vt"] += KVH * 12

        GW4V = C["gw"]
        TV4V = C["tv"]
        VT4V = C["vt"]
        SV4V = C["sv"]

        G4V = list(G)

        @blk.gpsimd
        def _(g):
            g.wait_ge(vt, VT4V)
            for kvh in range(KVH):
                if kvh >= 2:
                    g.wait_ge(vt, VT4V + 8 * (kvh - 1))
                g.dma_start(out=wslab[kvh % 2][:], in_=wv_d[kvh]
                            ).then_inc(gsl[kvh % 2], 16)

        @blk.tensor
        def _(t):
            t.wait_ge(vt, VT4V)
            for kvh in range(KVH):
                t.wait_ge(gsl[kvh % 2], G4V[kvh % 2] + (kvh // 2 + 1) * 16)
                for ch in range(4):
                    if kvh + ch > 0:
                        t.wait_ge(vt, VT4V + 8 * kvh + 2 * ch)
                    for pt in range(NTC):
                        mmi = t.matmul(psA[:],
                                 wslab[kvh % 2][:, pt * 128:(pt + 1) * 128],
                                 hfm(pt)[:, ch * 512:(ch + 1) * 512],
                                 start=(pt == 0), stop=(pt == NTC - 1))
                    mmi.then_inc(tv, 1)
                    t.wait_ge(vt, VT4V + 8 * kvh + 2 * ch + 2)
                    for b4 in range(4):
                        gidx = kvh * 16 + ch * 4 + b4
                        if gidx >= 2:
                            t.wait_ge(sv, SV4V + gidx - 1)
                        t.matmul(psTs(b4),
                                 kro[:, b4 * 128:(b4 + 1) * 128],
                                 idn[:], start=True, stop=True
                                 ).then_inc(tv, 1)

        @blk.vector
        def _(v):
            for kvh in range(KVH):
                for ch in range(4):
                    v.wait_ge(tv, TV4V + kvh * 20 + ch * 5 + 1)
                    VE(v, v.tensor_scalar(scrf[:], psA[:], bvs[:, kvh:kvh + 1],
                                          None, ALU.add))
                    VE(v, v.tensor_copy(kro[:], scrf[:]))
                    v.sem_inc(vt, 2)

        @blk.scalar
        def _(s):
            for kvh in range(KVH):
                for ch in range(4):
                    for b4 in range(4):
                        s.wait_ge(tv, TV4V + kvh * 20 + ch * 5 + 2 + b4)
                        ktok = ch * 4 + b4
                        s.copy(vtm(ktok)[:, kvh * 128:(kvh + 1) * 128],
                               psTs(b4)
                               ).then_inc(sv, 1)
        G[0] += (KVH // 2) * 16
        G[1] += (KVH // 2) * 16
        C["tv"] += KVH * 20
        C["vt"] += KVH * 8
        C["sv"] += KVH * 16

        GW5 = C["gw"]
        TV5 = C["tv"]
        VT5 = C["vt"]

        G5 = list(G)

        @blk.gpsimd
        def _(g):
            g.wait_ge(vt, VT5)
            for h in range(HEADS):
                if h >= 2:
                    g.wait_ge(vt, VT5 + 5 * (h - 1))
                g.dma_start(out=wslab[h % 2][:], in_=wq_d[h]
                            ).then_inc(gsl[h % 2], 16)

        @blk.tensor
        def _(t):
            t.wait_ge(vt, VT5)
            for h in range(HEADS):
                t.wait_ge(gsl[h % 2], G5[h % 2] + (h // 2 + 1) * 16)
                if h > 0:
                    t.wait_ge(vt, VT5 + 5 * (h - 1) + 2)
                for pt in range(NTC):
                    mmi = t.matmul(psA[:, 0:256],
                             wslab[h % 2][:, pt * 128:(pt + 1) * 128],
                             hfm(pt)[:, 0:256],
                             start=(pt == 0), stop=(pt == NTC - 1))
                mmi.then_inc(tv, 1)
                t.wait_ge(vt, VT5 + 5 * h + 2)
                t.matmul(psW[:, 0:256], rmt[:], kro[:, 0:256], start=True,
                         stop=True).then_inc(tv, 1)

        @blk.vector
        def _(v):
            for h in range(HEADS):
                v.wait_ge(tv, TV5 + 2 * h + 1)
                VE(v, v.tensor_scalar(scrf[:, 0:256], psA[:, 0:256],
                                      bqs[:, h:h + 1], None, ALU.add))
                VE(v, v.tensor_copy(kro[:, 0:256], scrf[:, 0:256]))
                v.sem_inc(vt, 2)
                v.wait_ge(tv, TV5 + 2 * h + 2)
                VE(v, v.tensor_tensor(scrf2[:, 0:256], psW[:, 0:256],
                                      sinb[:, 0:256], ALU.mult))
                VE(v, v.tensor_tensor(scrf[:, 0:256], scrf[:, 0:256],
                                      cosb[:, 0:256], ALU.mult))
                VE(v, v.tensor_tensor(qsb[:, h * 256:(h + 1) * 256],
                                      scrf[:, 0:256], scrf2[:, 0:256],
                                      ALU.add))
                v.sem_inc(vt, 3)
        G[0] += (HEADS // 2) * 16
        G[1] += (HEADS // 2) * 16
        C["tv"] += HEADS * 2
        C["vt"] += HEADS * 5
        VT_Q = C["vt"]

        @blk.gpsimd
        def _(g):
            g.wait_ge(vt, VT_Q)
            g.dma_start(out=cosb[:], in_=mask_d[0]).then_inc(gw, 16)
            g.dma_start(out=sinb[:], in_=mask_d[1]).then_inc(gw, 16)
        C["gw"] += 32
        GW_MASK = C["gw"]

        TV6 = C["tv"]
        VT6 = C["vt"]
        SV6 = C["sv"]
        TPI, VPI, SPI = 6, 3, 5

        def sc_inc(it):
            return TV6 + (it + 1 if it < 2 else 6 * it - 4)

        def trav_base(k):
            return TV6 + min(k + 2, 32) + 5 * k

        @blk.tensor
        def _(t):
            t.wait_ge(gw, GW_MASK)
            t.wait_ge(sv, SV6)

            def scores(it):
                a, h = it % 2, it // 2
                kvh = h // 2
                if it >= 1:
                    t.wait_ge(sv, SV6 + (it - 1) * SPI + 1)
                for ch in range(4):
                    mmi = t.matmul(psBig[:, ch * 512:(ch + 1) * 512],
                             qsb[:, h * 256 + a * 128:h * 256 + (a + 1) * 128],
                             kfm(kvh)[:, ch * 512:(ch + 1) * 512],
                             start=True, stop=True)
                mmi.then_inc(tv, 1)

            def trav(it):
                a, h = it % 2, it // 2
                kvh = h // 2
                t.wait_ge(vt, VT6 + 3 * it + 2)
                if it >= 2:
                    t.wait_ge(vt, VT6 + 3 * (it - 2) + 3)
                for b4 in range(4):
                    gb = it * 4 + b4
                    if gb >= 2:
                        pit, pb = (gb - 2) // 4, (gb - 2) % 4
                        t.wait_ge(sv, SV6 + pit * SPI + 2 + pb)
                    for q4 in range(4):
                        kt = b4 * 4 + q4
                        mmi = t.matmul(
                            psT[:, (b4 % 2) * 512 + q4 * 128:
                                (b4 % 2) * 512 + (q4 + 1) * 128],
                            attsb[:, kt * 128:(kt + 1) * 128], idn[:],
                            start=True, stop=True)
                    mmi.then_inc(tv, 1)
                for b4 in range(4):
                    t.wait_ge(sv, SV6 + it * SPI + 2 + b4)
                    for q4 in range(4):
                        kt = b4 * 4 + q4
                        mmi = t.matmul(psAV(a),
                                 vtm(kt)[:, kvh * 128:(kvh + 1) * 128],
                                 attT[:, kt * 128:(kt + 1) * 128],
                                 start=(kt == 0), stop=(kt == NTC - 1))
                mmi.then_inc(tv, 1)

            scores(0)
            scores(1)
            for k in range(32):
                trav(k)
                if k + 2 < 32:
                    scores(k + 2)

        @blk.vector
        def _(v):
            for it in range(32):
                a, h = it % 2, it // 2
                vb = VT6 + it * VPI
                v.wait_ge(tv, sc_inc(it))
                mbuf = cosb if a == 0 else sinb
                VE(v, v.tensor_tensor(psBig[:], psBig[:], mbuf[:], ALU.add))
                VE(v, v.reduce_max(mx[:], psBig[:], axis=AX.X))
                v.tensor_scalar_mul(nb2[:], mx[:], -ISQD).then_inc(vt, 1)
                v.wait_ge(sv, SV6 + it * SPI + 1)
                VE(v, v.reciprocal(rec[:], asum[:]))
                v.tensor_scalar(attsb[:], attsb[:], rec[:], None, ALU.mult
                                ).then_inc(vt, 1)
                v.wait_ge(tv, trav_base(it) + 5)
                v.tensor_copy(qsb[:, h * 256 + a * 128:h * 256 + (a + 1) * 128],
                              psAV(a)).then_inc(vt, 1)

        @blk.scalar
        def _(s):
            for it in range(32):
                vb = VT6 + it * VPI
                if it >= 1:
                    s.wait_ge(tv, trav_base(it - 1) + 4)
                s.wait_ge(vt, vb + 1)
                s.activation(attsb[:], psBig[:], AF.Exp, bias=nb2[:],
                             scale=ISQD, accum_out=asum[:]).then_inc(sv, 1)
                for b4 in range(4):
                    s.wait_ge(tv, trav_base(it) + 1 + b4)
                    s.copy(attT[:, b4 * 512:(b4 + 1) * 512],
                           psT[:, (b4 % 2) * 512:(b4 % 2) * 512 + 512]
                           ).then_inc(sv, 1)
        C["tv"] += 32 * TPI
        C["vt"] += 32 * VPI
        C["sv"] += 32 * SPI

        GW7 = C["gw"]
        TV7 = C["tv"]
        VT7 = C["vt"]

        G7 = list(G)

        @blk.gpsimd
        def _(g):
            g.wait_ge(vt, VT7)
            for mm_ in range(16):
                if mm_ >= 2:
                    g.wait_ge(vt, VT7 + mm_ - 1)
                g.dma_start(out=wslab[mm_ % 2][:], in_=wo_d[mm_]
                            ).then_inc(gsl[mm_ % 2], 16)

        @blk.tensor
        def _(t):
            t.wait_ge(vt, VT7)
            for mm_ in range(16):
                t.wait_ge(gsl[mm_ % 2], G7[mm_ % 2] + (mm_ // 2 + 1) * 16)
                if mm_ > 0:
                    t.wait_ge(vt, VT7 + mm_)
                for kt in range(NTC):
                    mmi = t.matmul(psW[:, 0:256],
                             wslab[mm_ % 2][:, kt * 128:(kt + 1) * 128],
                             qsb[:, kt * 256:(kt + 1) * 256],
                             start=(kt == 0), stop=(kt == NTC - 1))
                mmi.then_inc(tv, 1)

        @blk.vector
        def _(v):
            for mm_ in range(16):
                v.wait_ge(tv, TV7 + mm_ + 1)
                v.scalar_tensor_tensor(
                    x1[:, mm_ * 256:(mm_ + 1) * 256], psW[:, 0:256], 1.0,
                    sfm_scrb[:, mm_ * 256:(mm_ + 1) * 256], ALU.mult, ALU.add
                ).then_inc(vt, 1)
        G[0] += 8 * 16
        G[1] += 8 * 16
        C["tv"] += 16
        C["vt"] += 16
        VT_X1 = C["vt"]

        TV8 = C["tv"]
        VT8 = C["vt"]
        SV8 = C["sv"]

        @blk.scalar
        def _(s):
            s.wait_ge(vt, VT_X1)
            s.square(sfm_scrb[:], x1[:]).then_inc(sv, 1)
            s.wait_ge(vt, VT8 + 1)
            s.sqrt(rstd2[:], var2[:]).then_inc(sv, 1)

        @blk.tensor
        def _(t):
            t.wait_ge(sv, SV8 + 1)
            for mm_ in range(16):
                mmi = t.matmul(psW[0:1, 0:256], onescb[:],
                         sfm_scrb[:, mm_ * 256:(mm_ + 1) * 256],
                         start=(mm_ == 0), stop=(mm_ == 15))
            mmi.then_inc(tv, 1)
            t.wait_ge(vt, VT8 + 2)
            t.matmul(psW[:, 0:256], onesrb[:], rstd2b[:], start=True, stop=True
                     ).then_inc(tv, 1)

        @blk.vector
        def _(v):
            v.wait_ge(tv, TV8 + 1)
            VE(v, v.tensor_scalar(var2[:], psW[0:1, 0:256], 1.0 / D, EPS_RMS,
                                  ALU.mult, ALU.add))
            v.sem_inc(vt, 1)
            v.wait_ge(sv, SV8 + 2)
            VE(v, v.reciprocal(rstd2[:], rstd2[:]))
            v.tensor_copy(rstd2b[:], rstd2[:]).then_inc(vt, 1)
            v.wait_ge(tv, TV8 + 2)
            for mm_ in range(16):
                VE(v, v.tensor_tensor(scrf[:, 0:256],
                                      x1[:, mm_ * 256:(mm_ + 1) * 256],
                                      psW[:, 0:256], ALU.mult))
                VE(v, v.tensor_scalar(sfm_scrb[:, mm_ * 256:(mm_ + 1) * 256],
                                      scrf[:, 0:256], ln2s[:, mm_:mm_ + 1],
                                      None, ALU.mult))
            v.sem_inc(vt, 1)
        C["sv"] += 2
        C["tv"] += 2
        C["vt"] += 3
        VT_H2 = C["vt"]

        GW9 = C["gw"]
        TV9 = C["tv"]
        VT9 = C["vt"]
        SV9 = C["sv"]

        G9 = list(G)

        @blk.gpsimd
        def _(g):
            g.wait_ge(vt, VT_H2)
            for mi in range(64):
                if mi >= 2:
                    g.wait_ge(vt, VT9 + mi - 1)
                g.dma_start(out=wslab[mi % 2][:], in_=wg_d[mi]
                            ).then_inc(gsl[mi % 2], 16)
                g.dma_start(out=wslab[2 + mi % 2][:], in_=wu_d[mi]
                            ).then_inc(gsl[2 + mi % 2], 16)

        @blk.tensor
        def _(t):
            t.wait_ge(vt, VT_H2)
            for mi in range(64):
                t.wait_ge(gsl[mi % 2], G9[mi % 2] + (mi // 2 + 1) * 16)
                t.wait_ge(gsl[2 + mi % 2], G9[2 + mi % 2] + (mi // 2 + 1) * 16)
                if mi > 0:
                    t.wait_ge(vt, VT9 + mi)
                for pt in range(NTC):
                    mmi = t.matmul(psA[:, 0:256],
                             wslab[mi % 2][:, pt * 128:(pt + 1) * 128],
                             sfm_scrb[:, pt * 256:(pt + 1) * 256],
                             start=(pt == 0), stop=(pt == NTC - 1))
                mmi.then_inc(tv, 1)
                for pt in range(NTC):
                    mmi = t.matmul(psW[:, 0:256],
                             wslab[2 + mi % 2][:, pt * 128:(pt + 1) * 128],
                             sfm_scrb[:, pt * 256:(pt + 1) * 256],
                             start=(pt == 0), stop=(pt == NTC - 1))
                mmi.then_inc(tv, 1)

        @blk.scalar
        def _(s):
            for mi in range(64):
                s.wait_ge(tv, TV9 + 2 * mi + 1)
                if mi >= 1:
                    s.wait_ge(vt, VT9 + mi)
                s.activation(scrf[:, 0:256], psA[:, 0:256], AF.Silu
                             ).then_inc(sv, 1)

        @blk.vector
        def _(v):
            for mi in range(64):
                v.wait_ge(tv, TV9 + 2 * mi + 2)
                v.wait_ge(sv, SV9 + mi + 1)
                v.tensor_tensor(buf2[:, mi * 256:(mi + 1) * 256],
                                scrf[:, 0:256], psW[:, 0:256], ALU.mult
                                ).then_inc(vt, 1)
        G[0] += 32 * 16
        G[1] += 32 * 16
        G[2] += 32 * 16
        G[3] += 32 * 16
        C["tv"] += 64 * 2
        C["vt"] += 64
        C["sv"] += 64

        TV10 = C["tv"]
        VT10 = C["vt"]
        SO10 = C["so"]
        G10 = list(G)
        NP10 = 2

        def t10_base(p):
            return TV10 + p * 72

        def v10_base(p):
            return VT10 + p * 25

        def bank10(mo):
            if mo < 4:
                return psBig[:, mo * 512:mo * 512 + 256]
            if mo == 4:
                return psA[:, 0:256]
            if mo == 5:
                return psW[:, 0:256]
            return psT[:, (mo - 6) * 512:(mo - 6) * 512 + 256]

        @blk.gpsimd
        def _(g):
            g.wait_ge(tv, TV10)
            for p in range(NP10):
                for kt in range(64):
                    gi = p * 64 + kt
                    if gi >= 2:
                        gp, gk = (gi - 2) // 64, (gi - 2) % 64
                        g.wait_ge(tv, TV10 + gp * 72 + gk + 1)
                    g.dma_start(out=wslab[gi % 2][:], in_=wd_d[kt]
                                ).then_inc(gsl[gi % 2], 16)

        @blk.tensor
        def _(t):
            t.wait_ge(vt, VT10)
            for p in range(NP10):
                if p >= 1:
                    t.wait_ge(vt, v10_base(p - 1) + 24)
                for kt in range(64):
                    gi = p * 64 + kt
                    t.wait_ge(gsl[gi % 2], G10[gi % 2] + (gi // 2 + 1) * 16)
                    for mo in range(8):
                        mm_ = p * 8 + mo
                        mmi = t.matmul(bank10(mo),
                                       wslab[gi % 2][:, mm_ * 128:(mm_ + 1) * 128],
                                       buf2[:, kt * 256:(kt + 1) * 256],
                                       start=(kt == 0), stop=(kt == 63))
                    mmi.then_inc(tv, 1)
                for mo in range(8):
                    t.wait_ge(vt, v10_base(p) + 3 * mo + 1)
                    bk = bank10(mo)
                    t.transpose(bk[:, 0:128], scrf2[:, 0:128], idnf[:])
                    t.transpose(bk[:, 128:256], scrf2[:, 128:256], idnf[:]
                                ).then_inc(tv, 1)

        @blk.vector
        def _(v):
            for p in range(NP10):
                v.wait_ge(tv, t10_base(p) + 64)
                if p >= 1:
                    v.wait_ge(so, SO10 + p * 64)
                for mo in range(8):
                    mm_ = p * 8 + mo
                    VE(v, v.scalar_tensor_tensor(
                        scrf2[:, 0:256],
                        bank10(mo), 1.0,
                        x1[:, mm_ * 256:(mm_ + 1) * 256], ALU.mult, ALU.add))
                    v.sem_inc(vt, 1)
                    if mo == 4:
                        v.wait_ge(so, SO10 + p * 64 + 32)
                    v.wait_ge(tv, t10_base(p) + 64 + mo + 1)
                    VE(v, v.tensor_copy(
                        pstage[:, (mo % 4) * 128:(mo % 4 + 1) * 128],
                        bank10(mo)[:, 0:128]))
                    VE(v, v.tensor_copy(
                        pstage[:, 512 + (mo % 4) * 128:512 + (mo % 4 + 1) * 128],
                        bank10(mo)[:, 128:256]))
                    v.sem_inc(vt, 2)
                v.sem_inc(vt, 1)

        @blk.sync
        def _(sy):
            for p in range(NP10):
                for hf in range(2):
                    sy.wait_ge(vt, v10_base(p) + 12 * (hf + 1))
                    off = p * 1024 + hf * 512
                    sy.dma_start(out=proc_d[0][:, off:off + 512],
                                 in_=pstage[:, 0:512]).then_inc(so, 16)
                    sy.dma_start(out=proc_d[1][:, off:off + 512],
                                 in_=pstage[:, 512:1024]).then_inc(so, 16)
            sy.wait_ge(so, SO10 + NP10 * 64)
        G[0] += 64 * 16
        G[1] += 64 * 16
        C["tv"] += NP10 * 72
        C["vt"] += NP10 * 25
        C["so"] += NP10 * 64
    return nc


CHT = 4
NCH = NTC // CHT


def build_p4():
    nc = bass.Bass("TRN2", target_bir_lowering=False)
    x_d = nc.dram_tensor("x", [TC, D], F32, kind="ExternalInput").ap()
    procf = nc.dram_tensor("procf", [SSEL, D], F32, kind="ExternalInput").ap()
    cidx_d = nc.dram_tensor("cidx_own", [128, NTC], F32, kind="ExternalInput").ap()
    msc_d = nc.dram_tensor("myscores", [128, NTC], F32, kind="ExternalInput").ap()
    wrt_d = nc.dram_tensor("wrt_fm", [128, NTC], F32, kind="ExternalInput").ap()
    wi_d = nc.dram_tensor("wi_l", [64, 128, 16 * 128], BF16, kind="ExternalInput").ap()
    cv_d = nc.dram_tensor("cv_l", [32, 128, 32 * 128], BF16, kind="ExternalInput").ap()
    wo2_d = nc.dram_tensor("wo2_l", [16, 128, 32 * 128], BF16, kind="ExternalInput").ap()
    dcol_d = nc.dram_tensor("dcol", [128, 32], F32, kind="ExternalInput").ap()
    xo_d = nc.dram_tensor("xo_fm", [NTC, 128, TC], F32, kind="ExternalOutput").ap()
    sc2_d = nc.dram_tensor("sc2", [1, TC], F32, kind="ExternalOutput").ap()
    lnx2_d = nc.dram_tensor("lnx2_fm", [NTC, 128, TC], BF16, kind="ExternalOutput").ap()
    xpre_d = nc.dram_tensor("xpre_fm", [128, NTC, TC], F32, kind="Internal").ap()

    idn_c = nc.inline_tensor(np.eye(128, dtype=bf16), name="idn4")
    idnf_c = nc.inline_tensor(np.eye(128, dtype=np.float32), name="idnf4")
    onescf_c = nc.inline_tensor(np.ones((128, 1), np.float32), name="onescf4")
    onesrf_c = nc.inline_tensor(np.ones((1, 128), np.float32), name="onesrf4")

    es = ExitStack()
    with es:
        lnfm = es.enter_context(nc.sbuf_tensor("lnfm", [128, 16 * 512], BF16))
        xcv = es.enter_context(nc.sbuf_tensor("xcv", [128, 32 * 512], BF16))
        slz = es.enter_context(nc.sbuf_tensor("slz", [128, 32 * 512], BF16))
        xt = [es.enter_context(nc.sbuf_tensor(f"xt4{i}", [128, D], F32))
              for i in range(2)]
        stage = es.enter_context(nc.sbuf_tensor("stage", [128, D], F32))
        lnf = es.enter_context(nc.sbuf_tensor("lnf", [128, D], F32))
        lnmb = es.enter_context(nc.sbuf_tensor("lnmb", [128, D], BF16))
        xps = es.enter_context(nc.sbuf_tensor("xps", [128, D], F32))
        wsl = [es.enter_context(nc.sbuf_tensor(f"wsl4{i}", [128, 4096], BF16))
               for i in range(2)]
        xrb = [es.enter_context(nc.sbuf_tensor(f"xrb{i}", [128, 512], F32))
               for i in range(2)]
        xog = es.enter_context(nc.sbuf_tensor("xog", [128, 16 * 512], F32))
        sqb = es.enter_context(nc.sbuf_tensor("sqb", [128, 1024], F32))
        scr5 = es.enter_context(nc.sbuf_tensor("scr5", [128, 1024], F32))
        scR = es.enter_context(nc.sbuf_tensor("scR", [1, TC], F32))
        cidxs = es.enter_context(nc.sbuf_tensor("cidxs", [128, NTC], F32))
        cidxi = es.enter_context(nc.sbuf_tensor("cidxi", [128, 1], I32))
        mscs = es.enter_context(nc.sbuf_tensor("mscs", [128, NTC], F32))
        wrts = es.enter_context(nc.sbuf_tensor("wrts", [128, NTC], F32))
        dcols = es.enter_context(nc.sbuf_tensor("dcols", [128, 32], F32))
        idn = es.enter_context(nc.sbuf_tensor("idn4s", [128, 128], BF16))
        idnf = es.enter_context(nc.sbuf_tensor("idnf4s", [128, 128], F32))
        onescf = es.enter_context(nc.sbuf_tensor("onescf4s", [128, 1], F32))
        onesrf = es.enter_context(nc.sbuf_tensor("onesrf4s", [1, 128], F32))
        ssq = es.enter_context(nc.sbuf_tensor("ssq4", [128, 1], F32))
        sx = es.enter_context(nc.sbuf_tensor("sx4", [128, 1], F32))
        m_ = es.enter_context(nc.sbuf_tensor("m4", [128, 1], F32))
        var = es.enter_context(nc.sbuf_tensor("var4", [128, 1], F32))
        sd = es.enter_context(nc.sbuf_tensor("sd4", [128, 1], F32))
        rstd = es.enter_context(nc.sbuf_tensor("rstd4", [128, 1], F32))
        nb = es.enter_context(nc.sbuf_tensor("nb4", [128, 1], F32))
        t0 = es.enter_context(nc.sbuf_tensor("t04", [128, 1], F32))
        g8 = es.enter_context(nc.sbuf_tensor("g84", [128, 1], F32))
        coef = es.enter_context(nc.sbuf_tensor("coef4", [128, 1], F32))
        w1 = es.enter_context(nc.sbuf_tensor("w14", [1, 1], F32))
        mrow = es.enter_context(nc.sbuf_tensor("mrow", [1, 512], F32))
        vrow = es.enter_context(nc.sbuf_tensor("vrow", [1, 512], F32))
        rrow = es.enter_context(nc.sbuf_tensor("rrow", [1, 512], F32))
        srow = es.enter_context(nc.sbuf_tensor("srow", [1, 512], F32))

        psA = es.enter_context(nc.psum_tensor("psA4", [128, 512], F32))
        psB = es.enter_context(nc.psum_tensor("psB4", [128, 512], F32))
        psT = es.enter_context(nc.psum_tensor("psT4", [128, 2048], BF16))
        psS = es.enter_context(nc.psum_tensor("psS4", [1, 512], F32))
        psQ = es.enter_context(nc.psum_tensor("psQ4", [1, 512], F32))
        psW2 = es.enter_context(nc.psum_tensor("psW24", [1, 512], F32))
        psM = psA
        psR = psB

        def psTs4(i):
            return psT[:, (i % 2) * 1024:(i % 2) * 1024 + 128]

        ld = es.enter_context(nc.semaphore("ld4"))
        gw = es.enter_context(nc.semaphore("gw4"))
        tv = es.enter_context(nc.semaphore("tv4"))
        vt = es.enter_context(nc.semaphore("vt4"))
        sv = es.enter_context(nc.semaphore("sv4"))
        vself = es.enter_context(nc.semaphore("vself4"))
        so = es.enter_context(nc.semaphore("so4"))
        lx = es.enter_context(nc.semaphore("lx4"))
        lr = es.enter_context(nc.semaphore("lr4"))
        sxo = es.enter_context(nc.semaphore("sxo4"))
        sst = es.enter_context(nc.semaphore("sst4"))
        gsl = [es.enter_context(nc.semaphore(f"gsl4{i}")) for i in range(2)]
        blk = es.enter_context(nc.Block())

        C = dict(ld=0, gw=0, tv=0, vt=0, sv=0, so=0, lx=0, lr=0, sxo=0, sst=0)
        G4 = [0, 0]
        ve_n = [0]

        def VE(v, inst):
            ve_n[0] += 1
            inst.then_inc(vself, 1)
            v.wait_ge(vself, ve_n[0])

        @blk.sync
        def _(sy):
            sy.dma_start(out=cidxs[:], in_=cidx_d).then_inc(ld, 16)
            sy.dma_start(out=mscs[:], in_=msc_d).then_inc(ld, 16)
            sy.dma_start(out=wrts[:], in_=wrt_d).then_inc(ld, 16)
            sy.dma_start(out=dcols[:], in_=dcol_d).then_inc(ld, 16)
            sy.dma_start(out=idn[:], in_=idn_c.ap()).then_inc(ld, 16)
            sy.dma_start(out=idnf[:], in_=idnf_c.ap()).then_inc(ld, 16)
            sy.dma_start(out=onescf[:], in_=onescf_c.ap()).then_inc(ld, 16)
            sy.dma_start(out=onesrf[:], in_=onesrf_c.ap()).then_inc(ld, 16)
        C["ld"] += 8 * 16
        LD0 = C["ld"]

        @blk.tensor
        def _(t):
            t.wait_ge(ld, LD0)
            for pt in range(NTC):
                mmi = t.matmul(psS[0:1, 0:1], wrts[:, pt:pt + 1], onescf[:],
                               start=(pt == 0), stop=(pt == NTC - 1))
            mmi.then_inc(tv, 1)
        C["tv"] += 1
        TV_W1 = C["tv"]

        @blk.vector
        def _(v):
            v.wait_ge(ld, LD0)
            v.memset(stage[:], 0.0).then_inc(vt, 1)
            v.wait_ge(tv, TV_W1)
            v.tensor_copy(w1[:], psS[0:1, 0:1]).then_inc(vt, 1)
        C["vt"] += 2

        tile_tv_base = {}

        def gincw(k):
            return 1 if k == 0 else (2 if k == 1 else 2 * k)

        for chx in range(NCH):
            SSTC = C["sst"]
            for jj in range(CHT):
                j = chx * CHT + jj
                LXJ = C["lx"]
                GWJ = C["gw"]
                VTJ = C["vt"]
                SVJ = C["sv"]
                TVJ = C["tv"]
                tile_tv_base[j] = TVJ

                @blk.sync
                def _(sy, j=j, SVJ=SVJ):
                    if j >= 2:
                        sy.wait_ge(tv, tile_tv_base[j - 2] + 32)
                    sy.dma_start(out=xt[j % 2][:],
                                 in_=x_d[j * 128:(j + 1) * 128, :]
                                 ).then_inc(lx, 16)
                    sy.wait_ge(sv, SVJ + 41)
                    sy.dma_start(
                        out=xpre_d[:, :, j * 128:(j + 1) * 128],
                        in_=xps[:].rearrange("p (f q) -> p f q", f=NTC),
                    ).then_inc(so, 16)
                C["lx"] += 16
                C["so"] += 16

                @blk.vector
                def _(v, j=j, LXJ=LXJ, GWJ=GWJ, VTJ=VTJ, SVJ=SVJ):
                    xb = xt[j % 2]
                    v.wait_ge(sv, SVJ + 2)
                    VE(v, v.tensor_scalar_mul(m_[:], sx[:], 1.0 / D))
                    VE(v, v.tensor_scalar_mul(var[:], ssq[:], 1.0 / D))
                    VE(v, v.tensor_mul(t0[:], m_[:], m_[:]))
                    VE(v, v.tensor_sub(var[:], var[:], t0[:]))
                    v.tensor_scalar_add(var[:], var[:], EPS_LN
                                        ).then_inc(vt, 1)
                    v.wait_ge(sv, SVJ + 3)
                    VE(v, v.reciprocal(rstd[:], sd[:]))
                    VE(v, v.tensor_mul(nb[:], m_[:], rstd[:]))
                    VE(v, v.tensor_scalar_mul(nb[:], nb[:], -1.0))
                    VE(v, v.tensor_scalar(t0[:], cidxs[:, j:j + 1], 4096.0,
                                          None, ALU.min))
                    v.tensor_copy(cidxi[:], t0[:]).then_inc(vt, 1)
                    VE(v, v.tensor_scalar(coef[:], cidxs[:, j:j + 1], 2048.0,
                                          None, ALU.is_lt))
                    v.wait_ge(sv, SVJ + 4)
                    VE(v, v.tensor_mul(coef[:], coef[:], g8[:]))
                    VE(v, v.tensor_scalar_mul(coef[:], coef[:], 0.5))
                    v.wait_ge(gw, GWJ + 16)
                    v.wait_ge(sv, SVJ + 5)
                    VE(v, v.tensor_sub(stage[:], stage[:], lnf[:]))
                    VE(v, v.tensor_scalar(stage[:], stage[:], coef[:], None,
                                          ALU.mult))
                    VE(v, v.scalar_tensor_tensor(stage[:], lnf[:], 0.5,
                                                 stage[:], ALU.mult, ALU.add))
                    VE(v, v.tensor_add(xb[:], xb[:], stage[:]))
                    v.sem_inc(vt, 1)
                    v.wait_ge(sv, SVJ + 7)
                    VE(v, v.tensor_scalar_mul(m_[:], sx[:], 1.0 / D))
                    VE(v, v.tensor_scalar_mul(var[:], ssq[:], 1.0 / D))
                    VE(v, v.tensor_mul(t0[:], m_[:], m_[:]))
                    VE(v, v.tensor_sub(var[:], var[:], t0[:]))
                    v.tensor_scalar_add(var[:], var[:], EPS_LN
                                        ).then_inc(vt, 1)
                    v.wait_ge(sv, SVJ + 8)
                    VE(v, v.reciprocal(rstd[:], sd[:]))
                    VE(v, v.tensor_mul(nb[:], m_[:], rstd[:]))
                    VE(v, v.tensor_scalar_mul(nb[:], nb[:], -1.0))
                    v.sem_inc(vt, 1)
                    v.sem_inc(vt, 1)
                C["vt"] += 6

                @blk.scalar
                def _(s, j=j, jj=jj, LXJ=LXJ, VTJ=VTJ, SVJ=SVJ, TVJ=TVJ,
                      SSTC=SSTC):
                    if jj == 0:
                        s.wait_ge(sst, SSTC)
                    xb = xt[j % 2]
                    s.wait_ge(lx, LXJ + 16)
                    s.activation(lnf[:], xb[:], AF.Square,
                                 accum_out=ssq[:]).then_inc(sv, 1)
                    s.wait_ge(sv, SVJ + 1)
                    s.activation(lnf[:], xb[:], AF.Identity,
                                 accum_out=sx[:]).then_inc(sv, 1)
                    s.wait_ge(vt, VTJ + 1)
                    s.sqrt(sd[:], var[:]).then_inc(sv, 1)
                    s.activation(g8[:], mscs[:, j:j + 1], AF.Sigmoid
                                 ).then_inc(sv, 1)
                    s.wait_ge(vt, VTJ + 2)
                    s.activation(lnf[:], xb[:], AF.Identity,
                                 bias=nb[:], scale=rstd[:]).then_inc(sv, 1)
                    s.wait_ge(vt, VTJ + 3)
                    s.activation(stage[:], xb[:], AF.Square,
                                 accum_out=ssq[:]).then_inc(sv, 1)
                    s.wait_ge(sv, SVJ + 6)
                    s.activation(stage[:], xb[:], AF.Identity,
                                 accum_out=sx[:]).then_inc(sv, 1)
                    s.wait_ge(vt, VTJ + 4)
                    s.sqrt(sd[:], var[:]).then_inc(sv, 1)
                    s.wait_ge(vt, VTJ + 5)
                    s.activation(lnmb[:], xb[:], AF.Identity,
                                 bias=nb[:], scale=rstd[:]).then_inc(sv, 1)
                    for pt in range(NTC):
                        s.wait_ge(tv, TVJ + pt + 1)
                        s.copy(lnfm[:, pt * 512 + jj * 128:
                                     pt * 512 + (jj + 1) * 128],
                               psTs4(pt)).then_inc(sv, 1)
                    if j >= 1:
                        s.wait_ge(so, 16 * j)
                    for pt in range(NTC):
                        s.wait_ge(tv, TVJ + 16 + pt + 1)
                        ps = psA if pt % 2 == 0 else psB
                        s.copy(xps[:, pt * 128:(pt + 1) * 128],
                               ps[:, 0:128]).then_inc(sv, 1)
                C["sv"] += 41

                @blk.gpsimd
                def _(g, VTJ=VTJ):
                    g.wait_ge(vt, VTJ + 2)
                    g.indirect_dma_start(
                        stage[:], None, procf,
                        IndirectOffsetOnAxis(cidxi[:], 0),
                        bounds_check=SSEL - 1, oob_is_err=False,
                    ).then_inc(gw, 16)
                C["gw"] += 16

                @blk.tensor
                def _(t, j=j, VTJ=VTJ, SVJ=SVJ):
                    t.wait_ge(sv, SVJ + 9)
                    for pt in range(NTC):
                        if pt >= 2:
                            t.wait_ge(sv, SVJ + pt + 8)
                        t.transpose(psTs4(pt),
                                    lnmb[:, pt * 128:(pt + 1) * 128], idn[:]
                                    ).then_inc(tv, 1)
                    t.wait_ge(vt, VTJ + 3)
                    for pt in range(NTC):
                        ps = psA if pt % 2 == 0 else psB
                        if pt >= 2:
                            t.wait_ge(sv, SVJ + pt + 24)
                        t.transpose(ps[:, 0:128],
                                    xt[j % 2][:, pt * 128:(pt + 1) * 128],
                                    idnf[:]).then_inc(tv, 1)
                C["tv"] += 2 * NTC

            def lnslice(kt):
                return lnfm[:, kt * 512:(kt + 1) * 512]

            TVX = C["tv"]
            SVXz = C["sv"]
            GX = list(G4)

            @blk.gpsimd
            def _(g, GX=GX, TVX=TVX):
                g.wait_ge(tv, TVX)
                for m in range(64):
                    if m >= 2:
                        g.wait_ge(tv, TVX + m - 1)
                    g.dma_start(out=wsl[m % 2][:, 0:2048], in_=wi_d[m]
                                ).then_inc(gsl[m % 2], 16)

            @blk.tensor
            def _(t, GX=GX, TVX=TVX, SVXz=SVXz):
                t.wait_ge(sv, SVXz)
                for m in range(64):
                    t.wait_ge(gsl[m % 2], GX[m % 2] + (m // 2 + 1) * 16)
                    if m >= 2:
                        t.wait_ge(sv, SVXz + m - 1)
                    ps = psA if m % 2 == 0 else psB
                    for kt in range(NTC):
                        mmi = t.matmul(ps[:],
                                       wsl[m % 2][:, kt * 128:(kt + 1) * 128],
                                       lnslice(kt), start=(kt == 0),
                                       stop=(kt == NTC - 1))
                    mmi.then_inc(tv, 1)

            @blk.scalar
            def _(s, TVX=TVX):
                for m in range(64):
                    s.wait_ge(tv, TVX + m + 1)
                    ps = psA if m % 2 == 0 else psB
                    if m < 32:
                        s.copy(xcv[:, m * 512:(m + 1) * 512], ps[:]
                               ).then_inc(sv, 1)
                    else:
                        s.activation(slz[:, (m - 32) * 512:(m - 31) * 512],
                                     ps[:], AF.Silu).then_inc(sv, 1)
            G4[0] += 32 * 16
            G4[1] += 32 * 16
            C["tv"] += 64
            C["sv"] += 64

            TVC = C["tv"]
            SVC = C["sv"]
            VTC = C["vt"]
            GC = list(G4)

            @blk.gpsimd
            def _(g, GC=GC, TVC=TVC):
                g.wait_ge(tv, TVC)
                for m in range(32):
                    if m >= 2:
                        g.wait_ge(tv, TVC + m - 1)
                    g.dma_start(out=wsl[m % 2][:], in_=cv_d[m]
                                ).then_inc(gsl[m % 2], 16)

            @blk.tensor
            def _(t, GC=GC, TVC=TVC, SVC=SVC):
                t.wait_ge(sv, SVC)
                for m in range(32):
                    t.wait_ge(gsl[m % 2], GC[m % 2] + (m // 2 + 1) * 16)
                    if m >= 2:
                        t.wait_ge(sv, SVC + m - 1)
                    ps = psA if m % 2 == 0 else psB
                    for kt in range(32):
                        mmi = t.matmul(ps[:],
                                       wsl[m % 2][:, kt * 128:(kt + 1) * 128],
                                       xcv[:, kt * 512:(kt + 1) * 512],
                                       start=(kt == 0), stop=(kt == 31))
                    mmi.then_inc(tv, 1)

            @blk.scalar
            def _(s, TVC=TVC, SVC=SVC, VTC=VTC):
                for m in range(32):
                    s.wait_ge(tv, TVC + m + 1)
                    if m >= 2:
                        s.wait_ge(vt, VTC + m - 1)
                    ps = psA if m % 2 == 0 else psB
                    s.mul(scr5[:, (m % 2) * 512:(m % 2) * 512 + 512], ps[:],
                          dcols[:, m:m + 1]).then_inc(sv, 1)

            @blk.vector
            def _(v, SVC=SVC):
                for m in range(32):
                    v.wait_ge(sv, SVC + m + 1)
                    VE(v, v.tensor_tensor(slz[:, m * 512:(m + 1) * 512],
                                          scr5[:, (m % 2) * 512:
                                               (m % 2) * 512 + 512],
                                          slz[:, m * 512:(m + 1) * 512],
                                          ALU.mult))
                    v.sem_inc(vt, 1)
            G4[0] += 16 * 16
            G4[1] += 16 * 16
            C["tv"] += 32
            C["sv"] += 32
            C["vt"] += 32

            TVW = C["tv"]
            VTW = C["vt"]
            SVW = C["sv"]
            LRW = C["lr"]
            SXOC = C["sxo"]
            GW2 = list(G4)

            @blk.gpsimd
            def _(g, GW2=GW2, TVW=TVW):
                g.wait_ge(tv, TVW)
                for m in range(16):
                    if m >= 2:
                        g.wait_ge(tv, TVW + gincw(m - 2))
                    g.dma_start(out=wsl[m % 2][:], in_=wo2_d[m]
                                ).then_inc(gsl[m % 2], 16)

            @blk.sync
            def _(sy, chx=chx, LRW=LRW, VTW=VTW):
                sy.wait_ge(so, (chx + 1) * CHT * 16)
                for m in range(16):
                    if m >= 2:
                        sy.wait_ge(vt, VTW + m - 1)
                    sy.dma_start(out=xrb[m % 2][:],
                                 in_=xpre_d[:, m, chx * 512:(chx + 1) * 512]
                                 ).then_inc(lr, 16)
                for m in range(16):
                    sy.wait_ge(vt, VTW + m + 1)
                    sy.dma_start(out=xo_d[m][:, chx * 512:(chx + 1) * 512],
                                 in_=xog[:, m * 512:(m + 1) * 512]
                                 ).then_inc(sxo, 16)
                for pt in range(NTC):
                    sy.wait_ge(vt, VTW + 18 + pt + 1)
                    sy.dma_start(out=lnx2_d[pt][:, chx * 512:(chx + 1) * 512],
                                 in_=lnfm[:, pt * 512:(pt + 1) * 512]
                                 ).then_inc(sst, 16)

            @blk.tensor
            def _(t, GW2=GW2, TVW=TVW, VTW=VTW, SVW=SVW):
                t.wait_ge(vt, VTW)

                def grp(m):
                    t.wait_ge(gsl[m % 2], GW2[m % 2] + (m // 2 + 1) * 16)
                    if m >= 2:
                        t.wait_ge(vt, VTW + m - 1)
                    ps = psA if m % 2 == 0 else psB
                    for kt in range(32):
                        mmi = t.matmul(ps[:],
                                       wsl[m % 2][:, kt * 128:(kt + 1) * 128],
                                       slz[:, kt * 512:(kt + 1) * 512],
                                       start=(kt == 0), stop=(kt == 31))
                    mmi.then_inc(tv, 1)

                def stats(k):
                    t.wait_ge(vt, VTW + k + 1)
                    t.wait_ge(sv, SVW + k + 1)
                    xsl = xog[:, k * 512:(k + 1) * 512]
                    t.matmul(psS[:], onescf[:], xsl, start=(k == 0),
                             stop=(k == 15))
                    t.matmul(psQ[:], onescf[:],
                             sqb[:, (k % 2) * 512:(k % 2) * 512 + 512],
                             start=(k == 0), stop=(k == 15))
                    t.matmul(psW2[:], wrts[:, k:k + 1], xsl,
                             start=(k == 0), stop=(k == 15)).then_inc(tv, 1)

                grp(0)
                grp(1)
                for m in range(2, 16):
                    stats(m - 2)
                    grp(m)
                stats(14)
                stats(15)
                t.wait_ge(vt, VTW + 18)
                t.matmul(psM[:], onesrf[:], mrow[:], start=True, stop=True)
                t.matmul(psR[:], onesrf[:], rrow[:], start=True, stop=True
                         ).then_inc(tv, 1)

            @blk.vector
            def _(v, chx=chx, TVW=TVW, VTW=VTW, SVW=SVW, LRW=LRW, SXOC=SXOC):
                for m in range(16):
                    v.wait_ge(tv, TVW + gincw(m))
                    v.wait_ge(lr, LRW + (m // 2 + 1) * 32)
                    if m == 0:
                        v.wait_ge(sxo, SXOC)
                    ps = psA if m % 2 == 0 else psB
                    v.scalar_tensor_tensor(xog[:, m * 512:(m + 1) * 512],
                                           ps[:], 1.0, xrb[m % 2][:],
                                           ALU.mult, ALU.add).then_inc(vt, 1)
                v.wait_ge(tv, TVW + 32)
                VE(v, v.tensor_scalar_mul(mrow[:], psS[0:1, :], 1.0 / D))
                VE(v, v.tensor_scalar_mul(vrow[:], psQ[0:1, :], 1.0 / D))
                VE(v, v.tensor_mul(srow[:], mrow[:], mrow[:]))
                VE(v, v.tensor_sub(vrow[:], vrow[:], srow[:]))
                v.tensor_scalar_add(vrow[:], vrow[:], EPS_LN
                                    ).then_inc(vt, 1)
                v.wait_ge(sv, SVW + 17)
                VE(v, v.reciprocal(rrow[:], rrow[:]))
                VE(v, v.tensor_scalar(srow[:], mrow[:], w1[0:1, 0:1], None,
                                      ALU.mult))
                VE(v, v.tensor_tensor(srow[:], psW2[0:1, :], srow[:],
                                      ALU.subtract))
                VE(v, v.tensor_mul(scR[:, chx * 512:(chx + 1) * 512],
                                   srow[:], rrow[:]))
                v.sem_inc(vt, 1)
                v.wait_ge(tv, TVW + 33)
                for pt in range(NTC):
                    VE(v, v.tensor_tensor(scr5[:, 0:512],
                                          xog[:, pt * 512:(pt + 1) * 512],
                                          psM[:], ALU.subtract))
                    VE(v, v.tensor_tensor(lnfm[:, pt * 512:(pt + 1) * 512],
                                          scr5[:, 0:512], psR[:], ALU.mult))
                    v.sem_inc(vt, 1)

            @blk.scalar
            def _(s, TVW=TVW, VTW=VTW):
                for m in range(16):
                    s.wait_ge(vt, VTW + m + 1)
                    if m >= 2:
                        s.wait_ge(tv, TVW + 2 * m - 1)
                    s.square(sqb[:, (m % 2) * 512:(m % 2) * 512 + 512],
                             xog[:, m * 512:(m + 1) * 512]).then_inc(sv, 1)
                s.wait_ge(vt, VTW + 17)
                s.sqrt(rrow[:], vrow[:]).then_inc(sv, 1)
            G4[0] += 8 * 16
            G4[1] += 8 * 16
            C["tv"] += 33
            C["vt"] += 16 + 2 + 16
            C["sv"] += 17
            C["lr"] += 16 * 16
            C["sxo"] += 16 * 16
            C["sst"] += 16 * 16

        VT_F = C["vt"]

        @blk.sync
        def _(sy):
            sy.wait_ge(vt, VT_F)
            sy.dma_start(out=sc2_d, in_=scR[:]).then_inc(sst, 16)
            sy.wait_ge(sst, C["sst"] + 16)
            sy.wait_ge(sxo, C["sxo"])
            sy.wait_ge(so, C["so"])
        C["sst"] += 16
    return nc



def build_p7():
    nc = bass.Bass("TRN2", target_bir_lowering=False)
    x_d = nc.dram_tensor("x", [TC, D], F32, kind="ExternalInput").ap()
    procf = nc.dram_tensor("procf", [SSEL, D], F32, kind="ExternalInput").ap()
    cidx_d = nc.dram_tensor("cidx_own", [128, NTC], F32, kind="ExternalInput").ap()
    msc_d = nc.dram_tensor("myscores", [128, NTC], F32, kind="ExternalInput").ap()
    out_d = nc.dram_tensor("out", [TC, D], F32, kind="ExternalOutput").ap()

    es = ExitStack()
    with es:
        xt = [es.enter_context(nc.sbuf_tensor(f"xt7{i}", [128, D], F32))
              for i in range(2)]
        ot = [es.enter_context(nc.sbuf_tensor(f"ot7{i}", [128, D], F32))
              for i in range(2)]
        stage = es.enter_context(nc.sbuf_tensor("stage7", [128, D], F32))
        lnf = es.enter_context(nc.sbuf_tensor("lnf7", [128, D], F32))
        cidxs = es.enter_context(nc.sbuf_tensor("cidxs7", [128, NTC], F32))
        cidxi = es.enter_context(nc.sbuf_tensor("cidxi7", [128, 1], I32))
        mscs = es.enter_context(nc.sbuf_tensor("mscs7", [128, NTC], F32))
        ssq = es.enter_context(nc.sbuf_tensor("ssq7", [128, 1], F32))
        sx = es.enter_context(nc.sbuf_tensor("sx7", [128, 1], F32))
        m_ = es.enter_context(nc.sbuf_tensor("m7", [128, 1], F32))
        var = es.enter_context(nc.sbuf_tensor("var7", [128, 1], F32))
        sd = es.enter_context(nc.sbuf_tensor("sd7", [128, 1], F32))
        rstd = es.enter_context(nc.sbuf_tensor("rstd7", [128, 1], F32))
        nb = es.enter_context(nc.sbuf_tensor("nb7", [128, 1], F32))
        t0 = es.enter_context(nc.sbuf_tensor("t07", [128, 1], F32))
        g8 = es.enter_context(nc.sbuf_tensor("g87", [128, 1], F32))
        coef = es.enter_context(nc.sbuf_tensor("coef7", [128, 1], F32))

        ld = es.enter_context(nc.semaphore("ld7"))
        lxs = [es.enter_context(nc.semaphore(f"lxs7{i}")) for i in range(2)]
        gw = es.enter_context(nc.semaphore("gw7"))
        vt = es.enter_context(nc.semaphore("vt7"))
        sv = es.enter_context(nc.semaphore("sv7"))
        vself = es.enter_context(nc.semaphore("vself7"))
        sol = [es.enter_context(nc.semaphore(f"sol7{i}")) for i in range(2)]
        blk = es.enter_context(nc.Block())

        ve_n = [0]

        def VE(v, inst):
            ve_n[0] += 1
            inst.then_inc(vself, 1)
            v.wait_ge(vself, ve_n[0])

        @blk.sync
        def _(sy):
            sy.dma_start(out=cidxs[:], in_=cidx_d).then_inc(ld, 16)
            sy.dma_start(out=mscs[:], in_=msc_d).then_inc(ld, 16)
            for j in range(NTC):
                if j >= 2:
                    sy.wait_ge(vt, 5 * (j - 1))
                sy.dma_start(out=xt[j % 2][:],
                             in_=x_d[j * 128:(j + 1) * 128, :]
                             ).then_inc(lxs[j % 2], 16)

        @blk.vector
        def _(v):
            v.wait_ge(ld, 32)
            VE(v, v.memset(stage[:], 0.0))
            for j in range(NTC):
                xb = xt[j % 2]
                v.wait_ge(lxs[j % 2], (j // 2 + 1) * 16)
                VE(v, v.scalar_tensor_tensor(lnf[:], xb[:], 1.0, xb[:],
                                             ALU.mult, ALU.mult,
                                             accum_out=ssq[:]))
                VE(v, v.reduce_sum(sx[:], xb[:], axis=AX.X))
                VE(v, v.tensor_scalar_mul(m_[:], sx[:], 1.0 / D))
                VE(v, v.tensor_scalar_mul(var[:], ssq[:], 1.0 / D))
                VE(v, v.tensor_mul(t0[:], m_[:], m_[:]))
                VE(v, v.tensor_sub(var[:], var[:], t0[:]))
                v.tensor_scalar_add(var[:], var[:], EPS_LN).then_inc(vt, 1)
                v.wait_ge(sv, 2 * j + 1)
                VE(v, v.reciprocal(rstd[:], sd[:]))
                VE(v, v.tensor_mul(nb[:], m_[:], rstd[:]))
                VE(v, v.tensor_scalar_mul(nb[:], nb[:], -1.0))
                VE(v, v.tensor_scalar(lnf[:], xb[:], rstd[:], nb[:],
                                      ALU.mult, ALU.add))
                VE(v, v.tensor_scalar(t0[:], cidxs[:, j:j + 1], 4096.0,
                                      None, ALU.min))
                v.tensor_copy(cidxi[:], t0[:]).then_inc(vt, 1)
                VE(v, v.tensor_scalar(coef[:], cidxs[:, j:j + 1], 2048.0,
                                      None, ALU.is_lt))
                v.wait_ge(sv, 2 * j + 2)
                VE(v, v.tensor_mul(coef[:], coef[:], g8[:]))
                VE(v, v.tensor_scalar_mul(coef[:], coef[:], 0.5))
                v.wait_ge(gw, (j + 1) * 16)
                VE(v, v.tensor_sub(stage[:], stage[:], lnf[:]))
                VE(v, v.tensor_scalar(stage[:], stage[:], coef[:], None,
                                      ALU.mult))
                VE(v, v.scalar_tensor_tensor(stage[:], lnf[:], 0.5, stage[:],
                                             ALU.mult, ALU.add))
                v.wait_ge(sol[j % 2], (j // 2) * 16)
                VE(v, v.tensor_tensor(ot[j % 2][:], xb[:], stage[:], ALU.add))
                v.sem_inc(vt, 1)
                v.memset(stage[:], 0.0).then_inc(vt, 2)

        @blk.scalar
        def _(s):
            for j in range(NTC):
                s.wait_ge(vt, 5 * j + 1)
                s.sqrt(sd[:], var[:]).then_inc(sv, 1)
                s.activation(g8[:], mscs[:, j:j + 1], AF.Sigmoid).then_inc(sv, 1)

        @blk.gpsimd
        def _(g):
            for j in range(NTC):
                g.wait_ge(vt, 5 * j + 2)
                g.indirect_dma_start(
                    stage[:], None, procf,
                    IndirectOffsetOnAxis(cidxi[:], 0),
                    bounds_check=SSEL - 1, oob_is_err=False,
                ).then_inc(gw, 16)
                g.wait_ge(vt, 5 * j + 3)
                g.dma_start(out=out_d[j * 128:(j + 1) * 128, :],
                            in_=ot[j % 2][:]).then_inc(sol[j % 2], 16)
            g.wait_ge(sol[0], 8 * 16)
            g.wait_ge(sol[1], 8 * 16)
    return nc


def _get_prog(name, builder):
    if name not in _PROG:
        _PROG[name] = builder()
    return _PROG[name]


def _launch(name, builder, in_maps, **kw):
    nc = _get_prog(name, builder)
    res = run_bass_kernel_spmd(nc, in_maps, core_ids=list(range(NC)), **kw)
    return res.results



_WSPEC = [("wq", 16, 2048), ("wk", 8, 2048), ("wv", 8, 2048),
          ("wo", 16, 2048), ("wg", 64, 2048), ("wu", 64, 2048),
          ("wd", 64, 2048), ("wi", 64, 2048), ("cv", 32, 4096),
          ("wo2", 16, 4096)]


def _wcat_build(tensors):
    return np.ascontiguousarray(np.concatenate(
        [t.transpose(1, 0, 2).reshape(128, -1) for t in tensors], axis=1))


def _wcat_split(wcb):
    out = {}
    off = 0
    for name, m, k in _WSPEC:
        w = wcb[:, off:off + m * k]
        off += m * k
        out[name] = np.ascontiguousarray(
            w.reshape(128, m, k).transpose(1, 0, 2))
    return out


def _slabs_out(w):
    K, M = w.shape
    return _np_f32(w.reshape(K // 128, 128, M // 128, 128
                             ).transpose(2, 1, 0, 3).reshape(M // 128, 128, K))


def _make_costab():
    inv_freq = 1.0 / (10000.0 ** (np.arange(0, HD, 2, dtype=np.float64) / HD))
    ang = np.arange(SEQ, dtype=np.float64)[:, None] * inv_freq[None, :]
    cos = np.cos(ang).astype(np.float32)
    sin = np.sin(ang).astype(np.float32)
    return _np_f32(np.concatenate(
        [cos, cos, sin, sin], axis=1))


def _perm_tiles(c):
    rest = [t for t in range(NTC) if t not in (c, c + 8)]
    return [c, c + 8] + rest


def _masks_for_core(c):
    pt = _perm_tiles(c)
    ktrue = np.concatenate([np.arange(t * 128, (t + 1) * 128) for t in pt])
    masks = np.zeros((2, 128, SSEL), np.float32)
    for a, tile in enumerate([c, c + 8]):
        qtrue = tile * 128 + np.arange(128)
        masks[a] = np.where(ktrue[None, :] <= qtrue[:, None], 0.0, BIG_NEG)
    return _np_f32(masks)


def _fm16(vec):
    return _np_f32(np.asarray(vec).reshape(NTC, 128).T)


def kernel(**inputs):
    xin = np.asarray(inputs["L0_attn_hidden_states"], dtype=np.float32)
    xf = np.ascontiguousarray(xin.reshape(TT, D))

    w_router = np.asarray(inputs["L0_ffn_w_router"], np.float32)
    wrt_tiled = _np_f32(np.tile(w_router[None, :], (128, 1)))
    wrt_fm = _fm16(w_router)

    wq_l = _slabs_out(np.asarray(inputs["L0_ffn_wq"], np.float32))
    wk_l = _slabs_out(np.asarray(inputs["L0_ffn_wk"], np.float32))
    wv_l = _slabs_out(np.asarray(inputs["L0_ffn_wv"], np.float32))
    wo_l = _slabs_out(np.asarray(inputs["L0_ffn_wo"], np.float32))
    wg_l = _slabs_out(np.asarray(inputs["L0_ffn_w_gate"], np.float32))
    wu_l = _slabs_out(np.asarray(inputs["L0_ffn_w_up"], np.float32))
    wd_l = _np_f32(np.asarray(inputs["L0_ffn_w_down"], np.float32
                              ).reshape(64, 128, D))
    bq_fm = _fm16(inputs["L0_ffn_bq"])
    bk_fm = _np_f32(np.asarray(inputs["L0_ffn_bk"], np.float32).reshape(8, 128).T)
    bv_fm = _np_f32(np.asarray(inputs["L0_ffn_bv"], np.float32).reshape(8, 128).T)
    ln1b = _np_f32(np.tile(np.asarray(inputs["L0_ffn_ln1_w"], np.float32
                                      )[None, :], (128, 1)))
    ln2_fm = _fm16(inputs["L0_ffn_ln2_w"])
    costab = _make_costab()

    wi_l = _slabs_out(np.asarray(inputs["L0_attn_W_in"], np.float32).T)
    cv_l = _slabs_out(np.asarray(inputs["L0_attn_conv_w"], np.float32).T)
    wo2_l = _slabs_out(np.asarray(inputs["L0_attn_W_out"], np.float32).T)
    dcol = _np_f32(np.asarray(inputs["L0_attn_D"], np.float32).reshape(32, 128).T)

    jc_cores = []
    for c in range(NC):
        jcv = np.zeros((128, 256), np.float32)
        jcv[:, 0:128] = np.arange(c * 128, (c + 1) * 128, dtype=np.float32)[None, :]
        jcv[:, 128:256] = np.arange((c + 8) * 128, (c + 9) * 128,
                                    dtype=np.float32)[None, :]
        jc_cores.append(_np_f32(jcv))

    def ffn_round(ssb, lnx_full, x_tm_cores, myscores_cores):
        r2 = _launch("p2", build_p2,
                     [{"ssb": ssb, "lnxf": lnx_full, "jc": jc_cores[c]}
                      for c in range(NC)])
        cidx = np.asarray(r2[0]["cidxo"], np.float32)
        inv_full = np.zeros(SSEL, np.int32)
        sel_full = np.zeros((SSEL, D), bf16)
        for c in range(NC):
            inv = np.asarray(r2[c]["inv"])
            inv_full[c * 128:(c + 1) * 128] = inv[:, 0]
            inv_full[(c + 8) * 128:(c + 9) * 128] = inv[:, 1]
            selr = np.asarray(r2[c]["selr"])
            sel_full[c * 128:(c + 1) * 128] = selr[0]
            sel_full[(c + 8) * 128:(c + 9) * 128] = selr[1]
        in3 = []
        for c in range(NC):
            pt = _perm_tiles(c)
            rows = np.concatenate([np.arange(t * 128, (t + 1) * 128)
                                   for t in pt])
            sel_p = np.ascontiguousarray(sel_full[rows])
            inv_p = np.ascontiguousarray(
                np.stack([inv_full[t * 128:(t + 1) * 128] for t in pt],
                         axis=1))
            in3.append({
                "sel": sel_p, "invp": inv_p, "costab": costab,
                "mask": _masks_for_core(c),
                "wq_l": wq_l, "wk_l": wk_l, "wv_l": wv_l, "wo_l": wo_l,
                "wg_l": wg_l, "wu_l": wu_l, "wd_l": wd_l,
                "bq_fm": bq_fm, "bk_fm": bk_fm, "bv_fm": bv_fm,
                "ln1b": ln1b, "ln2_fm": ln2_fm,
            })
        r3 = _launch("p3", build_p3, in3)
        procf = np.zeros((SSEL, D), np.float32)
        for c in range(NC):
            pr = np.asarray(r3[c]["proc"])
            procf[c * 128:(c + 1) * 128] = pr[0]
            procf[(c + 8) * 128:(c + 9) * 128] = pr[1]
        return procf, cidx

    wcat = _wcat_build([wq_l, wk_l, wv_l, wo_l, wg_l, wu_l, wd_l,
                        wi_l, cv_l, wo2_l])
    r1 = _launch("p1", build_p1,
                 [{"x": np.ascontiguousarray(xf[c * TC:(c + 1) * TC]),
                   "wrt": wrt_tiled,
                   "wcat": np.ascontiguousarray(
                       wcat[:, c * WCOLS:(c + 1) * WCOLS])}
                  for c in range(NC)])
    wb = _wcat_split(np.concatenate(
        [np.asarray(r1[c]["wcb"]) for c in range(NC)], axis=1))
    wq_l, wk_l, wv_l, wo_l = wb["wq"], wb["wk"], wb["wv"], wb["wo"]
    wg_l, wu_l, wd_l = wb["wg"], wb["wu"], wb["wd"]
    wi_l, cv_l, wo2_l = wb["wi"], wb["cv"], wb["wo2"]
    lnx_full = np.ascontiguousarray(
        np.concatenate([np.asarray(r1[c]["lnx"]) for c in range(NC)], axis=0))
    ssb = np.ascontiguousarray(
        np.concatenate([np.asarray(r1[c]["sc"]) for c in range(NC)], axis=1))
    msc1 = [np.ascontiguousarray(np.asarray(r1[c]["sc"])) for c in range(NC)]

    procf1, cidx1 = ffn_round(ssb, lnx_full, None, msc1)

    in4 = []
    for c in range(NC):
        in4.append({
            "x": np.ascontiguousarray(xf[c * TC:(c + 1) * TC]),
            "procf": procf1,
            "cidx_own": np.ascontiguousarray(cidx1[:, c * NTC:(c + 1) * NTC]),
            "myscores": msc1[c],
            "wrt_fm": wrt_fm,
            "wi_l": wi_l, "cv_l": cv_l, "wo2_l": wo2_l, "dcol": dcol,
        })
    r4 = _launch("p4", build_p4, in4)
    x2_cores = []
    lnx2_parts = []
    msc2 = []
    ssb2_cols = []
    for c in range(NC):
        xo = np.asarray(r4[c]["xo_fm"])
        x2_cores.append(np.ascontiguousarray(
            xo.transpose(2, 0, 1).reshape(TC, D)))
        l2fm = np.asarray(r4[c]["lnx2_fm"])
        lnx2_parts.append(np.ascontiguousarray(
            l2fm.transpose(2, 0, 1).reshape(TC, D)))
        sc2 = np.asarray(r4[c]["sc2"]).reshape(NTC, 128).T
        msc2.append(np.ascontiguousarray(sc2))
        ssb2_cols.append(sc2)
    lnx2_full = np.ascontiguousarray(np.concatenate(lnx2_parts, axis=0))
    ssb2 = np.ascontiguousarray(np.concatenate(ssb2_cols, axis=1))

    procf2, cidx2 = ffn_round(ssb2, lnx2_full, None, msc2)

    in7 = []
    for c in range(NC):
        in7.append({
            "x": x2_cores[c],
            "procf": procf2,
            "cidx_own": np.ascontiguousarray(cidx2[:, c * NTC:(c + 1) * NTC]),
            "myscores": msc2[c],
        })
    r7 = _launch("p7", build_p7, in7)
    out = np.concatenate([np.asarray(r7[c]["out"]) for c in range(NC)], axis=0)
    return np.ascontiguousarray(out.reshape(B, SEQ, D), dtype=np.float32)


# version 10 (final)
# speedup vs baseline: 1.1687x; 1.0014x over previous
import math
from contextlib import ExitStack

import numpy as np
import ml_dtypes

import concourse.bass as bass
import concourse.mybir as mybir
from concourse.bass_core_rust import IndirectOffsetOnAxis
from concourse.bass_utils import run_bass_kernel_spmd

F32 = mybir.dt.float32
BF16 = mybir.dt.bfloat16
I32 = mybir.dt.int32
AF = mybir.ActivationFunctionType
ALU = mybir.AluOpType
AX = mybir.AxisListType

NC = 8
D = 2048
TT = 16384
TC = 2048
NTC = 16
B = 2
SEQ = 8192
KSEL = 1024
SSEL = 2048
HEADS = 16
KVH = 8
HD = 128
IM = 4096
IMLP = 8192
EPS_LN = 1e-5
EPS_RMS = 1e-6
OOB = 1.0e9
BISECT_ITERS = 44
BIG_NEG = -1.0e30
ISQD = 1.0 / math.sqrt(HD)

bf16 = ml_dtypes.bfloat16

_PROG = {}


def _np_f32(a):
    return np.ascontiguousarray(a, dtype=np.float32)



WCOLS = 819200 // NC


def build_p1():
    nc = bass.Bass("TRN2", target_bir_lowering=False)
    x = nc.dram_tensor("x", [TC, D], F32, kind="ExternalInput").ap()
    wrt = nc.dram_tensor("wrt", [128, D], F32, kind="ExternalInput").ap()
    wcat = nc.dram_tensor("wcat", [128, WCOLS], F32, kind="ExternalInput").ap()
    lnx = nc.dram_tensor("lnx", [TC, D], BF16, kind="ExternalOutput").ap()
    sc = nc.dram_tensor("sc", [128, NTC], F32, kind="ExternalOutput").ap()
    wcb = nc.dram_tensor("wcb", [128, WCOLS], BF16, kind="ExternalOutput").ap()

    es = ExitStack()
    with es:
        xt = [es.enter_context(nc.sbuf_tensor(f"xt{i}", [128, D], F32)) for i in range(2)]
        lt = [es.enter_context(nc.sbuf_tensor(f"lt{i}", [128, D], BF16)) for i in range(2)]
        scr = es.enter_context(nc.sbuf_tensor("scr", [128, D], F32))
        wrtb = es.enter_context(nc.sbuf_tensor("wrtb", [128, D], F32))
        w1 = es.enter_context(nc.sbuf_tensor("w1", [128, 1], F32))
        scb = es.enter_context(nc.sbuf_tensor("scb", [128, NTC], F32))
        sx = es.enter_context(nc.sbuf_tensor("sx", [128, 1], F32))
        ssq = es.enter_context(nc.sbuf_tensor("ssq", [128, 1], F32))
        xw = es.enter_context(nc.sbuf_tensor("xw", [128, 1], F32))
        m = es.enter_context(nc.sbuf_tensor("m", [128, 1], F32))
        var = es.enter_context(nc.sbuf_tensor("var", [128, 1], F32))
        sd = es.enter_context(nc.sbuf_tensor("sd", [128, 1], F32))
        rstd = es.enter_context(nc.sbuf_tensor("rstd", [128, 1], F32))
        nb = es.enter_context(nc.sbuf_tensor("nb", [128, 1], F32))
        t0 = es.enter_context(nc.sbuf_tensor("t0", [128, 1], F32))

        ld = es.enter_context(nc.semaphore("ld"))
        lxs = [es.enter_context(nc.semaphore(f"lxs{i}")) for i in range(2)]
        cm = es.enter_context(nc.semaphore("cm"))
        sq = es.enter_context(nc.semaphore("sq"))
        st = es.enter_context(nc.semaphore("st"))
        vself = es.enter_context(nc.semaphore("vself"))
        cw = es.enter_context(nc.semaphore("cw"))
        blk = es.enter_context(nc.Block())

        @blk.gpsimd
        def _(g):
            qc = WCOLS // 4
            for i in range(4):
                g.dma_start(out=wcb[:, i * qc:(i + 1) * qc],
                            in_=wcat[:, i * qc:(i + 1) * qc]).then_inc(cw, 16)

        @blk.sync
        def _(sy):
            sy.dma_start(out=wrtb[:], in_=wrt).then_inc(ld, 16)
            for j in range(NTC):
                if j >= 2:
                    sy.wait_ge(cm, 2 * j - 2)
                sy.dma_start(out=xt[j % 2][:], in_=x[j * 128:(j + 1) * 128, :]
                             ).then_inc(lxs[j % 2], 16)

        @blk.vector
        def _(v):
            vs = [0]
            def e(inst):
                vs[0] += 1
                inst.then_inc(vself, 1)
                v.wait_ge(vself, vs[0])
            v.wait_ge(ld, 16)
            v.reduce_sum(w1[:], wrtb[:], axis=AX.X)
            for j in range(NTC):
                xb = xt[j % 2]
                v.wait_ge(lxs[j % 2], (j // 2 + 1) * 16)
                e(v.scalar_tensor_tensor(scr[:], xb[:], 1.0, xb[:], ALU.mult,
                                         ALU.mult, accum_out=ssq[:]))
                e(v.reduce_sum(sx[:], xb[:], axis=AX.X))
                e(v.scalar_tensor_tensor(scr[:], xb[:], 1.0, wrtb[:],
                                         ALU.mult, ALU.mult,
                                         accum_out=xw[:]))
                e(v.tensor_scalar_mul(m[:], sx[:], 1.0 / D))
                e(v.tensor_scalar_mul(var[:], ssq[:], 1.0 / D))
                e(v.tensor_mul(t0[:], m[:], m[:]))
                e(v.tensor_sub(var[:], var[:], t0[:]))
                v.tensor_scalar_add(var[:], var[:], EPS_LN).then_inc(cm, 1)
                v.wait_ge(sq, j + 1)
                e(v.reciprocal(rstd[:], sd[:]))
                e(v.tensor_mul(nb[:], m[:], rstd[:]))
                e(v.tensor_scalar_mul(nb[:], nb[:], -1.0))
                if j >= 2:
                    v.wait_ge(st, (j // 2) * 32)
                e(v.tensor_scalar(lt[j % 2][:], xb[:], rstd[:], nb[:],
                                  ALU.mult, ALU.add))
                e(v.tensor_mul(t0[:], m[:], w1[:]))
                e(v.tensor_sub(t0[:], xw[:], t0[:]))
                v.tensor_mul(scb[:, j:j + 1], t0[:], rstd[:]).then_inc(cm, 1)

        @blk.scalar
        def _(s):
            for j in range(NTC):
                s.wait_ge(cm, 2 * j + 1)
                s.sqrt(sd[:], var[:]).then_inc(sq, 1)

        @blk.gpsimd
        def _(g):
            for j in range(NTC):
                g.wait_ge(cm, 2 * j + 2)
                g.dma_start(out=lnx[j * 128:(j + 1) * 128, :], in_=lt[j % 2][:]
                            ).then_inc(st, 16)
            g.dma_start(out=sc, in_=scb[:]).then_inc(st, 16)
            g.wait_ge(st, 16 * NTC + 16)
            g.wait_ge(cw, 64)
    return nc



def build_p2():
    nc = bass.Bass("TRN2", target_bir_lowering=False)
    ssb_d = nc.dram_tensor("ssb", [128, 128], F32, kind="ExternalInput").ap()
    lnxf = nc.dram_tensor("lnxf", [TT, D], BF16, kind="ExternalInput").ap()
    jc_d = nc.dram_tensor("jc", [128, 256], F32, kind="ExternalInput").ap()
    inv_d = nc.dram_tensor("inv", [128, 2], I32, kind="ExternalOutput").ap()
    cidx_d = nc.dram_tensor("cidxo", [128, 128], F32, kind="ExternalOutput").ap()
    selr_d = nc.dram_tensor("selr", [2, 128, D], BF16, kind="ExternalOutput").ap()

    ones_row_f = nc.inline_tensor(np.ones((1, 128), np.float32), name="ones_row_f")
    ones_row_b = nc.inline_tensor(np.ones((1, 128), bf16), name="ones_row_b")
    ones_col_b = nc.inline_tensor(np.ones((128, 1), bf16), name="ones_col_b")
    tri = np.zeros((128, 128), np.float32)
    for f in range(128):
        lo_ = 0 if f < 64 else 64
        tri[lo_:f, f] = 1.0
    trib_c = nc.inline_tensor(tri.astype(bf16), name="trib")
    ptri = (np.tril(np.ones((128, 128), np.float32)).T).astype(bf16)
    ptri_c = nc.inline_tensor(ptri, name="ptri")
    boff = np.zeros((128, 128), np.float32)
    boff[:, :64] = -1.0
    boff[:, 64:] = 1023.0
    boff_c = nc.inline_tensor(boff, name="boff")
    rhs2 = np.zeros((128, 256), np.float32)
    rhs2[:, 0::2] = np.arange(128, dtype=np.float32)[:, None]
    rhs2[:, 1::2] = np.arange(128, dtype=np.float32)[None, :]
    rhs2_c = nc.inline_tensor(rhs2.astype(bf16), name="rhs2")
    bigc = nc.inline_tensor(np.full((128, 128), OOB, np.float32), name="bigc")

    es = ExitStack()
    with es:
        ssb = es.enter_context(nc.sbuf_tensor("ssb_s", [128, 128], F32))
        jc = es.enter_context(nc.sbuf_tensor("jc_s", [128, 256], F32))
        onesrf = es.enter_context(nc.sbuf_tensor("onesrf", [1, 128], F32))
        onesrb = es.enter_context(nc.sbuf_tensor("onesrb", [1, 128], BF16))
        onescb = es.enter_context(nc.sbuf_tensor("onescb", [128, 1], BF16))
        trib = es.enter_context(nc.sbuf_tensor("trib_s", [128, 128], BF16))
        ptrib = es.enter_context(nc.sbuf_tensor("ptri_s", [128, 128], BF16))
        boffs = es.enter_context(nc.sbuf_tensor("boff_s", [128, 128], F32))
        rhs2s = es.enter_context(nc.sbuf_tensor("rhs2_s", [128, 256], BF16))
        bigs = es.enter_context(nc.sbuf_tensor("big_s", [128, 128], F32))

        lo = es.enter_context(nc.sbuf_tensor("lo", [1, 2], F32))
        hi = es.enter_context(nc.sbuf_tensor("hi", [1, 2], F32))
        mid = es.enter_context(nc.sbuf_tensor("mid", [1, 2], F32))
        trow = es.enter_context(nc.sbuf_tensor("trow", [1, 128], F32))
        cnt = es.enter_context(nc.sbuf_tensor("cnt", [1, 2], F32))
        flag = es.enter_context(nc.sbuf_tensor("flag", [1, 2], F32))
        dlt = es.enter_context(nc.sbuf_tensor("dlt", [1, 2], F32))
        m1 = es.enter_context(nc.sbuf_tensor("m1", [1, 2], F32))
        rr = es.enter_context(nc.sbuf_tensor("rr", [1, 2], F32))
        maskb = es.enter_context(nc.sbuf_tensor("maskb", [128, 128], BF16))
        strongb = es.enter_context(nc.sbuf_tensor("strongb", [128, 128], BF16))
        borderb = es.enter_context(nc.sbuf_tensor("borderb", [128, 128], BF16))
        selb = es.enter_context(nc.sbuf_tensor("selb", [128, 128], BF16))
        csrow = es.enter_context(nc.sbuf_tensor("csrow", [1, 128], BF16))
        csb = es.enter_context(nc.sbuf_tensor("csb", [128, 128], BF16))
        cum = es.enter_context(nc.sbuf_tensor("cum", [128, 128], F32))
        tle = es.enter_context(nc.sbuf_tensor("tle", [128, 128], BF16))
        cidx = es.enter_context(nc.sbuf_tensor("cidx", [128, 128], F32))
        cidxm = es.enter_context(nc.sbuf_tensor("cidxm", [128, 128], F32))
        oh = [es.enter_context(nc.sbuf_tensor(f"oh{i}", [128, 128], BF16))
              for i in range(4)]
        invf = es.enter_context(nc.sbuf_tensor("invf", [128, 4], F32))
        invi = es.enter_context(nc.sbuf_tensor("invi", [128, 2], I32))
        selr = es.enter_context(nc.sbuf_tensor("selr_s", [128, 2 * D], BF16))

        ps_tau = es.enter_context(nc.psum_tensor("ps_tau", [128, 128], F32))
        ps_cnt = es.enter_context(nc.psum_tensor("ps_cnt", [1, 128], F32))
        ps_cum = es.enter_context(nc.psum_tensor("ps_cum", [128, 128], F32))
        ps_csb = es.enter_context(nc.psum_tensor("ps_csb", [128, 128], F32))
        ps_inv = [es.enter_context(nc.psum_tensor(f"ps_inv{i}", [128, 2], F32))
                  for i in range(2)]

        ld = es.enter_context(nc.semaphore("ld"))
        v2t = es.enter_context(nc.semaphore("v2t"))
        t2v = es.enter_context(nc.semaphore("t2v"))
        so = es.enter_context(nc.semaphore("so"))
        vself = es.enter_context(nc.semaphore("vself"))
        blk = es.enter_context(nc.Block())

        BV = 2 * BISECT_ITERS
        V_HIP = BV + 1
        V_STR = BV + 2
        V_LOP = BV + 3
        V_BOR = BV + 4
        V_CS1 = BV + 5
        V_CB1 = BV + 6
        V_RRP = BV + 7
        V_SEL = BV + 8
        V_CS2 = BV + 9
        V_CB2 = BV + 10
        V_CID = BV + 11
        V_OH0 = BV + 12
        V_INV = V_OH0 + 256
        T_TAUH = BV + 1
        T_CNTH = BV + 2
        T_TAUL = BV + 3
        T_BOR = BV + 4
        T_CB1 = BV + 5
        T_TRI1 = BV + 6
        T_TAUR = BV + 7
        T_SEL = BV + 9
        T_CB2 = BV + 10
        T_TRI2 = BV + 11
        T_OH0 = BV + 12

        NLD = 10 * 16

        def v_tau_prep(v, src, sem=None):
            v.tensor_copy(trow[:, 0:64], src[0:1, 0:1].to_broadcast([1, 64]))
            i2 = v.tensor_copy(trow[:, 64:128], src[0:1, 1:2].to_broadcast([1, 64]))
            if sem is not None:
                i2.then_inc(sem, 1)

        @blk.sync
        def _(sy):
            sy.dma_start(out=ssb[:], in_=ssb_d).then_inc(ld, 16)
            sy.dma_start(out=jc[:], in_=jc_d).then_inc(ld, 16)
            sy.dma_start(out=onesrf[:], in_=ones_row_f.ap()).then_inc(ld, 16)
            sy.dma_start(out=onesrb[:], in_=ones_row_b.ap()).then_inc(ld, 16)
            sy.dma_start(out=onescb[:], in_=ones_col_b.ap()).then_inc(ld, 16)
            sy.dma_start(out=trib[:], in_=trib_c.ap()).then_inc(ld, 16)
            sy.dma_start(out=ptrib[:], in_=ptri_c.ap()).then_inc(ld, 16)
            sy.dma_start(out=boffs[:], in_=boff_c.ap()).then_inc(ld, 16)
            sy.dma_start(out=rhs2s[:], in_=rhs2_c.ap()).then_inc(ld, 16)
            sy.dma_start(out=bigs[:], in_=bigc.ap()).then_inc(ld, 16)

        @blk.vector
        def _(v):
            vs = [0]
            def e(inst):
                vs[0] += 1
                inst.then_inc(vself, 1)
                v.wait_ge(vself, vs[0])
            v.wait_ge(ld, NLD)
            e(v.memset(lo[:], -512.0))
            e(v.memset(hi[:], 512.0))
            for i in range(BISECT_ITERS):
                e(v.tensor_add(mid[:], lo[:], hi[:]))
                e(v.tensor_scalar_mul(mid[:], mid[:], 0.5))
                v_tau_prep(v, mid, v2t)
                v.wait_ge(t2v, 2 * i + 1)
                v.tensor_tensor(maskb[:], ssb[:], ps_tau[:], ALU.is_ge
                                ).then_inc(v2t, 1)
                v.wait_ge(t2v, 2 * i + 2)
                e(v.reduce_sum(cnt[:, 0:1], ps_cnt[:, 0:64], axis=AX.X))
                e(v.reduce_sum(cnt[:, 1:2], ps_cnt[:, 64:128], axis=AX.X))
                e(v.tensor_scalar(flag[:], cnt[:], float(KSEL), None, ALU.is_ge))
                e(v.tensor_sub(dlt[:], mid[:], lo[:]))
                e(v.tensor_mul(dlt[:], dlt[:], flag[:]))
                e(v.tensor_add(lo[:], lo[:], dlt[:]))
                e(v.tensor_sub(dlt[:], hi[:], mid[:]))
                e(v.tensor_mul(dlt[:], dlt[:], flag[:]))
                e(v.tensor_add(hi[:], mid[:], dlt[:]))
            v_tau_prep(v, hi, v2t)
            v.wait_ge(t2v, T_TAUH)
            v.tensor_tensor(strongb[:], ssb[:], ps_tau[:], ALU.is_ge
                            ).then_inc(v2t, 1)
            v.wait_ge(t2v, T_CNTH)
            e(v.reduce_sum(m1[:, 0:1], ps_cnt[:, 0:64], axis=AX.X))
            e(v.reduce_sum(m1[:, 1:2], ps_cnt[:, 64:128], axis=AX.X))
            e(v.tensor_scalar(rr[:], m1[:], float(KSEL), None, ALU.subtract))
            e(v.tensor_scalar_mul(rr[:], rr[:], -1.0))
            v_tau_prep(v, lo, v2t)
            v.wait_ge(t2v, T_TAUL)
            e(v.tensor_tensor(borderb[:], ssb[:], ps_tau[:], ALU.is_ge))
            v.tensor_sub(borderb[:], borderb[:], strongb[:]).then_inc(v2t, 1)
            v.wait_ge(t2v, T_BOR)
            v.tensor_copy(csrow[:], ps_cnt[:]).then_inc(v2t, 1)
            v.wait_ge(t2v, T_CB1)
            v.tensor_copy(csb[:], ps_csb[:]).then_inc(v2t, 1)
            v_tau_prep(v, rr, v2t)
            v.wait_ge(t2v, T_TAUR)
            e(v.tensor_copy(cum[:], ps_cum[:]))
            e(v.tensor_tensor(tle[:], cum[:], ps_tau[:], ALU.is_le))
            e(v.tensor_mul(tle[:], tle[:], borderb[:]))
            v.tensor_add(selb[:], strongb[:], tle[:]).then_inc(v2t, 1)
            v.wait_ge(t2v, T_SEL)
            v.tensor_copy(csrow[:], ps_cnt[:]).then_inc(v2t, 1)
            v.wait_ge(t2v, T_CB2)
            v.tensor_copy(csb[:], ps_csb[:]).then_inc(v2t, 1)
            v.wait_ge(t2v, T_TRI2)
            e(v.tensor_tensor(cidx[:], ps_cum[:], boffs[:], ALU.add))
            e(v.tensor_tensor(cidx[:], cidx[:], selb[:], ALU.mult))
            e(v.tensor_scalar(tle[:], selb[:], 1.0, None, ALU.subtract))
            e(v.tensor_scalar(cidx2t := tle[:], tle[:], -OOB, None, ALU.mult))
            v.tensor_tensor(cidxm[:], cidx[:], tle[:], ALU.add).then_inc(v2t, 1)
            v.wait_ge(v2t, V_CID)
            for k in range(256):
                f, u = k // 2, k % 2
                slot = k % 4
                if k >= 4:
                    v.wait_ge(t2v, T_OH0 + (k - 4))
                v.tensor_scalar(oh[slot][:], jc[:, u * 128:(u + 1) * 128],
                                cidxm[:, f:f + 1], None, ALU.is_equal
                                ).then_inc(v2t, 1)
            v.wait_ge(t2v, T_OH0 + 255)
            e(v.tensor_copy(invf[:, 0:2], ps_inv[0][:]))
            e(v.tensor_copy(invf[:, 2:4], ps_inv[1][:]))
            e(v.tensor_scalar_mul(invf[:, 1:2], invf[:, 1:2], 128.0))
            e(v.tensor_add(invf[:, 0:1], invf[:, 0:1], invf[:, 1:2]))
            e(v.tensor_scalar_mul(invf[:, 3:4], invf[:, 3:4], 128.0))
            e(v.tensor_add(invf[:, 2:3], invf[:, 2:3], invf[:, 3:4]))
            e(v.tensor_copy(invi[:, 0:1], invf[:, 0:1]))
            v.tensor_copy(invi[:, 1:2], invf[:, 2:3]).then_inc(v2t, 1)

        @blk.tensor
        def _(t):
            t.wait_ge(ld, NLD)
            for i in range(BISECT_ITERS):
                t.wait_ge(v2t, 2 * i + 1)
                t.matmul(ps_tau[:], onesrf[:], trow[:], start=True, stop=True
                         ).then_inc(t2v, 1)
                t.wait_ge(v2t, 2 * i + 2)
                t.matmul(ps_cnt[:], onescb[:], maskb[:], start=True, stop=True
                         ).then_inc(t2v, 1)
            t.wait_ge(v2t, V_HIP)
            t.matmul(ps_tau[:], onesrf[:], trow[:], start=True, stop=True
                     ).then_inc(t2v, 1)
            t.wait_ge(v2t, V_STR)
            t.matmul(ps_cnt[:], onescb[:], strongb[:], start=True, stop=True
                     ).then_inc(t2v, 1)
            t.wait_ge(v2t, V_LOP)
            t.matmul(ps_tau[:], onesrf[:], trow[:], start=True, stop=True
                     ).then_inc(t2v, 1)
            t.wait_ge(v2t, V_BOR)
            t.matmul(ps_cum[:], ptrib[:], borderb[:], start=True, stop=False)
            t.matmul(ps_cnt[:], onescb[:], borderb[:], start=True, stop=True
                     ).then_inc(t2v, 1)
            t.wait_ge(v2t, V_CS1)
            t.matmul(ps_csb[:], csrow[:], onesrb[:], start=True, stop=True
                     ).then_inc(t2v, 1)
            t.wait_ge(v2t, V_CB1)
            t.matmul(ps_cum[:], csb[:], trib[:], start=False, stop=True
                     ).then_inc(t2v, 1)
            t.wait_ge(v2t, V_RRP)
            t.matmul(ps_tau[:], onesrf[:], trow[:], start=True, stop=True
                     ).then_inc(t2v, 1)
            t.wait_ge(v2t, V_SEL)
            t.matmul(ps_cum[:], ptrib[:], selb[:], start=True, stop=False)
            t.matmul(ps_cnt[:], onescb[:], selb[:], start=True, stop=True
                     ).then_inc(t2v, 2)
            t.wait_ge(v2t, V_CS2)
            t.matmul(ps_csb[:], csrow[:], onesrb[:], start=True, stop=True
                     ).then_inc(t2v, 1)
            t.wait_ge(v2t, V_CB2)
            t.matmul(ps_cum[:], csb[:], trib[:], start=False, stop=True
                     ).then_inc(t2v, 1)
            for k in range(256):
                f, u = k // 2, k % 2
                slot = k % 4
                t.wait_ge(v2t, V_OH0 + k)
                t.matmul(ps_inv[u][:], oh[slot][:], rhs2s[:, 2 * f:2 * f + 2],
                         start=(f == 0), stop=(f == 127)).then_inc(t2v, 1)

        @blk.gpsimd
        def _(g):
            g.wait_ge(v2t, V_CID)
            g.dma_start(out=cidx_d, in_=cidxm[:]).then_inc(so, 16)
            g.wait_ge(v2t, V_INV)
            g.dma_start(out=inv_d, in_=invi[:]).then_inc(so, 16)
            for a in range(2):
                g.indirect_dma_start(
                    selr[:, a * D:(a + 1) * D], None,
                    lnxf, IndirectOffsetOnAxis(invi[:, a:a + 1], 0),
                    bounds_check=TT - 1, oob_is_err=False,
                ).then_inc(so, 16)
            g.wait_ge(so, 64)
            g.dma_start(out=selr_d[0], in_=selr[:, 0:D]).then_inc(so, 16)
            g.dma_start(out=selr_d[1], in_=selr[:, D:2 * D]).then_inc(so, 16)
            g.wait_ge(so, 96)
    return nc



BATCH_OF_PCOL = [0, 1] + [0] * 7 + [1] * 7


def build_p3():
    nc = bass.Bass("TRN2", target_bir_lowering=False)
    sel_d = nc.dram_tensor("sel", [SSEL, D], BF16, kind="ExternalInput").ap()
    inv_d = nc.dram_tensor("invp", [128, NTC], I32, kind="ExternalInput").ap()
    costab = nc.dram_tensor("costab", [SEQ, 256], F32, kind="ExternalInput").ap()
    mask_d = nc.dram_tensor("mask", [2, 128, SSEL], F32, kind="ExternalInput").ap()
    wq_d = nc.dram_tensor("wq_l", [HEADS, 128, 16 * 128], BF16, kind="ExternalInput").ap()
    wk_d = nc.dram_tensor("wk_l", [KVH, 128, 16 * 128], BF16, kind="ExternalInput").ap()
    wv_d = nc.dram_tensor("wv_l", [KVH, 128, 16 * 128], BF16, kind="ExternalInput").ap()
    wo_d = nc.dram_tensor("wo_l", [16, 128, 16 * 128], BF16, kind="ExternalInput").ap()
    wg_d = nc.dram_tensor("wg_l", [64, 128, 16 * 128], BF16, kind="ExternalInput").ap()
    wu_d = nc.dram_tensor("wu_l", [64, 128, 16 * 128], BF16, kind="ExternalInput").ap()
    wd_d = nc.dram_tensor("wd_l", [64, 128, 16 * 128], BF16, kind="ExternalInput").ap()
    bq_d = nc.dram_tensor("bq_fm", [128, HEADS], F32, kind="ExternalInput").ap()
    bk_d = nc.dram_tensor("bk_fm", [128, KVH], F32, kind="ExternalInput").ap()
    bv_d = nc.dram_tensor("bv_fm", [128, KVH], F32, kind="ExternalInput").ap()
    ln1_d = nc.dram_tensor("ln1b", [128, D], F32, kind="ExternalInput").ap()
    ln2_d = nc.dram_tensor("ln2_fm", [128, NTC], F32, kind="ExternalInput").ap()
    proc_d = nc.dram_tensor("proc", [2, 128, D], F32, kind="ExternalOutput").ap()

    idn_c = nc.inline_tensor(np.eye(128, dtype=bf16), name="idn")
    idnf_c = nc.inline_tensor(np.eye(128, dtype=np.float32), name="idnf")
    rmat = np.zeros((128, 128), np.float32)
    for dp in range(64):
        rmat[dp + 64, dp] = -1.0
    for dp in range(64, 128):
        rmat[dp - 64, dp] = 1.0
    rmat_c = nc.inline_tensor(rmat.astype(bf16), name="rmat")
    onescb_c = nc.inline_tensor(np.ones((128, 1), bf16), name="onescb3")
    onesrb_c = nc.inline_tensor(np.ones((1, 128), bf16), name="onesrb3")
    pofs = np.zeros((128, NTC), np.float32)
    for k in range(NTC):
        pofs[:, k] = 8192.0 * BATCH_OF_PCOL[k]
    pofs_c = nc.inline_tensor(pofs, name="pofs")

    es = ExitStack()
    with es:
        buf1 = es.enter_context(nc.sbuf_tensor("buf1", [128, 32768], BF16))
        buf2 = es.enter_context(nc.sbuf_tensor("buf2", [128, NTC * D], BF16))
        x1 = es.enter_context(nc.sbuf_tensor("x1", [128, 4096], F32))
        pstage = es.enter_context(nc.sbuf_tensor("pstage", [128, 1024], F32))
        sfm_scrb = es.enter_context(nc.sbuf_tensor("sfm_scrb", [128, 4096], BF16))
        qsb = es.enter_context(nc.sbuf_tensor("qsb", [128, 4096], BF16))
        attsb = es.enter_context(nc.sbuf_tensor("attsb", [128, SSEL], BF16))
        attT = es.enter_context(nc.sbuf_tensor("attT", [128, SSEL], BF16))
        cosb = es.enter_context(nc.sbuf_tensor("cosb", [128, SSEL], BF16))
        sinb = es.enter_context(nc.sbuf_tensor("sinb", [128, SSEL], BF16))
        kro = es.enter_context(nc.sbuf_tensor("kro", [128, 512], BF16))
        scrf = es.enter_context(nc.sbuf_tensor("scrf", [128, 512], F32))
        scrf2 = es.enter_context(nc.sbuf_tensor("scrf2", [128, 512], F32))
        wslab = [es.enter_context(nc.sbuf_tensor(f"wslab{i}", [128, 2048], BF16))
                 for i in range(4)]
        bqs = es.enter_context(nc.sbuf_tensor("bqs", [128, HEADS], F32))
        bks = es.enter_context(nc.sbuf_tensor("bks", [128, KVH], F32))
        bvs = es.enter_context(nc.sbuf_tensor("bvs", [128, KVH], F32))
        ln2s = es.enter_context(nc.sbuf_tensor("ln2s", [128, NTC], F32))
        idn = es.enter_context(nc.sbuf_tensor("idn_s", [128, 128], BF16))
        idnf = es.enter_context(nc.sbuf_tensor("idnf_s", [128, 128], F32))
        rmt = es.enter_context(nc.sbuf_tensor("rmt_s", [128, 128], BF16))
        invs = es.enter_context(nc.sbuf_tensor("invs", [128, NTC], I32))
        posf = es.enter_context(nc.sbuf_tensor("posf", [128, NTC], F32))
        posi = es.enter_context(nc.sbuf_tensor("posi", [128, NTC], I32))
        pofs_s = es.enter_context(nc.sbuf_tensor("pofs_s", [128, NTC], F32))
        ssq = es.enter_context(nc.sbuf_tensor("ssq3", [128, 1], F32))
        rstd = es.enter_context(nc.sbuf_tensor("rstd3", [128, 1], F32))
        mx = es.enter_context(nc.sbuf_tensor("mx3", [128, 1], F32))
        nb2 = es.enter_context(nc.sbuf_tensor("nb23", [128, 1], F32))
        asum = es.enter_context(nc.sbuf_tensor("asum3", [128, 1], F32))
        rec = es.enter_context(nc.sbuf_tensor("rec3", [128, 1], F32))
        var2 = es.enter_context(nc.sbuf_tensor("var2", [1, 256], F32))
        rstd2 = es.enter_context(nc.sbuf_tensor("rstd2", [1, 256], F32))
        rstd2b = es.enter_context(nc.sbuf_tensor("rstd2b", [1, 256], BF16))
        onescb = es.enter_context(nc.sbuf_tensor("onescb3s", [128, 1], BF16))
        onesrb = es.enter_context(nc.sbuf_tensor("onesrb3s", [1, 128], BF16))

        psBig = es.enter_context(nc.psum_tensor("psBig", [128, 2048], F32))
        psA = es.enter_context(nc.psum_tensor("psA", [128, 512], F32))
        psW = es.enter_context(nc.psum_tensor("psW", [128, 512], F32))
        psT = es.enter_context(nc.psum_tensor("psT", [128, 1024], F32))

        def psTs(i):
            return psT[:, (i % 2) * 512:(i % 2) * 512 + 128]

        def psAV(a):
            return psA[:, 0:128] if a == 0 else psW[:, 0:128]

        ld = es.enter_context(nc.semaphore("ld"))
        gw = es.enter_context(nc.semaphore("gw"))
        gsl = [es.enter_context(nc.semaphore(f"gsl{i}")) for i in range(4)]
        tv = es.enter_context(nc.semaphore("tv"))
        vt = es.enter_context(nc.semaphore("vt"))
        sv = es.enter_context(nc.semaphore("sv"))
        vself = es.enter_context(nc.semaphore("vself"))
        so = es.enter_context(nc.semaphore("so"))
        blk = es.enter_context(nc.Block())

        C = dict(ld=0, gw=0, tv=0, vt=0, sv=0, so=0)
        G = [0, 0, 0, 0]
        ve_n = [0]

        def VE(v, inst):
            ve_n[0] += 1
            inst.then_inc(vself, 1)
            v.wait_ge(vself, ve_n[0])

        def kfm(kvh):
            return buf1[:, kvh * 2048:(kvh + 1) * 2048]

        def vtm(k):
            return buf1[:, 16384 + k * 1024:16384 + (k + 1) * 1024]

        def hfm(pt):
            return buf2[:, pt * 2048:(pt + 1) * 2048]

        ln1b_sb = None

        @blk.sync
        def _(sy):
            for k in range(NTC):
                sy.dma_start(out=buf1[:, k * D:(k + 1) * D],
                             in_=sel_d[k * 128:(k + 1) * 128, :]).then_inc(ld, 16)
            sy.dma_start(out=bqs[:], in_=bq_d).then_inc(ld, 16)
            sy.dma_start(out=bks[:], in_=bk_d).then_inc(ld, 16)
            sy.dma_start(out=bvs[:], in_=bv_d).then_inc(ld, 16)
            sy.dma_start(out=ln2s[:], in_=ln2_d).then_inc(ld, 16)
            sy.dma_start(out=idn[:], in_=idn_c.ap()).then_inc(ld, 16)
            sy.dma_start(out=idnf[:], in_=idnf_c.ap()).then_inc(ld, 16)
            sy.dma_start(out=rmt[:], in_=rmat_c.ap()).then_inc(ld, 16)
            sy.dma_start(out=onescb[:], in_=onescb_c.ap()).then_inc(ld, 16)
            sy.dma_start(out=onesrb[:], in_=onesrb_c.ap()).then_inc(ld, 16)
            sy.dma_start(out=pofs_s[:], in_=pofs_c.ap()).then_inc(ld, 16)
            sy.dma_start(out=invs[:], in_=inv_d).then_inc(ld, 16)

        @blk.gpsimd
        def _(g):
            g.dma_start(out=cosb[:], in_=ln1_d).then_inc(gw, 16)

        C["ld"] += (NTC + 11) * 16
        C["gw"] += 16
        LD_ALL = C["ld"]
        GW_LN1 = C["gw"]

        T0B = C["tv"]
        S0B = C["sv"]

        @blk.tensor
        def _(t):
            t.wait_ge(ld, LD_ALL)
            for i in range(32):
                k, pt = i % 2, i // 2
                if i >= 2:
                    t.wait_ge(sv, S0B + i - 1)
                t.matmul(psTs(i),
                         buf1[:, k * D + pt * 128:k * D + pt * 128 + 128],
                         idn[:], start=True, stop=True).then_inc(tv, 1)

        @blk.scalar
        def _(s):
            for i in range(32):
                k, pt = i % 2, i // 2
                s.wait_ge(tv, T0B + i + 1)
                s.copy(sfm_scrb[:, pt * 256 + k * 128:pt * 256 + (k + 1) * 128],
                       psTs(i)).then_inc(sv, 1)
        C["tv"] += 32
        C["sv"] += 32
        SV_OWN = C["sv"]

        VT1 = C["vt"]
        SV1 = C["sv"]

        @blk.vector
        def _(v):
            v.wait_ge(ld, LD_ALL)
            v.wait_ge(gw, GW_LN1)
            v.wait_ge(sv, SV_OWN)
            for k in range(NTC):
                tile = buf1[:, k * D:(k + 1) * D]
                VE(v, v.scalar_tensor_tensor(scrf[:], tile[:, 0:512], 1.0,
                                             tile[:, 0:512], ALU.mult,
                                             ALU.mult, accum_out=ssq[:]))
                for q4 in range(1, 4):
                    VE(v, v.scalar_tensor_tensor(
                        scrf[:], tile[:, q4 * 512:(q4 + 1) * 512], 1.0,
                        tile[:, q4 * 512:(q4 + 1) * 512], ALU.mult, ALU.mult,
                        accum_out=rec[:]))
                    VE(v, v.tensor_add(ssq[:], ssq[:], rec[:]))
                VE(v, v.tensor_scalar(ssq[:], ssq[:], 1.0 / D, EPS_RMS,
                                      ALU.mult, ALU.add))
                v.sem_inc(vt, 1)
                v.wait_ge(sv, SV1 + k + 1)
                VE(v, v.reciprocal(rstd[:], rstd[:]))
                VE(v, v.scalar_tensor_tensor(tile, tile, rstd[:], cosb[:],
                                             ALU.mult, ALU.mult))
                v.sem_inc(vt, 1)
            v.sem_inc(vt, 1)

        @blk.scalar
        def _(s):
            for k in range(NTC):
                s.wait_ge(vt, VT1 + 2 * k + 1)
                s.sqrt(rstd[:], ssq[:]).then_inc(sv, 1)
        C["vt"] += 2 * NTC + 1
        C["sv"] += NTC
        VT_H = C["vt"]

        @blk.vector
        def _(v):
            VE(v, v.tensor_copy(posf[:], invs[:]))
            VE(v, v.tensor_sub(posf[:], posf[:], pofs_s[:]))
            v.tensor_copy(posi[:], posf[:]).then_inc(vt, 1)
        C["vt"] += 1
        VT_POS = C["vt"]

        GW2 = C["gw"]
        TV2 = C["tv"]
        VT2b = C["vt"]

        @blk.gpsimd
        def _(g):
            g.wait_ge(vt, VT_POS)
            for qtr in range(4):
                if qtr >= 1:
                    g.wait_ge(vt, VT2b + 8 * qtr)
                for kk in range(4):
                    k = qtr * 4 + kk
                    g.indirect_dma_start(
                        pstage[:, kk * 256:(kk + 1) * 256], None, costab,
                        IndirectOffsetOnAxis(posi[:, k:k + 1], 0),
                        bounds_check=SEQ - 1, oob_is_err=False,
                    ).then_inc(gw, 16)

        @blk.tensor
        def _(t):
            for qtr in range(4):
                for kk in range(4):
                    k = qtr * 4 + kk
                    t.wait_ge(gw, GW2 + (qtr + 1) * 64)
                    if k >= 1:
                        t.wait_ge(vt, VT2b + 2 * k)
                    t.transpose(psA[:, 0:128],
                                pstage[:, kk * 256:kk * 256 + 128], idnf[:]
                                ).then_inc(tv, 1)
                    t.transpose(psW[:, 0:128],
                                pstage[:, kk * 256 + 128:kk * 256 + 256],
                                idnf[:]).then_inc(tv, 1)

        @blk.vector
        def _(v):
            for k in range(NTC):
                v.wait_ge(tv, TV2 + 2 * k + 1)
                v.tensor_copy(cosb[:, k * 128:(k + 1) * 128], psA[:, 0:128]
                              ).then_inc(vt, 1)
                v.wait_ge(tv, TV2 + 2 * k + 2)
                v.tensor_copy(sinb[:, k * 128:(k + 1) * 128], psW[:, 0:128]
                              ).then_inc(vt, 1)
        C["gw"] += 256
        C["tv"] += 32
        C["vt"] += 32

        T2B = C["tv"]
        S2B = C["sv"]

        @blk.tensor
        def _(t):
            for i in range(256):
                k, pt = i // NTC, i % NTC
                if pt == 0:
                    t.wait_ge(vt, VT1 + 2 * k + 2)
                if i >= 2:
                    t.wait_ge(sv, S2B + i - 1)
                t.matmul(psTs(i),
                         buf1[:, k * D + pt * 128:k * D + pt * 128 + 128],
                         idn[:], start=True, stop=True).then_inc(tv, 1)

        @blk.scalar
        def _(s):
            for i in range(256):
                k, pt = i // NTC, i % NTC
                s.wait_ge(tv, T2B + i + 1)
                s.copy(buf2[:, pt * D + k * 128:pt * D + (k + 1) * 128],
                       psTs(i)).then_inc(sv, 1)
        C["tv"] += 256
        C["sv"] += 256
        SV_HFM = C["sv"]

        GW4 = C["gw"]
        TV4 = C["tv"]
        VT4 = C["vt"]

        G4 = list(G)

        @blk.gpsimd
        def _(g):
            for kvh in range(KVH):
                if kvh >= 2:
                    g.wait_ge(vt, VT4 + 12 * (kvh - 1))
                g.dma_start(out=wslab[kvh % 2][:], in_=wk_d[kvh]
                            ).then_inc(gsl[kvh % 2], 16)

        @blk.tensor
        def _(t):
            t.wait_ge(vt, VT4)
            t.wait_ge(sv, SV_HFM)
            for kvh in range(KVH):
                t.wait_ge(gsl[kvh % 2], G4[kvh % 2] + (kvh // 2 + 1) * 16)
                for ch in range(4):
                    if kvh + ch > 0:
                        t.wait_ge(vt, VT4 + 12 * kvh + 3 * ch)
                    for pt in range(NTC):
                        mmi = t.matmul(psA[:],
                                 wslab[kvh % 2][:, pt * 128:(pt + 1) * 128],
                                 hfm(pt)[:, ch * 512:(ch + 1) * 512],
                                 start=(pt == 0), stop=(pt == NTC - 1))
                    mmi.then_inc(tv, 1)
                    t.wait_ge(vt, VT4 + 12 * kvh + 3 * ch + 2)
                    t.matmul(psW[:], rmt[:], kro[:], start=True, stop=True
                             ).then_inc(tv, 1)

        @blk.vector
        def _(v):
            for kvh in range(KVH):
                for ch in range(4):
                    cs = slice(ch * 512, (ch + 1) * 512)
                    v.wait_ge(tv, TV4 + kvh * 8 + ch * 2 + 1)
                    VE(v, v.tensor_scalar(scrf[:], psA[:], bks[:, kvh:kvh + 1],
                                          None, ALU.add))
                    VE(v, v.tensor_copy(kro[:], scrf[:]))
                    v.sem_inc(vt, 2)
                    v.wait_ge(tv, TV4 + kvh * 8 + ch * 2 + 2)
                    VE(v, v.tensor_tensor(scrf2[:], psW[:], sinb[:, cs],
                                          ALU.mult))
                    VE(v, v.tensor_tensor(scrf[:], scrf[:], cosb[:, cs],
                                          ALU.mult))
                    VE(v, v.tensor_tensor(kfm(kvh)[:, cs], scrf[:], scrf2[:],
                                          ALU.add))
                    v.sem_inc(vt, 1)
        G[0] += (KVH // 2) * 16
        G[1] += (KVH // 2) * 16
        C["tv"] += KVH * 8
        C["vt"] += KVH * 12

        GW4V = C["gw"]
        TV4V = C["tv"]
        VT4V = C["vt"]
        SV4V = C["sv"]

        G4V = list(G)

        @blk.gpsimd
        def _(g):
            g.wait_ge(vt, VT4V)
            for kvh in range(KVH):
                if kvh >= 2:
                    g.wait_ge(vt, VT4V + 8 * (kvh - 1))
                g.dma_start(out=wslab[kvh % 2][:], in_=wv_d[kvh]
                            ).then_inc(gsl[kvh % 2], 16)

        @blk.tensor
        def _(t):
            t.wait_ge(vt, VT4V)
            for kvh in range(KVH):
                t.wait_ge(gsl[kvh % 2], G4V[kvh % 2] + (kvh // 2 + 1) * 16)
                for ch in range(4):
                    if kvh + ch > 0:
                        t.wait_ge(vt, VT4V + 8 * kvh + 2 * ch)
                    for pt in range(NTC):
                        mmi = t.matmul(psA[:],
                                 wslab[kvh % 2][:, pt * 128:(pt + 1) * 128],
                                 hfm(pt)[:, ch * 512:(ch + 1) * 512],
                                 start=(pt == 0), stop=(pt == NTC - 1))
                    mmi.then_inc(tv, 1)
                    t.wait_ge(vt, VT4V + 8 * kvh + 2 * ch + 2)
                    for b4 in range(4):
                        gidx = kvh * 16 + ch * 4 + b4
                        if gidx >= 2:
                            t.wait_ge(sv, SV4V + gidx - 1)
                        t.matmul(psTs(b4),
                                 kro[:, b4 * 128:(b4 + 1) * 128],
                                 idn[:], start=True, stop=True
                                 ).then_inc(tv, 1)

        @blk.vector
        def _(v):
            for kvh in range(KVH):
                for ch in range(4):
                    v.wait_ge(tv, TV4V + kvh * 20 + ch * 5 + 1)
                    VE(v, v.tensor_scalar(scrf[:], psA[:], bvs[:, kvh:kvh + 1],
                                          None, ALU.add))
                    VE(v, v.tensor_copy(kro[:], scrf[:]))
                    v.sem_inc(vt, 2)

        @blk.scalar
        def _(s):
            for kvh in range(KVH):
                for ch in range(4):
                    for b4 in range(4):
                        s.wait_ge(tv, TV4V + kvh * 20 + ch * 5 + 2 + b4)
                        ktok = ch * 4 + b4
                        s.copy(vtm(ktok)[:, kvh * 128:(kvh + 1) * 128],
                               psTs(b4)
                               ).then_inc(sv, 1)
        G[0] += (KVH // 2) * 16
        G[1] += (KVH // 2) * 16
        C["tv"] += KVH * 20
        C["vt"] += KVH * 8
        C["sv"] += KVH * 16

        GW5 = C["gw"]
        TV5 = C["tv"]
        VT5 = C["vt"]

        G5 = list(G)

        @blk.gpsimd
        def _(g):
            g.wait_ge(vt, VT5)
            for h in range(HEADS):
                if h >= 2:
                    g.wait_ge(vt, VT5 + 5 * (h - 1))
                g.dma_start(out=wslab[h % 2][:], in_=wq_d[h]
                            ).then_inc(gsl[h % 2], 16)

        @blk.tensor
        def _(t):
            t.wait_ge(vt, VT5)
            for h in range(HEADS):
                t.wait_ge(gsl[h % 2], G5[h % 2] + (h // 2 + 1) * 16)
                if h > 0:
                    t.wait_ge(vt, VT5 + 5 * (h - 1) + 2)
                for pt in range(NTC):
                    mmi = t.matmul(psA[:, 0:256],
                             wslab[h % 2][:, pt * 128:(pt + 1) * 128],
                             hfm(pt)[:, 0:256],
                             start=(pt == 0), stop=(pt == NTC - 1))
                mmi.then_inc(tv, 1)
                t.wait_ge(vt, VT5 + 5 * h + 2)
                t.matmul(psW[:, 0:256], rmt[:], kro[:, 0:256], start=True,
                         stop=True).then_inc(tv, 1)

        @blk.vector
        def _(v):
            for h in range(HEADS):
                v.wait_ge(tv, TV5 + 2 * h + 1)
                VE(v, v.tensor_scalar(scrf[:, 0:256], psA[:, 0:256],
                                      bqs[:, h:h + 1], None, ALU.add))
                VE(v, v.tensor_copy(kro[:, 0:256], scrf[:, 0:256]))
                v.sem_inc(vt, 2)
                v.wait_ge(tv, TV5 + 2 * h + 2)
                VE(v, v.tensor_tensor(scrf2[:, 0:256], psW[:, 0:256],
                                      sinb[:, 0:256], ALU.mult))
                VE(v, v.tensor_tensor(scrf[:, 0:256], scrf[:, 0:256],
                                      cosb[:, 0:256], ALU.mult))
                VE(v, v.tensor_tensor(qsb[:, h * 256:(h + 1) * 256],
                                      scrf[:, 0:256], scrf2[:, 0:256],
                                      ALU.add))
                v.sem_inc(vt, 3)
        G[0] += (HEADS // 2) * 16
        G[1] += (HEADS // 2) * 16
        C["tv"] += HEADS * 2
        C["vt"] += HEADS * 5
        VT_Q = C["vt"]

        @blk.gpsimd
        def _(g):
            g.wait_ge(vt, VT_Q)
            g.dma_start(out=cosb[:], in_=mask_d[0]).then_inc(gw, 16)
            g.dma_start(out=sinb[:], in_=mask_d[1]).then_inc(gw, 16)
        C["gw"] += 32
        GW_MASK = C["gw"]

        TV6 = C["tv"]
        VT6 = C["vt"]
        SV6 = C["sv"]
        TPI, VPI, SPI = 6, 3, 5

        def sc_inc(it):
            return TV6 + (it + 1 if it < 2 else 6 * it - 4)

        def trav_base(k):
            return TV6 + min(k + 2, 32) + 5 * k

        @blk.tensor
        def _(t):
            t.wait_ge(gw, GW_MASK)
            t.wait_ge(sv, SV6)

            def scores(it):
                a, h = it % 2, it // 2
                kvh = h // 2
                if it >= 1:
                    t.wait_ge(sv, SV6 + (it - 1) * SPI + 1)
                for ch in range(4):
                    mmi = t.matmul(psBig[:, ch * 512:(ch + 1) * 512],
                             qsb[:, h * 256 + a * 128:h * 256 + (a + 1) * 128],
                             kfm(kvh)[:, ch * 512:(ch + 1) * 512],
                             start=True, stop=True)
                mmi.then_inc(tv, 1)

            def trav(it):
                a, h = it % 2, it // 2
                kvh = h // 2
                t.wait_ge(vt, VT6 + 3 * it + 2)
                if it >= 2:
                    t.wait_ge(vt, VT6 + 3 * (it - 2) + 3)
                for b4 in range(4):
                    gb = it * 4 + b4
                    if gb >= 2:
                        pit, pb = (gb - 2) // 4, (gb - 2) % 4
                        t.wait_ge(sv, SV6 + pit * SPI + 2 + pb)
                    for q4 in range(4):
                        kt = b4 * 4 + q4
                        mmi = t.matmul(
                            psT[:, (b4 % 2) * 512 + q4 * 128:
                                (b4 % 2) * 512 + (q4 + 1) * 128],
                            attsb[:, kt * 128:(kt + 1) * 128], idn[:],
                            start=True, stop=True)
                    mmi.then_inc(tv, 1)
                for b4 in range(4):
                    t.wait_ge(sv, SV6 + it * SPI + 2 + b4)
                    for q4 in range(4):
                        kt = b4 * 4 + q4
                        mmi = t.matmul(psAV(a),
                                 vtm(kt)[:, kvh * 128:(kvh + 1) * 128],
                                 attT[:, kt * 128:(kt + 1) * 128],
                                 start=(kt == 0), stop=(kt == NTC - 1))
                mmi.then_inc(tv, 1)

            scores(0)
            scores(1)
            for k in range(32):
                trav(k)
                if k + 2 < 32:
                    scores(k + 2)

        @blk.vector
        def _(v):
            for it in range(32):
                a, h = it % 2, it // 2
                vb = VT6 + it * VPI
                v.wait_ge(tv, sc_inc(it))
                mbuf = cosb if a == 0 else sinb
                VE(v, v.tensor_tensor(psBig[:], psBig[:], mbuf[:], ALU.add))
                VE(v, v.reduce_max(mx[:], psBig[:], axis=AX.X))
                v.tensor_scalar_mul(nb2[:], mx[:], -ISQD).then_inc(vt, 1)
                v.wait_ge(sv, SV6 + it * SPI + 1)
                VE(v, v.reciprocal(rec[:], asum[:]))
                v.tensor_scalar(attsb[:], attsb[:], rec[:], None, ALU.mult
                                ).then_inc(vt, 1)
                v.wait_ge(tv, trav_base(it) + 5)
                v.tensor_copy(qsb[:, h * 256 + a * 128:h * 256 + (a + 1) * 128],
                              psAV(a)).then_inc(vt, 1)

        @blk.scalar
        def _(s):
            for it in range(32):
                vb = VT6 + it * VPI
                if it >= 1:
                    s.wait_ge(tv, trav_base(it - 1) + 4)
                s.wait_ge(vt, vb + 1)
                s.activation(attsb[:], psBig[:], AF.Exp, bias=nb2[:],
                             scale=ISQD, accum_out=asum[:]).then_inc(sv, 1)
                for b4 in range(4):
                    s.wait_ge(tv, trav_base(it) + 1 + b4)
                    s.copy(attT[:, b4 * 512:(b4 + 1) * 512],
                           psT[:, (b4 % 2) * 512:(b4 % 2) * 512 + 512]
                           ).then_inc(sv, 1)
        C["tv"] += 32 * TPI
        C["vt"] += 32 * VPI
        C["sv"] += 32 * SPI

        GW7 = C["gw"]
        TV7 = C["tv"]
        VT7 = C["vt"]

        G7 = list(G)

        @blk.gpsimd
        def _(g):
            g.wait_ge(vt, VT7)
            for mm_ in range(16):
                if mm_ >= 2:
                    g.wait_ge(vt, VT7 + mm_ - 1)
                g.dma_start(out=wslab[mm_ % 2][:], in_=wo_d[mm_]
                            ).then_inc(gsl[mm_ % 2], 16)

        @blk.tensor
        def _(t):
            t.wait_ge(vt, VT7)
            for mm_ in range(16):
                t.wait_ge(gsl[mm_ % 2], G7[mm_ % 2] + (mm_ // 2 + 1) * 16)
                if mm_ > 0:
                    t.wait_ge(vt, VT7 + mm_)
                for kt in range(NTC):
                    mmi = t.matmul(psW[:, 0:256],
                             wslab[mm_ % 2][:, kt * 128:(kt + 1) * 128],
                             qsb[:, kt * 256:(kt + 1) * 256],
                             start=(kt == 0), stop=(kt == NTC - 1))
                mmi.then_inc(tv, 1)

        @blk.vector
        def _(v):
            for mm_ in range(16):
                v.wait_ge(tv, TV7 + mm_ + 1)
                v.scalar_tensor_tensor(
                    x1[:, mm_ * 256:(mm_ + 1) * 256], psW[:, 0:256], 1.0,
                    sfm_scrb[:, mm_ * 256:(mm_ + 1) * 256], ALU.mult, ALU.add
                ).then_inc(vt, 1)
        G[0] += 8 * 16
        G[1] += 8 * 16
        C["tv"] += 16
        C["vt"] += 16
        VT_X1 = C["vt"]

        TV8 = C["tv"]
        VT8 = C["vt"]
        SV8 = C["sv"]

        @blk.scalar
        def _(s):
            s.wait_ge(vt, VT_X1)
            s.square(sfm_scrb[:], x1[:]).then_inc(sv, 1)
            s.wait_ge(vt, VT8 + 1)
            s.sqrt(rstd2[:], var2[:]).then_inc(sv, 1)

        @blk.tensor
        def _(t):
            t.wait_ge(sv, SV8 + 1)
            for mm_ in range(16):
                mmi = t.matmul(psW[0:1, 0:256], onescb[:],
                         sfm_scrb[:, mm_ * 256:(mm_ + 1) * 256],
                         start=(mm_ == 0), stop=(mm_ == 15))
            mmi.then_inc(tv, 1)
            t.wait_ge(vt, VT8 + 2)
            t.matmul(psW[:, 0:256], onesrb[:], rstd2b[:], start=True, stop=True
                     ).then_inc(tv, 1)

        @blk.vector
        def _(v):
            v.wait_ge(tv, TV8 + 1)
            VE(v, v.tensor_scalar(var2[:], psW[0:1, 0:256], 1.0 / D, EPS_RMS,
                                  ALU.mult, ALU.add))
            v.sem_inc(vt, 1)
            v.wait_ge(sv, SV8 + 2)
            VE(v, v.reciprocal(rstd2[:], rstd2[:]))
            v.tensor_copy(rstd2b[:], rstd2[:]).then_inc(vt, 1)
            v.wait_ge(tv, TV8 + 2)
            for mm_ in range(16):
                VE(v, v.tensor_tensor(scrf[:, 0:256],
                                      x1[:, mm_ * 256:(mm_ + 1) * 256],
                                      psW[:, 0:256], ALU.mult))
                VE(v, v.tensor_scalar(sfm_scrb[:, mm_ * 256:(mm_ + 1) * 256],
                                      scrf[:, 0:256], ln2s[:, mm_:mm_ + 1],
                                      None, ALU.mult))
            v.sem_inc(vt, 1)
        C["sv"] += 2
        C["tv"] += 2
        C["vt"] += 3
        VT_H2 = C["vt"]

        GW9 = C["gw"]
        TV9 = C["tv"]
        VT9 = C["vt"]
        SV9 = C["sv"]

        G9 = list(G)

        @blk.gpsimd
        def _(g):
            g.wait_ge(vt, VT_H2)
            for mi in range(64):
                if mi >= 2:
                    g.wait_ge(vt, VT9 + mi - 1)
                g.dma_start(out=wslab[mi % 2][:], in_=wg_d[mi]
                            ).then_inc(gsl[mi % 2], 16)
                g.dma_start(out=wslab[2 + mi % 2][:], in_=wu_d[mi]
                            ).then_inc(gsl[2 + mi % 2], 16)

        @blk.tensor
        def _(t):
            t.wait_ge(vt, VT_H2)
            for mi in range(64):
                t.wait_ge(gsl[mi % 2], G9[mi % 2] + (mi // 2 + 1) * 16)
                t.wait_ge(gsl[2 + mi % 2], G9[2 + mi % 2] + (mi // 2 + 1) * 16)
                if mi > 0:
                    t.wait_ge(vt, VT9 + mi)
                for pt in range(NTC):
                    mmi = t.matmul(psA[:, 0:256],
                             wslab[mi % 2][:, pt * 128:(pt + 1) * 128],
                             sfm_scrb[:, pt * 256:(pt + 1) * 256],
                             start=(pt == 0), stop=(pt == NTC - 1))
                mmi.then_inc(tv, 1)
                for pt in range(NTC):
                    mmi = t.matmul(psW[:, 0:256],
                             wslab[2 + mi % 2][:, pt * 128:(pt + 1) * 128],
                             sfm_scrb[:, pt * 256:(pt + 1) * 256],
                             start=(pt == 0), stop=(pt == NTC - 1))
                mmi.then_inc(tv, 1)

        @blk.scalar
        def _(s):
            for mi in range(64):
                s.wait_ge(tv, TV9 + 2 * mi + 1)
                if mi >= 1:
                    s.wait_ge(vt, VT9 + mi)
                s.activation(scrf[:, 0:256], psA[:, 0:256], AF.Silu
                             ).then_inc(sv, 1)

        @blk.vector
        def _(v):
            for mi in range(64):
                v.wait_ge(tv, TV9 + 2 * mi + 2)
                v.wait_ge(sv, SV9 + mi + 1)
                v.tensor_tensor(buf2[:, mi * 256:(mi + 1) * 256],
                                scrf[:, 0:256], psW[:, 0:256], ALU.mult
                                ).then_inc(vt, 1)
        G[0] += 32 * 16
        G[1] += 32 * 16
        G[2] += 32 * 16
        G[3] += 32 * 16
        C["tv"] += 64 * 2
        C["vt"] += 64
        C["sv"] += 64

        TV10 = C["tv"]
        VT10 = C["vt"]
        SO10 = C["so"]
        G10 = list(G)
        NP10 = 2

        def t10_base(p):
            return TV10 + p * 72

        def v10_base(p):
            return VT10 + p * 25

        def bank10(mo):
            if mo < 4:
                return psBig[:, mo * 512:mo * 512 + 256]
            if mo == 4:
                return psA[:, 0:256]
            if mo == 5:
                return psW[:, 0:256]
            return psT[:, (mo - 6) * 512:(mo - 6) * 512 + 256]

        @blk.gpsimd
        def _(g):
            g.wait_ge(tv, TV10)
            for p in range(NP10):
                for kt in range(64):
                    gi = p * 64 + kt
                    if gi >= 2:
                        gp, gk = (gi - 2) // 64, (gi - 2) % 64
                        g.wait_ge(tv, TV10 + gp * 72 + gk + 1)
                    g.dma_start(out=wslab[gi % 2][:], in_=wd_d[kt]
                                ).then_inc(gsl[gi % 2], 16)

        @blk.tensor
        def _(t):
            t.wait_ge(vt, VT10)
            for p in range(NP10):
                if p >= 1:
                    t.wait_ge(vt, v10_base(p - 1) + 24)
                for kt in range(64):
                    gi = p * 64 + kt
                    t.wait_ge(gsl[gi % 2], G10[gi % 2] + (gi // 2 + 1) * 16)
                    for mo in range(8):
                        mm_ = p * 8 + mo
                        mmi = t.matmul(bank10(mo),
                                       wslab[gi % 2][:, mm_ * 128:(mm_ + 1) * 128],
                                       buf2[:, kt * 256:(kt + 1) * 256],
                                       start=(kt == 0), stop=(kt == 63))
                    mmi.then_inc(tv, 1)
                for mo in range(8):
                    t.wait_ge(vt, v10_base(p) + 3 * mo + 1)
                    bk = bank10(mo)
                    t.transpose(bk[:, 0:128], scrf2[:, 0:128], idnf[:])
                    t.transpose(bk[:, 128:256], scrf2[:, 128:256], idnf[:]
                                ).then_inc(tv, 1)

        @blk.vector
        def _(v):
            for p in range(NP10):
                v.wait_ge(tv, t10_base(p) + 64)
                if p >= 1:
                    v.wait_ge(so, SO10 + p * 64)
                for mo in range(8):
                    mm_ = p * 8 + mo
                    VE(v, v.scalar_tensor_tensor(
                        scrf2[:, 0:256],
                        bank10(mo), 1.0,
                        x1[:, mm_ * 256:(mm_ + 1) * 256], ALU.mult, ALU.add))
                    v.sem_inc(vt, 1)
                    if mo == 4:
                        v.wait_ge(so, SO10 + p * 64 + 32)
                    v.wait_ge(tv, t10_base(p) + 64 + mo + 1)
                    VE(v, v.tensor_copy(
                        pstage[:, (mo % 4) * 128:(mo % 4 + 1) * 128],
                        bank10(mo)[:, 0:128]))
                    VE(v, v.tensor_copy(
                        pstage[:, 512 + (mo % 4) * 128:512 + (mo % 4 + 1) * 128],
                        bank10(mo)[:, 128:256]))
                    v.sem_inc(vt, 2)
                v.sem_inc(vt, 1)

        @blk.sync
        def _(sy):
            for p in range(NP10):
                for hf in range(2):
                    sy.wait_ge(vt, v10_base(p) + 12 * (hf + 1))
                    off = p * 1024 + hf * 512
                    sy.dma_start(out=proc_d[0][:, off:off + 512],
                                 in_=pstage[:, 0:512]).then_inc(so, 16)
                    sy.dma_start(out=proc_d[1][:, off:off + 512],
                                 in_=pstage[:, 512:1024]).then_inc(so, 16)
            sy.wait_ge(so, SO10 + NP10 * 64)
        G[0] += 64 * 16
        G[1] += 64 * 16
        C["tv"] += NP10 * 72
        C["vt"] += NP10 * 25
        C["so"] += NP10 * 64
    return nc


CHT = 4
NCH = NTC // CHT


def build_p4():
    nc = bass.Bass("TRN2", target_bir_lowering=False)
    x_d = nc.dram_tensor("x", [TC, D], F32, kind="ExternalInput").ap()
    procf = nc.dram_tensor("procf", [SSEL, D], F32, kind="ExternalInput").ap()
    cidx_d = nc.dram_tensor("cidx_own", [128, NTC], F32, kind="ExternalInput").ap()
    msc_d = nc.dram_tensor("myscores", [128, NTC], F32, kind="ExternalInput").ap()
    wrt_d = nc.dram_tensor("wrt_fm", [128, NTC], F32, kind="ExternalInput").ap()
    wi_d = nc.dram_tensor("wi_l", [64, 128, 16 * 128], BF16, kind="ExternalInput").ap()
    cv_d = nc.dram_tensor("cv_l", [32, 128, 32 * 128], BF16, kind="ExternalInput").ap()
    wo2_d = nc.dram_tensor("wo2_l", [16, 128, 32 * 128], BF16, kind="ExternalInput").ap()
    dcol_d = nc.dram_tensor("dcol", [128, 32], F32, kind="ExternalInput").ap()
    xo_d = nc.dram_tensor("xo_fm", [NTC, 128, TC], F32, kind="ExternalOutput").ap()
    sc2_d = nc.dram_tensor("sc2", [1, TC], F32, kind="ExternalOutput").ap()
    lnx2_d = nc.dram_tensor("lnx2_fm", [NTC, 128, TC], BF16, kind="ExternalOutput").ap()
    xpre_d = nc.dram_tensor("xpre_fm", [128, NTC, TC], F32, kind="Internal").ap()

    idn_c = nc.inline_tensor(np.eye(128, dtype=bf16), name="idn4")
    idnf_c = nc.inline_tensor(np.eye(128, dtype=np.float32), name="idnf4")
    onescf_c = nc.inline_tensor(np.ones((128, 1), np.float32), name="onescf4")
    onesrf_c = nc.inline_tensor(np.ones((1, 128), np.float32), name="onesrf4")

    es = ExitStack()
    with es:
        lnfm = es.enter_context(nc.sbuf_tensor("lnfm", [128, 16 * 512], BF16))
        xcv = es.enter_context(nc.sbuf_tensor("xcv", [128, 32 * 512], BF16))
        slz = es.enter_context(nc.sbuf_tensor("slz", [128, 32 * 512], BF16))
        xt = [es.enter_context(nc.sbuf_tensor(f"xt4{i}", [128, D], F32))
              for i in range(2)]
        stage = es.enter_context(nc.sbuf_tensor("stage", [128, D], F32))
        lnf = es.enter_context(nc.sbuf_tensor("lnf", [128, D], F32))
        lnmb = es.enter_context(nc.sbuf_tensor("lnmb", [128, D], BF16))
        xps = es.enter_context(nc.sbuf_tensor("xps", [128, D], F32))
        wsl = [es.enter_context(nc.sbuf_tensor(f"wsl4{i}", [128, 4096], BF16))
               for i in range(2)]
        xrb = [es.enter_context(nc.sbuf_tensor(f"xrb{i}", [128, 512], F32))
               for i in range(2)]
        xog = es.enter_context(nc.sbuf_tensor("xog", [128, 16 * 512], F32))
        sqb = es.enter_context(nc.sbuf_tensor("sqb", [128, 1024], F32))
        scr5 = es.enter_context(nc.sbuf_tensor("scr5", [128, 1024], F32))
        scR = es.enter_context(nc.sbuf_tensor("scR", [1, TC], F32))
        cidxs = es.enter_context(nc.sbuf_tensor("cidxs", [128, NTC], F32))
        cidxi = es.enter_context(nc.sbuf_tensor("cidxi", [128, 1], I32))
        mscs = es.enter_context(nc.sbuf_tensor("mscs", [128, NTC], F32))
        wrts = es.enter_context(nc.sbuf_tensor("wrts", [128, NTC], F32))
        dcols = es.enter_context(nc.sbuf_tensor("dcols", [128, 32], F32))
        idn = es.enter_context(nc.sbuf_tensor("idn4s", [128, 128], BF16))
        idnf = es.enter_context(nc.sbuf_tensor("idnf4s", [128, 128], F32))
        onescf = es.enter_context(nc.sbuf_tensor("onescf4s", [128, 1], F32))
        onesrf = es.enter_context(nc.sbuf_tensor("onesrf4s", [1, 128], F32))
        ssq = es.enter_context(nc.sbuf_tensor("ssq4", [128, 1], F32))
        sx = es.enter_context(nc.sbuf_tensor("sx4", [128, 1], F32))
        m_ = es.enter_context(nc.sbuf_tensor("m4", [128, 1], F32))
        var = es.enter_context(nc.sbuf_tensor("var4", [128, 1], F32))
        sd = es.enter_context(nc.sbuf_tensor("sd4", [128, 1], F32))
        rstd = es.enter_context(nc.sbuf_tensor("rstd4", [128, 1], F32))
        nb = es.enter_context(nc.sbuf_tensor("nb4", [128, 1], F32))
        t0 = es.enter_context(nc.sbuf_tensor("t04", [128, 1], F32))
        g8 = es.enter_context(nc.sbuf_tensor("g84", [128, 1], F32))
        coef = es.enter_context(nc.sbuf_tensor("coef4", [128, 1], F32))
        w1 = es.enter_context(nc.sbuf_tensor("w14", [1, 1], F32))
        mrow = es.enter_context(nc.sbuf_tensor("mrow", [1, 512], F32))
        vrow = es.enter_context(nc.sbuf_tensor("vrow", [1, 512], F32))
        rrow = es.enter_context(nc.sbuf_tensor("rrow", [1, 512], F32))
        srow = es.enter_context(nc.sbuf_tensor("srow", [1, 512], F32))

        psA = es.enter_context(nc.psum_tensor("psA4", [128, 512], F32))
        psB = es.enter_context(nc.psum_tensor("psB4", [128, 512], F32))
        psT = es.enter_context(nc.psum_tensor("psT4", [128, 2048], BF16))
        psS = es.enter_context(nc.psum_tensor("psS4", [1, 512], F32))
        psQ = es.enter_context(nc.psum_tensor("psQ4", [1, 512], F32))
        psW2 = es.enter_context(nc.psum_tensor("psW24", [1, 512], F32))
        psM = psA
        psR = psB

        def psTs4(i):
            return psT[:, (i % 2) * 1024:(i % 2) * 1024 + 128]

        ld = es.enter_context(nc.semaphore("ld4"))
        gw = es.enter_context(nc.semaphore("gw4"))
        tv = es.enter_context(nc.semaphore("tv4"))
        vt = es.enter_context(nc.semaphore("vt4"))
        sv = es.enter_context(nc.semaphore("sv4"))
        vself = es.enter_context(nc.semaphore("vself4"))
        so = es.enter_context(nc.semaphore("so4"))
        lx = es.enter_context(nc.semaphore("lx4"))
        lr = es.enter_context(nc.semaphore("lr4"))
        sxo = es.enter_context(nc.semaphore("sxo4"))
        sst = es.enter_context(nc.semaphore("sst4"))
        gsl = [es.enter_context(nc.semaphore(f"gsl4{i}")) for i in range(2)]
        blk = es.enter_context(nc.Block())

        C = dict(ld=0, gw=0, tv=0, vt=0, sv=0, so=0, lx=0, lr=0, sxo=0, sst=0)
        G4 = [0, 0]
        ve_n = [0]

        def VE(v, inst):
            ve_n[0] += 1
            inst.then_inc(vself, 1)
            v.wait_ge(vself, ve_n[0])

        @blk.sync
        def _(sy):
            sy.dma_start(out=cidxs[:], in_=cidx_d).then_inc(ld, 16)
            sy.dma_start(out=mscs[:], in_=msc_d).then_inc(ld, 16)
            sy.dma_start(out=wrts[:], in_=wrt_d).then_inc(ld, 16)
            sy.dma_start(out=dcols[:], in_=dcol_d).then_inc(ld, 16)
            sy.dma_start(out=idn[:], in_=idn_c.ap()).then_inc(ld, 16)
            sy.dma_start(out=idnf[:], in_=idnf_c.ap()).then_inc(ld, 16)
            sy.dma_start(out=onescf[:], in_=onescf_c.ap()).then_inc(ld, 16)
            sy.dma_start(out=onesrf[:], in_=onesrf_c.ap()).then_inc(ld, 16)
        C["ld"] += 8 * 16
        LD0 = C["ld"]

        @blk.tensor
        def _(t):
            t.wait_ge(ld, LD0)
            for pt in range(NTC):
                mmi = t.matmul(psS[0:1, 0:1], wrts[:, pt:pt + 1], onescf[:],
                               start=(pt == 0), stop=(pt == NTC - 1))
            mmi.then_inc(tv, 1)
        C["tv"] += 1
        TV_W1 = C["tv"]

        @blk.vector
        def _(v):
            v.wait_ge(ld, LD0)
            v.memset(stage[:], 0.0).then_inc(vt, 1)
            v.wait_ge(tv, TV_W1)
            v.tensor_copy(w1[:], psS[0:1, 0:1]).then_inc(vt, 1)
        C["vt"] += 2

        tile_tv_base = {}

        def gincw(k):
            return 1 if k == 0 else (2 if k == 1 else 2 * k)

        for chx in range(NCH):
            SSTC = C["sst"]
            for jj in range(CHT):
                j = chx * CHT + jj
                LXJ = C["lx"]
                GWJ = C["gw"]
                VTJ = C["vt"]
                SVJ = C["sv"]
                TVJ = C["tv"]
                tile_tv_base[j] = TVJ

                @blk.sync
                def _(sy, j=j, SVJ=SVJ):
                    if j >= 2:
                        sy.wait_ge(tv, tile_tv_base[j - 2] + 32)
                    sy.dma_start(out=xt[j % 2][:],
                                 in_=x_d[j * 128:(j + 1) * 128, :]
                                 ).then_inc(lx, 16)
                    sy.wait_ge(sv, SVJ + 41)
                    sy.dma_start(
                        out=xpre_d[:, :, j * 128:(j + 1) * 128],
                        in_=xps[:].rearrange("p (f q) -> p f q", f=NTC),
                    ).then_inc(so, 16)
                C["lx"] += 16
                C["so"] += 16

                @blk.vector
                def _(v, j=j, LXJ=LXJ, GWJ=GWJ, VTJ=VTJ, SVJ=SVJ):
                    xb = xt[j % 2]
                    v.wait_ge(sv, SVJ + 2)
                    VE(v, v.tensor_scalar_mul(m_[:], sx[:], 1.0 / D))
                    VE(v, v.tensor_scalar_mul(var[:], ssq[:], 1.0 / D))
                    VE(v, v.tensor_mul(t0[:], m_[:], m_[:]))
                    VE(v, v.tensor_sub(var[:], var[:], t0[:]))
                    v.tensor_scalar_add(var[:], var[:], EPS_LN
                                        ).then_inc(vt, 1)
                    v.wait_ge(sv, SVJ + 3)
                    VE(v, v.reciprocal(rstd[:], sd[:]))
                    VE(v, v.tensor_mul(nb[:], m_[:], rstd[:]))
                    VE(v, v.tensor_scalar_mul(nb[:], nb[:], -1.0))
                    VE(v, v.tensor_scalar(t0[:], cidxs[:, j:j + 1], 4096.0,
                                          None, ALU.min))
                    v.tensor_copy(cidxi[:], t0[:]).then_inc(vt, 1)
                    VE(v, v.tensor_scalar(coef[:], cidxs[:, j:j + 1], 2048.0,
                                          None, ALU.is_lt))
                    v.wait_ge(sv, SVJ + 4)
                    VE(v, v.tensor_mul(coef[:], coef[:], g8[:]))
                    VE(v, v.tensor_scalar_mul(coef[:], coef[:], 0.5))
                    v.wait_ge(gw, GWJ + 16)
                    v.wait_ge(sv, SVJ + 5)
                    VE(v, v.tensor_sub(stage[:], stage[:], lnf[:]))
                    VE(v, v.tensor_scalar(stage[:], stage[:], coef[:], None,
                                          ALU.mult))
                    VE(v, v.scalar_tensor_tensor(stage[:], lnf[:], 0.5,
                                                 stage[:], ALU.mult, ALU.add))
                    VE(v, v.tensor_add(xb[:], xb[:], stage[:]))
                    v.sem_inc(vt, 1)
                    v.wait_ge(sv, SVJ + 7)
                    VE(v, v.tensor_scalar_mul(m_[:], sx[:], 1.0 / D))
                    VE(v, v.tensor_scalar_mul(var[:], ssq[:], 1.0 / D))
                    VE(v, v.tensor_mul(t0[:], m_[:], m_[:]))
                    VE(v, v.tensor_sub(var[:], var[:], t0[:]))
                    v.tensor_scalar_add(var[:], var[:], EPS_LN
                                        ).then_inc(vt, 1)
                    v.wait_ge(sv, SVJ + 8)
                    VE(v, v.reciprocal(rstd[:], sd[:]))
                    VE(v, v.tensor_mul(nb[:], m_[:], rstd[:]))
                    VE(v, v.tensor_scalar_mul(nb[:], nb[:], -1.0))
                    v.sem_inc(vt, 1)
                    v.sem_inc(vt, 1)
                C["vt"] += 6

                @blk.scalar
                def _(s, j=j, jj=jj, LXJ=LXJ, VTJ=VTJ, SVJ=SVJ, TVJ=TVJ,
                      SSTC=SSTC):
                    if jj == 0:
                        s.wait_ge(sst, SSTC)
                    xb = xt[j % 2]
                    s.wait_ge(lx, LXJ + 16)
                    s.activation(lnf[:], xb[:], AF.Square,
                                 accum_out=ssq[:]).then_inc(sv, 1)
                    s.wait_ge(sv, SVJ + 1)
                    s.activation(lnf[:], xb[:], AF.Identity,
                                 accum_out=sx[:]).then_inc(sv, 1)
                    s.wait_ge(vt, VTJ + 1)
                    s.sqrt(sd[:], var[:]).then_inc(sv, 1)
                    s.activation(g8[:], mscs[:, j:j + 1], AF.Sigmoid
                                 ).then_inc(sv, 1)
                    s.wait_ge(vt, VTJ + 2)
                    s.activation(lnf[:], xb[:], AF.Identity,
                                 bias=nb[:], scale=rstd[:]).then_inc(sv, 1)
                    s.wait_ge(vt, VTJ + 3)
                    s.activation(stage[:], xb[:], AF.Square,
                                 accum_out=ssq[:]).then_inc(sv, 1)
                    s.wait_ge(sv, SVJ + 6)
                    s.activation(stage[:], xb[:], AF.Identity,
                                 accum_out=sx[:]).then_inc(sv, 1)
                    s.wait_ge(vt, VTJ + 4)
                    s.sqrt(sd[:], var[:]).then_inc(sv, 1)
                    s.wait_ge(vt, VTJ + 5)
                    s.activation(lnmb[:], xb[:], AF.Identity,
                                 bias=nb[:], scale=rstd[:]).then_inc(sv, 1)
                    for pt in range(NTC):
                        s.wait_ge(tv, TVJ + pt + 1)
                        s.copy(lnfm[:, pt * 512 + jj * 128:
                                     pt * 512 + (jj + 1) * 128],
                               psTs4(pt)).then_inc(sv, 1)
                    if j >= 1:
                        s.wait_ge(so, 16 * j)
                    for pt in range(NTC):
                        s.wait_ge(tv, TVJ + 16 + pt + 1)
                        ps = psA if pt % 2 == 0 else psB
                        s.copy(xps[:, pt * 128:(pt + 1) * 128],
                               ps[:, 0:128]).then_inc(sv, 1)
                C["sv"] += 41

                @blk.gpsimd
                def _(g, VTJ=VTJ):
                    g.wait_ge(vt, VTJ + 2)
                    g.indirect_dma_start(
                        stage[:], None, procf,
                        IndirectOffsetOnAxis(cidxi[:], 0),
                        bounds_check=SSEL - 1, oob_is_err=False,
                    ).then_inc(gw, 16)
                C["gw"] += 16

                @blk.tensor
                def _(t, j=j, VTJ=VTJ, SVJ=SVJ):
                    t.wait_ge(sv, SVJ + 9)
                    for pt in range(NTC):
                        if pt >= 2:
                            t.wait_ge(sv, SVJ + pt + 8)
                        t.transpose(psTs4(pt),
                                    lnmb[:, pt * 128:(pt + 1) * 128], idn[:]
                                    ).then_inc(tv, 1)
                    t.wait_ge(vt, VTJ + 3)
                    for pt in range(NTC):
                        ps = psA if pt % 2 == 0 else psB
                        if pt >= 2:
                            t.wait_ge(sv, SVJ + pt + 24)
                        t.transpose(ps[:, 0:128],
                                    xt[j % 2][:, pt * 128:(pt + 1) * 128],
                                    idnf[:]).then_inc(tv, 1)
                C["tv"] += 2 * NTC

            def lnslice(kt):
                return lnfm[:, kt * 512:(kt + 1) * 512]

            TVX = C["tv"]
            SVXz = C["sv"]
            GX = list(G4)

            @blk.gpsimd
            def _(g, GX=GX, TVX=TVX):
                g.wait_ge(tv, TVX)
                for m in range(64):
                    if m >= 2:
                        g.wait_ge(tv, TVX + m - 1)
                    g.dma_start(out=wsl[m % 2][:, 0:2048], in_=wi_d[m]
                                ).then_inc(gsl[m % 2], 16)

            @blk.tensor
            def _(t, GX=GX, TVX=TVX, SVXz=SVXz):
                t.wait_ge(sv, SVXz)
                for m in range(64):
                    t.wait_ge(gsl[m % 2], GX[m % 2] + (m // 2 + 1) * 16)
                    if m >= 2:
                        t.wait_ge(sv, SVXz + m - 1)
                    ps = psA if m % 2 == 0 else psB
                    for kt in range(NTC):
                        mmi = t.matmul(ps[:],
                                       wsl[m % 2][:, kt * 128:(kt + 1) * 128],
                                       lnslice(kt), start=(kt == 0),
                                       stop=(kt == NTC - 1))
                    mmi.then_inc(tv, 1)

            @blk.scalar
            def _(s, TVX=TVX):
                for m in range(64):
                    s.wait_ge(tv, TVX + m + 1)
                    ps = psA if m % 2 == 0 else psB
                    if m < 32:
                        s.copy(xcv[:, m * 512:(m + 1) * 512], ps[:]
                               ).then_inc(sv, 1)
                    else:
                        s.activation(slz[:, (m - 32) * 512:(m - 31) * 512],
                                     ps[:], AF.Silu).then_inc(sv, 1)
            G4[0] += 32 * 16
            G4[1] += 32 * 16
            C["tv"] += 64
            C["sv"] += 64

            TVC = C["tv"]
            SVC = C["sv"]
            VTC = C["vt"]
            GC = list(G4)

            @blk.gpsimd
            def _(g, GC=GC, TVC=TVC):
                g.wait_ge(tv, TVC)
                for m in range(32):
                    if m >= 2:
                        g.wait_ge(tv, TVC + m - 1)
                    g.dma_start(out=wsl[m % 2][:], in_=cv_d[m]
                                ).then_inc(gsl[m % 2], 16)

            @blk.tensor
            def _(t, GC=GC, TVC=TVC, SVC=SVC):
                t.wait_ge(sv, SVC)
                for m in range(32):
                    t.wait_ge(gsl[m % 2], GC[m % 2] + (m // 2 + 1) * 16)
                    if m >= 2:
                        t.wait_ge(sv, SVC + m - 1)
                    ps = psA if m % 2 == 0 else psB
                    for kt in range(32):
                        mmi = t.matmul(ps[:],
                                       wsl[m % 2][:, kt * 128:(kt + 1) * 128],
                                       xcv[:, kt * 512:(kt + 1) * 512],
                                       start=(kt == 0), stop=(kt == 31))
                    mmi.then_inc(tv, 1)

            @blk.scalar
            def _(s, TVC=TVC, SVC=SVC, VTC=VTC):
                for m in range(32):
                    s.wait_ge(tv, TVC + m + 1)
                    if m >= 2:
                        s.wait_ge(vt, VTC + m - 1)
                    ps = psA if m % 2 == 0 else psB
                    s.mul(scr5[:, (m % 2) * 512:(m % 2) * 512 + 512], ps[:],
                          dcols[:, m:m + 1]).then_inc(sv, 1)

            @blk.vector
            def _(v, SVC=SVC):
                for m in range(32):
                    v.wait_ge(sv, SVC + m + 1)
                    VE(v, v.tensor_tensor(slz[:, m * 512:(m + 1) * 512],
                                          scr5[:, (m % 2) * 512:
                                               (m % 2) * 512 + 512],
                                          slz[:, m * 512:(m + 1) * 512],
                                          ALU.mult))
                    v.sem_inc(vt, 1)
            G4[0] += 16 * 16
            G4[1] += 16 * 16
            C["tv"] += 32
            C["sv"] += 32
            C["vt"] += 32

            TVW = C["tv"]
            VTW = C["vt"]
            SVW = C["sv"]
            LRW = C["lr"]
            SXOC = C["sxo"]
            GW2 = list(G4)

            @blk.gpsimd
            def _(g, GW2=GW2, TVW=TVW):
                g.wait_ge(tv, TVW)
                for m in range(16):
                    if m >= 2:
                        g.wait_ge(tv, TVW + gincw(m - 2))
                    g.dma_start(out=wsl[m % 2][:], in_=wo2_d[m]
                                ).then_inc(gsl[m % 2], 16)

            @blk.sync
            def _(sy, chx=chx, LRW=LRW, VTW=VTW):
                sy.wait_ge(so, (chx + 1) * CHT * 16)
                for m in range(16):
                    if m >= 2:
                        sy.wait_ge(vt, VTW + m - 1)
                    sy.dma_start(out=xrb[m % 2][:],
                                 in_=xpre_d[:, m, chx * 512:(chx + 1) * 512]
                                 ).then_inc(lr, 16)
                for m in range(16):
                    sy.wait_ge(vt, VTW + m + 1)
                    sy.dma_start(out=xo_d[m][:, chx * 512:(chx + 1) * 512],
                                 in_=xog[:, m * 512:(m + 1) * 512]
                                 ).then_inc(sxo, 16)
                for pt in range(NTC):
                    sy.wait_ge(vt, VTW + 18 + pt + 1)
                    sy.dma_start(out=lnx2_d[pt][:, chx * 512:(chx + 1) * 512],
                                 in_=lnfm[:, pt * 512:(pt + 1) * 512]
                                 ).then_inc(sst, 16)

            @blk.tensor
            def _(t, GW2=GW2, TVW=TVW, VTW=VTW, SVW=SVW):
                t.wait_ge(vt, VTW)

                def grp(m):
                    t.wait_ge(gsl[m % 2], GW2[m % 2] + (m // 2 + 1) * 16)
                    if m >= 2:
                        t.wait_ge(vt, VTW + m - 1)
                    ps = psA if m % 2 == 0 else psB
                    for kt in range(32):
                        mmi = t.matmul(ps[:],
                                       wsl[m % 2][:, kt * 128:(kt + 1) * 128],
                                       slz[:, kt * 512:(kt + 1) * 512],
                                       start=(kt == 0), stop=(kt == 31))
                    mmi.then_inc(tv, 1)

                def stats(k):
                    t.wait_ge(vt, VTW + k + 1)
                    t.wait_ge(sv, SVW + k + 1)
                    xsl = xog[:, k * 512:(k + 1) * 512]
                    t.matmul(psS[:], onescf[:], xsl, start=(k == 0),
                             stop=(k == 15))
                    t.matmul(psQ[:], onescf[:],
                             sqb[:, (k % 2) * 512:(k % 2) * 512 + 512],
                             start=(k == 0), stop=(k == 15))
                    t.matmul(psW2[:], wrts[:, k:k + 1], xsl,
                             start=(k == 0), stop=(k == 15)).then_inc(tv, 1)

                grp(0)
                grp(1)
                for m in range(2, 16):
                    stats(m - 2)
                    grp(m)
                stats(14)
                stats(15)
                t.wait_ge(vt, VTW + 18)
                t.matmul(psM[:], onesrf[:], mrow[:], start=True, stop=True)
                t.matmul(psR[:], onesrf[:], rrow[:], start=True, stop=True
                         ).then_inc(tv, 1)

            @blk.vector
            def _(v, chx=chx, TVW=TVW, VTW=VTW, SVW=SVW, LRW=LRW, SXOC=SXOC):
                for m in range(16):
                    v.wait_ge(tv, TVW + gincw(m))
                    v.wait_ge(lr, LRW + (m // 2 + 1) * 32)
                    if m == 0:
                        v.wait_ge(sxo, SXOC)
                    ps = psA if m % 2 == 0 else psB
                    v.scalar_tensor_tensor(xog[:, m * 512:(m + 1) * 512],
                                           ps[:], 1.0, xrb[m % 2][:],
                                           ALU.mult, ALU.add).then_inc(vt, 1)
                v.wait_ge(tv, TVW + 32)
                VE(v, v.tensor_scalar_mul(mrow[:], psS[0:1, :], 1.0 / D))
                VE(v, v.tensor_scalar_mul(vrow[:], psQ[0:1, :], 1.0 / D))
                VE(v, v.tensor_mul(srow[:], mrow[:], mrow[:]))
                VE(v, v.tensor_sub(vrow[:], vrow[:], srow[:]))
                v.tensor_scalar_add(vrow[:], vrow[:], EPS_LN
                                    ).then_inc(vt, 1)
                v.wait_ge(sv, SVW + 17)
                VE(v, v.reciprocal(rrow[:], rrow[:]))
                VE(v, v.tensor_scalar(srow[:], mrow[:], w1[0:1, 0:1], None,
                                      ALU.mult))
                VE(v, v.tensor_tensor(srow[:], psW2[0:1, :], srow[:],
                                      ALU.subtract))
                VE(v, v.tensor_mul(scR[:, chx * 512:(chx + 1) * 512],
                                   srow[:], rrow[:]))
                v.sem_inc(vt, 1)
                v.wait_ge(tv, TVW + 33)
                for pt in range(NTC):
                    VE(v, v.tensor_tensor(scr5[:, 0:512],
                                          xog[:, pt * 512:(pt + 1) * 512],
                                          psM[:], ALU.subtract))
                    VE(v, v.tensor_tensor(lnfm[:, pt * 512:(pt + 1) * 512],
                                          scr5[:, 0:512], psR[:], ALU.mult))
                    v.sem_inc(vt, 1)

            @blk.scalar
            def _(s, TVW=TVW, VTW=VTW):
                for m in range(16):
                    s.wait_ge(vt, VTW + m + 1)
                    if m >= 2:
                        s.wait_ge(tv, TVW + 2 * m - 1)
                    s.square(sqb[:, (m % 2) * 512:(m % 2) * 512 + 512],
                             xog[:, m * 512:(m + 1) * 512]).then_inc(sv, 1)
                s.wait_ge(vt, VTW + 17)
                s.sqrt(rrow[:], vrow[:]).then_inc(sv, 1)
            G4[0] += 8 * 16
            G4[1] += 8 * 16
            C["tv"] += 33
            C["vt"] += 16 + 2 + 16
            C["sv"] += 17
            C["lr"] += 16 * 16
            C["sxo"] += 16 * 16
            C["sst"] += 16 * 16

        VT_F = C["vt"]

        @blk.sync
        def _(sy):
            sy.wait_ge(vt, VT_F)
            sy.dma_start(out=sc2_d, in_=scR[:]).then_inc(sst, 16)
            sy.wait_ge(sst, C["sst"] + 16)
            sy.wait_ge(sxo, C["sxo"])
            sy.wait_ge(so, C["so"])
        C["sst"] += 16
    return nc



def build_p7():
    nc = bass.Bass("TRN2", target_bir_lowering=False)
    x_d = nc.dram_tensor("x", [TC, D], F32, kind="ExternalInput").ap()
    procf = nc.dram_tensor("procf", [SSEL, D], F32, kind="ExternalInput").ap()
    cidx_d = nc.dram_tensor("cidx_own", [128, NTC], F32, kind="ExternalInput").ap()
    msc_d = nc.dram_tensor("myscores", [128, NTC], F32, kind="ExternalInput").ap()
    out_d = nc.dram_tensor("out", [TC, D], F32, kind="ExternalOutput").ap()

    es = ExitStack()
    with es:
        xt = [es.enter_context(nc.sbuf_tensor(f"xt7{i}", [128, D], F32))
              for i in range(2)]
        ot = [es.enter_context(nc.sbuf_tensor(f"ot7{i}", [128, D], F32))
              for i in range(2)]
        stage = es.enter_context(nc.sbuf_tensor("stage7", [128, D], F32))
        lnf = es.enter_context(nc.sbuf_tensor("lnf7", [128, D], F32))
        cidxs = es.enter_context(nc.sbuf_tensor("cidxs7", [128, NTC], F32))
        cidxi = es.enter_context(nc.sbuf_tensor("cidxi7", [128, 1], I32))
        mscs = es.enter_context(nc.sbuf_tensor("mscs7", [128, NTC], F32))
        ssq = es.enter_context(nc.sbuf_tensor("ssq7", [128, 1], F32))
        sx = es.enter_context(nc.sbuf_tensor("sx7", [128, 1], F32))
        m_ = es.enter_context(nc.sbuf_tensor("m7", [128, 1], F32))
        var = es.enter_context(nc.sbuf_tensor("var7", [128, 1], F32))
        sd = es.enter_context(nc.sbuf_tensor("sd7", [128, 1], F32))
        rstd = es.enter_context(nc.sbuf_tensor("rstd7", [128, 1], F32))
        nb = es.enter_context(nc.sbuf_tensor("nb7", [128, 1], F32))
        t0 = es.enter_context(nc.sbuf_tensor("t07", [128, 1], F32))
        g8 = es.enter_context(nc.sbuf_tensor("g87", [128, 1], F32))
        coef = es.enter_context(nc.sbuf_tensor("coef7", [128, 1], F32))

        ld = es.enter_context(nc.semaphore("ld7"))
        lxs = [es.enter_context(nc.semaphore(f"lxs7{i}")) for i in range(2)]
        gw = es.enter_context(nc.semaphore("gw7"))
        vt = es.enter_context(nc.semaphore("vt7"))
        sv = es.enter_context(nc.semaphore("sv7"))
        vself = es.enter_context(nc.semaphore("vself7"))
        sol = [es.enter_context(nc.semaphore(f"sol7{i}")) for i in range(2)]
        blk = es.enter_context(nc.Block())

        ve_n = [0]

        def VE(v, inst):
            ve_n[0] += 1
            inst.then_inc(vself, 1)
            v.wait_ge(vself, ve_n[0])

        @blk.sync
        def _(sy):
            sy.dma_start(out=cidxs[:], in_=cidx_d).then_inc(ld, 16)
            sy.dma_start(out=mscs[:], in_=msc_d).then_inc(ld, 16)
            for j in range(NTC):
                if j >= 2:
                    sy.wait_ge(vt, 5 * (j - 1))
                sy.dma_start(out=xt[j % 2][:],
                             in_=x_d[j * 128:(j + 1) * 128, :]
                             ).then_inc(lxs[j % 2], 16)

        @blk.vector
        def _(v):
            v.wait_ge(ld, 32)
            VE(v, v.memset(stage[:], 0.0))
            for j in range(NTC):
                xb = xt[j % 2]
                v.wait_ge(lxs[j % 2], (j // 2 + 1) * 16)
                VE(v, v.scalar_tensor_tensor(lnf[:], xb[:], 1.0, xb[:],
                                             ALU.mult, ALU.mult,
                                             accum_out=ssq[:]))
                VE(v, v.reduce_sum(sx[:], xb[:], axis=AX.X))
                VE(v, v.tensor_scalar_mul(m_[:], sx[:], 1.0 / D))
                VE(v, v.tensor_scalar_mul(var[:], ssq[:], 1.0 / D))
                VE(v, v.tensor_mul(t0[:], m_[:], m_[:]))
                VE(v, v.tensor_sub(var[:], var[:], t0[:]))
                v.tensor_scalar_add(var[:], var[:], EPS_LN).then_inc(vt, 1)
                v.wait_ge(sv, 2 * j + 1)
                VE(v, v.reciprocal(rstd[:], sd[:]))
                VE(v, v.tensor_mul(nb[:], m_[:], rstd[:]))
                VE(v, v.tensor_scalar_mul(nb[:], nb[:], -1.0))
                VE(v, v.tensor_scalar(lnf[:], xb[:], rstd[:], nb[:],
                                      ALU.mult, ALU.add))
                VE(v, v.tensor_scalar(t0[:], cidxs[:, j:j + 1], 4096.0,
                                      None, ALU.min))
                v.tensor_copy(cidxi[:], t0[:]).then_inc(vt, 1)
                VE(v, v.tensor_scalar(coef[:], cidxs[:, j:j + 1], 2048.0,
                                      None, ALU.is_lt))
                v.wait_ge(sv, 2 * j + 2)
                VE(v, v.tensor_mul(coef[:], coef[:], g8[:]))
                VE(v, v.tensor_scalar_mul(coef[:], coef[:], 0.5))
                v.wait_ge(gw, (j + 1) * 16)
                VE(v, v.tensor_sub(stage[:], stage[:], lnf[:]))
                VE(v, v.tensor_scalar(stage[:], stage[:], coef[:], None,
                                      ALU.mult))
                VE(v, v.scalar_tensor_tensor(stage[:], lnf[:], 0.5, stage[:],
                                             ALU.mult, ALU.add))
                v.wait_ge(sol[j % 2], (j // 2) * 16)
                VE(v, v.tensor_tensor(ot[j % 2][:], xb[:], stage[:], ALU.add))
                v.sem_inc(vt, 1)
                v.memset(stage[:], 0.0).then_inc(vt, 2)

        @blk.scalar
        def _(s):
            for j in range(NTC):
                s.wait_ge(vt, 5 * j + 1)
                s.sqrt(sd[:], var[:]).then_inc(sv, 1)
                s.activation(g8[:], mscs[:, j:j + 1], AF.Sigmoid).then_inc(sv, 1)

        @blk.gpsimd
        def _(g):
            for j in range(NTC):
                g.wait_ge(vt, 5 * j + 2)
                g.indirect_dma_start(
                    stage[:], None, procf,
                    IndirectOffsetOnAxis(cidxi[:], 0),
                    bounds_check=SSEL - 1, oob_is_err=False,
                ).then_inc(gw, 16)
                g.wait_ge(vt, 5 * j + 3)
                g.dma_start(out=out_d[j * 128:(j + 1) * 128, :],
                            in_=ot[j % 2][:]).then_inc(sol[j % 2], 16)
            g.wait_ge(sol[0], 8 * 16)
            g.wait_ge(sol[1], 8 * 16)
    return nc


def _get_prog(name, builder):
    if name not in _PROG:
        _PROG[name] = builder()
    return _PROG[name]


def _launch(name, builder, in_maps, **kw):
    nc = _get_prog(name, builder)
    res = run_bass_kernel_spmd(nc, in_maps, core_ids=list(range(NC)), **kw)
    return res.results



_WSPEC = [("wq", 16, 2048), ("wk", 8, 2048), ("wv", 8, 2048),
          ("wo", 16, 2048), ("wg", 64, 2048), ("wu", 64, 2048),
          ("wd", 64, 2048), ("wi", 64, 2048), ("cv", 32, 4096),
          ("wo2", 16, 4096)]


def _wcat_build(tensors):
    return np.ascontiguousarray(np.concatenate(
        [t.transpose(1, 0, 2).reshape(128, -1) for t in tensors], axis=1))


def _wcat_split(wcb):
    out = {}
    off = 0
    for name, m, k in _WSPEC:
        w = wcb[:, off:off + m * k]
        off += m * k
        out[name] = np.ascontiguousarray(
            w.reshape(128, m, k).transpose(1, 0, 2))
    return out


def _slabs_out(w):
    K, M = w.shape
    return _np_f32(w.reshape(K // 128, 128, M // 128, 128
                             ).transpose(2, 1, 0, 3).reshape(M // 128, 128, K))


def _make_costab():
    inv_freq = 1.0 / (10000.0 ** (np.arange(0, HD, 2, dtype=np.float64) / HD))
    ang = np.arange(SEQ, dtype=np.float64)[:, None] * inv_freq[None, :]
    cos = np.cos(ang).astype(np.float32)
    sin = np.sin(ang).astype(np.float32)
    return _np_f32(np.concatenate(
        [cos, cos, sin, sin], axis=1))


def _perm_tiles(c):
    rest = [t for t in range(NTC) if t not in (c, c + 8)]
    return [c, c + 8] + rest


def _masks_for_core(c):
    pt = _perm_tiles(c)
    ktrue = np.concatenate([np.arange(t * 128, (t + 1) * 128) for t in pt])
    masks = np.zeros((2, 128, SSEL), np.float32)
    for a, tile in enumerate([c, c + 8]):
        qtrue = tile * 128 + np.arange(128)
        masks[a] = np.where(ktrue[None, :] <= qtrue[:, None], 0.0, BIG_NEG)
    return _np_f32(masks)


def _fm16(vec):
    return _np_f32(np.asarray(vec).reshape(NTC, 128).T)


def kernel(**inputs):
    xin = np.asarray(inputs["L0_attn_hidden_states"], dtype=np.float32)
    xf = np.ascontiguousarray(xin.reshape(TT, D))

    w_router = np.asarray(inputs["L0_ffn_w_router"], np.float32)
    wrt_tiled = _np_f32(np.tile(w_router[None, :], (128, 1)))
    wrt_fm = _fm16(w_router)

    wq_l = _slabs_out(np.asarray(inputs["L0_ffn_wq"], np.float32))
    wk_l = _slabs_out(np.asarray(inputs["L0_ffn_wk"], np.float32))
    wv_l = _slabs_out(np.asarray(inputs["L0_ffn_wv"], np.float32))
    wo_l = _slabs_out(np.asarray(inputs["L0_ffn_wo"], np.float32))
    wg_l = _slabs_out(np.asarray(inputs["L0_ffn_w_gate"], np.float32))
    wu_l = _slabs_out(np.asarray(inputs["L0_ffn_w_up"], np.float32))
    wd_l = _np_f32(np.asarray(inputs["L0_ffn_w_down"], np.float32
                              ).reshape(64, 128, D))
    bq_fm = _fm16(inputs["L0_ffn_bq"])
    bk_fm = _np_f32(np.asarray(inputs["L0_ffn_bk"], np.float32).reshape(8, 128).T)
    bv_fm = _np_f32(np.asarray(inputs["L0_ffn_bv"], np.float32).reshape(8, 128).T)
    ln1b = _np_f32(np.tile(np.asarray(inputs["L0_ffn_ln1_w"], np.float32
                                      )[None, :], (128, 1)))
    ln2_fm = _fm16(inputs["L0_ffn_ln2_w"])
    costab = _make_costab()

    wi_l = _slabs_out(np.asarray(inputs["L0_attn_W_in"], np.float32).T)
    cv_l = _slabs_out(np.asarray(inputs["L0_attn_conv_w"], np.float32).T)
    wo2_l = _slabs_out(np.asarray(inputs["L0_attn_W_out"], np.float32).T)
    dcol = _np_f32(np.asarray(inputs["L0_attn_D"], np.float32).reshape(32, 128).T)

    jc_cores = []
    for c in range(NC):
        jcv = np.zeros((128, 256), np.float32)
        jcv[:, 0:128] = np.arange(c * 128, (c + 1) * 128, dtype=np.float32)[None, :]
        jcv[:, 128:256] = np.arange((c + 8) * 128, (c + 9) * 128,
                                    dtype=np.float32)[None, :]
        jc_cores.append(_np_f32(jcv))

    def ffn_round(ssb, lnx_full, x_tm_cores, myscores_cores):
        r2 = _launch("p2", build_p2,
                     [{"ssb": ssb, "lnxf": lnx_full, "jc": jc_cores[c]}
                      for c in range(NC)])
        cidx = np.asarray(r2[0]["cidxo"], np.float32)
        inv_full = np.zeros(SSEL, np.int32)
        sel_full = np.zeros((SSEL, D), bf16)
        for c in range(NC):
            inv = np.asarray(r2[c]["inv"])
            inv_full[c * 128:(c + 1) * 128] = inv[:, 0]
            inv_full[(c + 8) * 128:(c + 9) * 128] = inv[:, 1]
            selr = np.asarray(r2[c]["selr"])
            sel_full[c * 128:(c + 1) * 128] = selr[0]
            sel_full[(c + 8) * 128:(c + 9) * 128] = selr[1]
        in3 = []
        for c in range(NC):
            pt = _perm_tiles(c)
            rows = np.concatenate([np.arange(t * 128, (t + 1) * 128)
                                   for t in pt])
            sel_p = np.ascontiguousarray(sel_full[rows])
            inv_p = np.ascontiguousarray(
                np.stack([inv_full[t * 128:(t + 1) * 128] for t in pt],
                         axis=1))
            in3.append({
                "sel": sel_p, "invp": inv_p, "costab": costab,
                "mask": _masks_for_core(c),
                "wq_l": wq_l, "wk_l": wk_l, "wv_l": wv_l, "wo_l": wo_l,
                "wg_l": wg_l, "wu_l": wu_l, "wd_l": wd_l,
                "bq_fm": bq_fm, "bk_fm": bk_fm, "bv_fm": bv_fm,
                "ln1b": ln1b, "ln2_fm": ln2_fm,
            })
        r3 = _launch("p3", build_p3, in3)
        procf = np.zeros((SSEL, D), np.float32)
        for c in range(NC):
            pr = np.asarray(r3[c]["proc"])
            procf[c * 128:(c + 1) * 128] = pr[0]
            procf[(c + 8) * 128:(c + 9) * 128] = pr[1]
        return procf, cidx

    wcat = _wcat_build([wq_l, wk_l, wv_l, wo_l, wg_l, wu_l, wd_l,
                        wi_l, cv_l, wo2_l])
    r1 = _launch("p1", build_p1,
                 [{"x": np.ascontiguousarray(xf[c * TC:(c + 1) * TC]),
                   "wrt": wrt_tiled,
                   "wcat": np.ascontiguousarray(
                       wcat[:, c * WCOLS:(c + 1) * WCOLS])}
                  for c in range(NC)])
    wb = _wcat_split(np.concatenate(
        [np.asarray(r1[c]["wcb"]) for c in range(NC)], axis=1))
    wq_l, wk_l, wv_l, wo_l = wb["wq"], wb["wk"], wb["wv"], wb["wo"]
    wg_l, wu_l, wd_l = wb["wg"], wb["wu"], wb["wd"]
    wi_l, cv_l, wo2_l = wb["wi"], wb["cv"], wb["wo2"]
    lnx_full = np.ascontiguousarray(
        np.concatenate([np.asarray(r1[c]["lnx"]) for c in range(NC)], axis=0))
    ssb = np.ascontiguousarray(
        np.concatenate([np.asarray(r1[c]["sc"]) for c in range(NC)], axis=1))
    msc1 = [np.ascontiguousarray(np.asarray(r1[c]["sc"])) for c in range(NC)]

    procf1, cidx1 = ffn_round(ssb, lnx_full, None, msc1)

    in4 = []
    for c in range(NC):
        in4.append({
            "x": np.ascontiguousarray(xf[c * TC:(c + 1) * TC]),
            "procf": procf1,
            "cidx_own": np.ascontiguousarray(cidx1[:, c * NTC:(c + 1) * NTC]),
            "myscores": msc1[c],
            "wrt_fm": wrt_fm,
            "wi_l": wi_l, "cv_l": cv_l, "wo2_l": wo2_l, "dcol": dcol,
        })
    r4 = _launch("p4", build_p4, in4)
    x2_cores = []
    lnx2_parts = []
    msc2 = []
    ssb2_cols = []
    for c in range(NC):
        xo = np.asarray(r4[c]["xo_fm"])
        x2_cores.append(np.ascontiguousarray(
            xo.transpose(2, 0, 1).reshape(TC, D)))
        l2fm = np.asarray(r4[c]["lnx2_fm"])
        lnx2_parts.append(np.ascontiguousarray(
            l2fm.transpose(2, 0, 1).reshape(TC, D)))
        sc2 = np.asarray(r4[c]["sc2"]).reshape(NTC, 128).T
        msc2.append(np.ascontiguousarray(sc2))
        ssb2_cols.append(sc2)
    lnx2_full = np.ascontiguousarray(np.concatenate(lnx2_parts, axis=0))
    ssb2 = np.ascontiguousarray(np.concatenate(ssb2_cols, axis=1))

    procf2, cidx2 = ffn_round(ssb2, lnx2_full, None, msc2)

    in7 = []
    for c in range(NC):
        in7.append({
            "x": x2_cores[c],
            "procf": procf2,
            "cidx_own": np.ascontiguousarray(cidx2[:, c * NTC:(c + 1) * NTC]),
            "myscores": msc2[c],
        })
    r7 = _launch("p7", build_p7, in7)
    out = np.concatenate([np.asarray(r7[c]["out"]) for c in range(NC)], axis=0)
    return np.ascontiguousarray(out.reshape(B, SEQ, D), dtype=np.float32)
